# version 03
# baseline (speedup 1.0000x reference)
import math

import numpy as np
import ml_dtypes

import concourse.bass as bass
import concourse.mybir as mybir
from concourse.bass_utils import run_bass_kernel_spmd

F32 = mybir.dt.float32
BF16 = mybir.dt.bfloat16
I32 = mybir.dt.int32
AF = mybir.ActivationFunctionType
ALU = mybir.AluOpType
BF = ml_dtypes.bfloat16

NCORE = 8
D = 4096
DI = 512
DFI = 2048
FF = 11008
NFT = FF // 128
NKD = D // 128
NLAYER = 4
EPS = 1e-5
FCH = 40

_TP = 2 * math.pi
CW_C1 = float(np.float32(np.ldexp(np.round(np.ldexp(_TP, 8)), -8)))
CW_C2 = float(np.float32(np.ldexp(np.round(np.ldexp(_TP - CW_C1, 19)), -19)))
CW_C3 = float(np.float32(_TP - CW_C1 - CW_C2))
MAGIC = float(np.float32(1.5 * 2.0 ** 23))
INV2PI = float(np.float32(1.0 / _TP))
PI = float(np.float32(math.pi))
PI_2 = float(np.float32(math.pi / 2))

_DEBUG = []


class KB:
    def __init__(self, T):
        self.T = T
        self.NCK = T // 512
        self.NTT = T // 128
        nc = bass.Bass("TRN2", target_bir_lowering=False)
        self.nc = nc
        self.q = {e: [] for e in ("sync", "gpsimd", "tensor", "vector", "scalar")}
        self.semh, self.semn = {}, {}
        for s in ("dg", "ds", "tm", "vm", "am", "cc"):
            self.semh[s] = nc.alloc_semaphore("sem_" + s)
            self.semn[s] = 0
        self.bank_w = [("tm", 0)] * 8
        self.bank_r = [("tm", 0)] * 8
        self.pc_items = []
        self.pc_mark = {}
        self.dumped_stats = False

    def emit(self, eng, fn):
        self.q[eng].append(fn)

    def W(self, eng, sem, val):
        if val:
            self.emit(eng, lambda e, h=self.semh[sem], v=val: e.wait_ge(h, v))

    def bump(self, sem, n):
        self.semn[sem] += n
        return self.semn[sem]

    def DMA(self, eng, dst, src, transpose=False):
        sem = {"sync": "ds", "gpsimd": "dg"}[eng]
        self.bump(sem, 16)
        self.emit(eng, lambda e, d=dst, s=src, h=self.semh[sem], t=transpose:
                  e.dma_start(d, s, transpose=t).then_inc(h, 16))
        return self.semn[sem]

    def O(self, eng, meth, *args, inc=False, drain=False, **kw):
        sem = {"tensor": "tm", "vector": "vm", "scalar": "am", "gpsimd": "vm"}[eng]
        if inc:
            self.bump(sem, 1)
        dflush = inc and eng in ("vector", "scalar", "gpsimd")

        def f(e, meth=meth, args=args, kw=kw, inc=inc, drain=drain,
              dflush=dflush, h=self.semh[sem]):
            if drain:
                e.drain()
            r = getattr(e, meth)(*args, **kw)
            if inc and dflush:
                e.drain().then_inc(h, 1)
            elif inc:
                r.then_inc(h, 1)
        self.emit(eng, f)
        return self.semn[sem]

    def mm_group(self, bank, pairs, waits=()):
        s, v = self.bank_r[bank]
        self.W("tensor", s, v)
        for (ss, vv) in waits:
            self.W("tensor", ss, vv)
        n = len(pairs)
        for i, (l, r) in enumerate(pairs):
            self.O("tensor", "matmul", self.psum[bank][:, :], l, r,
                   start=(i == 0), stop=(i == n - 1), inc=(i == n - 1))
        self.bank_w[bank] = ("tm", self.semn["tm"])
        return self.semn["tm"]

    def evac(self, bank, eng, emit_cb, waits=()):
        s, v = self.bank_w[bank]
        self.W(eng, s, v)
        for (ss, vv) in waits:
            self.W(eng, ss, vv)
        emit_cb()
        sem = {"vector": "vm", "scalar": "am"}[eng]
        self.bank_r[bank] = (sem, self.semn[sem])
        return self.semn[sem]

    def build(self):
        nc, T, NCK, NTT = self.nc, self.T, self.NCK, self.NTT
        W, O, DMA, E = self.W, self.O, self.DMA, self.emit
        dt = nc.dram_tensor

        x_in = dt("x_in", [T, D], F32, kind="ExternalInput").ap()
        winT = dt("winT", [128, NKD * DI], F32, kind="ExternalInput").ap()
        w1h = dt("w1h", [NLAYER, 128, 4 * DFI], F32, kind="ExternalInput").ap()
        w2h = dt("w2h", [NLAYER, 128, 16 * DI], F32, kind="ExternalInput").ap()
        woutT = dt("woutT", [128, 4 * D], F32, kind="ExternalInput").ap()
        gateT = dt("gateT", [NFT, 128, D], I32, kind="ExternalInput").ap()
        upT = dt("upT", [NFT, 128, D], I32, kind="ExternalInput").ap()
        downT = dt("downT", [NFT, 128, D], I32, kind="ExternalInput").ap()
        PCOLS = [("ln1w", 16), ("ln1b", 16), ("ln2w", 16), ("ln2b", 16),
                 ("scl", 16), ("b2", 16), ("pre", 16), ("pim", 16),
                 ("lcr", 16), ("lci", 16), ("b1", 64), ("bin", 4),
                 ("lnlw", 4), ("lnlb", 4), ("bout", 32), ("gs", NFT),
                 ("us", NFT), ("ds", 32), ("toff", 1), ("mask", 8)]
        pin = {nm: dt("pp_" + nm, [128, n], F32, kind="ExternalInput").ap()
               for nm, n in PCOLS}
        out_d = dt("out", [T, D], F32, kind="ExternalOutput").ap()
        dbg_aps = {nm: dt("dbg_" + nm, list(sh), F32, kind="ExternalOutput").ap()
                   for nm, sh in _DEBUG}

        g_scr = dt("g_scr", [NFT, 128, D], BF16, kind="Internal").ap()
        u_scr = dt("u_scr", [NFT, 128, D], BF16, kind="Internal").ap()
        d_scr = dt("d_scr", [NFT, 128, D], BF16, kind="Internal").ap()
        xlna_tm = dt("xlna_tm", [T, D], BF16, kind="Internal").ap()
        ao_fm = dt("ao_fm", [D, T], BF16, kind="Internal").ap()
        x1_tm = dt("x1_tm", [T, D], F32, kind="Internal").ap()
        xlnb_tm = dt("xlnb_tm", [T, D], BF16, kind="Internal").ap()
        ffo_fm = dt("ffo_fm", [D, T], BF16, kind="Internal").ap()
        ncc = NLAYER
        cc_i = [dt(f"cc_i{k}", [128, 8], F32, kind="Internal").ap() for k in range(ncc)]
        cc_o = [dt(f"cc_o{k}", [NCORE * 128, 8], F32, kind="Internal").ap()
                for k in range(ncc)]

        pos = [16384]

        def place(name, shape, dtype):
            nb = int(np.prod(shape[1:])) * (4 if dtype != BF16 else 2)
            at = pos[0]
            pos[0] = at + ((nb + 63) // 64) * 64
            return nc.alloc_sbuf_tensor_at(name, list(shape), dtype, offset=at), at

        def over(name, shape, dtype, at):
            return nc.alloc_sbuf_tensor_at(name, list(shape), dtype, offset=at)

        TB = T * 4
        h_t = []
        H0 = pos[0]
        for c in range(4):
            h_t.append(place(f"h{c}", [128, T], F32)[0])
        tglob, TG0 = place("tglob", [128, T], F32)
        params, _ = place("params", [128, 1024], F32)
        statsf, _ = place("statsf", [128, 64], F32)
        var_scr, VS0 = place("var_scr", [1, T], F32)
        ccg, _ = place("ccg", [128, 64], F32)
        czt, _ = place("czt", [128, 8], F32)
        dpk2, _ = place("dpk2", [128, 8], F32)
        ccd, _ = place("ccd", [128, 64], F32)
        ones_bf, _ = place("ones_bf", [128, 1], BF16)
        onesr_f, _ = place("onesr_f", [1, 128], F32)
        nonesr_f, _ = place("nonesr_f", [1, 128], F32)
        c_pi2, _ = place("c_pi2", [128, 1], F32)
        c_eps, _ = place("c_eps", [128, 1], F32)
        c_zero, _ = place("c_zero", [128, 1], F32)
        state_pk, _ = place("state_pk", [128, 8], F32)
        STG0 = pos[0]
        stg = [place(f"stg{s}", [128, D], BF16)[0] for s in range(2)]
        aoev = [place(f"aoev{s}", [128, 512], BF16)[0] for s in range(2)]

        SCAN0 = pos[0]
        scn = [place(f"scn{i}", [128, T], F32)[0] for i in range(4)]
        SC_COS = [scn[0], scn[2]]
        SC_SIN = [scn[1], scn[3]]
        if pos[0] - SCAN0 < 32768:
            pos[0] = SCAN0 + 32768
        WA0 = pos[0]
        w1sb, _ = place("w1sb", [128, 4 * DFI], BF16)
        w2sb, _ = place("w2sb", [128, 16 * DI], BF16)
        MM0 = pos[0]
        yb = [place(f"yb{c}", [128, T], BF16)[0] for c in range(4)]
        hb = [place(f"hb{s}", [128, T], BF16)[0] for s in range(2)]
        ffn1, _ = place("ffn1", [128, 16 * 512], BF16)
        sc1, SC1_0 = place("sc1", [128, T], F32)
        sc2, SC2_0 = place("sc2", [128, T], F32)
        if pos[0] - MM0 < 49152:
            pos[0] = MM0 + 49152
        END0 = pos[0]
        assert END0 <= 229376, f"sbuf static overflow: {END0}"
        SCAN_B = WA0 - SCAN0
        MM_B = END0 - MM0

        xtm = [over(f"xtm{i}", [128, D], F32, SCAN0 + i * 16384) for i in range(2)]
        xtm4 = [over(f"xtm4_{i}", [128, D], F32, H0 + i * 16384) for i in range(2)]
        aotm4 = over("aotm4", [128, D], BF16, TG0)
        jnkb4 = [over(f"jnkb4_{i}", [128, D], BF16, STG0 + i * 8192)
                 for i in range(2)]
        assert SCAN_B >= 2 * 16384
        aotm = over("aotm", [128, D], BF16, MM0)
        jnkb = over("jnkb", [128, D], BF16, MM0 + 8192)
        winsb = over("winsb", [128, NKD * DI], BF16, MM0 + 16384)
        xlnw = over("xlnw", [128, NKD * 512], BF16, SCAN0)
        woutsb = over("woutsb", [128, 4 * D], BF16, SCAN0)
        xlnffn = over("xlnffn", [128, NKD * 512], BF16, SCAN0)
        gusb = [over(f"gusb{s}", [128, D], BF16, WA0 + s * 8192) for s in range(4)]
        assert WA0 + 4 * 8192 <= END0 + 4096
        if T * 4 >= 6144:
            dsb = [over(f"dsb{s}", [128, 512], BF16, VS0 + s * 1024)
                   for s in range(4)]
            gsil = [over(f"gsil{s}", [128, 512], BF16, VS0 + 4096 + s * 1024)
                    for s in range(2)]
        else:
            dsb = [place(f"dsb{s}", [128, 512], BF16)[0] for s in range(4)]
            gsil = [place(f"gsil{s}", [128, 512], BF16)[0] for s in range(2)]
        hch = over("hch", [128, FCH * 512], BF16, MM0)
        if T == 2048:
            acc_off = []
            for i in range(5 * TB // 2048):
                acc_off.append(H0 + i * 2048)
            need = NKD - len(acc_off)
            for i in range(need):
                acc_off.append(MM0 + FCH * 1024 + i * 2048)
            assert MM0 + FCH * 1024 + need * 2048 <= 229376
            accum = [over(f"acc{i}", [128, 512], F32, acc_off[i])
                     for i in range(NKD)]
        else:
            acc_off = [H0 + i * 2048 for i in range(5 * TB // 2048)]
            acc_off += [MM0 + FCH * 1024 + i * 2048
                        for i in range((49152 - FCH * 1024) // 2048)]
            accum = [over(f"acc{i}", [128, 512], F32, acc_off[i])
                     if i < len(acc_off)
                     else place(f"acc{i}", [128, 512], F32)[0]
                     for i in range(NKD)]
            assert pos[0] <= 229376, pos[0]

        mu_row = over("mu_row", [1, T], F32, SC1_0)
        sq_row = over("sq_row", [1, T], F32, SC2_0)
        self.psum = [nc.alloc_psum_tensor(f"ps{i}", [128, 512], F32)
                     for i in range(8)]

        PC = {}
        pcn = [0]
        for nm, n in PCOLS + [("rho", 16), ("lnrho", 16), ("theta", 16),
                              ("cth", 16), ("sth", 16), ("ainit", 16),
                              ("binit", 16), ("dpk", 16), ("pb", 16),
                              ("t1", 16), ("t2", 16), ("t3", 16), ("t4", 16),
                              ("u1", 1)]:
            PC[nm] = pcn[0]
            pcn[0] += n
        assert pcn[0] <= 1024

        def P(nm, i=0, n=1):
            c = PC[nm] + i
            return params[:, c:c + n]

        def A16(nm):
            return P(nm, 0, 16)

        for nm, n in PCOLS:
            DMA("sync", P(nm, 0, n), pin[nm])
        v_pld = self.semn["ds"]

        O("vector", "memset", c_pi2[:, :], PI_2)
        O("vector", "memset", c_eps[:, :], float(EPS))
        O("vector", "memset", c_zero[:, :], 0.0)
        O("vector", "memset", ones_bf[:, :], 1.0 / DI)
        O("vector", "memset", onesr_f[:, :], 1.0)
        O("vector", "memset", nonesr_f[:, :], -1.0, inc=True)
        O("gpsimd", "iota", tglob[:, :], [[1, T]], base=0, channel_multiplier=0,
          allow_small_or_imprecise_dtypes=True, inc=True)
        W("vector", "vm", self.semn["vm"])
        W("vector", "ds", v_pld)
        O("vector", "tensor_scalar", tglob[:, :], tglob[:, :], P("toff"), None,
          ALU.add, ALU.bypass, inc=True)

        W("scalar", "ds", v_pld)
        W("scalar", "vm", self.semn["vm"])
        O("scalar", "activation", A16("t1"), A16("pre"), AF.Square, bias=c_zero[:, 0:1])
        O("scalar", "activation", A16("t2"), A16("pim"), AF.Square, drain=True,
          inc=True, bias=c_zero[:, 0:1])
        W("vector", "am", self.semn["am"])
        O("vector", "tensor_tensor", A16("t1"), A16("t1"), A16("t2"), ALU.add,
          drain=True, inc=True)
        W("scalar", "vm", self.semn["vm"])
        O("scalar", "activation", A16("t2"), A16("t1"), AF.Sqrt, drain=True,
          inc=True, bias=c_zero[:, 0:1])
        W("vector", "am", self.semn["am"])
        O("vector", "reciprocal", A16("t3"), A16("t2"), drain=True)
        O("vector", "tensor_tensor", A16("t3"), A16("t3"), A16("t1"), ALU.mult,
          drain=True)
        O("vector", "tensor_tensor", A16("t3"), A16("t3"), A16("t2"), ALU.add,
          drain=True)
        O("vector", "tensor_scalar", A16("t3"), A16("t3"), 0.5, None, ALU.mult,
          drain=True, inc=True)
        W("scalar", "vm", self.semn["vm"])
        O("scalar", "activation", A16("rho"), A16("t3"), AF.Tanh, drain=True,
          inc=True, bias=c_zero[:, 0:1])
        W("vector", "am", self.semn["am"])
        O("vector", "tensor_scalar", A16("t4"), A16("rho"), 1e-36, None,
          ALU.max, drain=True, inc=True)
        W("scalar", "vm", self.semn["vm"])
        O("scalar", "activation", A16("lnrho"), A16("t4"), AF.Ln, drain=True, bias=c_zero[:, 0:1])
        O("scalar", "activation", A16("dpk"), A16("lnrho"), AF.Exp,
          scale=float(T), drain=True, bias=c_zero[:, 0:1])
        O("scalar", "activation", A16("t1"), A16("pre"), AF.Abs, drain=True, bias=c_zero[:, 0:1])
        O("scalar", "activation", A16("t2"), A16("pim"), AF.Abs, drain=True,
          inc=True, bias=c_zero[:, 0:1])
        W("vector", "am", self.semn["am"])
        O("vector", "tensor_tensor", A16("t3"), A16("t2"), A16("t1"), ALU.is_gt,
          drain=True)
        O("vector", "tensor_tensor", A16("t4"), A16("t1"), A16("t2"), ALU.min,
          drain=True)
        O("vector", "tensor_tensor", A16("t2"), A16("t1"), A16("t2"), ALU.max,
          drain=True)
        O("vector", "tensor_scalar", A16("t2"), A16("t2"), 1e-30, None, ALU.max,
          drain=True)
        O("vector", "reciprocal", A16("t1"), A16("t2"), drain=True)
        O("vector", "tensor_tensor", A16("t1"), A16("t1"), A16("t4"), ALU.mult,
          drain=True, inc=True)
        W("scalar", "vm", self.semn["vm"])
        O("scalar", "activation", A16("t4"), A16("t1"), AF.Arctan, drain=True,
          inc=True, bias=c_zero[:, 0:1])
        W("vector", "am", self.semn["am"])
        O("vector", "tensor_scalar", A16("t1"), A16("t4"), -2.0, PI_2, ALU.mult,
          ALU.add, drain=True)
        O("vector", "tensor_tensor", A16("t1"), A16("t1"), A16("t3"), ALU.mult,
          drain=True)
        O("vector", "tensor_tensor", A16("t4"), A16("t4"), A16("t1"), ALU.add,
          drain=True)
        O("vector", "tensor_scalar", A16("t3"), A16("pre"), 0.0, None, ALU.is_lt,
          drain=True)
        O("vector", "tensor_scalar", A16("t1"), A16("t4"), -2.0, PI, ALU.mult,
          ALU.add, drain=True)
        O("vector", "tensor_tensor", A16("t1"), A16("t1"), A16("t3"), ALU.mult,
          drain=True)
        O("vector", "tensor_tensor", A16("t4"), A16("t4"), A16("t1"), ALU.add,
          drain=True)
        O("vector", "tensor_scalar", A16("t1"), A16("pim"), 0.0, None, ALU.is_gt,
          drain=True)
        O("vector", "tensor_scalar", A16("t2"), A16("pim"), 0.0, None, ALU.is_lt,
          drain=True)
        O("vector", "tensor_tensor", A16("t1"), A16("t1"), A16("t2"),
          ALU.subtract, drain=True)
        O("vector", "tensor_tensor", A16("theta"), A16("t4"), A16("t1"),
          ALU.mult, drain=True, inc=True)
        W("scalar", "vm", self.semn["vm"])
        O("scalar", "activation", A16("sth"), A16("theta"), AF.Sin, drain=True, bias=c_zero[:, 0:1])
        O("scalar", "activation", A16("t1"), A16("theta"), AF.Abs, drain=True, bias=c_zero[:, 0:1])
        O("scalar", "activation", A16("cth"), A16("t1"), AF.Sin, scale=-1.0,
          bias=c_pi2[:, 0:1], drain=True, inc=True)
        W("vector", "am", self.semn["am"])
        O("vector", "tensor_tensor", A16("t1"), A16("cth"), A16("lcr"), ALU.mult,
          drain=True)
        O("vector", "tensor_tensor", A16("t2"), A16("sth"), A16("lci"), ALU.mult,
          drain=True)
        O("vector", "tensor_tensor", A16("ainit"), A16("t1"), A16("t2"),
          ALU.subtract, drain=True)
        O("vector", "tensor_tensor", A16("t1"), A16("sth"), A16("lcr"), ALU.mult,
          drain=True)
        O("vector", "tensor_tensor", A16("t2"), A16("cth"), A16("lci"), ALU.mult,
          drain=True)
        O("vector", "tensor_tensor", A16("binit"), A16("t1"), A16("t2"), ALU.add,
          drain=True)
        O("vector", "tensor_scalar", A16("binit"), A16("binit"), -1.0, None,
          ALU.mult, drain=True)
        O("vector", "tensor_scalar", A16("ainit"), A16("ainit"), P("mask", 7),
          None, ALU.mult, drain=True)
        O("vector", "tensor_scalar", A16("binit"), A16("binit"), P("mask", 7),
          None, ALU.mult, drain=True)
        O("vector", "tensor_scalar", P("u1"), P("toff"), -1.0, 1.0, ALU.mult,
          ALU.add, drain=True)
        O("vector", "tensor_scalar", A16("pb"), A16("lnrho"), P("u1"), None,
          ALU.mult, drain=True)
        O("vector", "engine_nop", inc=True)
        v_p0 = self.semn["vm"]
        if "theta" in dbg_aps:
            W("sync", "vm", v_p0)
            DMA("sync", dbg_aps["theta"], A16("theta"))
            DMA("sync", dbg_aps["rho"], A16("rho"))
            DMA("sync", dbg_aps["ainit"], A16("ainit"))
            DMA("sync", dbg_aps["lnrho"], A16("lnrho"))
            DMA("sync", dbg_aps["tg"], tglob[:, :])

        g_win = DMA("gpsimd", winsb[:, :], winT)
        g_w1 = {0: DMA("gpsimd", w1sb[:, :], w1h[0])}
        g_w2 = {0: DMA("gpsimd", w2sb[:, :], w2h[0])}

        def make_pc(tag, src, scr):
            last_wb = {}
            for ft in range(NFT):
                def item(ft=ft, tag=tag, src=src, scr=scr, last_wb=last_wb):
                    s = ft % 2
                    if ft >= 2:
                        W("gpsimd", "dg", last_wb[ft - 2])
                    ld = DMA("gpsimd", stg[s][:, :], src[ft])
                    W("gpsimd", "dg", ld)
                    last_wb[ft] = DMA("gpsimd", scr[ft], stg[s][:, :])
                    self.pc_mark[tag] = last_wb[ft]
                self.pc_items.append(item)

        make_pc("g", gateT, g_scr)
        make_pc("u", upT, u_scr)
        make_pc("d", downT, d_scr)

        def drain_pc(n):
            while n > 0 and self.pc_items:
                self.pc_items.pop(0)()
                n -= 1

        musum, sqsum = statsf[:, 0:16], statsf[:, 16:32]
        nmr, rst = statsf[:, 32:48], statsf[:, 48:64]

        def tm_stats(src_ap, junk_ap, col, ld, vwaits=()):
            c = slice(col, col + 1)
            W("vector", "ds", ld)
            for (ss, vv) in vwaits:
                W("vector", ss, vv)
            O("vector", "tensor_scalar", junk_ap, src_ap, 1.0, 0.0, ALU.mult,
              ALU.add, accum_out=musum[:, c], inc=True)
            W("scalar", "ds", ld)
            for (ss, vv) in vwaits:
                W("scalar", ss, vv)
            O("scalar", "activation", junk_ap, src_ap, AF.Square,
              accum_out=sqsum[:, c], inc=True, bias=c_zero[:, 0:1])
            a_sq = self.semn["am"]
            W("vector", "am", a_sq)
            O("vector", "tensor_scalar", musum[:, c], musum[:, c], 1.0 / D, None,
              ALU.mult, drain=True)
            O("vector", "tensor_scalar", sqsum[:, c], sqsum[:, c], 1.0 / D, None,
              ALU.mult, drain=True)
            O("vector", "tensor_tensor", rst[:, c], musum[:, c], musum[:, c],
              ALU.mult, drain=True)
            O("vector", "tensor_tensor", rst[:, c], sqsum[:, c], rst[:, c],
              ALU.subtract, drain=True, inc=True)
            W("scalar", "vm", self.semn["vm"])
            O("scalar", "activation", rst[:, c], rst[:, c], AF.Ln,
              bias=c_eps[:, 0:1], drain=True)
            O("scalar", "activation", rst[:, c], rst[:, c], AF.Exp, scale=-0.5,
              drain=True, inc=True, bias=c_zero[:, 0:1])
            a_r = self.semn["am"]
            W("vector", "am", a_r)
            O("vector", "tensor_tensor", nmr[:, c], musum[:, c], rst[:, c],
              ALU.mult, drain=True)
            O("vector", "tensor_scalar", nmr[:, c], nmr[:, c], -1.0, None,
              ALU.mult, drain=True, inc=True)
            return self.semn["vm"], a_r

        v_used, st_done = {}, {}
        a_used = {}
        for tt in range(NTT):
            sl = tt % 2
            if tt >= 2:
                W("sync", "vm", v_used[tt - 2])
                W("sync", "am", a_used[tt - 2])
            ld = DMA("sync", xtm[sl][:, :], x_in[tt * 128:(tt + 1) * 128, :])
            if tt == 0:
                W("vector", "ds", ld)
                W("vector", "vm", v_p0)
            col = tt % 16
            v_s, a_s = tm_stats(xtm[sl][:, :], jnkb[:, :], col, ld)
            a_used[tt] = a_s
            if tt >= 1:
                W("vector", "ds", st_done[tt - 1])
            cc = slice(col, col + 1)
            O("vector", "tensor_scalar", aotm[:, :], xtm[sl][:, :], rst[:, cc],
              nmr[:, cc], ALU.mult, ALU.add, inc=True)
            v_x = self.semn["vm"]
            v_used[tt] = v_x
            W("sync", "vm", v_x)
            st_done[tt] = DMA("sync", xlna_tm[tt * 128:(tt + 1) * 128, :],
                              aotm[:, :])
            if tt % 4 == 3:
                drain_pc(4)

        xb_mm = {}
        a_p2 = 0
        for k in range(NCK):
            W("sync", "dg", g_win)
            W("sync", "ds", st_done[4 * k + 3])
            if k >= 1:
                W("sync", "tm", xb_mm[k - 1])
            ldv = 0
            for kv in range(NKD):
                ldv = DMA("sync", xlnw[:, kv * 512:(kv + 1) * 512],
                          xlna_tm[k * 512:(k + 1) * 512,
                                  kv * 128:(kv + 1) * 128], transpose=True)
            for dtile in range(4):
                bank = dtile
                pairs = [(winsb[:, kv * 512 + dtile * 128:
                                kv * 512 + dtile * 128 + 128],
                          xlnw[:, kv * 512:(kv + 1) * 512]) for kv in range(NKD)]
                self.mm_group(bank, pairs, waits=[("ds", ldv)])
                a_p2 = self.evac(bank, "scalar", lambda b=bank, d=dtile, k=k:
                                 O("scalar", "activation",
                                   h_t[d][:, k * 512:(k + 1) * 512],
                                   self.psum[b][:, :], AF.Identity,
                                   bias=P("bin", d), inc=True))
            xb_mm[k] = self.semn["tm"]
            drain_pc(6)
        if "h_win" in dbg_aps:
            W("sync", "am", a_p2)
            for c in range(4):
                DMA("sync", dbg_aps["h_win"][c * 128:(c + 1) * 128, :],
                    h_t[c][:, :])

        def inner_ln_stats(extra_waits):
            t_c = {}
            for c in range(4):
                sl = c % 2
                if c == 0:
                    for (ss, vv) in extra_waits:
                        W("vector", ss, vv)
                if c >= 2:
                    W("vector", "tm", t_c[c - 2])
                O("vector", "tensor_copy", hb[sl][:, :], h_t[c][:, :], inc=True)
                v_cast = self.semn["vm"]
                W("scalar", "vm", v_cast)
                if c == 0:
                    for (ss, vv) in extra_waits:
                        W("scalar", ss, vv)
                else:
                    W("scalar", "tm", t_c[c - 1])
                O("scalar", "activation", ffn1[:, 0:T], hb[sl][:, :], AF.Square,
                  inc=True, bias=c_zero[:, 0:1])
                a_sqc = self.semn["am"]
                W("tensor", "vm", v_cast)
                W("tensor", "am", a_sqc)
                if c == 0:
                    for b in range(8):
                        s, v = self.bank_r[b]
                        W("tensor", s, v)
                for k in range(NCK):
                    O("tensor", "matmul", self.psum[k][0:1, :], ones_bf[:, :],
                      hb[sl][:, k * 512:(k + 1) * 512],
                      start=(c == 0), stop=(c == 3))
                for k in range(NCK):
                    O("tensor", "matmul", self.psum[4 + k][0:1, :],
                      ones_bf[:, :], ffn1[:, k * 512:(k + 1) * 512],
                      start=(c == 0), stop=(c == 3),
                      inc=(k == NCK - 1))
                t_c[c] = self.semn["tm"]
            t_stats = self.semn["tm"]
            for b in range(8):
                self.bank_w[b] = ("tm", t_stats)
            W("vector", "tm", t_stats)
            for k in range(NCK):
                O("vector", "tensor_copy", mu_row[0:1, k * 512:(k + 1) * 512],
                  self.psum[k][0:1, :])
                O("vector", "tensor_copy", sq_row[0:1, k * 512:(k + 1) * 512],
                  self.psum[4 + k][0:1, :])
            O("vector", "engine_nop", inc=True)
            v_ev = self.semn["vm"]
            for b in range(8):
                self.bank_r[b] = ("vm", v_ev)
            O("vector", "tensor_tensor", var_scr[0:1, 0:T], mu_row[0:1, 0:T],
              mu_row[0:1, 0:T], ALU.mult)
            O("vector", "tensor_tensor", sq_row[0:1, 0:T], sq_row[0:1, 0:T],
              var_scr[0:1, 0:T], ALU.subtract, inc=True)
            W("scalar", "vm", self.semn["vm"])
            O("scalar", "activation", sq_row[0:1, 0:T], sq_row[0:1, 0:T], AF.Ln,
              bias=c_eps[0:1, 0:1])
            O("scalar", "activation", sq_row[0:1, 0:T], sq_row[0:1, 0:T], AF.Exp,
              scale=-0.5, inc=True, bias=c_zero[0:1, 0:1])
            W("vector", "am", self.semn["am"])
            O("vector", "tensor_tensor", mu_row[0:1, 0:T], mu_row[0:1, 0:T],
              sq_row[0:1, 0:T], ALU.mult, inc=True)
            v_rr = self.semn["vm"]
            if "murow" in dbg_aps and not self.dumped_stats:
                self.dumped_stats = True
                W("sync", "vm", v_rr)
                dmp = DMA("sync", dbg_aps["murow"], mu_row[0:1, 0:T])
                dmp = DMA("sync", dbg_aps["sqrow"], sq_row[0:1, 0:T])
                W("vector", "ds", dmp)
            for k in range(NCK):
                self.mm_group(k, [(nonesr_f[:, :],
                                   mu_row[0:1, k * 512:(k + 1) * 512])],
                              waits=[("vm", v_rr)])
                self.mm_group(4 + k, [(onesr_f[:, :],
                                       sq_row[0:1, k * 512:(k + 1) * 512])])

        def normalize_to_sc1(c):
            for k in range(NCK):
                for b in (k, 4 + k):
                    s, v = self.bank_w[b]
                    W("vector", s, v)
                sl5 = slice(k * 512, (k + 1) * 512)
                O("vector", "tensor_tensor", sc1[:, sl5], h_t[c][:, sl5],
                  self.psum[4 + k][:, :], ALU.mult)
                O("vector", "tensor_tensor", sc1[:, sl5], sc1[:, sl5],
                  self.psum[k][:, :], ALU.add, inc=(k == NCK - 1))
            v_n = self.semn["vm"]
            for k in range(NCK):
                self.bank_r[k] = ("vm", v_n)
                self.bank_r[4 + k] = ("vm", v_n)
            return v_n

        def corr_combine(g_rd):
            W("vector", "dg", g_rd)
            O("vector", "memset", czt[:, :], 0.0)
            for j in range(7):
                O("vector", "tensor_tensor", czt[:, :], czt[:, :],
                  ccd[:, j * 8:j * 8 + 8], ALU.mult, drain=True)
                O("vector", "scalar_tensor_tensor", czt[:, :],
                  ccg[:, j * 8:j * 8 + 8], P("mask", j), czt[:, :],
                  ALU.mult, ALU.add, drain=True)
            O("vector", "engine_nop", drain=True, inc=True)
            return self.semn["vm"]

        def apply_corr(l, c, slot, recompute):
            dcol = l * 4 + c
            cosb, sinb = SC_COS[slot][:, :], SC_SIN[slot][:, :]
            if recompute:
                O("vector", "tensor_scalar", cosb, tglob[:, :],
                  P("theta", dcol), None, ALU.mult)
                O("vector", "tensor_scalar", sinb, cosb, INV2PI, MAGIC,
                  ALU.mult, ALU.add)
                O("vector", "tensor_scalar", sinb, sinb, -MAGIC, None, ALU.add)
                O("vector", "scalar_tensor_tensor", cosb, sinb, -CW_C1, cosb,
                  ALU.mult, ALU.add)
                O("vector", "scalar_tensor_tensor", cosb, sinb, -CW_C2, cosb,
                  ALU.mult, ALU.add)
                O("vector", "scalar_tensor_tensor", cosb, sinb, -CW_C3, cosb,
                  ALU.mult, ALU.add)
                O("vector", "tensor_scalar", cosb, cosb, PI, -PI, ALU.min,
                  ALU.max, inc=True)
                W("scalar", "vm", self.semn["vm"])
                O("scalar", "activation", sinb, cosb, AF.Sin,
                  bias=c_zero[:, 0:1])
                O("scalar", "activation", sc2[:, :], cosb, AF.Abs,
                  bias=c_zero[:, 0:1])
                O("scalar", "activation", cosb, sc2[:, :], AF.Sin, scale=-1.0,
                  bias=c_pi2[:, 0:1], inc=True)
                W("vector", "am", self.semn["am"])
            W("scalar", "vm", self.semn["vm"])
            O("scalar", "activation", sc2[:, :], tglob[:, :], AF.Exp,
              scale=P("lnrho", dcol), bias=P("pb", dcol), inc=True)
            a_rp = self.semn["am"]
            W("vector", "am", a_rp)
            O("vector", "tensor_scalar", sc1[:, :], cosb,
              czt[:, 2 * c:2 * c + 1], None, ALU.mult)
            O("vector", "scalar_tensor_tensor", sc1[:, :], sinb,
              czt[:, 2 * c + 1:2 * c + 2], sc1[:, :], ALU.mult, ALU.add)
            O("vector", "tensor_tensor", sc1[:, :], sc1[:, :], sc2[:, :],
              ALU.mult)
            O("vector", "scalar_tensor_tensor", h_t[c][:, :], sc1[:, :],
              P("scl", dcol), h_t[c][:, :], ALU.mult, ALU.add, inc=True)
            return self.semn["vm"]

        t_w2_prev = [xb_mm[NCK - 1]]
        a_prev = [a_p2]
        for l in range(NLAYER):
            inner_ln_stats([("am", a_prev[0]), ("tm", t_w2_prev[0])])
            W("vector", "vm", v_p0)
            for c4 in range(4):
                O("vector", "tensor_copy", dpk2[:, 2 * c4:2 * c4 + 1],
                  P("dpk", l * 4 + c4), drain=True)
                O("vector", "tensor_copy", dpk2[:, 2 * c4 + 1:2 * c4 + 2],
                  P("dpk", l * 4 + c4), drain=True)
            for j in range(7):
                O("vector", "tensor_scalar", ccd[:, j * 8:j * 8 + 8],
                  dpk2[:, :], P("mask", j), None, ALU.mult, drain=True)
                O("vector", "tensor_scalar", P("u1"), P("mask", j), -1.0, 1.0,
                  ALU.mult, ALU.add, drain=True)
                O("vector", "tensor_scalar", ccd[:, j * 8:j * 8 + 8],
                  ccd[:, j * 8:j * 8 + 8], P("u1"), None, ALU.add, drain=True)
            O("vector", "engine_nop", inc=True)

            for c in range(4):
                slot = c % 2
                dcol = l * 4 + c
                v_n = normalize_to_sc1(c)
                W("scalar", "vm", v_n)
                O("scalar", "activation", sc1[:, :], sc1[:, :], AF.Silu,
                  scale=P("ln1w", dcol), bias=P("ln1b", dcol), inc=True)
                a_yt = self.semn["am"]
                cosb = SC_COS[slot][:, :]
                sinb = SC_SIN[slot][:, :]
                W("vector", "am", a_yt)
                O("vector", "tensor_scalar", cosb, tglob[:, :],
                  P("theta", dcol), None, ALU.mult, inc=True)
                if l == 0 and c == 0 and "v00" in dbg_aps:
                    W("sync", "vm", self.semn["vm"])
                    dmp = DMA("sync", dbg_aps["v00"], cosb)
                    W("vector", "ds", dmp)
                O("vector", "tensor_scalar", sinb, cosb, INV2PI, MAGIC,
                  ALU.mult, ALU.add)
                O("vector", "tensor_scalar", sinb, sinb, -MAGIC, None, ALU.add)
                O("vector", "scalar_tensor_tensor", cosb, sinb, -CW_C1, cosb,
                  ALU.mult, ALU.add)
                O("vector", "scalar_tensor_tensor", cosb, sinb, -CW_C2, cosb,
                  ALU.mult, ALU.add)
                O("vector", "scalar_tensor_tensor", cosb, sinb, -CW_C3, cosb,
                  ALU.mult, ALU.add)
                O("vector", "tensor_scalar", cosb, cosb, PI, -PI, ALU.min,
                  ALU.max, inc=True)
                if l == 0 and c == 0 and "r00" in dbg_aps:
                    W("sync", "vm", self.semn["vm"])
                    dmp = DMA("sync", dbg_aps["r00"], cosb)
                    W("scalar", "ds", dmp)
                W("scalar", "vm", self.semn["vm"])
                O("scalar", "activation", sinb, cosb, AF.Sin, bias=c_zero[:, 0:1])
                O("scalar", "activation", sc2[:, :], cosb, AF.Abs, bias=c_zero[:, 0:1])
                O("scalar", "activation", cosb, sc2[:, :], AF.Sin, scale=-1.0,
                  bias=c_pi2[:, 0:1], inc=True)
                a_cs = self.semn["am"]
                if l == 0 and c == 0 and "yt00" in dbg_aps:
                    W("sync", "am", a_yt)
                    dmp = DMA("sync", dbg_aps["yt00"], sc1[:, :])
                    W("vector", "ds", dmp)
                if l == 0 and c == 0 and "cos00" in dbg_aps:
                    W("sync", "am", a_cs)
                    dmp = DMA("sync", dbg_aps["cos00"], cosb)
                    dmp = DMA("sync", dbg_aps["sin00"], sinb)
                    W("vector", "ds", dmp)
                    W("scalar", "ds", dmp)
                W("vector", "am", a_cs)
                rho_b = P("rho", dcol).broadcast_to((128, T))
                O("vector", "tensor_tensor", sc2[:, :], cosb, sc1[:, :],
                  ALU.mult)
                O("vector", "tensor_tensor_scan", sc2[:, :], rho_b, sc2[:, :],
                  P("ainit", dcol), ALU.mult, ALU.add)
                O("vector", "tensor_tensor", sc1[:, :], sinb, sc1[:, :],
                  ALU.mult)
                O("vector", "tensor_tensor_scan", sc1[:, :], rho_b, sc1[:, :],
                  P("binit", dcol), ALU.mult, ALU.add, inc=True)
                O("vector", "drain")
                if c == 0:
                    W("vector", "dg", getattr(self, "last_g_st", 0))
                O("vector", "tensor_copy", state_pk[:, 2 * c:2 * c + 1],
                  sc2[:, T - 1:T])
                O("vector", "tensor_copy", state_pk[:, 2 * c + 1:2 * c + 2],
                  sc1[:, T - 1:T])
                O("vector", "engine_nop", drain=True, inc=True)
                v_st = self.semn["vm"]
                if l == 0 and c == 0 and "A00" in dbg_aps:
                    W("sync", "vm", v_st)
                    dmp = DMA("sync", dbg_aps["A00"], sc2[:, :])
                    dmp = DMA("sync", dbg_aps["B00"], sc1[:, :])
                    W("vector", "ds", dmp)
                O("vector", "tensor_tensor", sc2[:, :], cosb, sc2[:, :],
                  ALU.mult)
                O("vector", "tensor_tensor", sc1[:, :], sinb, sc1[:, :],
                  ALU.mult)
                O("vector", "tensor_tensor", sc1[:, :], sc1[:, :], sc2[:, :],
                  ALU.add)
                O("vector", "scalar_tensor_tensor", h_t[c][:, :], sc1[:, :],
                  P("scl", dcol), h_t[c][:, :], ALU.mult, ALU.add, inc=True)
                drain_pc(8)
            W("gpsimd", "vm", v_st)
            g_st = DMA("gpsimd", cc_i[l], state_pk[:, :])
            self.last_g_st = g_st
            W("gpsimd", "dg", g_st)
            self.bump("cc", 1)
            E("gpsimd", lambda e, i=l, h=self.semh["cc"]:
              e.collective_compute("AllGather", ALU.bypass,
                                   [list(range(NCORE))],
                                   ins=[cc_i[i].opt()],
                                   outs=[cc_o[i].opt()]).then_inc(h, 1))
            W("gpsimd", "cc", self.semn["cc"])
            g_rd = DMA("gpsimd",
                       ccg[:, :].rearrange("p (r c) -> p r c", r=NCORE),
                       cc_o[l].rearrange("(r p) c -> p r c", p=128))
            corr_combine(g_rd)
            apply_corr(l, 2, 0, False)
            apply_corr(l, 3, 1, False)
            apply_corr(l, 0, 0, True)
            apply_corr(l, 1, 1, True)
            v_mix = self.semn["vm"]

            inner_ln_stats([("vm", v_mix)])
            W("vector", "tm", t_w2_prev[0])
            for c in range(4):
                dcol = l * 4 + c
                normalize_to_sc1(c)
                O("vector", "tensor_scalar", yb[c][:, :], sc1[:, :],
                  P("ln2w", dcol), P("ln2b", dcol), ALU.mult, ALU.add,
                  inc=True)
            v_yb = self.semn["vm"]

            W("tensor", "dg", g_w1[l])
            W("tensor", "dg", g_w2[l])
            W("tensor", "vm", v_yb)
            for k in range(NCK):
                for ft in range(16):
                    bank = ft % 4
                    pairs = [(w1sb[:, c * DFI + ft * 128:c * DFI + ft * 128 + 128],
                              yb[c][:, k * 512:(k + 1) * 512]) for c in range(4)]
                    self.mm_group(bank, pairs)
                    self.evac(bank, "scalar", lambda b=bank, ft=ft, l=l:
                              O("scalar", "activation",
                                ffn1[:, ft * 512:(ft + 1) * 512],
                                self.psum[b][:, :], AF.Silu, scale=1.0,
                                bias=P("b1", l * 16 + ft), inc=True))
                a_f1 = self.semn["am"]
                for dtile in range(4):
                    bank = 4 + (dtile % 2)
                    pairs = [(w2sb[:, ft * DI + dtile * 128:
                                   ft * DI + dtile * 128 + 128],
                              ffn1[:, ft * 512:(ft + 1) * 512])
                             for ft in range(16)]
                    self.mm_group(bank, pairs, waits=[("am", a_f1)])
                    self.evac(bank, "vector", lambda b=bank, d=dtile, k=k, l=l:
                              O("vector", "scalar_tensor_tensor",
                                h_t[d][:, k * 512:(k + 1) * 512],
                                self.psum[b][:, :], P("b2", l * 4 + d),
                                h_t[d][:, k * 512:(k + 1) * 512],
                                ALU.add, ALU.add, inc=True))
                drain_pc(3)
            t_w2 = self.semn["tm"]
            t_w2_prev[0] = t_w2
            a_prev[0] = self.semn["am"]
            v_hdone = self.semn["vm"]
            if (l + 1) < NLAYER:
                W("gpsimd", "tm", t_w2)
                g_w1[l + 1] = DMA("gpsimd", w1sb[:, :], w1h[l + 1])
                g_w2[l + 1] = DMA("gpsimd", w2sb[:, :], w2h[l + 1])
            if f"h_l{l}" in dbg_aps:
                W("sync", "vm", v_hdone)
                W("sync", "am", self.semn["am"])
                for c in range(4):
                    DMA("sync", dbg_aps[f"h_l{l}"][c * 128:(c + 1) * 128, :],
                        h_t[c][:, :])
        v_layers = self.semn["vm"]

        W("gpsimd", "vm", v_layers)
        g_wout = DMA("gpsimd", woutsb[:, :], woutT)

        inner_ln_stats([("vm", v_layers), ("tm", t_w2_prev[0]),
                        ("am", a_prev[0])])
        W("vector", "tm", t_w2_prev[0])
        for c in range(4):
            normalize_to_sc1(c)
            O("vector", "tensor_scalar", yb[c][:, :], sc1[:, :],
              P("lnlw", c), P("lnlb", c), ALU.mult, ALU.add, inc=True)
        v_hlb = self.semn["vm"]

        st_ao = {}
        i_ao = 0
        for k in range(NCK):
            for vt in range(NKD):
                bank = vt % 4
                slv = i_ao % 2
                pairs = [(woutsb[:, c * D + vt * 128:c * D + vt * 128 + 128],
                          yb[c][:, k * 512:(k + 1) * 512]) for c in range(4)]
                self.mm_group(bank, pairs,
                              waits=([("dg", g_wout), ("vm", v_hlb)]
                                     if (k == 0 and vt == 0) else ()))
                ew = [("ds", st_ao[i_ao - 2])] if i_ao >= 2 else []
                self.evac(bank, "scalar", lambda b=bank, vt=vt, s=slv:
                          O("scalar", "activation", aoev[s][:, :],
                            self.psum[b][:, :], AF.Identity,
                            bias=P("bout", vt), inc=True), waits=ew)
                W("sync", "am", self.semn["am"])
                st_ao[i_ao] = DMA(
                    "sync", ao_fm[vt * 128:(vt + 1) * 128, k * 512:(k + 1) * 512],
                    aoev[slv][:, :])
                i_ao += 1
            drain_pc(6)
        v_ao_all = self.semn["ds"]

        v_jnk = {}
        st4 = {}
        W("sync", "dg", self.pc_mark.get("d", 0))
        v_p4tail = 0
        for tt in range(NTT):
            sl = tt % 2
            if tt == 0:
                W("sync", "ds", v_ao_all)
            if tt >= 2:
                W("sync", "vm", v_used[tt - 2])
                W("sync", "ds", st4.get(tt - 2, 0))
                W("sync", "vm", v_jnk[tt - 2])
            ld_ao = DMA("sync", jnkb4[sl][:, :],
                        ao_fm[0:D, tt * 128:(tt + 1) * 128], transpose=True)
            ld_x = DMA("sync", xtm4[sl][:, :], x_in[tt * 128:(tt + 1) * 128, :])
            col = tt % 16
            cc = slice(col, col + 1)
            v_s, a_s = tm_stats(jnkb4[sl][:, :], aotm4[:, :], col, ld_ao)
            W("vector", "ds", ld_x)
            O("vector", "tensor_scalar", xtm4[sl][:, :], xtm4[sl][:, :],
              nmr[:, cc], None, ALU.add)
            O("vector", "scalar_tensor_tensor", xtm4[sl][:, :], jnkb4[sl][:, :],
              rst[:, cc], xtm4[sl][:, :], ALU.mult, ALU.add, inc=True)
            v_x1 = self.semn["vm"]
            W("sync", "vm", v_x1)
            st4[tt] = DMA("sync", x1_tm[tt * 128:(tt + 1) * 128, :],
                          xtm4[sl][:, :])
            v_s2, a_s2 = tm_stats(xtm4[sl][:, :], aotm4[:, :], col, 0,
                                  vwaits=[("vm", v_x1)])
            O("vector", "tensor_scalar", jnkb4[sl][:, :], xtm4[sl][:, :], rst[:, cc],
              nmr[:, cc], ALU.mult, ALU.add, inc=True)
            v_xb = self.semn["vm"]
            v_used[tt] = v_xb
            v_jnk[tt] = v_xb
            W("sync", "vm", v_xb)
            st4[tt] = DMA("sync", xlnb_tm[tt * 128:(tt + 1) * 128, :],
                          jnkb4[sl][:, :])
            if tt % 4 == 3:
                drain_pc(4)
        v_p4 = self.semn["ds"]
        v_p4vm = self.semn["vm"]
        if "x1d" in dbg_aps:
            W("sync", "ds", v_p4)
            DMA("sync", dbg_aps["x1d"], x1_tm)
            W("gpsimd", "ds", v_p4)
            DMA("gpsimd", dbg_aps["aod"], ao_fm)
            DMA("gpsimd", dbg_aps["xlnbd"], xlnb_tm)

        drain_pc(10 ** 9)
        fchunks = [(0, FCH), (FCH, 2 * FCH), (2 * FCH, NFT)]
        st_ffo = {}
        self.gsil_v = {}
        gu_t = {}
        ds_t = {}
        W("sync", "dg", self.pc_mark.get("g", 0))
        t_guplast = {}
        for blk in range(NCK):
            tsl = slice(blk * 512, (blk + 1) * 512)
            W("sync", "ds", st4[4 * blk + 3])
            if blk >= 1:
                W("sync", "tm", t_guplast[blk - 1])
            ldx = 0
            for kv in range(NKD):
                ldx = DMA("sync", xlnffn[:, kv * 512:(kv + 1) * 512],
                          xlnb_tm[tsl, kv * 128:(kv + 1) * 128], transpose=True)
            if blk == 0 and "xlf0" in dbg_aps:
                W("sync", "ds", ldx)
                DMA("gpsimd", dbg_aps["xlf0"], xlnffn[:, :]) if False else None
                W("gpsimd", "ds", ldx)
                dmp = DMA("gpsimd", dbg_aps["xlf0"], xlnffn[:, :])
                W("gpsimd", "dg", self.pc_mark["g"])
                DMA("gpsimd", dbg_aps["gscr0"], g_scr[0])
                DMA("gpsimd", dbg_aps["dscr0"], d_scr[0])
            pend = {}

            def issue_gu_loads(ft, blk=blk):
                gu2 = ft % 2
                if gu2 in gu_t:
                    W("sync", "tm", gu_t[gu2])
                if blk == 0 and ft == 0:
                    W("sync", "dg", self.pc_mark.get("u", 0))
                lg = DMA("sync", gusb[gu2][:, :], g_scr[ft])
                lu = DMA("sync", gusb[2 + gu2][:, :], u_scr[ft])
                pend[ft] = (lg, lu)

            issue_gu_loads(0 if blk >= 0 else 0) if False else None
            first_issued = [False]
            for (f0, f1) in fchunks:
                first_chunk = (f0 == 0)
                v_h = 0
                if not first_issued[0]:
                    issue_gu_loads(f0)
                    first_issued[0] = True
                for ft in range(f0, f1):
                    gu = ft % 2
                    if ft + 1 < NFT and (ft + 1) not in pend:
                        issue_gu_loads(ft + 1)
                    ldg, ldu = pend.pop(ft)
                    bank_g, bank_u = ft % 4, 4 + ft % 4
                    pairs = [(gusb[gu][:, kv * 128:(kv + 1) * 128],
                              xlnffn[:, kv * 512:(kv + 1) * 512])
                             for kv in range(NKD)]
                    self.mm_group(bank_g, pairs, waits=[("ds", max(ldg, ldx))])
                    gw = ([("vm", self.gsil_v[ft % 2])]
                          if (ft % 2) in self.gsil_v else [])
                    a_g = self.evac(bank_g, "scalar",
                                    lambda b=bank_g, ft=ft, s=ft % 2:
                                    O("scalar", "activation", gsil[s][:, :],
                                      self.psum[b][:, :], AF.Silu,
                                      scale=P("gs", ft), inc=True, bias=c_zero[:, 0:1]), waits=gw)
                    pairs = [(gusb[2 + gu][:, kv * 128:(kv + 1) * 128],
                              xlnffn[:, kv * 512:(kv + 1) * 512])
                             for kv in range(NKD)]
                    gu_t[gu] = self.mm_group(bank_u, pairs,
                                             waits=[("ds", ldu)])
                    t_guplast[blk] = gu_t[gu]
                    v_h = self.evac(bank_u, "vector",
                                    lambda b=bank_u, ft=ft, f0=f0, s=ft % 2:
                                    O("vector", "scalar_tensor_tensor",
                                      hch[:, (ft - f0) * 512:(ft - f0 + 1) * 512],
                                      self.psum[b][:, :], P("us", ft),
                                      gsil[s][:, :], ALU.mult, ALU.mult,
                                      inc=True),
                                    waits=[("am", a_g)])
                    self.gsil_v[ft % 2] = v_h
                if blk == 0 and first_chunk and "hch0" in dbg_aps:
                    W("gpsimd", "vm", v_h)
                    DMA("gpsimd", dbg_aps["hch0"], hch[:, :])
                if blk == 0 and first_chunk:
                    W("sync", "dg", self.pc_mark.get("d", 0))
                for dg_i in range(4):
                    pend_d = {}

                    def issue_d_loads(ft, dg_i=dg_i):
                        s2b = (ft % 2) * 2
                        if ft % 2 in ds_t:
                            W("sync", "tm", ds_t[ft % 2])
                        DMA("sync", dsb[s2b][:, :],
                            d_scr[ft][:, dg_i * 1024:dg_i * 1024 + 512])
                        pend_d[ft] = DMA(
                            "sync", dsb[s2b + 1][:, :],
                            d_scr[ft][:, dg_i * 1024 + 512:dg_i * 1024 + 1024])

                    issue_d_loads(f0)
                    for ft in range(f0, f1):
                        s2 = (ft % 2) * 2
                        if ft + 1 < f1:
                            issue_d_loads(ft + 1)
                        W("tensor", "ds", pend_d.pop(ft))
                        if ft == f0:
                            W("tensor", "vm", v_h)
                            for b in range(8):
                                s, v = self.bank_r[b]
                                W("tensor", s, v)
                        for di in range(8):
                            buf = dsb[s2] if di < 4 else dsb[s2 + 1]
                            off = (di % 4) * 128
                            O("tensor", "matmul", self.psum[di][:, :],
                              buf[:, off:off + 128],
                              hch[:, (ft - f0) * 512:(ft - f0 + 1) * 512],
                              start=(ft == f0), stop=(ft == f1 - 1),
                              inc=(di == 7))
                        ds_t[ft % 2] = self.semn["tm"]
                    t_dg = self.semn["tm"]
                    for b in range(8):
                        self.bank_w[b] = ("tm", t_dg)
                    W("vector", "tm", t_dg)
                    if blk == 0 and first_chunk and dg_i == 0:
                        W("vector", "vm", v_p4vm)
                        W("vector", "ds", v_p4)
                    for di in range(8):
                        dtile = dg_i * 8 + di
                        if first_chunk:
                            O("vector", "tensor_copy", accum[dtile][:, :],
                              self.psum[di][:, :], inc=True)
                        else:
                            O("vector", "tensor_tensor", accum[dtile][:, :],
                              accum[dtile][:, :], self.psum[di][:, :], ALU.add,
                              inc=True)
                        self.bank_r[di] = ("vm", self.semn["vm"])
            for dtile in range(NKD):
                s = dtile % 2
                if dtile >= 2:
                    W("vector", "ds", st_ffo[(blk, dtile - 2)])
                O("vector", "tensor_scalar", aoev[s][:, :], accum[dtile][:, :],
                  P("ds", dtile), None, ALU.mult, inc=True)
                W("sync", "vm", self.semn["vm"])
                st_ffo[(blk, dtile)] = DMA(
                    "sync", ffo_fm[dtile * 128:(dtile + 1) * 128, tsl],
                    aoev[s][:, :])
        v_ffo_all = self.semn["ds"]
        if "ffod" in dbg_aps:
            W("gpsimd", "ds", v_ffo_all)
            DMA("gpsimd", dbg_aps["ffod"], ffo_fm)

        v_o = {}
        st6 = {}
        for tt in range(NTT):
            sl = tt % 2
            if tt == 0:
                W("sync", "ds", v_ffo_all)
            if tt >= 2:
                W("sync", "vm", v_o[tt - 2])
                W("sync", "ds", st6.get(tt - 2, 0))
            ld_fo = DMA("sync", jnkb4[sl][:, :],
                        ffo_fm[0:D, tt * 128:(tt + 1) * 128], transpose=True)
            ld_x1 = DMA("sync", xtm4[sl][:, :], x1_tm[tt * 128:(tt + 1) * 128, :])
            col = tt % 16
            cc = slice(col, col + 1)
            tm_stats(jnkb4[sl][:, :], aotm4[:, :], col, ld_fo)
            W("vector", "ds", ld_x1)
            O("vector", "tensor_scalar", xtm4[sl][:, :], xtm4[sl][:, :],
              nmr[:, cc], None, ALU.add)
            O("vector", "scalar_tensor_tensor", xtm4[sl][:, :], jnkb4[sl][:, :],
              rst[:, cc], xtm4[sl][:, :], ALU.mult, ALU.add, inc=True)
            v_o[tt] = self.semn["vm"]
            W("sync", "vm", v_o[tt])
            st6[tt] = DMA("sync", out_d[tt * 128:(tt + 1) * 128, :],
                          xtm4[sl][:, :])
        W("sync", "ds", self.semn["ds"])

        with nc.Block() as blk_:
            for meth, eng in ((blk_.sync, "sync"), (blk_.gpsimd, "gpsimd"),
                              (blk_.tensor, "tensor"), (blk_.vector, "vector"),
                              (blk_.scalar, "scalar")):
                ops = self.q[eng]

                def body(e, ops=ops):
                    for f in ops:
                        f(e)
                meth(body)
        return nc


def _host_layout(inputs):
    rs = np.ascontiguousarray

    def pack_chan(a):
        return rs(np.asarray(a).reshape(4, 4, 128).transpose(2, 0, 1)
                  .reshape(128, 16))

    W1 = np.asarray(inputs["L0_attn_W1"])
    W2 = np.asarray(inputs["L0_attn_W2"])
    g = {}
    g["winT"] = rs(np.transpose(np.asarray(inputs["L0_attn_Win"]))
                   .reshape(32, 128, 4, 128).transpose(1, 0, 2, 3)
                   .reshape(128, 32 * 512))
    g["w1h"] = rs(np.stack([np.transpose(W1[l]).reshape(4, 128, 2048)
                            .transpose(1, 0, 2).reshape(128, 4 * 2048)
                            for l in range(4)]))
    g["w2h"] = rs(np.stack([np.transpose(W2[l]).reshape(16, 128, 512)
                            .transpose(1, 0, 2).reshape(128, 16 * 512)
                            for l in range(4)]))
    g["woutT"] = rs(np.transpose(np.asarray(inputs["L0_attn_Wout"]))
                    .reshape(4, 128, 4096).transpose(1, 0, 2)
                    .reshape(128, 4 * 4096))
    for nm, key in [("gateT", "L0_ffn_gate_w"), ("upT", "L0_ffn_up_w")]:
        w = np.asarray(inputs[key])
        g[nm] = rs(w.reshape(NFT, 128, NKD, 128).transpose(0, 3, 2, 1))
    g["downT"] = rs(np.transpose(np.asarray(inputs["L0_ffn_down_w"]))
                    .reshape(NFT, 128, NKD, 128))
    pp = {"ln1w": pack_chan(inputs["L0_attn_ln1_w"]),
          "ln1b": pack_chan(inputs["L0_attn_ln1_b"]),
          "ln2w": pack_chan(inputs["L0_attn_ln2_w"]),
          "ln2b": pack_chan(inputs["L0_attn_ln2_b"]),
          "scl": pack_chan(inputs["L0_attn_sc_lin"]),
          "b2": pack_chan(inputs["L0_attn_b2"]),
          "pre": pack_chan(inputs["L0_attn_phz_re"]),
          "pim": pack_chan(inputs["L0_attn_phz_im"]),
          "lcr": pack_chan(inputs["L0_attn_lc_re"]),
          "lci": pack_chan(inputs["L0_attn_lc_im"]),
          "b1": rs(np.asarray(inputs["L0_attn_b1"]).reshape(4, 16, 128)
                   .transpose(2, 0, 1).reshape(128, 64)),
          "bin": rs(np.asarray(inputs["L0_attn_bin"]).reshape(4, 128).T),
          "lnlw": rs(np.asarray(inputs["L0_attn_lnl_w"]).reshape(4, 128).T),
          "lnlb": rs(np.asarray(inputs["L0_attn_lnl_b"]).reshape(4, 128).T),
          "bout": rs(np.asarray(inputs["L0_attn_bout"]).reshape(32, 128).T),
          "gs": rs(np.asarray(inputs["L0_ffn_gate_s"]).reshape(NFT, 128).T),
          "us": rs(np.asarray(inputs["L0_ffn_up_s"]).reshape(NFT, 128).T),
          "ds": rs(np.asarray(inputs["L0_ffn_down_s"]).reshape(32, 128).T)}
    for k, v in pp.items():
        g["pp_" + k] = rs(np.asarray(v, np.float32))
    return g


def kernel(**inputs):
    T = 16384 // NCORE
    shared = _host_layout(inputs)
    x = np.asarray(inputs["L0_attn_x"])[0]
    in_maps = []
    for ci in range(NCORE):
        m = dict(shared)
        m["x_in"] = np.ascontiguousarray(x[ci * T:(ci + 1) * T, :], np.float32)
        m["pp_toff"] = np.full((128, 1), float(ci * T), np.float32)
        msk = np.zeros((128, 8), np.float32)
        msk[:, :7] = (np.arange(7)[None, :] < ci)
        msk[:, 7] = 1.0 if ci == 0 else 0.0
        m["pp_mask"] = msk
        in_maps.append(m)
    kb = KB(T)
    nc = kb.build()
    res = run_bass_kernel_spmd(nc, in_maps, core_ids=list(range(NCORE)))
    out = np.concatenate([np.asarray(r["out"]) for r in res.results], axis=0)
    return out[None, :, :].astype(np.float32)


# version 05
# speedup vs baseline: 1.0781x; 1.0781x over previous
import math

import numpy as np
import ml_dtypes

import concourse.bass as bass
import concourse.mybir as mybir
from concourse.bass_utils import run_bass_kernel_spmd

F32 = mybir.dt.float32
BF16 = mybir.dt.bfloat16
I32 = mybir.dt.int32
AF = mybir.ActivationFunctionType
ALU = mybir.AluOpType
BF = ml_dtypes.bfloat16

NCORE = 8
D = 4096
DI = 512
DFI = 2048
FF = 11008
NFT = FF // 128
NKD = D // 128
NLAYER = 4
EPS = 1e-5
FCH = 40

_TP = 2 * math.pi
CW_C1 = float(np.float32(np.ldexp(np.round(np.ldexp(_TP, 8)), -8)))
CW_C2 = float(np.float32(np.ldexp(np.round(np.ldexp(_TP - CW_C1, 19)), -19)))
CW_C3 = float(np.float32(_TP - CW_C1 - CW_C2))
MAGIC = float(np.float32(1.5 * 2.0 ** 23))
INV2PI = float(np.float32(1.0 / _TP))
PI = float(np.float32(math.pi))
PI_2 = float(np.float32(math.pi / 2))

_DEBUG = []


class KB:
    def __init__(self, T):
        self.T = T
        self.NCK = T // 512
        self.NTT = T // 128
        nc = bass.Bass("TRN2", target_bir_lowering=False)
        self.nc = nc
        self.q = {e: [] for e in ("sync", "gpsimd", "tensor", "vector", "scalar")}
        self.semh, self.semn = {}, {}
        for s in ("dg", "ds", "tm", "vm", "am", "cc"):
            self.semh[s] = nc.alloc_semaphore("sem_" + s)
            self.semn[s] = 0
        self.bank_w = [("tm", 0)] * 8
        self.bank_r = [("tm", 0)] * 8
        self.pc_items = []
        self.pc_mark = {}
        self.dumped_stats = False

    def emit(self, eng, fn):
        self.q[eng].append(fn)

    def W(self, eng, sem, val):
        if val:
            self.emit(eng, lambda e, h=self.semh[sem], v=val: e.wait_ge(h, v))

    def bump(self, sem, n):
        self.semn[sem] += n
        return self.semn[sem]

    def DMA(self, eng, dst, src, transpose=False):
        sem = {"sync": "ds", "gpsimd": "dg"}[eng]
        self.bump(sem, 16)
        self.emit(eng, lambda e, d=dst, s=src, h=self.semh[sem], t=transpose:
                  e.dma_start(d, s, transpose=t).then_inc(h, 16))
        return self.semn[sem]

    def O(self, eng, meth, *args, inc=False, drain=False, **kw):
        sem = {"tensor": "tm", "vector": "vm", "scalar": "am", "gpsimd": "vm"}[eng]
        if inc:
            self.bump(sem, 1)
        def f(e, meth=meth, args=args, kw=kw, inc=inc, drain=drain,
              h=self.semh[sem]):
            if drain:
                e.drain()
            r = getattr(e, meth)(*args, **kw)
            if inc:
                r.then_inc(h, 1)
        self.emit(eng, f)
        return self.semn[sem]

    def mm_group(self, bank, pairs, waits=()):
        s, v = self.bank_r[bank]
        self.W("tensor", s, v)
        for (ss, vv) in waits:
            self.W("tensor", ss, vv)
        n = len(pairs)
        for i, (l, r) in enumerate(pairs):
            self.O("tensor", "matmul", self.psum[bank][:, :], l, r,
                   start=(i == 0), stop=(i == n - 1), inc=(i == n - 1))
        self.bank_w[bank] = ("tm", self.semn["tm"])
        return self.semn["tm"]

    def evac(self, bank, eng, emit_cb, waits=()):
        s, v = self.bank_w[bank]
        self.W(eng, s, v)
        for (ss, vv) in waits:
            self.W(eng, ss, vv)
        emit_cb()
        sem = {"vector": "vm", "scalar": "am"}[eng]
        self.bank_r[bank] = (sem, self.semn[sem])
        return self.semn[sem]

    def build(self):
        nc, T, NCK, NTT = self.nc, self.T, self.NCK, self.NTT
        W, O, DMA, E = self.W, self.O, self.DMA, self.emit
        dt = nc.dram_tensor

        x_in = dt("x_in", [T, D], F32, kind="ExternalInput").ap()
        winT = dt("winT", [128, NKD * DI], F32, kind="ExternalInput").ap()
        w1h = dt("w1h", [NLAYER, 128, 4 * DFI], F32, kind="ExternalInput").ap()
        w2h = dt("w2h", [NLAYER, 128, 16 * DI], F32, kind="ExternalInput").ap()
        woutT = dt("woutT", [128, 4 * D], F32, kind="ExternalInput").ap()
        gateT = dt("gateT", [NFT, 128, D], mybir.dt.int8, kind="ExternalInput").ap()
        upT = dt("upT", [NFT, 128, D], mybir.dt.int8, kind="ExternalInput").ap()
        downT = dt("downT", [NFT, 128, D], mybir.dt.int8, kind="ExternalInput").ap()
        PCOLS = [("ln1w", 16), ("ln1b", 16), ("ln2w", 16), ("ln2b", 16),
                 ("scl", 16), ("b2", 16), ("pre", 16), ("pim", 16),
                 ("lcr", 16), ("lci", 16), ("b1", 64), ("bin", 4),
                 ("lnlw", 4), ("lnlb", 4), ("bout", 32), ("gs", NFT),
                 ("us", NFT), ("ds", 32), ("toff", 1), ("mask", 8)]
        pin = {nm: dt("pp_" + nm, [128, n], F32, kind="ExternalInput").ap()
               for nm, n in PCOLS}
        out_d = dt("out", [T, D], F32, kind="ExternalOutput").ap()
        dbg_aps = {nm: dt("dbg_" + nm, list(sh), F32, kind="ExternalOutput").ap()
                   for nm, sh in _DEBUG}

        xlna_tm = dt("xlna_tm", [T, D], BF16, kind="Internal").ap()
        ao_fm = dt("ao_fm", [D, T], BF16, kind="Internal").ap()
        x1_tm = dt("x1_tm", [T, D], F32, kind="Internal").ap()
        xlnb_tm = dt("xlnb_tm", [T, D], BF16, kind="Internal").ap()
        ffo_fm = dt("ffo_fm", [D, T], BF16, kind="Internal").ap()
        ncc = NLAYER
        cc_i = [dt(f"cc_i{k}", [128, 8], F32, kind="Internal").ap() for k in range(ncc)]
        cc_o = [dt(f"cc_o{k}", [NCORE * 128, 8], F32, kind="Internal").ap()
                for k in range(ncc)]

        pos = [16384]

        def place(name, shape, dtype):
            nb = int(np.prod(shape[1:])) * (4 if dtype != BF16 else 2)
            at = pos[0]
            pos[0] = at + ((nb + 63) // 64) * 64
            return nc.alloc_sbuf_tensor_at(name, list(shape), dtype, offset=at), at

        def over(name, shape, dtype, at):
            return nc.alloc_sbuf_tensor_at(name, list(shape), dtype, offset=at)

        TB = T * 4
        h_t = []
        H0 = pos[0]
        for c in range(4):
            h_t.append(place(f"h{c}", [128, T], F32)[0])
        tglob, TG0 = place("tglob", [128, T], F32)
        params, _ = place("params", [128, 1024], F32)
        statsf, _ = place("statsf", [128, 64], F32)
        var_scr, VS0 = place("var_scr", [1, T], F32)
        ccg, _ = place("ccg", [128, 64], F32)
        czt, _ = place("czt", [128, 8], F32)
        dpk2, _ = place("dpk2", [128, 8], F32)
        ccd, _ = place("ccd", [128, 64], F32)
        ones_bf, _ = place("ones_bf", [128, 1], BF16)
        onesr_f, _ = place("onesr_f", [1, 128], F32)
        nonesr_f, _ = place("nonesr_f", [1, 128], F32)
        c_pi2, _ = place("c_pi2", [128, 1], F32)
        c_eps, _ = place("c_eps", [128, 1], F32)
        c_zero, _ = place("c_zero", [128, 1], F32)
        state_pk, _ = place("state_pk", [128, 8], F32)
        STG0 = pos[0]
        stg = [place(f"stg{s}", [128, D], BF16)[0] for s in range(2)]
        aoev = [place(f"aoev{s}", [128, 512], BF16)[0] for s in range(2)]

        SCAN0 = pos[0]
        scn = [place(f"scn{i}", [128, T], F32)[0] for i in range(4)]
        SC_COS = [scn[0], scn[2]]
        SC_SIN = [scn[1], scn[3]]
        if pos[0] - SCAN0 < 32768:
            pos[0] = SCAN0 + 32768
        WA0 = pos[0]
        w1sb, _ = place("w1sb", [128, 4 * DFI], BF16)
        w2sb, _ = place("w2sb", [128, 16 * DI], BF16)
        MM0 = pos[0]
        yb = [place(f"yb{c}", [128, T], BF16)[0] for c in range(4)]
        hb = [place(f"hb{s}", [128, T], BF16)[0] for s in range(2)]
        ffn1, _ = place("ffn1", [128, 16 * 512], BF16)
        sc1, SC1_0 = place("sc1", [128, T], F32)
        sc2, SC2_0 = place("sc2", [128, T], F32)
        if pos[0] - MM0 < 49152:
            pos[0] = MM0 + 49152
        END0 = pos[0]
        assert END0 <= 229376, f"sbuf static overflow: {END0}"
        SCAN_B = WA0 - SCAN0
        MM_B = END0 - MM0

        xtm = [over(f"xtm{i}", [128, D], F32, SCAN0 + i * 16384) for i in range(2)]
        xtm4 = [over(f"xtm4_{i}", [128, D], F32, H0 + i * 16384) for i in range(2)]
        aotm4 = over("aotm4", [128, D], BF16, TG0)
        jnkb4 = [over(f"jnkb4_{i}", [128, D], BF16, STG0 + i * 8192)
                 for i in range(2)]
        assert SCAN_B >= 2 * 16384
        aotm = over("aotm", [128, D], BF16, MM0)
        jnkb = over("jnkb", [128, D], BF16, MM0 + 8192)
        winsb = over("winsb", [128, NKD * DI], BF16, MM0 + 16384)
        xlnw = over("xlnw", [128, NKD * 512], BF16, SCAN0)
        woutsb = over("woutsb", [128, 4 * D], BF16, SCAN0)
        xlnffn = over("xlnffn", [128, NKD * 512], BF16, SCAN0)
        gusb = [over(f"gusb{s}", [128, D], BF16, WA0 + s * 8192) for s in range(4)]
        assert WA0 + 4 * 8192 <= END0 + 4096
        if T * 4 >= 6144:
            dsb = [over(f"dsb{s}", [128, 512], BF16, VS0 + s * 1024)
                   for s in range(4)]
            gsil = [over(f"gsil{s}", [128, 512], BF16, VS0 + 4096 + s * 1024)
                    for s in range(2)]
        else:
            dsb = [place(f"dsb{s}", [128, 512], BF16)[0] for s in range(4)]
            gsil = [place(f"gsil{s}", [128, 512], BF16)[0] for s in range(2)]
        hch = over("hch", [128, FCH * 512], BF16, MM0)
        if T == 2048:
            acc_off = []
            for i in range(5 * TB // 2048):
                acc_off.append(H0 + i * 2048)
            need = NKD - len(acc_off)
            for i in range(need):
                acc_off.append(MM0 + FCH * 1024 + i * 2048)
            assert MM0 + FCH * 1024 + need * 2048 <= 229376
            accum = [over(f"acc{i}", [128, 512], F32, acc_off[i])
                     for i in range(NKD)]
        else:
            acc_off = [H0 + i * 2048 for i in range(5 * TB // 2048)]
            acc_off += [MM0 + FCH * 1024 + i * 2048
                        for i in range((49152 - FCH * 1024) // 2048)]
            accum = [over(f"acc{i}", [128, 512], F32, acc_off[i])
                     if i < len(acc_off)
                     else place(f"acc{i}", [128, 512], F32)[0]
                     for i in range(NKD)]
            assert pos[0] <= 229376, pos[0]

        mu_row = over("mu_row", [1, T], F32, SC1_0)
        sq_row = over("sq_row", [1, T], F32, SC2_0)
        self.psum = [nc.alloc_psum_tensor(f"ps{i}", [128, 512], F32)
                     for i in range(8)]

        PC = {}
        pcn = [0]
        for nm, n in PCOLS + [("rho", 16), ("lnrho", 16), ("theta", 16),
                              ("cth", 16), ("sth", 16), ("ainit", 16),
                              ("binit", 16), ("dpk", 16), ("pb", 16),
                              ("t1", 16), ("t2", 16), ("t3", 16), ("t4", 16),
                              ("u1", 1)]:
            PC[nm] = pcn[0]
            pcn[0] += n
        assert pcn[0] <= 1024

        def P(nm, i=0, n=1):
            c = PC[nm] + i
            return params[:, c:c + n]

        def A16(nm):
            return P(nm, 0, 16)

        for nm, n in PCOLS:
            DMA("sync", P(nm, 0, n), pin[nm])
        v_pld = self.semn["ds"]

        O("vector", "memset", c_pi2[:, :], PI_2)
        O("vector", "memset", c_eps[:, :], float(EPS))
        O("vector", "memset", c_zero[:, :], 0.0)
        O("vector", "memset", ones_bf[:, :], 1.0 / DI)
        O("vector", "memset", onesr_f[:, :], 1.0)
        O("vector", "memset", nonesr_f[:, :], -1.0, inc=True)
        O("gpsimd", "iota", tglob[:, :], [[1, T]], base=0, channel_multiplier=0,
          allow_small_or_imprecise_dtypes=True, inc=True)
        W("vector", "vm", self.semn["vm"])
        W("vector", "ds", v_pld)
        O("vector", "tensor_scalar", tglob[:, :], tglob[:, :], P("toff"), None,
          ALU.add, ALU.bypass, inc=True)

        W("scalar", "ds", v_pld)
        W("scalar", "vm", self.semn["vm"])
        O("scalar", "activation", A16("t1"), A16("pre"), AF.Square, bias=c_zero[:, 0:1])
        O("scalar", "activation", A16("t2"), A16("pim"), AF.Square, drain=True,
          inc=True, bias=c_zero[:, 0:1])
        W("vector", "am", self.semn["am"])
        O("vector", "tensor_tensor", A16("t1"), A16("t1"), A16("t2"), ALU.add,
          drain=True, inc=True)
        W("scalar", "vm", self.semn["vm"])
        O("scalar", "activation", A16("t2"), A16("t1"), AF.Sqrt, drain=True,
          inc=True, bias=c_zero[:, 0:1])
        W("vector", "am", self.semn["am"])
        O("vector", "reciprocal", A16("t3"), A16("t2"), drain=True)
        O("vector", "tensor_tensor", A16("t3"), A16("t3"), A16("t1"), ALU.mult,
          drain=True)
        O("vector", "tensor_tensor", A16("t3"), A16("t3"), A16("t2"), ALU.add,
          drain=True)
        O("vector", "tensor_scalar", A16("t3"), A16("t3"), 0.5, None, ALU.mult,
          drain=True, inc=True)
        W("scalar", "vm", self.semn["vm"])
        O("scalar", "activation", A16("rho"), A16("t3"), AF.Tanh, drain=True,
          inc=True, bias=c_zero[:, 0:1])
        W("vector", "am", self.semn["am"])
        O("vector", "tensor_scalar", A16("t4"), A16("rho"), 1e-36, None,
          ALU.max, drain=True, inc=True)
        W("scalar", "vm", self.semn["vm"])
        O("scalar", "activation", A16("lnrho"), A16("t4"), AF.Ln, drain=True, bias=c_zero[:, 0:1])
        O("scalar", "activation", A16("dpk"), A16("lnrho"), AF.Exp,
          scale=float(T), drain=True, bias=c_zero[:, 0:1])
        O("scalar", "activation", A16("t1"), A16("pre"), AF.Abs, drain=True, bias=c_zero[:, 0:1])
        O("scalar", "activation", A16("t2"), A16("pim"), AF.Abs, drain=True,
          inc=True, bias=c_zero[:, 0:1])
        W("vector", "am", self.semn["am"])
        O("vector", "tensor_tensor", A16("t3"), A16("t2"), A16("t1"), ALU.is_gt,
          drain=True)
        O("vector", "tensor_tensor", A16("t4"), A16("t1"), A16("t2"), ALU.min,
          drain=True)
        O("vector", "tensor_tensor", A16("t2"), A16("t1"), A16("t2"), ALU.max,
          drain=True)
        O("vector", "tensor_scalar", A16("t2"), A16("t2"), 1e-30, None, ALU.max,
          drain=True)
        O("vector", "reciprocal", A16("t1"), A16("t2"), drain=True)
        O("vector", "tensor_tensor", A16("t1"), A16("t1"), A16("t4"), ALU.mult,
          drain=True, inc=True)
        W("scalar", "vm", self.semn["vm"])
        O("scalar", "activation", A16("t4"), A16("t1"), AF.Arctan, drain=True,
          inc=True, bias=c_zero[:, 0:1])
        W("vector", "am", self.semn["am"])
        O("vector", "tensor_scalar", A16("t1"), A16("t4"), -2.0, PI_2, ALU.mult,
          ALU.add, drain=True)
        O("vector", "tensor_tensor", A16("t1"), A16("t1"), A16("t3"), ALU.mult,
          drain=True)
        O("vector", "tensor_tensor", A16("t4"), A16("t4"), A16("t1"), ALU.add,
          drain=True)
        O("vector", "tensor_scalar", A16("t3"), A16("pre"), 0.0, None, ALU.is_lt,
          drain=True)
        O("vector", "tensor_scalar", A16("t1"), A16("t4"), -2.0, PI, ALU.mult,
          ALU.add, drain=True)
        O("vector", "tensor_tensor", A16("t1"), A16("t1"), A16("t3"), ALU.mult,
          drain=True)
        O("vector", "tensor_tensor", A16("t4"), A16("t4"), A16("t1"), ALU.add,
          drain=True)
        O("vector", "tensor_scalar", A16("t1"), A16("pim"), 0.0, None, ALU.is_gt,
          drain=True)
        O("vector", "tensor_scalar", A16("t2"), A16("pim"), 0.0, None, ALU.is_lt,
          drain=True)
        O("vector", "tensor_tensor", A16("t1"), A16("t1"), A16("t2"),
          ALU.subtract, drain=True)
        O("vector", "tensor_tensor", A16("theta"), A16("t4"), A16("t1"),
          ALU.mult, drain=True, inc=True)
        W("scalar", "vm", self.semn["vm"])
        O("scalar", "activation", A16("sth"), A16("theta"), AF.Sin, drain=True, bias=c_zero[:, 0:1])
        O("scalar", "activation", A16("t1"), A16("theta"), AF.Abs, drain=True, bias=c_zero[:, 0:1])
        O("scalar", "activation", A16("cth"), A16("t1"), AF.Sin, scale=-1.0,
          bias=c_pi2[:, 0:1], drain=True, inc=True)
        W("vector", "am", self.semn["am"])
        O("vector", "tensor_tensor", A16("t1"), A16("cth"), A16("lcr"), ALU.mult,
          drain=True)
        O("vector", "tensor_tensor", A16("t2"), A16("sth"), A16("lci"), ALU.mult,
          drain=True)
        O("vector", "tensor_tensor", A16("ainit"), A16("t1"), A16("t2"),
          ALU.subtract, drain=True)
        O("vector", "tensor_tensor", A16("t1"), A16("sth"), A16("lcr"), ALU.mult,
          drain=True)
        O("vector", "tensor_tensor", A16("t2"), A16("cth"), A16("lci"), ALU.mult,
          drain=True)
        O("vector", "tensor_tensor", A16("binit"), A16("t1"), A16("t2"), ALU.add,
          drain=True)
        O("vector", "tensor_scalar", A16("binit"), A16("binit"), -1.0, None,
          ALU.mult, drain=True)
        O("vector", "tensor_scalar", A16("ainit"), A16("ainit"), P("mask", 7),
          None, ALU.mult, drain=True)
        O("vector", "tensor_scalar", A16("binit"), A16("binit"), P("mask", 7),
          None, ALU.mult, drain=True)
        O("vector", "tensor_scalar", P("u1"), P("toff"), -1.0, 1.0, ALU.mult,
          ALU.add, drain=True)
        O("vector", "tensor_scalar", A16("pb"), A16("lnrho"), P("u1"), None,
          ALU.mult, drain=True)
        O("vector", "engine_nop", inc=True)
        v_p0 = self.semn["vm"]
        if "theta" in dbg_aps:
            W("sync", "vm", v_p0)
            DMA("sync", dbg_aps["theta"], A16("theta"))
            DMA("sync", dbg_aps["rho"], A16("rho"))
            DMA("sync", dbg_aps["ainit"], A16("ainit"))
            DMA("sync", dbg_aps["lnrho"], A16("lnrho"))
            DMA("sync", dbg_aps["tg"], tglob[:, :])

        g_win = DMA("gpsimd", winsb[:, :], winT)
        g_w1 = {0: DMA("gpsimd", w1sb[:, :], w1h[0])}
        g_w2 = {0: DMA("gpsimd", w2sb[:, :], w2h[0])}

        def drain_pc(n):
            pass

        musum, sqsum = statsf[:, 0:16], statsf[:, 16:32]
        nmr, rst = statsf[:, 32:48], statsf[:, 48:64]

        def tm_stats(src_ap, junk_ap, col, ld, vwaits=()):
            c = slice(col, col + 1)
            W("vector", "ds", ld)
            for (ss, vv) in vwaits:
                W("vector", ss, vv)
            O("vector", "tensor_scalar", junk_ap, src_ap, 1.0, 0.0, ALU.mult,
              ALU.add, accum_out=musum[:, c], inc=True)
            W("scalar", "ds", ld)
            for (ss, vv) in vwaits:
                W("scalar", ss, vv)
            O("scalar", "activation", junk_ap, src_ap, AF.Square,
              accum_out=sqsum[:, c], inc=True, bias=c_zero[:, 0:1])
            a_sq = self.semn["am"]
            W("vector", "am", a_sq)
            O("vector", "tensor_scalar", musum[:, c], musum[:, c], 1.0 / D, None,
              ALU.mult, drain=True)
            O("vector", "tensor_scalar", sqsum[:, c], sqsum[:, c], 1.0 / D, None,
              ALU.mult, drain=True)
            O("vector", "tensor_tensor", rst[:, c], musum[:, c], musum[:, c],
              ALU.mult, drain=True)
            O("vector", "tensor_tensor", rst[:, c], sqsum[:, c], rst[:, c],
              ALU.subtract, drain=True, inc=True)
            W("scalar", "vm", self.semn["vm"])
            O("scalar", "activation", rst[:, c], rst[:, c], AF.Ln,
              bias=c_eps[:, 0:1], drain=True)
            O("scalar", "activation", rst[:, c], rst[:, c], AF.Exp, scale=-0.5,
              drain=True, inc=True, bias=c_zero[:, 0:1])
            a_r = self.semn["am"]
            W("vector", "am", a_r)
            O("vector", "tensor_tensor", nmr[:, c], musum[:, c], rst[:, c],
              ALU.mult, drain=True)
            O("vector", "tensor_scalar", nmr[:, c], nmr[:, c], -1.0, None,
              ALU.mult, drain=True, inc=True)
            return self.semn["vm"], a_r

        v_used, st_done = {}, {}
        a_used = {}
        for tt in range(NTT):
            sl = tt % 2
            if tt >= 2:
                W("sync", "vm", v_used[tt - 2])
                W("sync", "am", a_used[tt - 2])
            ld = DMA("sync", xtm[sl][:, :], x_in[tt * 128:(tt + 1) * 128, :])
            if tt == 0:
                W("vector", "ds", ld)
                W("vector", "vm", v_p0)
            col = tt % 16
            v_s, a_s = tm_stats(xtm[sl][:, :], jnkb[:, :], col, ld)
            a_used[tt] = a_s
            if tt >= 1:
                W("vector", "ds", st_done[tt - 1])
            cc = slice(col, col + 1)
            O("vector", "tensor_scalar", aotm[:, :], xtm[sl][:, :], rst[:, cc],
              nmr[:, cc], ALU.mult, ALU.add, inc=True)
            v_x = self.semn["vm"]
            v_used[tt] = v_x
            W("sync", "vm", v_x)
            st_done[tt] = DMA("sync", xlna_tm[tt * 128:(tt + 1) * 128, :],
                              aotm[:, :])
            if tt % 4 == 3:
                drain_pc(4)

        xb_mm = {}
        a_p2 = 0
        for k in range(NCK):
            W("sync", "dg", g_win)
            W("sync", "ds", st_done[4 * k + 3])
            if k >= 1:
                W("sync", "tm", xb_mm[k - 1])
            ldv = 0
            for kv in range(NKD):
                ldv = DMA("sync", xlnw[:, kv * 512:(kv + 1) * 512],
                          xlna_tm[k * 512:(k + 1) * 512,
                                  kv * 128:(kv + 1) * 128], transpose=True)
            for dtile in range(4):
                bank = dtile
                pairs = [(winsb[:, kv * 512 + dtile * 128:
                                kv * 512 + dtile * 128 + 128],
                          xlnw[:, kv * 512:(kv + 1) * 512]) for kv in range(NKD)]
                self.mm_group(bank, pairs, waits=[("ds", ldv)])
                a_p2 = self.evac(bank, "scalar", lambda b=bank, d=dtile, k=k:
                                 O("scalar", "activation",
                                   h_t[d][:, k * 512:(k + 1) * 512],
                                   self.psum[b][:, :], AF.Identity,
                                   bias=P("bin", d), inc=True))
            xb_mm[k] = self.semn["tm"]
            drain_pc(6)
        if "h_win" in dbg_aps:
            W("sync", "am", a_p2)
            for c in range(4):
                DMA("sync", dbg_aps["h_win"][c * 128:(c + 1) * 128, :],
                    h_t[c][:, :])

        def inner_ln_stats(extra_waits):
            t_c = {}
            for c in range(4):
                sl = c % 2
                if c == 0:
                    for (ss, vv) in extra_waits:
                        W("vector", ss, vv)
                if c >= 2:
                    W("vector", "tm", t_c[c - 2])
                O("vector", "tensor_copy", hb[sl][:, :], h_t[c][:, :], inc=True)
                v_cast = self.semn["vm"]
                W("scalar", "vm", v_cast)
                if c == 0:
                    for (ss, vv) in extra_waits:
                        W("scalar", ss, vv)
                else:
                    W("scalar", "tm", t_c[c - 1])
                O("scalar", "activation", ffn1[:, 0:T], hb[sl][:, :], AF.Square,
                  inc=True, bias=c_zero[:, 0:1])
                a_sqc = self.semn["am"]
                W("tensor", "vm", v_cast)
                W("tensor", "am", a_sqc)
                if c == 0:
                    for b in range(8):
                        s, v = self.bank_r[b]
                        W("tensor", s, v)
                for k in range(NCK):
                    O("tensor", "matmul", self.psum[k][0:1, :], ones_bf[:, :],
                      hb[sl][:, k * 512:(k + 1) * 512],
                      start=(c == 0), stop=(c == 3))
                for k in range(NCK):
                    O("tensor", "matmul", self.psum[4 + k][0:1, :],
                      ones_bf[:, :], ffn1[:, k * 512:(k + 1) * 512],
                      start=(c == 0), stop=(c == 3),
                      inc=(k == NCK - 1))
                t_c[c] = self.semn["tm"]
            t_stats = self.semn["tm"]
            for b in range(8):
                self.bank_w[b] = ("tm", t_stats)
            W("vector", "tm", t_stats)
            for k in range(NCK):
                O("vector", "tensor_copy", mu_row[0:1, k * 512:(k + 1) * 512],
                  self.psum[k][0:1, :])
                O("vector", "tensor_copy", sq_row[0:1, k * 512:(k + 1) * 512],
                  self.psum[4 + k][0:1, :])
            O("vector", "engine_nop", inc=True)
            v_ev = self.semn["vm"]
            for b in range(8):
                self.bank_r[b] = ("vm", v_ev)
            O("vector", "tensor_tensor", var_scr[0:1, 0:T], mu_row[0:1, 0:T],
              mu_row[0:1, 0:T], ALU.mult)
            O("vector", "tensor_tensor", sq_row[0:1, 0:T], sq_row[0:1, 0:T],
              var_scr[0:1, 0:T], ALU.subtract, inc=True)
            W("scalar", "vm", self.semn["vm"])
            O("scalar", "activation", sq_row[0:1, 0:T], sq_row[0:1, 0:T], AF.Ln,
              bias=c_eps[0:1, 0:1])
            O("scalar", "activation", sq_row[0:1, 0:T], sq_row[0:1, 0:T], AF.Exp,
              scale=-0.5, inc=True, bias=c_zero[0:1, 0:1])
            W("vector", "am", self.semn["am"])
            O("vector", "tensor_tensor", mu_row[0:1, 0:T], mu_row[0:1, 0:T],
              sq_row[0:1, 0:T], ALU.mult, inc=True)
            v_rr = self.semn["vm"]
            if "murow" in dbg_aps and not self.dumped_stats:
                self.dumped_stats = True
                W("sync", "vm", v_rr)
                dmp = DMA("sync", dbg_aps["murow"], mu_row[0:1, 0:T])
                dmp = DMA("sync", dbg_aps["sqrow"], sq_row[0:1, 0:T])
                W("vector", "ds", dmp)
            for k in range(NCK):
                self.mm_group(k, [(nonesr_f[:, :],
                                   mu_row[0:1, k * 512:(k + 1) * 512])],
                              waits=[("vm", v_rr)])
                self.mm_group(4 + k, [(onesr_f[:, :],
                                       sq_row[0:1, k * 512:(k + 1) * 512])])

        def normalize_to_sc1(c):
            for k in range(NCK):
                for b in (k, 4 + k):
                    s, v = self.bank_w[b]
                    W("vector", s, v)
                sl5 = slice(k * 512, (k + 1) * 512)
                O("vector", "tensor_tensor", sc1[:, sl5], h_t[c][:, sl5],
                  self.psum[4 + k][:, :], ALU.mult)
                O("vector", "tensor_tensor", sc1[:, sl5], sc1[:, sl5],
                  self.psum[k][:, :], ALU.add, inc=(k == NCK - 1))
            v_n = self.semn["vm"]
            for k in range(NCK):
                self.bank_r[k] = ("vm", v_n)
                self.bank_r[4 + k] = ("vm", v_n)
            return v_n

        def corr_combine(g_rd):
            W("vector", "dg", g_rd)
            O("vector", "memset", czt[:, :], 0.0)
            for j in range(7):
                O("vector", "tensor_tensor", czt[:, :], czt[:, :],
                  ccd[:, j * 8:j * 8 + 8], ALU.mult, drain=True)
                O("vector", "scalar_tensor_tensor", czt[:, :],
                  ccg[:, j * 8:j * 8 + 8], P("mask", j), czt[:, :],
                  ALU.mult, ALU.add, drain=True)
            O("vector", "engine_nop", drain=True, inc=True)
            return self.semn["vm"]

        def apply_corr(l, c, slot, recompute):
            dcol = l * 4 + c
            cosb, sinb = SC_COS[slot][:, :], SC_SIN[slot][:, :]
            if recompute:
                O("vector", "tensor_scalar", cosb, tglob[:, :],
                  P("theta", dcol), None, ALU.mult)
                O("vector", "tensor_scalar", sinb, cosb, INV2PI, MAGIC,
                  ALU.mult, ALU.add)
                O("vector", "tensor_scalar", sinb, sinb, -MAGIC, None, ALU.add)
                O("vector", "scalar_tensor_tensor", cosb, sinb, -CW_C1, cosb,
                  ALU.mult, ALU.add)
                O("vector", "scalar_tensor_tensor", cosb, sinb, -CW_C2, cosb,
                  ALU.mult, ALU.add)
                O("vector", "scalar_tensor_tensor", cosb, sinb, -CW_C3, cosb,
                  ALU.mult, ALU.add)
                O("vector", "tensor_scalar", cosb, cosb, PI, -PI, ALU.min,
                  ALU.max, inc=True)
                W("scalar", "vm", self.semn["vm"])
                O("scalar", "activation", sinb, cosb, AF.Sin,
                  bias=c_zero[:, 0:1])
                O("scalar", "activation", sc2[:, :], cosb, AF.Abs,
                  bias=c_zero[:, 0:1])
                O("scalar", "activation", cosb, sc2[:, :], AF.Sin, scale=-1.0,
                  bias=c_pi2[:, 0:1], inc=True)
                W("vector", "am", self.semn["am"])
            W("scalar", "vm", self.semn["vm"])
            O("scalar", "activation", sc2[:, :], tglob[:, :], AF.Exp,
              scale=P("lnrho", dcol), bias=P("pb", dcol), inc=True)
            a_rp = self.semn["am"]
            W("vector", "am", a_rp)
            O("vector", "tensor_scalar", sc1[:, :], cosb,
              czt[:, 2 * c:2 * c + 1], None, ALU.mult)
            O("vector", "scalar_tensor_tensor", sc1[:, :], sinb,
              czt[:, 2 * c + 1:2 * c + 2], sc1[:, :], ALU.mult, ALU.add)
            O("vector", "tensor_tensor", sc1[:, :], sc1[:, :], sc2[:, :],
              ALU.mult)
            O("vector", "scalar_tensor_tensor", h_t[c][:, :], sc1[:, :],
              P("scl", dcol), h_t[c][:, :], ALU.mult, ALU.add, inc=True)
            return self.semn["vm"]

        t_w2_prev = [xb_mm[NCK - 1]]
        a_prev = [a_p2]
        for l in range(NLAYER):
            inner_ln_stats([("am", a_prev[0]), ("tm", t_w2_prev[0])])
            W("vector", "vm", v_p0)
            for c4 in range(4):
                O("vector", "tensor_copy", dpk2[:, 2 * c4:2 * c4 + 1],
                  P("dpk", l * 4 + c4), drain=True)
                O("vector", "tensor_copy", dpk2[:, 2 * c4 + 1:2 * c4 + 2],
                  P("dpk", l * 4 + c4), drain=True)
            for j in range(7):
                O("vector", "tensor_scalar", ccd[:, j * 8:j * 8 + 8],
                  dpk2[:, :], P("mask", j), None, ALU.mult, drain=True)
                O("vector", "tensor_scalar", P("u1"), P("mask", j), -1.0, 1.0,
                  ALU.mult, ALU.add, drain=True)
                O("vector", "tensor_scalar", ccd[:, j * 8:j * 8 + 8],
                  ccd[:, j * 8:j * 8 + 8], P("u1"), None, ALU.add, drain=True)
            O("vector", "engine_nop", inc=True)

            for c in range(4):
                slot = c % 2
                dcol = l * 4 + c
                v_n = normalize_to_sc1(c)
                W("scalar", "vm", v_n)
                O("scalar", "activation", sc1[:, :], sc1[:, :], AF.Silu,
                  scale=P("ln1w", dcol), bias=P("ln1b", dcol), inc=True)
                a_yt = self.semn["am"]
                cosb = SC_COS[slot][:, :]
                sinb = SC_SIN[slot][:, :]
                W("vector", "am", a_yt)
                O("vector", "tensor_scalar", cosb, tglob[:, :],
                  P("theta", dcol), None, ALU.mult, inc=True)
                if l == 0 and c == 0 and "v00" in dbg_aps:
                    W("sync", "vm", self.semn["vm"])
                    dmp = DMA("sync", dbg_aps["v00"], cosb)
                    W("vector", "ds", dmp)
                O("vector", "tensor_scalar", sinb, cosb, INV2PI, MAGIC,
                  ALU.mult, ALU.add)
                O("vector", "tensor_scalar", sinb, sinb, -MAGIC, None, ALU.add)
                O("vector", "scalar_tensor_tensor", cosb, sinb, -CW_C1, cosb,
                  ALU.mult, ALU.add)
                O("vector", "scalar_tensor_tensor", cosb, sinb, -CW_C2, cosb,
                  ALU.mult, ALU.add)
                O("vector", "scalar_tensor_tensor", cosb, sinb, -CW_C3, cosb,
                  ALU.mult, ALU.add)
                O("vector", "tensor_scalar", cosb, cosb, PI, -PI, ALU.min,
                  ALU.max, inc=True)
                if l == 0 and c == 0 and "r00" in dbg_aps:
                    W("sync", "vm", self.semn["vm"])
                    dmp = DMA("sync", dbg_aps["r00"], cosb)
                    W("scalar", "ds", dmp)
                W("scalar", "vm", self.semn["vm"])
                O("scalar", "activation", sinb, cosb, AF.Sin, bias=c_zero[:, 0:1])
                O("scalar", "activation", sc2[:, :], cosb, AF.Abs, bias=c_zero[:, 0:1])
                O("scalar", "activation", cosb, sc2[:, :], AF.Sin, scale=-1.0,
                  bias=c_pi2[:, 0:1], inc=True)
                a_cs = self.semn["am"]
                if l == 0 and c == 0 and "yt00" in dbg_aps:
                    W("sync", "am", a_yt)
                    dmp = DMA("sync", dbg_aps["yt00"], sc1[:, :])
                    W("vector", "ds", dmp)
                if l == 0 and c == 0 and "cos00" in dbg_aps:
                    W("sync", "am", a_cs)
                    dmp = DMA("sync", dbg_aps["cos00"], cosb)
                    dmp = DMA("sync", dbg_aps["sin00"], sinb)
                    W("vector", "ds", dmp)
                    W("scalar", "ds", dmp)
                W("vector", "am", a_cs)
                rho_b = P("rho", dcol).broadcast_to((128, T))
                O("vector", "tensor_tensor", sc2[:, :], cosb, sc1[:, :],
                  ALU.mult)
                O("vector", "tensor_tensor_scan", sc2[:, :], rho_b, sc2[:, :],
                  P("ainit", dcol), ALU.mult, ALU.add)
                O("vector", "tensor_tensor", sc1[:, :], sinb, sc1[:, :],
                  ALU.mult)
                O("vector", "tensor_tensor_scan", sc1[:, :], rho_b, sc1[:, :],
                  P("binit", dcol), ALU.mult, ALU.add, inc=True)
                O("vector", "drain")
                if c == 0:
                    W("vector", "dg", getattr(self, "last_g_st", 0))
                O("vector", "tensor_copy", state_pk[:, 2 * c:2 * c + 1],
                  sc2[:, T - 1:T])
                O("vector", "tensor_copy", state_pk[:, 2 * c + 1:2 * c + 2],
                  sc1[:, T - 1:T])
                O("vector", "engine_nop", drain=True, inc=True)
                v_st = self.semn["vm"]
                if l == 0 and c == 0 and "A00" in dbg_aps:
                    W("sync", "vm", v_st)
                    dmp = DMA("sync", dbg_aps["A00"], sc2[:, :])
                    dmp = DMA("sync", dbg_aps["B00"], sc1[:, :])
                    W("vector", "ds", dmp)
                O("vector", "tensor_tensor", sc2[:, :], cosb, sc2[:, :],
                  ALU.mult)
                O("vector", "tensor_tensor", sc1[:, :], sinb, sc1[:, :],
                  ALU.mult)
                O("vector", "tensor_tensor", sc1[:, :], sc1[:, :], sc2[:, :],
                  ALU.add)
                O("vector", "scalar_tensor_tensor", h_t[c][:, :], sc1[:, :],
                  P("scl", dcol), h_t[c][:, :], ALU.mult, ALU.add, inc=True)
                drain_pc(8)
            W("gpsimd", "vm", v_st)
            g_st = DMA("gpsimd", cc_i[l], state_pk[:, :])
            self.last_g_st = g_st
            W("gpsimd", "dg", g_st)
            self.bump("cc", 1)
            E("gpsimd", lambda e, i=l, h=self.semh["cc"]:
              e.collective_compute("AllGather", ALU.bypass,
                                   [list(range(NCORE))],
                                   ins=[cc_i[i].opt()],
                                   outs=[cc_o[i].opt()]).then_inc(h, 1))
            W("gpsimd", "cc", self.semn["cc"])
            g_rd = DMA("gpsimd",
                       ccg[:, :].rearrange("p (r c) -> p r c", r=NCORE),
                       cc_o[l].rearrange("(r p) c -> p r c", p=128))
            corr_combine(g_rd)
            apply_corr(l, 2, 0, False)
            apply_corr(l, 3, 1, False)
            apply_corr(l, 0, 0, True)
            apply_corr(l, 1, 1, True)
            v_mix = self.semn["vm"]

            inner_ln_stats([("vm", v_mix)])
            W("vector", "tm", t_w2_prev[0])
            for c in range(4):
                dcol = l * 4 + c
                normalize_to_sc1(c)
                O("vector", "tensor_scalar", yb[c][:, :], sc1[:, :],
                  P("ln2w", dcol), P("ln2b", dcol), ALU.mult, ALU.add,
                  inc=True)
            v_yb = self.semn["vm"]

            W("tensor", "dg", g_w1[l])
            W("tensor", "dg", g_w2[l])
            W("tensor", "vm", v_yb)
            for k in range(NCK):
                for ft in range(16):
                    bank = ft % 4
                    pairs = [(w1sb[:, c * DFI + ft * 128:c * DFI + ft * 128 + 128],
                              yb[c][:, k * 512:(k + 1) * 512]) for c in range(4)]
                    self.mm_group(bank, pairs)
                    self.evac(bank, "scalar", lambda b=bank, ft=ft, l=l:
                              O("scalar", "activation",
                                ffn1[:, ft * 512:(ft + 1) * 512],
                                self.psum[b][:, :], AF.Silu, scale=1.0,
                                bias=P("b1", l * 16 + ft), inc=True))
                a_f1 = self.semn["am"]
                for dtile in range(4):
                    bank = 4 + (dtile % 2)
                    pairs = [(w2sb[:, ft * DI + dtile * 128:
                                   ft * DI + dtile * 128 + 128],
                              ffn1[:, ft * 512:(ft + 1) * 512])
                             for ft in range(16)]
                    self.mm_group(bank, pairs, waits=[("am", a_f1)])
                    self.evac(bank, "vector", lambda b=bank, d=dtile, k=k, l=l:
                              O("vector", "scalar_tensor_tensor",
                                h_t[d][:, k * 512:(k + 1) * 512],
                                self.psum[b][:, :], P("b2", l * 4 + d),
                                h_t[d][:, k * 512:(k + 1) * 512],
                                ALU.add, ALU.add, inc=True))
                drain_pc(3)
            t_w2 = self.semn["tm"]
            t_w2_prev[0] = t_w2
            a_prev[0] = self.semn["am"]
            v_hdone = self.semn["vm"]
            if (l + 1) < NLAYER:
                W("gpsimd", "tm", t_w2)
                g_w1[l + 1] = DMA("gpsimd", w1sb[:, :], w1h[l + 1])
                g_w2[l + 1] = DMA("gpsimd", w2sb[:, :], w2h[l + 1])
            if f"h_l{l}" in dbg_aps:
                W("sync", "vm", v_hdone)
                W("sync", "am", self.semn["am"])
                for c in range(4):
                    DMA("sync", dbg_aps[f"h_l{l}"][c * 128:(c + 1) * 128, :],
                        h_t[c][:, :])
        v_layers = self.semn["vm"]

        W("gpsimd", "vm", v_layers)
        g_wout = DMA("gpsimd", woutsb[:, :], woutT)

        inner_ln_stats([("vm", v_layers), ("tm", t_w2_prev[0]),
                        ("am", a_prev[0])])
        W("vector", "tm", t_w2_prev[0])
        for c in range(4):
            normalize_to_sc1(c)
            O("vector", "tensor_scalar", yb[c][:, :], sc1[:, :],
              P("lnlw", c), P("lnlb", c), ALU.mult, ALU.add, inc=True)
        v_hlb = self.semn["vm"]

        st_ao = {}
        i_ao = 0
        for k in range(NCK):
            for vt in range(NKD):
                bank = vt % 4
                slv = i_ao % 2
                pairs = [(woutsb[:, c * D + vt * 128:c * D + vt * 128 + 128],
                          yb[c][:, k * 512:(k + 1) * 512]) for c in range(4)]
                self.mm_group(bank, pairs,
                              waits=([("dg", g_wout), ("vm", v_hlb)]
                                     if (k == 0 and vt == 0) else ()))
                ew = [("ds", st_ao[i_ao - 2])] if i_ao >= 2 else []
                self.evac(bank, "scalar", lambda b=bank, vt=vt, s=slv:
                          O("scalar", "activation", aoev[s][:, :],
                            self.psum[b][:, :], AF.Identity,
                            bias=P("bout", vt), inc=True), waits=ew)
                W("sync", "am", self.semn["am"])
                st_ao[i_ao] = DMA(
                    "sync", ao_fm[vt * 128:(vt + 1) * 128, k * 512:(k + 1) * 512],
                    aoev[slv][:, :])
                i_ao += 1
            drain_pc(6)
        v_ao_all = self.semn["ds"]

        v_jnk = {}
        st4 = {}
        v_p4tail = 0
        for tt in range(NTT):
            sl = tt % 2
            if tt == 0:
                W("sync", "ds", v_ao_all)
            if tt >= 2:
                W("sync", "vm", v_used[tt - 2])
                W("sync", "ds", st4.get(tt - 2, 0))
                W("sync", "vm", v_jnk[tt - 2])
            ld_ao = DMA("sync", jnkb4[sl][:, :],
                        ao_fm[0:D, tt * 128:(tt + 1) * 128], transpose=True)
            ld_x = DMA("sync", xtm4[sl][:, :], x_in[tt * 128:(tt + 1) * 128, :])
            col = tt % 16
            cc = slice(col, col + 1)
            v_s, a_s = tm_stats(jnkb4[sl][:, :], aotm4[:, :], col, ld_ao)
            W("vector", "ds", ld_x)
            O("vector", "tensor_scalar", xtm4[sl][:, :], xtm4[sl][:, :],
              nmr[:, cc], None, ALU.add)
            O("vector", "scalar_tensor_tensor", xtm4[sl][:, :], jnkb4[sl][:, :],
              rst[:, cc], xtm4[sl][:, :], ALU.mult, ALU.add, inc=True)
            v_x1 = self.semn["vm"]
            W("sync", "vm", v_x1)
            st4[tt] = DMA("sync", x1_tm[tt * 128:(tt + 1) * 128, :],
                          xtm4[sl][:, :])
            v_s2, a_s2 = tm_stats(xtm4[sl][:, :], aotm4[:, :], col, 0,
                                  vwaits=[("vm", v_x1)])
            O("vector", "tensor_scalar", jnkb4[sl][:, :], xtm4[sl][:, :], rst[:, cc],
              nmr[:, cc], ALU.mult, ALU.add, inc=True)
            v_xb = self.semn["vm"]
            v_used[tt] = v_xb
            v_jnk[tt] = v_xb
            W("sync", "vm", v_xb)
            st4[tt] = DMA("sync", xlnb_tm[tt * 128:(tt + 1) * 128, :],
                          jnkb4[sl][:, :])
            if tt % 4 == 3:
                drain_pc(4)
        v_p4 = self.semn["ds"]
        v_p4vm = self.semn["vm"]
        if "x1d" in dbg_aps:
            W("sync", "ds", v_p4)
            DMA("sync", dbg_aps["x1d"], x1_tm)
            W("gpsimd", "ds", v_p4)
            DMA("gpsimd", dbg_aps["aod"], ao_fm)
            DMA("gpsimd", dbg_aps["xlnbd"], xlnb_tm)

        drain_pc(10 ** 9)
        fchunks = [(0, FCH), (FCH, 2 * FCH), (2 * FCH, NFT)]
        st_ffo = {}
        self.gsil_v = {}
        gu_t = {}
        ds_t = {}
        t_guplast = {}
        for blk in range(NCK):
            tsl = slice(blk * 512, (blk + 1) * 512)
            W("sync", "ds", st4[4 * blk + 3])
            if blk >= 1:
                W("sync", "tm", t_guplast[blk - 1])
            ldx = 0
            for kv in range(NKD):
                ldx = DMA("sync", xlnffn[:, kv * 512:(kv + 1) * 512],
                          xlnb_tm[tsl, kv * 128:(kv + 1) * 128], transpose=True)
            pend = {}

            def issue_gu_loads(ft, blk=blk):
                gu2 = ft % 2
                if gu2 in gu_t:
                    W("gpsimd", "tm", gu_t[gu2])
                lg = DMA("gpsimd", gusb[gu2][:, :], gateT[ft])
                lu = DMA("gpsimd", gusb[2 + gu2][:, :], upT[ft])
                pend[ft] = (lg, lu)

            issue_gu_loads(0 if blk >= 0 else 0) if False else None
            first_issued = [False]
            for (f0, f1) in fchunks:
                first_chunk = (f0 == 0)
                v_h = 0
                if not first_issued[0]:
                    issue_gu_loads(f0)
                    first_issued[0] = True
                for ft in range(f0, f1):
                    gu = ft % 2
                    if ft + 1 < NFT and (ft + 1) not in pend:
                        issue_gu_loads(ft + 1)
                    ldg, ldu = pend.pop(ft)
                    bank_g, bank_u = ft % 4, 4 + ft % 4
                    pairs = [(gusb[gu][:, kv * 128:(kv + 1) * 128],
                              xlnffn[:, kv * 512:(kv + 1) * 512])
                             for kv in range(NKD)]
                    self.mm_group(bank_g, pairs, waits=[("dg", ldg), ("ds", ldx)])
                    gw = ([("vm", self.gsil_v[ft % 2])]
                          if (ft % 2) in self.gsil_v else [])
                    a_g = self.evac(bank_g, "scalar",
                                    lambda b=bank_g, ft=ft, s=ft % 2:
                                    O("scalar", "activation", gsil[s][:, :],
                                      self.psum[b][:, :], AF.Silu,
                                      scale=P("gs", ft), inc=True, bias=c_zero[:, 0:1]), waits=gw)
                    pairs = [(gusb[2 + gu][:, kv * 128:(kv + 1) * 128],
                              xlnffn[:, kv * 512:(kv + 1) * 512])
                             for kv in range(NKD)]
                    gu_t[gu] = self.mm_group(bank_u, pairs,
                                             waits=[("dg", ldu)])
                    t_guplast[blk] = gu_t[gu]
                    v_h = self.evac(bank_u, "vector",
                                    lambda b=bank_u, ft=ft, f0=f0, s=ft % 2:
                                    O("vector", "scalar_tensor_tensor",
                                      hch[:, (ft - f0) * 512:(ft - f0 + 1) * 512],
                                      self.psum[b][:, :], P("us", ft),
                                      gsil[s][:, :], ALU.mult, ALU.mult,
                                      inc=True),
                                    waits=[("am", a_g)])
                    self.gsil_v[ft % 2] = v_h
                if blk == 0 and first_chunk and "hch0" in dbg_aps:
                    W("gpsimd", "vm", v_h)
                    DMA("gpsimd", dbg_aps["hch0"], hch[:, :])
                for dg_i in range(4):
                    pend_d = {}

                    def issue_d_loads(ft, dg_i=dg_i):
                        s2b = (ft % 2) * 2
                        if ft % 2 in ds_t:
                            W("gpsimd", "tm", ds_t[ft % 2])
                        DMA("gpsimd", dsb[s2b][:, :],
                            downT[ft][:, dg_i * 1024:dg_i * 1024 + 512])
                        pend_d[ft] = DMA(
                            "gpsimd", dsb[s2b + 1][:, :],
                            downT[ft][:, dg_i * 1024 + 512:dg_i * 1024 + 1024])

                    issue_d_loads(f0)
                    for ft in range(f0, f1):
                        s2 = (ft % 2) * 2
                        if ft + 1 < f1:
                            issue_d_loads(ft + 1)
                        W("tensor", "dg", pend_d.pop(ft))
                        if ft == f0:
                            W("tensor", "vm", v_h)
                            for b in range(8):
                                s, v = self.bank_r[b]
                                W("tensor", s, v)
                        for di in range(8):
                            buf = dsb[s2] if di < 4 else dsb[s2 + 1]
                            off = (di % 4) * 128
                            O("tensor", "matmul", self.psum[di][:, :],
                              buf[:, off:off + 128],
                              hch[:, (ft - f0) * 512:(ft - f0 + 1) * 512],
                              start=(ft == f0), stop=(ft == f1 - 1),
                              inc=(di == 7))
                        ds_t[ft % 2] = self.semn["tm"]
                    t_dg = self.semn["tm"]
                    for b in range(8):
                        self.bank_w[b] = ("tm", t_dg)
                    W("vector", "tm", t_dg)
                    if blk == 0 and first_chunk and dg_i == 0:
                        W("vector", "vm", v_p4vm)
                        W("vector", "ds", v_p4)
                    for di in range(8):
                        dtile = dg_i * 8 + di
                        if first_chunk:
                            O("vector", "tensor_copy", accum[dtile][:, :],
                              self.psum[di][:, :], inc=True)
                        else:
                            O("vector", "tensor_tensor", accum[dtile][:, :],
                              accum[dtile][:, :], self.psum[di][:, :], ALU.add,
                              inc=True)
                        self.bank_r[di] = ("vm", self.semn["vm"])
            for dtile in range(NKD):
                s = dtile % 2
                if dtile >= 2:
                    W("vector", "ds", st_ffo[(blk, dtile - 2)])
                O("vector", "tensor_scalar", aoev[s][:, :], accum[dtile][:, :],
                  P("ds", dtile), None, ALU.mult, inc=True)
                W("sync", "vm", self.semn["vm"])
                st_ffo[(blk, dtile)] = DMA(
                    "sync", ffo_fm[dtile * 128:(dtile + 1) * 128, tsl],
                    aoev[s][:, :])
        v_ffo_all = self.semn["ds"]
        if "ffod" in dbg_aps:
            W("gpsimd", "ds", v_ffo_all)
            DMA("gpsimd", dbg_aps["ffod"], ffo_fm)

        v_o = {}
        st6 = {}
        for tt in range(NTT):
            sl = tt % 2
            if tt == 0:
                W("sync", "ds", v_ffo_all)
            if tt >= 2:
                W("sync", "vm", v_o[tt - 2])
                W("sync", "ds", st6.get(tt - 2, 0))
            ld_fo = DMA("sync", jnkb4[sl][:, :],
                        ffo_fm[0:D, tt * 128:(tt + 1) * 128], transpose=True)
            ld_x1 = DMA("sync", xtm4[sl][:, :], x1_tm[tt * 128:(tt + 1) * 128, :])
            col = tt % 16
            cc = slice(col, col + 1)
            tm_stats(jnkb4[sl][:, :], aotm4[:, :], col, ld_fo)
            W("vector", "ds", ld_x1)
            O("vector", "tensor_scalar", xtm4[sl][:, :], xtm4[sl][:, :],
              nmr[:, cc], None, ALU.add)
            O("vector", "scalar_tensor_tensor", xtm4[sl][:, :], jnkb4[sl][:, :],
              rst[:, cc], xtm4[sl][:, :], ALU.mult, ALU.add, inc=True)
            v_o[tt] = self.semn["vm"]
            W("sync", "vm", v_o[tt])
            st6[tt] = DMA("sync", out_d[tt * 128:(tt + 1) * 128, :],
                          xtm4[sl][:, :])
        W("sync", "ds", self.semn["ds"])

        with nc.Block() as blk_:
            for meth, eng in ((blk_.sync, "sync"), (blk_.gpsimd, "gpsimd"),
                              (blk_.tensor, "tensor"), (blk_.vector, "vector"),
                              (blk_.scalar, "scalar")):
                ops = self.q[eng]

                def body(e, ops=ops):
                    for f in ops:
                        f(e)
                meth(body)
        return nc


def _host_layout(inputs):
    rs = np.ascontiguousarray

    def pack_chan(a):
        return rs(np.asarray(a).reshape(4, 4, 128).transpose(2, 0, 1)
                  .reshape(128, 16))

    W1 = np.asarray(inputs["L0_attn_W1"])
    W2 = np.asarray(inputs["L0_attn_W2"])
    g = {}
    g["winT"] = rs(np.transpose(np.asarray(inputs["L0_attn_Win"]))
                   .reshape(32, 128, 4, 128).transpose(1, 0, 2, 3)
                   .reshape(128, 32 * 512))
    g["w1h"] = rs(np.stack([np.transpose(W1[l]).reshape(4, 128, 2048)
                            .transpose(1, 0, 2).reshape(128, 4 * 2048)
                            for l in range(4)]))
    g["w2h"] = rs(np.stack([np.transpose(W2[l]).reshape(16, 128, 512)
                            .transpose(1, 0, 2).reshape(128, 16 * 512)
                            for l in range(4)]))
    g["woutT"] = rs(np.transpose(np.asarray(inputs["L0_attn_Wout"]))
                    .reshape(4, 128, 4096).transpose(1, 0, 2)
                    .reshape(128, 4 * 4096))
    def low_bytes(a):
        return rs(rs(a).view(np.int8)[..., 0::4])

    for nm, key in [("gateT", "L0_ffn_gate_w"), ("upT", "L0_ffn_up_w")]:
        w = np.asarray(inputs[key])
        g[nm] = low_bytes(w.reshape(NFT, 128, NKD, 128).transpose(0, 3, 2, 1)
                          .reshape(NFT, 128, NKD * 128))
    g["downT"] = low_bytes(np.transpose(np.asarray(inputs["L0_ffn_down_w"]))
                           .reshape(NFT, 128, NKD * 128))
    pp = {"ln1w": pack_chan(inputs["L0_attn_ln1_w"]),
          "ln1b": pack_chan(inputs["L0_attn_ln1_b"]),
          "ln2w": pack_chan(inputs["L0_attn_ln2_w"]),
          "ln2b": pack_chan(inputs["L0_attn_ln2_b"]),
          "scl": pack_chan(inputs["L0_attn_sc_lin"]),
          "b2": pack_chan(inputs["L0_attn_b2"]),
          "pre": pack_chan(inputs["L0_attn_phz_re"]),
          "pim": pack_chan(inputs["L0_attn_phz_im"]),
          "lcr": pack_chan(inputs["L0_attn_lc_re"]),
          "lci": pack_chan(inputs["L0_attn_lc_im"]),
          "b1": rs(np.asarray(inputs["L0_attn_b1"]).reshape(4, 16, 128)
                   .transpose(2, 0, 1).reshape(128, 64)),
          "bin": rs(np.asarray(inputs["L0_attn_bin"]).reshape(4, 128).T),
          "lnlw": rs(np.asarray(inputs["L0_attn_lnl_w"]).reshape(4, 128).T),
          "lnlb": rs(np.asarray(inputs["L0_attn_lnl_b"]).reshape(4, 128).T),
          "bout": rs(np.asarray(inputs["L0_attn_bout"]).reshape(32, 128).T),
          "gs": rs(np.asarray(inputs["L0_ffn_gate_s"]).reshape(NFT, 128).T),
          "us": rs(np.asarray(inputs["L0_ffn_up_s"]).reshape(NFT, 128).T),
          "ds": rs(np.asarray(inputs["L0_ffn_down_s"]).reshape(32, 128).T)}
    for k, v in pp.items():
        g["pp_" + k] = rs(np.asarray(v, np.float32))
    return g


def kernel(**inputs):
    T = 16384 // NCORE
    shared = _host_layout(inputs)
    x = np.asarray(inputs["L0_attn_x"])[0]
    in_maps = []
    for ci in range(NCORE):
        m = dict(shared)
        m["x_in"] = np.ascontiguousarray(x[ci * T:(ci + 1) * T, :], np.float32)
        m["pp_toff"] = np.full((128, 1), float(ci * T), np.float32)
        msk = np.zeros((128, 8), np.float32)
        msk[:, :7] = (np.arange(7)[None, :] < ci)
        msk[:, 7] = 1.0 if ci == 0 else 0.0
        m["pp_mask"] = msk
        in_maps.append(m)
    kb = KB(T)
    nc = kb.build()
    res = run_bass_kernel_spmd(nc, in_maps, core_ids=list(range(NCORE)))
    out = np.concatenate([np.asarray(r["out"]) for r in res.results], axis=0)
    return out[None, :, :].astype(np.float32)


# version 07
# speedup vs baseline: 1.1849x; 1.0991x over previous
import math

import numpy as np
import ml_dtypes

import concourse.bass as bass
import concourse.mybir as mybir
from concourse.bass_utils import run_bass_kernel_spmd

F32 = mybir.dt.float32
BF16 = mybir.dt.bfloat16
I32 = mybir.dt.int32
AF = mybir.ActivationFunctionType
ALU = mybir.AluOpType
BF = ml_dtypes.bfloat16

NCORE = 8
D = 4096
DI = 512
DFI = 2048
FF = 11008
NFT = FF // 128
NKD = D // 128
NLAYER = 4
EPS = 1e-5
FCH = 40

_TP = 2 * math.pi
CW_C1 = float(np.float32(np.ldexp(np.round(np.ldexp(_TP, 8)), -8)))
CW_C2 = float(np.float32(np.ldexp(np.round(np.ldexp(_TP - CW_C1, 19)), -19)))
CW_C3 = float(np.float32(_TP - CW_C1 - CW_C2))
MAGIC = float(np.float32(1.5 * 2.0 ** 23))
INV2PI = float(np.float32(1.0 / _TP))
PI = float(np.float32(math.pi))
PI_2 = float(np.float32(math.pi / 2))

_DEBUG = []


class KB:
    def __init__(self, T):
        self.T = T
        self.NCK = T // 512
        self.NTT = T // 128
        nc = bass.Bass("TRN2", target_bir_lowering=False)
        self.nc = nc
        self.q = {e: [] for e in ("sync", "gpsimd", "tensor", "vector", "scalar")}
        self.semh, self.semn = {}, {}
        for s in ("dg", "ds", "tm", "vm", "am", "cc"):
            self.semh[s] = nc.alloc_semaphore("sem_" + s)
            self.semn[s] = 0
        self.bank_w = [("tm", 0)] * 8
        self.bank_r = [("tm", 0)] * 8
        self.pc_items = []
        self.pc_mark = {}
        self.dumped_stats = False

    def emit(self, eng, fn):
        self.q[eng].append(fn)

    def W(self, eng, sem, val):
        if val:
            self.emit(eng, lambda e, h=self.semh[sem], v=val: e.wait_ge(h, v))

    def bump(self, sem, n):
        self.semn[sem] += n
        return self.semn[sem]

    def DMA(self, eng, dst, src, transpose=False):
        sem = {"sync": "ds", "gpsimd": "dg"}[eng]
        self.bump(sem, 16)
        self.emit(eng, lambda e, d=dst, s=src, h=self.semh[sem], t=transpose:
                  e.dma_start(d, s, transpose=t).then_inc(h, 16))
        return self.semn[sem]

    def O(self, eng, meth, *args, inc=False, drain=False, **kw):
        sem = {"tensor": "tm", "vector": "vm", "scalar": "am", "gpsimd": "vm"}[eng]
        if inc:
            self.bump(sem, 1)
        def f(e, meth=meth, args=args, kw=kw, inc=inc, drain=drain,
              h=self.semh[sem]):
            if drain:
                e.drain()
            r = getattr(e, meth)(*args, **kw)
            if inc:
                r.then_inc(h, 1)
        self.emit(eng, f)
        return self.semn[sem]

    def mm_group(self, bank, pairs, waits=()):
        s, v = self.bank_r[bank]
        self.W("tensor", s, v)
        for (ss, vv) in waits:
            self.W("tensor", ss, vv)
        n = len(pairs)
        for i, (l, r) in enumerate(pairs):
            self.O("tensor", "matmul", self.psum[bank][:, :], l, r,
                   start=(i == 0), stop=(i == n - 1), inc=(i == n - 1))
        self.bank_w[bank] = ("tm", self.semn["tm"])
        return self.semn["tm"]

    def evac(self, bank, eng, emit_cb, waits=()):
        s, v = self.bank_w[bank]
        self.W(eng, s, v)
        for (ss, vv) in waits:
            self.W(eng, ss, vv)
        emit_cb()
        sem = {"vector": "vm", "scalar": "am"}[eng]
        self.bank_r[bank] = (sem, self.semn[sem])
        return self.semn[sem]

    def build(self):
        nc, T, NCK, NTT = self.nc, self.T, self.NCK, self.NTT
        W, O, DMA, E = self.W, self.O, self.DMA, self.emit
        dt = nc.dram_tensor

        x_in = dt("x_in", [T, D], F32, kind="ExternalInput").ap()
        winT = dt("winT", [128, NKD * DI], F32, kind="ExternalInput").ap()
        w1h = dt("w1h", [NLAYER, 128, 4 * DFI], F32, kind="ExternalInput").ap()
        w2h = dt("w2h", [NLAYER, 128, 16 * DI], F32, kind="ExternalInput").ap()
        woutT = dt("woutT", [128, 4 * D], F32, kind="ExternalInput").ap()
        gateT = dt("gateT", [NFT, 128, D], mybir.dt.int8, kind="ExternalInput").ap()
        upT = dt("upT", [NFT, 128, D], mybir.dt.int8, kind="ExternalInput").ap()
        downT = dt("downT", [NFT, 128, D], mybir.dt.int8, kind="ExternalInput").ap()
        PCOLS = [("ln1w", 16), ("ln1b", 16), ("ln2w", 16), ("ln2b", 16),
                 ("scl", 16), ("b2", 16), ("pre", 16), ("pim", 16),
                 ("lcr", 16), ("lci", 16), ("b1", 64), ("bin", 4),
                 ("lnlw", 4), ("lnlb", 4), ("bout", 32), ("gs", NFT),
                 ("us", NFT), ("ds", 32), ("toff", 1), ("mask", 8)]
        pin = {nm: dt("pp_" + nm, [128, n], F32, kind="ExternalInput").ap()
               for nm, n in PCOLS}
        out_d = dt("out", [T, D], F32, kind="ExternalOutput").ap()
        dbg_aps = {nm: dt("dbg_" + nm, list(sh), F32, kind="ExternalOutput").ap()
                   for nm, sh in _DEBUG}

        xlna_tm = dt("xlna_tm", [T, D], BF16, kind="Internal").ap()
        ao_fm = dt("ao_fm", [D, T], BF16, kind="Internal").ap()
        x1_tm = dt("x1_tm", [T, D], F32, kind="Internal").ap()
        xlnb_tm = dt("xlnb_tm", [T, D], BF16, kind="Internal").ap()
        ffo_fm = dt("ffo_fm", [D, T], BF16, kind="Internal").ap()
        ncc = NLAYER
        cc_i = [dt(f"cc_i{k}", [128, 8], F32, kind="Internal").ap() for k in range(ncc)]
        cc_o = [dt(f"cc_o{k}", [NCORE * 128, 8], F32, kind="Internal").ap()
                for k in range(ncc)]

        pos = [16384]

        def place(name, shape, dtype):
            nb = int(np.prod(shape[1:])) * (4 if dtype != BF16 else 2)
            at = pos[0]
            pos[0] = at + ((nb + 63) // 64) * 64
            return nc.alloc_sbuf_tensor_at(name, list(shape), dtype, offset=at), at

        def over(name, shape, dtype, at):
            return nc.alloc_sbuf_tensor_at(name, list(shape), dtype, offset=at)

        TB = T * 4
        h_t = []
        H0 = pos[0]
        for c in range(4):
            h_t.append(place(f"h{c}", [128, T], F32)[0])
        tglob, TG0 = place("tglob", [128, T], F32)
        params, _ = place("params", [128, 1024], F32)
        statsf, _ = place("statsf", [128, 64], F32)
        var_scr, VS0 = place("var_scr", [1, T], F32)
        ccg, _ = place("ccg", [128, 64], F32)
        czt, _ = place("czt", [128, 8], F32)
        dpk2, _ = place("dpk2", [128, 8], F32)
        ccd, _ = place("ccd", [128, 64], F32)
        ones_bf, _ = place("ones_bf", [128, 1], BF16)
        onesr_f, _ = place("onesr_f", [1, 128], F32)
        nonesr_f, _ = place("nonesr_f", [1, 128], F32)
        c_pi2, _ = place("c_pi2", [128, 1], F32)
        c_eps, _ = place("c_eps", [128, 1], F32)
        c_zero, _ = place("c_zero", [128, 1], F32)
        state_pk, _ = place("state_pk", [128, 8], F32)
        STG0 = pos[0]
        stg = [place(f"stg{s}", [128, D], BF16)[0] for s in range(2)]
        aoev = [place(f"aoev{s}", [128, 512], BF16)[0] for s in range(2)]

        SCAN0 = pos[0]
        scn = [place(f"scn{i}", [128, T], F32)[0] for i in range(4)]
        SC_COS = [scn[0], scn[2]]
        SC_SIN = [scn[1], scn[3]]
        if pos[0] - SCAN0 < 32768:
            pos[0] = SCAN0 + 32768
        WA0 = pos[0]
        w1sb, _ = place("w1sb", [128, 4 * DFI], BF16)
        w2sb, _ = place("w2sb", [128, 16 * DI], BF16)
        MM0 = pos[0]
        yb = [place(f"yb{c}", [128, T], BF16)[0] for c in range(4)]
        hb = [place(f"hb{s}", [128, T], BF16)[0] for s in range(2)]
        ffn1, _ = place("ffn1", [128, 16 * 512], BF16)
        sc1, SC1_0 = place("sc1", [128, T], F32)
        sc2, SC2_0 = place("sc2", [128, T], F32)
        if pos[0] - MM0 < 49152:
            pos[0] = MM0 + 49152
        END0 = pos[0]
        assert END0 <= 229376, f"sbuf static overflow: {END0}"
        SCAN_B = WA0 - SCAN0
        MM_B = END0 - MM0

        xtm = [over(f"xtm{i}", [128, D], F32, SCAN0 + i * 16384) for i in range(2)]
        xtm4 = [over(f"xtm4_{i}", [128, D], F32, H0 + i * 16384) for i in range(2)]
        aotm4 = over("aotm4", [128, D], BF16, TG0)
        jnkb4 = [over(f"jnkb4_{i}", [128, D], BF16, STG0 + i * 8192)
                 for i in range(2)]
        assert SCAN_B >= 2 * 16384
        aotm = over("aotm", [128, D], BF16, MM0)
        jnkb = over("jnkb", [128, D], BF16, MM0 + 8192)
        winsb = over("winsb", [128, NKD * DI], BF16, MM0 + 16384)
        xlnw = over("xlnw", [128, NKD * 512], BF16, SCAN0)
        woutsb = over("woutsb", [128, 4 * D], BF16, SCAN0)
        xlnffn = over("xlnffn", [128, NKD * 512], BF16, SCAN0)
        gusb = [over(f"gusb{s}", [128, D], BF16, WA0 + s * 8192) for s in range(4)]
        assert WA0 + 4 * 8192 <= END0 + 4096
        if T * 4 >= 8192:
            dsb = [over(f"dsb{s}", [128, 1024], BF16, VS0 + s * 2048)
                   for s in range(4)]
        else:
            dsb = [place(f"dsb{s}", [128, 1024], BF16)[0] for s in range(4)]
        hch = over("hch", [128, FCH * 512], BF16, MM0)
        if T == 2048:
            acc_off = []
            for i in range(5 * TB // 2048):
                acc_off.append(H0 + i * 2048)
            need = NKD - len(acc_off)
            for i in range(need):
                acc_off.append(MM0 + FCH * 1024 + i * 2048)
            assert MM0 + FCH * 1024 + need * 2048 <= 229376
            accum = [over(f"acc{i}", [128, 512], F32, acc_off[i])
                     for i in range(NKD)]
            pos[0] = max(pos[0], MM0 + FCH * 1024 + need * 2048)
        else:
            acc_off = [H0 + i * 2048 for i in range(5 * TB // 2048)]
            acc_off += [MM0 + FCH * 1024 + i * 2048
                        for i in range((49152 - FCH * 1024) // 2048)]
            accum = [over(f"acc{i}", [128, 512], F32, acc_off[i])
                     if i < len(acc_off)
                     else place(f"acc{i}", [128, 512], F32)[0]
                     for i in range(NKD)]
            assert pos[0] <= 229376, pos[0]
        gsil = [place(f"gsil{s}", [128, 512], BF16)[0] for s in range(2)]
        assert pos[0] <= 229376, pos[0]

        mu_row = over("mu_row", [1, T], F32, SC1_0)
        sq_row = over("sq_row", [1, T], F32, SC2_0)
        self.psum = [nc.alloc_psum_tensor(f"ps{i}", [128, 512], F32)
                     for i in range(8)]

        PC = {}
        pcn = [0]
        for nm, n in PCOLS + [("rho", 16), ("lnrho", 16), ("theta", 16),
                              ("cth", 16), ("sth", 16), ("ainit", 16),
                              ("binit", 16), ("dpk", 16), ("pb", 16),
                              ("t1", 16), ("t2", 16), ("t3", 16), ("t4", 16),
                              ("u1", 1)]:
            PC[nm] = pcn[0]
            pcn[0] += n
        assert pcn[0] <= 1024

        def P(nm, i=0, n=1):
            c = PC[nm] + i
            return params[:, c:c + n]

        def A16(nm):
            return P(nm, 0, 16)

        for nm, n in PCOLS:
            DMA("sync", P(nm, 0, n), pin[nm])
        v_pld = self.semn["ds"]

        O("vector", "memset", c_pi2[:, :], PI_2)
        O("vector", "memset", c_eps[:, :], float(EPS))
        O("vector", "memset", c_zero[:, :], 0.0)
        O("vector", "memset", ones_bf[:, :], 1.0 / DI)
        O("vector", "memset", onesr_f[:, :], 1.0)
        O("vector", "memset", nonesr_f[:, :], -1.0, inc=True)
        O("gpsimd", "iota", tglob[:, :], [[1, T]], base=0, channel_multiplier=0,
          allow_small_or_imprecise_dtypes=True, inc=True)
        W("vector", "vm", self.semn["vm"])
        W("vector", "ds", v_pld)
        O("vector", "tensor_scalar", tglob[:, :], tglob[:, :], P("toff"), None,
          ALU.add, ALU.bypass, inc=True)

        W("scalar", "ds", v_pld)
        W("scalar", "vm", self.semn["vm"])
        O("scalar", "activation", A16("t1"), A16("pre"), AF.Square, bias=c_zero[:, 0:1])
        O("scalar", "activation", A16("t2"), A16("pim"), AF.Square, drain=True,
          inc=True, bias=c_zero[:, 0:1])
        W("vector", "am", self.semn["am"])
        O("vector", "tensor_tensor", A16("t1"), A16("t1"), A16("t2"), ALU.add,
          drain=True, inc=True)
        W("scalar", "vm", self.semn["vm"])
        O("scalar", "activation", A16("t2"), A16("t1"), AF.Sqrt, drain=True,
          inc=True, bias=c_zero[:, 0:1])
        W("vector", "am", self.semn["am"])
        O("vector", "reciprocal", A16("t3"), A16("t2"), drain=True)
        O("vector", "tensor_tensor", A16("t3"), A16("t3"), A16("t1"), ALU.mult,
          drain=True)
        O("vector", "tensor_tensor", A16("t3"), A16("t3"), A16("t2"), ALU.add,
          drain=True)
        O("vector", "tensor_scalar", A16("t3"), A16("t3"), 0.5, None, ALU.mult,
          drain=True, inc=True)
        W("scalar", "vm", self.semn["vm"])
        O("scalar", "activation", A16("rho"), A16("t3"), AF.Tanh, drain=True,
          inc=True, bias=c_zero[:, 0:1])
        W("vector", "am", self.semn["am"])
        O("vector", "tensor_scalar", A16("t4"), A16("rho"), 1e-36, None,
          ALU.max, drain=True, inc=True)
        W("scalar", "vm", self.semn["vm"])
        O("scalar", "activation", A16("lnrho"), A16("t4"), AF.Ln, drain=True, bias=c_zero[:, 0:1])
        O("scalar", "activation", A16("dpk"), A16("lnrho"), AF.Exp,
          scale=float(T), drain=True, bias=c_zero[:, 0:1])
        O("scalar", "activation", A16("t1"), A16("pre"), AF.Abs, drain=True, bias=c_zero[:, 0:1])
        O("scalar", "activation", A16("t2"), A16("pim"), AF.Abs, drain=True,
          inc=True, bias=c_zero[:, 0:1])
        W("vector", "am", self.semn["am"])
        O("vector", "tensor_tensor", A16("t3"), A16("t2"), A16("t1"), ALU.is_gt,
          drain=True)
        O("vector", "tensor_tensor", A16("t4"), A16("t1"), A16("t2"), ALU.min,
          drain=True)
        O("vector", "tensor_tensor", A16("t2"), A16("t1"), A16("t2"), ALU.max,
          drain=True)
        O("vector", "tensor_scalar", A16("t2"), A16("t2"), 1e-30, None, ALU.max,
          drain=True)
        O("vector", "reciprocal", A16("t1"), A16("t2"), drain=True)
        O("vector", "tensor_tensor", A16("t1"), A16("t1"), A16("t4"), ALU.mult,
          drain=True, inc=True)
        W("scalar", "vm", self.semn["vm"])
        O("scalar", "activation", A16("t4"), A16("t1"), AF.Arctan, drain=True,
          inc=True, bias=c_zero[:, 0:1])
        W("vector", "am", self.semn["am"])
        O("vector", "tensor_scalar", A16("t1"), A16("t4"), -2.0, PI_2, ALU.mult,
          ALU.add, drain=True)
        O("vector", "tensor_tensor", A16("t1"), A16("t1"), A16("t3"), ALU.mult,
          drain=True)
        O("vector", "tensor_tensor", A16("t4"), A16("t4"), A16("t1"), ALU.add,
          drain=True)
        O("vector", "tensor_scalar", A16("t3"), A16("pre"), 0.0, None, ALU.is_lt,
          drain=True)
        O("vector", "tensor_scalar", A16("t1"), A16("t4"), -2.0, PI, ALU.mult,
          ALU.add, drain=True)
        O("vector", "tensor_tensor", A16("t1"), A16("t1"), A16("t3"), ALU.mult,
          drain=True)
        O("vector", "tensor_tensor", A16("t4"), A16("t4"), A16("t1"), ALU.add,
          drain=True)
        O("vector", "tensor_scalar", A16("t1"), A16("pim"), 0.0, None, ALU.is_gt,
          drain=True)
        O("vector", "tensor_scalar", A16("t2"), A16("pim"), 0.0, None, ALU.is_lt,
          drain=True)
        O("vector", "tensor_tensor", A16("t1"), A16("t1"), A16("t2"),
          ALU.subtract, drain=True)
        O("vector", "tensor_tensor", A16("theta"), A16("t4"), A16("t1"),
          ALU.mult, drain=True, inc=True)
        W("scalar", "vm", self.semn["vm"])
        O("scalar", "activation", A16("sth"), A16("theta"), AF.Sin, drain=True, bias=c_zero[:, 0:1])
        O("scalar", "activation", A16("t1"), A16("theta"), AF.Abs, drain=True, bias=c_zero[:, 0:1])
        O("scalar", "activation", A16("cth"), A16("t1"), AF.Sin, scale=-1.0,
          bias=c_pi2[:, 0:1], drain=True, inc=True)
        W("vector", "am", self.semn["am"])
        O("vector", "tensor_tensor", A16("t1"), A16("cth"), A16("lcr"), ALU.mult,
          drain=True)
        O("vector", "tensor_tensor", A16("t2"), A16("sth"), A16("lci"), ALU.mult,
          drain=True)
        O("vector", "tensor_tensor", A16("ainit"), A16("t1"), A16("t2"),
          ALU.subtract, drain=True)
        O("vector", "tensor_tensor", A16("t1"), A16("sth"), A16("lcr"), ALU.mult,
          drain=True)
        O("vector", "tensor_tensor", A16("t2"), A16("cth"), A16("lci"), ALU.mult,
          drain=True)
        O("vector", "tensor_tensor", A16("binit"), A16("t1"), A16("t2"), ALU.add,
          drain=True)
        O("vector", "tensor_scalar", A16("binit"), A16("binit"), -1.0, None,
          ALU.mult, drain=True)
        O("vector", "tensor_scalar", A16("ainit"), A16("ainit"), P("mask", 7),
          None, ALU.mult, drain=True)
        O("vector", "tensor_scalar", A16("binit"), A16("binit"), P("mask", 7),
          None, ALU.mult, drain=True)
        O("vector", "tensor_scalar", P("u1"), P("toff"), -1.0, 1.0, ALU.mult,
          ALU.add, drain=True)
        O("vector", "tensor_scalar", A16("pb"), A16("lnrho"), P("u1"), None,
          ALU.mult, drain=True)
        O("vector", "engine_nop", inc=True)
        v_p0 = self.semn["vm"]
        if "theta" in dbg_aps:
            W("sync", "vm", v_p0)
            DMA("sync", dbg_aps["theta"], A16("theta"))
            DMA("sync", dbg_aps["rho"], A16("rho"))
            DMA("sync", dbg_aps["ainit"], A16("ainit"))
            DMA("sync", dbg_aps["lnrho"], A16("lnrho"))
            DMA("sync", dbg_aps["tg"], tglob[:, :])

        g_win = DMA("gpsimd", winsb[:, :], winT)
        g_w1 = {0: DMA("gpsimd", w1sb[:, :], w1h[0])}
        g_w2 = {0: DMA("gpsimd", w2sb[:, :], w2h[0])}

        def drain_pc(n):
            pass

        musum, sqsum = statsf[:, 0:16], statsf[:, 16:32]
        nmr, rst = statsf[:, 32:48], statsf[:, 48:64]

        def tm_stats(src_ap, junk_ap, col, ld, vwaits=()):
            c = slice(col, col + 1)
            W("vector", "ds", ld)
            for (ss, vv) in vwaits:
                W("vector", ss, vv)
            O("vector", "tensor_scalar", junk_ap, src_ap, 1.0, 0.0, ALU.mult,
              ALU.add, accum_out=musum[:, c], inc=True)
            W("scalar", "ds", ld)
            for (ss, vv) in vwaits:
                W("scalar", ss, vv)
            O("scalar", "activation", junk_ap, src_ap, AF.Square,
              accum_out=sqsum[:, c], inc=True, bias=c_zero[:, 0:1])
            a_sq = self.semn["am"]
            W("vector", "am", a_sq)
            O("vector", "tensor_scalar", musum[:, c], musum[:, c], 1.0 / D, None,
              ALU.mult, drain=True)
            O("vector", "tensor_scalar", sqsum[:, c], sqsum[:, c], 1.0 / D, None,
              ALU.mult, drain=True)
            O("vector", "tensor_tensor", rst[:, c], musum[:, c], musum[:, c],
              ALU.mult, drain=True)
            O("vector", "tensor_tensor", rst[:, c], sqsum[:, c], rst[:, c],
              ALU.subtract, drain=True, inc=True)
            W("scalar", "vm", self.semn["vm"])
            O("scalar", "activation", rst[:, c], rst[:, c], AF.Ln,
              bias=c_eps[:, 0:1], drain=True)
            O("scalar", "activation", rst[:, c], rst[:, c], AF.Exp, scale=-0.5,
              drain=True, inc=True, bias=c_zero[:, 0:1])
            a_r = self.semn["am"]
            W("vector", "am", a_r)
            O("vector", "tensor_tensor", nmr[:, c], musum[:, c], rst[:, c],
              ALU.mult, drain=True)
            O("vector", "tensor_scalar", nmr[:, c], nmr[:, c], -1.0, None,
              ALU.mult, drain=True, inc=True)
            return self.semn["vm"], a_r

        v_used, st_done = {}, {}
        a_used = {}
        for tt in range(NTT):
            sl = tt % 2
            if tt >= 2:
                W("sync", "vm", v_used[tt - 2])
                W("sync", "am", a_used[tt - 2])
            ld = DMA("sync", xtm[sl][:, :], x_in[tt * 128:(tt + 1) * 128, :])
            if tt == 0:
                W("vector", "ds", ld)
                W("vector", "vm", v_p0)
            col = tt % 16
            v_s, a_s = tm_stats(xtm[sl][:, :], jnkb[:, :], col, ld)
            a_used[tt] = a_s
            if tt >= 1:
                W("vector", "ds", st_done[tt - 1])
            cc = slice(col, col + 1)
            O("vector", "tensor_scalar", aotm[:, :], xtm[sl][:, :], rst[:, cc],
              nmr[:, cc], ALU.mult, ALU.add, inc=True)
            v_x = self.semn["vm"]
            v_used[tt] = v_x
            W("sync", "vm", v_x)
            st_done[tt] = DMA("sync", xlna_tm[tt * 128:(tt + 1) * 128, :],
                              aotm[:, :])
            if tt % 4 == 3:
                drain_pc(4)

        xb_mm = {}
        a_p2 = 0
        for k in range(NCK):
            W("sync", "dg", g_win)
            W("sync", "ds", st_done[4 * k + 3])
            if k >= 1:
                W("sync", "tm", xb_mm[k - 1])
            ldv = 0
            for kv in range(NKD):
                ldv = DMA("sync", xlnw[:, kv * 512:(kv + 1) * 512],
                          xlna_tm[k * 512:(k + 1) * 512,
                                  kv * 128:(kv + 1) * 128], transpose=True)
            for dtile in range(4):
                bank = dtile
                pairs = [(winsb[:, kv * 512 + dtile * 128:
                                kv * 512 + dtile * 128 + 128],
                          xlnw[:, kv * 512:(kv + 1) * 512]) for kv in range(NKD)]
                self.mm_group(bank, pairs, waits=[("ds", ldv)])
                a_p2 = self.evac(bank, "scalar", lambda b=bank, d=dtile, k=k:
                                 O("scalar", "activation",
                                   h_t[d][:, k * 512:(k + 1) * 512],
                                   self.psum[b][:, :], AF.Identity,
                                   bias=P("bin", d), inc=True))
            xb_mm[k] = self.semn["tm"]
            drain_pc(6)
        if "h_win" in dbg_aps:
            W("sync", "am", a_p2)
            for c in range(4):
                DMA("sync", dbg_aps["h_win"][c * 128:(c + 1) * 128, :],
                    h_t[c][:, :])

        def inner_ln_stats(extra_waits):
            t_c = {}
            for c in range(4):
                sl = c % 2
                if c == 0:
                    for (ss, vv) in extra_waits:
                        W("vector", ss, vv)
                if c >= 2:
                    W("vector", "tm", t_c[c - 2])
                O("vector", "tensor_copy", hb[sl][:, :], h_t[c][:, :], inc=True)
                v_cast = self.semn["vm"]
                W("scalar", "vm", v_cast)
                if c == 0:
                    for (ss, vv) in extra_waits:
                        W("scalar", ss, vv)
                else:
                    W("scalar", "tm", t_c[c - 1])
                O("scalar", "activation", ffn1[:, 0:T], hb[sl][:, :], AF.Square,
                  inc=True, bias=c_zero[:, 0:1])
                a_sqc = self.semn["am"]
                W("tensor", "vm", v_cast)
                W("tensor", "am", a_sqc)
                if c == 0:
                    for b in range(8):
                        s, v = self.bank_r[b]
                        W("tensor", s, v)
                for k in range(NCK):
                    O("tensor", "matmul", self.psum[k][0:1, :], ones_bf[:, :],
                      hb[sl][:, k * 512:(k + 1) * 512],
                      start=(c == 0), stop=(c == 3))
                for k in range(NCK):
                    O("tensor", "matmul", self.psum[4 + k][0:1, :],
                      ones_bf[:, :], ffn1[:, k * 512:(k + 1) * 512],
                      start=(c == 0), stop=(c == 3),
                      inc=(k == NCK - 1))
                t_c[c] = self.semn["tm"]
            t_stats = self.semn["tm"]
            for b in range(8):
                self.bank_w[b] = ("tm", t_stats)
            W("vector", "tm", t_stats)
            for k in range(NCK):
                O("vector", "tensor_copy", mu_row[0:1, k * 512:(k + 1) * 512],
                  self.psum[k][0:1, :])
                O("vector", "tensor_copy", sq_row[0:1, k * 512:(k + 1) * 512],
                  self.psum[4 + k][0:1, :])
            O("vector", "engine_nop", inc=True)
            v_ev = self.semn["vm"]
            for b in range(8):
                self.bank_r[b] = ("vm", v_ev)
            O("vector", "tensor_tensor", var_scr[0:1, 0:T], mu_row[0:1, 0:T],
              mu_row[0:1, 0:T], ALU.mult)
            O("vector", "tensor_tensor", sq_row[0:1, 0:T], sq_row[0:1, 0:T],
              var_scr[0:1, 0:T], ALU.subtract, inc=True)
            W("scalar", "vm", self.semn["vm"])
            O("scalar", "activation", sq_row[0:1, 0:T], sq_row[0:1, 0:T], AF.Ln,
              bias=c_eps[0:1, 0:1])
            O("scalar", "activation", sq_row[0:1, 0:T], sq_row[0:1, 0:T], AF.Exp,
              scale=-0.5, inc=True, bias=c_zero[0:1, 0:1])
            W("vector", "am", self.semn["am"])
            O("vector", "tensor_tensor", mu_row[0:1, 0:T], mu_row[0:1, 0:T],
              sq_row[0:1, 0:T], ALU.mult, inc=True)
            v_rr = self.semn["vm"]
            if "murow" in dbg_aps and not self.dumped_stats:
                self.dumped_stats = True
                W("sync", "vm", v_rr)
                dmp = DMA("sync", dbg_aps["murow"], mu_row[0:1, 0:T])
                dmp = DMA("sync", dbg_aps["sqrow"], sq_row[0:1, 0:T])
                W("vector", "ds", dmp)
            for k in range(NCK):
                self.mm_group(k, [(nonesr_f[:, :],
                                   mu_row[0:1, k * 512:(k + 1) * 512])],
                              waits=[("vm", v_rr)])
                self.mm_group(4 + k, [(onesr_f[:, :],
                                       sq_row[0:1, k * 512:(k + 1) * 512])])

        def normalize_to_sc1(c):
            for k in range(NCK):
                for b in (k, 4 + k):
                    s, v = self.bank_w[b]
                    W("vector", s, v)
                sl5 = slice(k * 512, (k + 1) * 512)
                O("vector", "tensor_tensor", sc1[:, sl5], h_t[c][:, sl5],
                  self.psum[4 + k][:, :], ALU.mult)
                O("vector", "tensor_tensor", sc1[:, sl5], sc1[:, sl5],
                  self.psum[k][:, :], ALU.add, inc=(k == NCK - 1))
            v_n = self.semn["vm"]
            for k in range(NCK):
                self.bank_r[k] = ("vm", v_n)
                self.bank_r[4 + k] = ("vm", v_n)
            return v_n

        def corr_combine(g_rd):
            W("vector", "dg", g_rd)
            O("vector", "memset", czt[:, :], 0.0)
            for j in range(7):
                O("vector", "tensor_tensor", czt[:, :], czt[:, :],
                  ccd[:, j * 8:j * 8 + 8], ALU.mult, drain=True)
                O("vector", "scalar_tensor_tensor", czt[:, :],
                  ccg[:, j * 8:j * 8 + 8], P("mask", j), czt[:, :],
                  ALU.mult, ALU.add, drain=True)
            O("vector", "engine_nop", drain=True, inc=True)
            return self.semn["vm"]

        def apply_corr(l, c, slot, recompute):
            dcol = l * 4 + c
            cosb, sinb = SC_COS[slot][:, :], SC_SIN[slot][:, :]
            if recompute:
                O("vector", "tensor_scalar", cosb, tglob[:, :],
                  P("theta", dcol), None, ALU.mult)
                O("vector", "tensor_scalar", sinb, cosb, INV2PI, MAGIC,
                  ALU.mult, ALU.add)
                O("vector", "tensor_scalar", sinb, sinb, -MAGIC, None, ALU.add)
                O("vector", "scalar_tensor_tensor", cosb, sinb, -CW_C1, cosb,
                  ALU.mult, ALU.add)
                O("vector", "scalar_tensor_tensor", cosb, sinb, -CW_C2, cosb,
                  ALU.mult, ALU.add)
                O("vector", "scalar_tensor_tensor", cosb, sinb, -CW_C3, cosb,
                  ALU.mult, ALU.add)
                O("vector", "tensor_scalar", cosb, cosb, PI, -PI, ALU.min,
                  ALU.max, inc=True)
                W("scalar", "vm", self.semn["vm"])
                O("scalar", "activation", sinb, cosb, AF.Sin,
                  bias=c_zero[:, 0:1])
                O("scalar", "activation", sc2[:, :], cosb, AF.Abs,
                  bias=c_zero[:, 0:1])
                O("scalar", "activation", cosb, sc2[:, :], AF.Sin, scale=-1.0,
                  bias=c_pi2[:, 0:1], inc=True)
                W("vector", "am", self.semn["am"])
            W("scalar", "vm", self.semn["vm"])
            O("scalar", "activation", sc2[:, :], tglob[:, :], AF.Exp,
              scale=P("lnrho", dcol), bias=P("pb", dcol), inc=True)
            a_rp = self.semn["am"]
            W("vector", "am", a_rp)
            O("vector", "tensor_scalar", sc1[:, :], cosb,
              czt[:, 2 * c:2 * c + 1], None, ALU.mult)
            O("vector", "scalar_tensor_tensor", sc1[:, :], sinb,
              czt[:, 2 * c + 1:2 * c + 2], sc1[:, :], ALU.mult, ALU.add)
            O("vector", "tensor_tensor", sc1[:, :], sc1[:, :], sc2[:, :],
              ALU.mult)
            O("vector", "scalar_tensor_tensor", h_t[c][:, :], sc1[:, :],
              P("scl", dcol), h_t[c][:, :], ALU.mult, ALU.add, inc=True)
            return self.semn["vm"]

        t_w2_prev = [xb_mm[NCK - 1]]
        a_prev = [a_p2]
        for l in range(NLAYER):
            inner_ln_stats([("am", a_prev[0]), ("tm", t_w2_prev[0])])
            W("vector", "vm", v_p0)
            for c4 in range(4):
                O("vector", "tensor_copy", dpk2[:, 2 * c4:2 * c4 + 1],
                  P("dpk", l * 4 + c4), drain=True)
                O("vector", "tensor_copy", dpk2[:, 2 * c4 + 1:2 * c4 + 2],
                  P("dpk", l * 4 + c4), drain=True)
            for j in range(7):
                O("vector", "tensor_scalar", ccd[:, j * 8:j * 8 + 8],
                  dpk2[:, :], P("mask", j), None, ALU.mult, drain=True)
                O("vector", "tensor_scalar", P("u1"), P("mask", j), -1.0, 1.0,
                  ALU.mult, ALU.add, drain=True)
                O("vector", "tensor_scalar", ccd[:, j * 8:j * 8 + 8],
                  ccd[:, j * 8:j * 8 + 8], P("u1"), None, ALU.add, drain=True)
            O("vector", "engine_nop", inc=True)

            for c in range(4):
                slot = c % 2
                dcol = l * 4 + c
                v_n = normalize_to_sc1(c)
                W("scalar", "vm", v_n)
                O("scalar", "activation", sc1[:, :], sc1[:, :], AF.Silu,
                  scale=P("ln1w", dcol), bias=P("ln1b", dcol), inc=True)
                a_yt = self.semn["am"]
                cosb = SC_COS[slot][:, :]
                sinb = SC_SIN[slot][:, :]
                W("vector", "am", a_yt)
                O("vector", "tensor_scalar", cosb, tglob[:, :],
                  P("theta", dcol), None, ALU.mult, inc=True)
                if l == 0 and c == 0 and "v00" in dbg_aps:
                    W("sync", "vm", self.semn["vm"])
                    dmp = DMA("sync", dbg_aps["v00"], cosb)
                    W("vector", "ds", dmp)
                O("vector", "tensor_scalar", sinb, cosb, INV2PI, MAGIC,
                  ALU.mult, ALU.add)
                O("vector", "tensor_scalar", sinb, sinb, -MAGIC, None, ALU.add)
                O("vector", "scalar_tensor_tensor", cosb, sinb, -CW_C1, cosb,
                  ALU.mult, ALU.add)
                O("vector", "scalar_tensor_tensor", cosb, sinb, -CW_C2, cosb,
                  ALU.mult, ALU.add)
                O("vector", "scalar_tensor_tensor", cosb, sinb, -CW_C3, cosb,
                  ALU.mult, ALU.add)
                O("vector", "tensor_scalar", cosb, cosb, PI, -PI, ALU.min,
                  ALU.max, inc=True)
                if l == 0 and c == 0 and "r00" in dbg_aps:
                    W("sync", "vm", self.semn["vm"])
                    dmp = DMA("sync", dbg_aps["r00"], cosb)
                    W("scalar", "ds", dmp)
                W("scalar", "vm", self.semn["vm"])
                O("scalar", "activation", sinb, cosb, AF.Sin, bias=c_zero[:, 0:1])
                O("scalar", "activation", sc2[:, :], cosb, AF.Abs, bias=c_zero[:, 0:1])
                O("scalar", "activation", cosb, sc2[:, :], AF.Sin, scale=-1.0,
                  bias=c_pi2[:, 0:1], inc=True)
                a_cs = self.semn["am"]
                if l == 0 and c == 0 and "yt00" in dbg_aps:
                    W("sync", "am", a_yt)
                    dmp = DMA("sync", dbg_aps["yt00"], sc1[:, :])
                    W("vector", "ds", dmp)
                if l == 0 and c == 0 and "cos00" in dbg_aps:
                    W("sync", "am", a_cs)
                    dmp = DMA("sync", dbg_aps["cos00"], cosb)
                    dmp = DMA("sync", dbg_aps["sin00"], sinb)
                    W("vector", "ds", dmp)
                    W("scalar", "ds", dmp)
                W("vector", "am", a_cs)
                rho_b = P("rho", dcol).broadcast_to((128, T))
                O("vector", "tensor_tensor", sc2[:, :], cosb, sc1[:, :],
                  ALU.mult)
                O("vector", "tensor_tensor_scan", sc2[:, :], rho_b, sc2[:, :],
                  P("ainit", dcol), ALU.mult, ALU.add)
                O("vector", "tensor_tensor", sc1[:, :], sinb, sc1[:, :],
                  ALU.mult)
                O("vector", "tensor_tensor_scan", sc1[:, :], rho_b, sc1[:, :],
                  P("binit", dcol), ALU.mult, ALU.add, inc=True)
                O("vector", "drain")
                if c == 0:
                    W("vector", "dg", getattr(self, "last_g_st", 0))
                O("vector", "tensor_copy", state_pk[:, 2 * c:2 * c + 1],
                  sc2[:, T - 1:T])
                O("vector", "tensor_copy", state_pk[:, 2 * c + 1:2 * c + 2],
                  sc1[:, T - 1:T])
                O("vector", "engine_nop", drain=True, inc=True)
                v_st = self.semn["vm"]
                if l == 0 and c == 0 and "A00" in dbg_aps:
                    W("sync", "vm", v_st)
                    dmp = DMA("sync", dbg_aps["A00"], sc2[:, :])
                    dmp = DMA("sync", dbg_aps["B00"], sc1[:, :])
                    W("vector", "ds", dmp)
                O("vector", "tensor_tensor", sc2[:, :], cosb, sc2[:, :],
                  ALU.mult)
                O("vector", "tensor_tensor", sc1[:, :], sinb, sc1[:, :],
                  ALU.mult)
                O("vector", "tensor_tensor", sc1[:, :], sc1[:, :], sc2[:, :],
                  ALU.add)
                O("vector", "scalar_tensor_tensor", h_t[c][:, :], sc1[:, :],
                  P("scl", dcol), h_t[c][:, :], ALU.mult, ALU.add, inc=True)
                drain_pc(8)
            W("gpsimd", "vm", v_st)
            g_st = DMA("gpsimd", cc_i[l], state_pk[:, :])
            self.last_g_st = g_st
            W("gpsimd", "dg", g_st)
            self.bump("cc", 1)
            E("gpsimd", lambda e, i=l, h=self.semh["cc"]:
              e.collective_compute("AllGather", ALU.bypass,
                                   [list(range(NCORE))],
                                   ins=[cc_i[i].opt()],
                                   outs=[cc_o[i].opt()]).then_inc(h, 1))
            W("gpsimd", "cc", self.semn["cc"])
            g_rd = DMA("gpsimd",
                       ccg[:, :].rearrange("p (r c) -> p r c", r=NCORE),
                       cc_o[l].rearrange("(r p) c -> p r c", p=128))
            corr_combine(g_rd)
            apply_corr(l, 2, 0, False)
            apply_corr(l, 3, 1, False)
            apply_corr(l, 0, 0, True)
            apply_corr(l, 1, 1, True)
            v_mix = self.semn["vm"]

            inner_ln_stats([("vm", v_mix)])
            W("vector", "tm", t_w2_prev[0])
            for c in range(4):
                dcol = l * 4 + c
                normalize_to_sc1(c)
                O("vector", "tensor_scalar", yb[c][:, :], sc1[:, :],
                  P("ln2w", dcol), P("ln2b", dcol), ALU.mult, ALU.add,
                  inc=True)
            v_yb = self.semn["vm"]

            W("tensor", "dg", g_w1[l])
            W("tensor", "dg", g_w2[l])
            W("tensor", "vm", v_yb)
            for k in range(NCK):
                for ft in range(16):
                    bank = ft % 4
                    pairs = [(w1sb[:, c * DFI + ft * 128:c * DFI + ft * 128 + 128],
                              yb[c][:, k * 512:(k + 1) * 512]) for c in range(4)]
                    self.mm_group(bank, pairs)
                    self.evac(bank, "scalar", lambda b=bank, ft=ft, l=l:
                              O("scalar", "activation",
                                ffn1[:, ft * 512:(ft + 1) * 512],
                                self.psum[b][:, :], AF.Silu, scale=1.0,
                                bias=P("b1", l * 16 + ft), inc=True))
                a_f1 = self.semn["am"]
                for dtile in range(4):
                    bank = 4 + (dtile % 2)
                    pairs = [(w2sb[:, ft * DI + dtile * 128:
                                   ft * DI + dtile * 128 + 128],
                              ffn1[:, ft * 512:(ft + 1) * 512])
                             for ft in range(16)]
                    self.mm_group(bank, pairs, waits=[("am", a_f1)])
                    self.evac(bank, "vector", lambda b=bank, d=dtile, k=k, l=l:
                              O("vector", "scalar_tensor_tensor",
                                h_t[d][:, k * 512:(k + 1) * 512],
                                self.psum[b][:, :], P("b2", l * 4 + d),
                                h_t[d][:, k * 512:(k + 1) * 512],
                                ALU.add, ALU.add, inc=True))
                drain_pc(3)
            t_w2 = self.semn["tm"]
            t_w2_prev[0] = t_w2
            a_prev[0] = self.semn["am"]
            v_hdone = self.semn["vm"]
            if (l + 1) < NLAYER:
                W("gpsimd", "tm", t_w2)
                g_w1[l + 1] = DMA("gpsimd", w1sb[:, :], w1h[l + 1])
                g_w2[l + 1] = DMA("gpsimd", w2sb[:, :], w2h[l + 1])
            if f"h_l{l}" in dbg_aps:
                W("sync", "vm", v_hdone)
                W("sync", "am", self.semn["am"])
                for c in range(4):
                    DMA("sync", dbg_aps[f"h_l{l}"][c * 128:(c + 1) * 128, :],
                        h_t[c][:, :])
        v_layers = self.semn["vm"]

        W("gpsimd", "vm", v_layers)
        g_wout = DMA("gpsimd", woutsb[:, :], woutT)

        inner_ln_stats([("vm", v_layers), ("tm", t_w2_prev[0]),
                        ("am", a_prev[0])])
        W("vector", "tm", t_w2_prev[0])
        for c in range(4):
            normalize_to_sc1(c)
            O("vector", "tensor_scalar", yb[c][:, :], sc1[:, :],
              P("lnlw", c), P("lnlb", c), ALU.mult, ALU.add, inc=True)
        v_hlb = self.semn["vm"]

        st_ao = {}
        i_ao = 0
        for k in range(NCK):
            for vt in range(NKD):
                bank = vt % 4
                slv = i_ao % 2
                pairs = [(woutsb[:, c * D + vt * 128:c * D + vt * 128 + 128],
                          yb[c][:, k * 512:(k + 1) * 512]) for c in range(4)]
                self.mm_group(bank, pairs,
                              waits=([("dg", g_wout), ("vm", v_hlb)]
                                     if (k == 0 and vt == 0) else ()))
                ew = [("ds", st_ao[i_ao - 2])] if i_ao >= 2 else []
                self.evac(bank, "scalar", lambda b=bank, vt=vt, s=slv:
                          O("scalar", "activation", aoev[s][:, :],
                            self.psum[b][:, :], AF.Identity,
                            bias=P("bout", vt), inc=True), waits=ew)
                W("sync", "am", self.semn["am"])
                st_ao[i_ao] = DMA(
                    "sync", ao_fm[vt * 128:(vt + 1) * 128, k * 512:(k + 1) * 512],
                    aoev[slv][:, :])
                i_ao += 1
            drain_pc(6)
        v_ao_all = self.semn["ds"]

        v_jnk = {}
        st4 = {}
        v_p4tail = 0
        for tt in range(NTT):
            sl = tt % 2
            W("sync", "ds", st_ao[(tt // 4) * NKD + NKD - 1])
            if tt >= 2:
                W("sync", "vm", v_used[tt - 2])
                W("sync", "ds", st4.get(tt - 2, 0))
                W("sync", "vm", v_jnk[tt - 2])
            ld_ao = DMA("sync", jnkb4[sl][:, :],
                        ao_fm[0:D, tt * 128:(tt + 1) * 128], transpose=True)
            ld_x = DMA("sync", xtm4[sl][:, :], x_in[tt * 128:(tt + 1) * 128, :])
            col = tt % 16
            cc = slice(col, col + 1)
            v_s, a_s = tm_stats(jnkb4[sl][:, :], aotm4[:, :], col, ld_ao)
            W("vector", "ds", ld_x)
            O("vector", "tensor_scalar", xtm4[sl][:, :], xtm4[sl][:, :],
              nmr[:, cc], None, ALU.add)
            O("vector", "scalar_tensor_tensor", xtm4[sl][:, :], jnkb4[sl][:, :],
              rst[:, cc], xtm4[sl][:, :], ALU.mult, ALU.add, inc=True)
            v_x1 = self.semn["vm"]
            W("sync", "vm", v_x1)
            st4[tt] = DMA("sync", x1_tm[tt * 128:(tt + 1) * 128, :],
                          xtm4[sl][:, :])
            v_s2, a_s2 = tm_stats(xtm4[sl][:, :], aotm4[:, :], col, 0,
                                  vwaits=[("vm", v_x1)])
            O("vector", "tensor_scalar", jnkb4[sl][:, :], xtm4[sl][:, :], rst[:, cc],
              nmr[:, cc], ALU.mult, ALU.add, inc=True)
            v_xb = self.semn["vm"]
            v_used[tt] = v_xb
            v_jnk[tt] = v_xb
            W("sync", "vm", v_xb)
            st4[tt] = DMA("sync", xlnb_tm[tt * 128:(tt + 1) * 128, :],
                          jnkb4[sl][:, :])
            if tt % 4 == 3:
                drain_pc(4)
        v_p4 = self.semn["ds"]
        v_p4vm = self.semn["vm"]
        if "x1d" in dbg_aps:
            W("sync", "ds", v_p4)
            DMA("sync", dbg_aps["x1d"], x1_tm)
            W("gpsimd", "ds", v_p4)
            DMA("gpsimd", dbg_aps["aod"], ao_fm)
            DMA("gpsimd", dbg_aps["xlnbd"], xlnb_tm)

        drain_pc(10 ** 9)
        fchunks = [(0, FCH), (FCH, 2 * FCH), (2 * FCH, NFT)]
        st_ffo = {}
        self.gsil_v = {}
        gu_t = {}
        ds_t = {}
        t_guplast = {}
        for blk in range(NCK):
            tsl = slice(blk * 512, (blk + 1) * 512)
            W("sync", "ds", st4[4 * blk + 3])
            if blk >= 1:
                W("sync", "tm", t_guplast[blk - 1])
            ldx = 0
            for kv in range(NKD):
                ldx = DMA("sync", xlnffn[:, kv * 512:(kv + 1) * 512],
                          xlnb_tm[tsl, kv * 128:(kv + 1) * 128], transpose=True)
            pend = {}

            def issue_gu_loads(ft, blk=blk):
                gu2 = ft % 2
                if gu2 in gu_t:
                    W("gpsimd", "tm", gu_t[gu2])
                lg = DMA("gpsimd", gusb[gu2][:, :], gateT[ft])
                lu = DMA("gpsimd", gusb[2 + gu2][:, :], upT[ft])
                pend[ft] = (lg, lu)

            issue_gu_loads(0 if blk >= 0 else 0) if False else None
            first_issued = [False]
            for (f0, f1) in fchunks:
                first_chunk = (f0 == 0)
                v_h = 0
                if not first_issued[0]:
                    issue_gu_loads(f0)
                    first_issued[0] = True
                for ft in range(f0, f1):
                    gu = ft % 2
                    if ft + 1 < NFT and (ft + 1) not in pend:
                        issue_gu_loads(ft + 1)
                    ldg, ldu = pend.pop(ft)
                    bank_g, bank_u = ft % 4, 4 + ft % 4
                    pairs = [(gusb[gu][:, kv * 128:(kv + 1) * 128],
                              xlnffn[:, kv * 512:(kv + 1) * 512])
                             for kv in range(NKD)]
                    self.mm_group(bank_g, pairs, waits=[("dg", ldg), ("ds", ldx)])
                    gw = ([("vm", self.gsil_v[ft % 2])]
                          if (ft % 2) in self.gsil_v else [])
                    a_g = self.evac(bank_g, "scalar",
                                    lambda b=bank_g, ft=ft, s=ft % 2:
                                    O("scalar", "activation", gsil[s][:, :],
                                      self.psum[b][:, :], AF.Silu,
                                      scale=P("gs", ft), inc=True, bias=c_zero[:, 0:1]), waits=gw)
                    pairs = [(gusb[2 + gu][:, kv * 128:(kv + 1) * 128],
                              xlnffn[:, kv * 512:(kv + 1) * 512])
                             for kv in range(NKD)]
                    gu_t[gu] = self.mm_group(bank_u, pairs,
                                             waits=[("dg", ldu)])
                    t_guplast[blk] = gu_t[gu]
                    v_h = self.evac(bank_u, "vector",
                                    lambda b=bank_u, ft=ft, f0=f0, s=ft % 2:
                                    O("vector", "scalar_tensor_tensor",
                                      hch[:, (ft - f0) * 512:(ft - f0 + 1) * 512],
                                      self.psum[b][:, :], P("us", ft),
                                      gsil[s][:, :], ALU.mult, ALU.mult,
                                      inc=True),
                                    waits=[("am", a_g)])
                    self.gsil_v[ft % 2] = v_h
                if blk == 0 and first_chunk and "hch0" in dbg_aps:
                    W("gpsimd", "vm", v_h)
                    DMA("gpsimd", dbg_aps["hch0"], hch[:, :])
                for dg_i in range(4):
                    pend_d = {}

                    def issue_d_loads(ft, dg_i=dg_i):
                        sl4 = ft % 4
                        if sl4 in ds_t:
                            W("gpsimd", "tm", ds_t[sl4])
                        pend_d[ft] = DMA(
                            "gpsimd", dsb[sl4][:, :],
                            downT[ft][:, dg_i * 1024:dg_i * 1024 + 1024])

                    issue_d_loads(f0)
                    issue_d_loads(f0 + 1)
                    issue_d_loads(f0 + 2)
                    for ft in range(f0, f1):
                        sl4 = ft % 4
                        if ft + 3 < f1:
                            issue_d_loads(ft + 3)
                        W("tensor", "dg", pend_d.pop(ft))
                        if ft == f0:
                            W("tensor", "vm", v_h)
                            for b in range(8):
                                s, v = self.bank_r[b]
                                W("tensor", s, v)
                        for di in range(8):
                            O("tensor", "matmul", self.psum[di][:, :],
                              dsb[sl4][:, di * 128:di * 128 + 128],
                              hch[:, (ft - f0) * 512:(ft - f0 + 1) * 512],
                              start=(ft == f0), stop=(ft == f1 - 1),
                              inc=(di == 7))
                        ds_t[sl4] = self.semn["tm"]
                    t_dg = self.semn["tm"]
                    for b in range(8):
                        self.bank_w[b] = ("tm", t_dg)
                    W("vector", "tm", t_dg)
                    if blk == 0 and first_chunk and dg_i == 0:
                        W("vector", "vm", v_p4vm)
                        W("vector", "ds", v_p4)
                    for di in range(8):
                        dtile = dg_i * 8 + di
                        if first_chunk:
                            O("vector", "tensor_copy", accum[dtile][:, :],
                              self.psum[di][:, :], inc=True)
                        else:
                            O("vector", "tensor_tensor", accum[dtile][:, :],
                              accum[dtile][:, :], self.psum[di][:, :], ALU.add,
                              inc=True)
                        self.bank_r[di] = ("vm", self.semn["vm"])
            for dtile in range(NKD):
                s = dtile % 2
                if dtile >= 2:
                    W("vector", "ds", st_ffo[(blk, dtile - 2)])
                O("vector", "tensor_scalar", aoev[s][:, :], accum[dtile][:, :],
                  P("ds", dtile), None, ALU.mult, inc=True)
                W("sync", "vm", self.semn["vm"])
                st_ffo[(blk, dtile)] = DMA(
                    "sync", ffo_fm[dtile * 128:(dtile + 1) * 128, tsl],
                    aoev[s][:, :])
        v_ffo_all = self.semn["ds"]
        if "ffod" in dbg_aps:
            W("gpsimd", "ds", v_ffo_all)
            DMA("gpsimd", dbg_aps["ffod"], ffo_fm)

        v_o = {}
        st6 = {}
        for tt in range(NTT):
            sl = tt % 2
            if tt == 0:
                W("sync", "ds", v_ffo_all)
            if tt >= 2:
                W("sync", "vm", v_o[tt - 2])
                W("sync", "ds", st6.get(tt - 2, 0))
            ld_fo = DMA("sync", jnkb4[sl][:, :],
                        ffo_fm[0:D, tt * 128:(tt + 1) * 128], transpose=True)
            ld_x1 = DMA("sync", xtm4[sl][:, :], x1_tm[tt * 128:(tt + 1) * 128, :])
            col = tt % 16
            cc = slice(col, col + 1)
            tm_stats(jnkb4[sl][:, :], aotm4[:, :], col, ld_fo)
            W("vector", "ds", ld_x1)
            O("vector", "tensor_scalar", xtm4[sl][:, :], xtm4[sl][:, :],
              nmr[:, cc], None, ALU.add)
            O("vector", "scalar_tensor_tensor", xtm4[sl][:, :], jnkb4[sl][:, :],
              rst[:, cc], xtm4[sl][:, :], ALU.mult, ALU.add, inc=True)
            v_o[tt] = self.semn["vm"]
            W("sync", "vm", v_o[tt])
            st6[tt] = DMA("sync", out_d[tt * 128:(tt + 1) * 128, :],
                          xtm4[sl][:, :])
        W("sync", "ds", self.semn["ds"])

        with nc.Block() as blk_:
            for meth, eng in ((blk_.sync, "sync"), (blk_.gpsimd, "gpsimd"),
                              (blk_.tensor, "tensor"), (blk_.vector, "vector"),
                              (blk_.scalar, "scalar")):
                ops = self.q[eng]

                def body(e, ops=ops):
                    for f in ops:
                        f(e)
                meth(body)
        return nc


def _host_layout(inputs):
    rs = np.ascontiguousarray

    def pack_chan(a):
        return rs(np.asarray(a).reshape(4, 4, 128).transpose(2, 0, 1)
                  .reshape(128, 16))

    W1 = np.asarray(inputs["L0_attn_W1"])
    W2 = np.asarray(inputs["L0_attn_W2"])
    g = {}
    g["winT"] = rs(np.transpose(np.asarray(inputs["L0_attn_Win"]))
                   .reshape(32, 128, 4, 128).transpose(1, 0, 2, 3)
                   .reshape(128, 32 * 512))
    g["w1h"] = rs(np.stack([np.transpose(W1[l]).reshape(4, 128, 2048)
                            .transpose(1, 0, 2).reshape(128, 4 * 2048)
                            for l in range(4)]))
    g["w2h"] = rs(np.stack([np.transpose(W2[l]).reshape(16, 128, 512)
                            .transpose(1, 0, 2).reshape(128, 16 * 512)
                            for l in range(4)]))
    g["woutT"] = rs(np.transpose(np.asarray(inputs["L0_attn_Wout"]))
                    .reshape(4, 128, 4096).transpose(1, 0, 2)
                    .reshape(128, 4 * 4096))
    def low_bytes(a):
        return rs(rs(a).view(np.int8)[..., 0::4])

    for nm, key in [("gateT", "L0_ffn_gate_w"), ("upT", "L0_ffn_up_w")]:
        w = np.asarray(inputs[key])
        g[nm] = low_bytes(w.reshape(NFT, 128, NKD, 128).transpose(0, 3, 2, 1)
                          .reshape(NFT, 128, NKD * 128))
    g["downT"] = low_bytes(np.transpose(np.asarray(inputs["L0_ffn_down_w"]))
                           .reshape(NFT, 128, NKD * 128))
    pp = {"ln1w": pack_chan(inputs["L0_attn_ln1_w"]),
          "ln1b": pack_chan(inputs["L0_attn_ln1_b"]),
          "ln2w": pack_chan(inputs["L0_attn_ln2_w"]),
          "ln2b": pack_chan(inputs["L0_attn_ln2_b"]),
          "scl": pack_chan(inputs["L0_attn_sc_lin"]),
          "b2": pack_chan(inputs["L0_attn_b2"]),
          "pre": pack_chan(inputs["L0_attn_phz_re"]),
          "pim": pack_chan(inputs["L0_attn_phz_im"]),
          "lcr": pack_chan(inputs["L0_attn_lc_re"]),
          "lci": pack_chan(inputs["L0_attn_lc_im"]),
          "b1": rs(np.asarray(inputs["L0_attn_b1"]).reshape(4, 16, 128)
                   .transpose(2, 0, 1).reshape(128, 64)),
          "bin": rs(np.asarray(inputs["L0_attn_bin"]).reshape(4, 128).T),
          "lnlw": rs(np.asarray(inputs["L0_attn_lnl_w"]).reshape(4, 128).T),
          "lnlb": rs(np.asarray(inputs["L0_attn_lnl_b"]).reshape(4, 128).T),
          "bout": rs(np.asarray(inputs["L0_attn_bout"]).reshape(32, 128).T),
          "gs": rs(np.asarray(inputs["L0_ffn_gate_s"]).reshape(NFT, 128).T),
          "us": rs(np.asarray(inputs["L0_ffn_up_s"]).reshape(NFT, 128).T),
          "ds": rs(np.asarray(inputs["L0_ffn_down_s"]).reshape(32, 128).T)}
    for k, v in pp.items():
        g["pp_" + k] = rs(np.asarray(v, np.float32))
    return g


def kernel(**inputs):
    T = 16384 // NCORE
    shared = _host_layout(inputs)
    x = np.asarray(inputs["L0_attn_x"])[0]
    in_maps = []
    for ci in range(NCORE):
        m = dict(shared)
        m["x_in"] = np.ascontiguousarray(x[ci * T:(ci + 1) * T, :], np.float32)
        m["pp_toff"] = np.full((128, 1), float(ci * T), np.float32)
        msk = np.zeros((128, 8), np.float32)
        msk[:, :7] = (np.arange(7)[None, :] < ci)
        msk[:, 7] = 1.0 if ci == 0 else 0.0
        m["pp_mask"] = msk
        in_maps.append(m)
    kb = KB(T)
    nc = kb.build()
    res = run_bass_kernel_spmd(nc, in_maps, core_ids=list(range(NCORE)))
    out = np.concatenate([np.asarray(r["out"]) for r in res.results], axis=0)
    return out[None, :, :].astype(np.float32)


# version 08
# speedup vs baseline: 1.1987x; 1.0117x over previous
import math

import numpy as np
import ml_dtypes

import concourse.bass as bass
import concourse.mybir as mybir
from concourse.bass_utils import run_bass_kernel_spmd

F32 = mybir.dt.float32
BF16 = mybir.dt.bfloat16
I32 = mybir.dt.int32
AF = mybir.ActivationFunctionType
ALU = mybir.AluOpType
BF = ml_dtypes.bfloat16

NCORE = 8
D = 4096
DI = 512
DFI = 2048
FF = 11008
NFT = FF // 128
NKD = D // 128
NLAYER = 4
EPS = 1e-5
FCH = 40

_TP = 2 * math.pi
CW_C1 = float(np.float32(np.ldexp(np.round(np.ldexp(_TP, 8)), -8)))
CW_C2 = float(np.float32(np.ldexp(np.round(np.ldexp(_TP - CW_C1, 19)), -19)))
CW_C3 = float(np.float32(_TP - CW_C1 - CW_C2))
MAGIC = float(np.float32(1.5 * 2.0 ** 23))
INV2PI = float(np.float32(1.0 / _TP))
PI = float(np.float32(math.pi))
PI_2 = float(np.float32(math.pi / 2))

_DEBUG = []


class KB:
    def __init__(self, T):
        self.T = T
        self.NCK = T // 512
        self.NTT = T // 128
        nc = bass.Bass("TRN2", target_bir_lowering=False)
        self.nc = nc
        self.q = {e: [] for e in ("sync", "gpsimd", "tensor", "vector", "scalar")}
        self.semh, self.semn = {}, {}
        for s in ("dg", "ds", "tm", "vm", "am", "cc"):
            self.semh[s] = nc.alloc_semaphore("sem_" + s)
            self.semn[s] = 0
        self.bank_w = [("tm", 0)] * 8
        self.bank_r = [("tm", 0)] * 8
        self.pc_items = []
        self.pc_mark = {}
        self.dumped_stats = False

    def emit(self, eng, fn):
        self.q[eng].append(fn)

    def W(self, eng, sem, val):
        if val:
            self.emit(eng, lambda e, h=self.semh[sem], v=val: e.wait_ge(h, v))

    def bump(self, sem, n):
        self.semn[sem] += n
        return self.semn[sem]

    def DMA(self, eng, dst, src, transpose=False):
        sem = {"sync": "ds", "gpsimd": "dg"}[eng]
        self.bump(sem, 16)
        self.emit(eng, lambda e, d=dst, s=src, h=self.semh[sem], t=transpose:
                  e.dma_start(d, s, transpose=t).then_inc(h, 16))
        return self.semn[sem]

    def O(self, eng, meth, *args, inc=False, drain=False, **kw):
        sem = {"tensor": "tm", "vector": "vm", "scalar": "am", "gpsimd": "vm"}[eng]
        if inc:
            self.bump(sem, 1)
        def f(e, meth=meth, args=args, kw=kw, inc=inc, drain=drain,
              h=self.semh[sem]):
            if drain:
                e.drain()
            r = getattr(e, meth)(*args, **kw)
            if inc:
                r.then_inc(h, 1)
        self.emit(eng, f)
        return self.semn[sem]

    def mm_group(self, bank, pairs, waits=()):
        s, v = self.bank_r[bank]
        self.W("tensor", s, v)
        for (ss, vv) in waits:
            self.W("tensor", ss, vv)
        n = len(pairs)
        for i, (l, r) in enumerate(pairs):
            self.O("tensor", "matmul", self.psum[bank][:, :], l, r,
                   start=(i == 0), stop=(i == n - 1), inc=(i == n - 1))
        self.bank_w[bank] = ("tm", self.semn["tm"])
        return self.semn["tm"]

    def evac(self, bank, eng, emit_cb, waits=()):
        s, v = self.bank_w[bank]
        self.W(eng, s, v)
        for (ss, vv) in waits:
            self.W(eng, ss, vv)
        emit_cb()
        sem = {"vector": "vm", "scalar": "am"}[eng]
        self.bank_r[bank] = (sem, self.semn[sem])
        return self.semn[sem]

    def build(self):
        nc, T, NCK, NTT = self.nc, self.T, self.NCK, self.NTT
        W, O, DMA, E = self.W, self.O, self.DMA, self.emit
        dt = nc.dram_tensor

        x_in = dt("x_in", [T, D], F32, kind="ExternalInput").ap()
        winT = dt("winT", [128, NKD * DI], F32, kind="ExternalInput").ap()
        w1h = dt("w1h", [NLAYER, 128, 4 * DFI], F32, kind="ExternalInput").ap()
        w2h = dt("w2h", [NLAYER, 128, 16 * DI], F32, kind="ExternalInput").ap()
        woutT = dt("woutT", [128, 4 * D], F32, kind="ExternalInput").ap()
        gateT = dt("gateT", [NFT, 128, D], mybir.dt.int8, kind="ExternalInput").ap()
        upT = dt("upT", [NFT, 128, D], mybir.dt.int8, kind="ExternalInput").ap()
        downT = dt("downT", [NFT, 128, D], mybir.dt.int8, kind="ExternalInput").ap()
        PCOLS = [("ln1w", 16), ("ln1b", 16), ("ln2w", 16), ("ln2b", 16),
                 ("scl", 16), ("b2", 16), ("pre", 16), ("pim", 16),
                 ("lcr", 16), ("lci", 16), ("b1", 64), ("bin", 4),
                 ("lnlw", 4), ("lnlb", 4), ("bout", 32), ("gs", NFT),
                 ("us", NFT), ("ds", 32), ("toff", 1), ("mask", 8)]
        pin = {nm: dt("pp_" + nm, [128, n], F32, kind="ExternalInput").ap()
               for nm, n in PCOLS}
        out_d = dt("out", [T, D], F32, kind="ExternalOutput").ap()
        dbg_aps = {nm: dt("dbg_" + nm, list(sh), F32, kind="ExternalOutput").ap()
                   for nm, sh in _DEBUG}

        xlna_tm = dt("xlna_tm", [T, D], BF16, kind="Internal").ap()
        ao_fm = dt("ao_fm", [D, T], BF16, kind="Internal").ap()
        x1_tm = dt("x1_tm", [T, D], F32, kind="Internal").ap()
        xlnb_tm = dt("xlnb_tm", [T, D], BF16, kind="Internal").ap()
        ffo_fm = dt("ffo_fm", [D, T], BF16, kind="Internal").ap()
        ncc = NLAYER
        cc_i = [dt(f"cc_i{k}", [128, 8], F32, kind="Internal").ap() for k in range(ncc)]
        cc_o = [dt(f"cc_o{k}", [NCORE * 128, 8], F32, kind="Internal").ap()
                for k in range(ncc)]

        pos = [16384]

        def place(name, shape, dtype):
            nb = int(np.prod(shape[1:])) * (4 if dtype != BF16 else 2)
            at = pos[0]
            pos[0] = at + ((nb + 63) // 64) * 64
            return nc.alloc_sbuf_tensor_at(name, list(shape), dtype, offset=at), at

        def over(name, shape, dtype, at):
            return nc.alloc_sbuf_tensor_at(name, list(shape), dtype, offset=at)

        TB = T * 4
        h_t = []
        H0 = pos[0]
        for c in range(4):
            h_t.append(place(f"h{c}", [128, T], F32)[0])
        tglob, TG0 = place("tglob", [128, T], F32)
        params, _ = place("params", [128, 1024], F32)
        statsf, _ = place("statsf", [128, 64], F32)
        var_scr, VS0 = place("var_scr", [1, T], F32)
        ccg, _ = place("ccg", [128, 64], F32)
        czt, _ = place("czt", [128, 8], F32)
        dpk2, _ = place("dpk2", [128, 8], F32)
        ccd, _ = place("ccd", [128, 64], F32)
        ones_bf, _ = place("ones_bf", [128, 1], BF16)
        onesr_f, _ = place("onesr_f", [1, 128], F32)
        nonesr_f, _ = place("nonesr_f", [1, 128], F32)
        c_pi2, _ = place("c_pi2", [128, 1], F32)
        c_eps, _ = place("c_eps", [128, 1], F32)
        c_zero, _ = place("c_zero", [128, 1], F32)
        state_pk, _ = place("state_pk", [128, 8], F32)
        STG0 = pos[0]
        stg = [place(f"stg{s}", [128, D], BF16)[0] for s in range(2)]
        aoev = [place(f"aoev{s}", [128, 512], BF16)[0] for s in range(2)]

        SCAN0 = pos[0]
        scn = [place(f"scn{i}", [128, T], F32)[0] for i in range(4)]
        SC_COS = [scn[0], scn[2]]
        SC_SIN = [scn[1], scn[3]]
        if pos[0] - SCAN0 < 32768:
            pos[0] = SCAN0 + 32768
        WA0 = pos[0]
        w1sb, _ = place("w1sb", [128, 4 * DFI], BF16)
        w2sb, _ = place("w2sb", [128, 16 * DI], BF16)
        MM0 = pos[0]
        yb = [place(f"yb{c}", [128, T], BF16)[0] for c in range(4)]
        hb = [place(f"hb{s}", [128, T], BF16)[0] for s in range(2)]
        ffn1, _ = place("ffn1", [128, 16 * 512], BF16)
        sc1, SC1_0 = place("sc1", [128, T], F32)
        sc2, SC2_0 = place("sc2", [128, T], F32)
        if pos[0] - MM0 < 49152:
            pos[0] = MM0 + 49152
        END0 = pos[0]
        assert END0 <= 229376, f"sbuf static overflow: {END0}"
        SCAN_B = WA0 - SCAN0
        MM_B = END0 - MM0

        xtm = [over(f"xtm{i}", [128, D], F32, SCAN0 + i * 16384) for i in range(2)]
        xtm4 = [over(f"xtm4_{i}", [128, D], F32, H0 + i * 16384) for i in range(2)]
        aotm4 = over("aotm4", [128, D], BF16, TG0)
        jnkb4 = [over(f"jnkb4_{i}", [128, D], BF16, STG0 + i * 8192)
                 for i in range(2)]
        assert SCAN_B >= 2 * 16384
        aotm = over("aotm", [128, D], BF16, MM0)
        jnkb = over("jnkb", [128, D], BF16, MM0 + 8192)
        winsb = over("winsb", [128, NKD * DI], BF16, MM0 + 16384)
        xlnw = over("xlnw", [128, NKD * 512], BF16, SCAN0)
        woutsb = over("woutsb", [128, 4 * D], BF16, SCAN0)
        xlnffn = over("xlnffn", [128, NKD * 512], BF16, SCAN0)
        gusb = [over(f"gusb{s}", [128, D], BF16, WA0 + s * 8192) for s in range(4)]
        assert WA0 + 4 * 8192 <= END0 + 4096
        if T * 4 >= 8192:
            dsb = [over(f"dsb{s}", [128, 1024], BF16, VS0 + s * 2048)
                   for s in range(4)]
        else:
            dsb = [place(f"dsb{s}", [128, 1024], BF16)[0] for s in range(4)]
        hch = over("hch", [128, FCH * 512], BF16, MM0)
        if T == 2048:
            acc_off = []
            for i in range(5 * TB // 2048):
                acc_off.append(H0 + i * 2048)
            need = NKD - len(acc_off)
            for i in range(need):
                acc_off.append(MM0 + FCH * 1024 + i * 2048)
            assert MM0 + FCH * 1024 + need * 2048 <= 229376
            accum = [over(f"acc{i}", [128, 512], F32, acc_off[i])
                     for i in range(NKD)]
            pos[0] = max(pos[0], MM0 + FCH * 1024 + need * 2048)
        else:
            acc_off = [H0 + i * 2048 for i in range(5 * TB // 2048)]
            acc_off += [MM0 + FCH * 1024 + i * 2048
                        for i in range((49152 - FCH * 1024) // 2048)]
            accum = [over(f"acc{i}", [128, 512], F32, acc_off[i])
                     if i < len(acc_off)
                     else place(f"acc{i}", [128, 512], F32)[0]
                     for i in range(NKD)]
            assert pos[0] <= 229376, pos[0]
        gsil = [place(f"gsil{s}", [128, 512], BF16)[0] for s in range(2)]
        assert pos[0] <= 229376, pos[0]

        mu_row = over("mu_row", [1, T], F32, SC1_0)
        sq_row = over("sq_row", [1, T], F32, SC2_0)
        self.psum = [nc.alloc_psum_tensor(f"ps{i}", [128, 512], F32)
                     for i in range(8)]

        PC = {}
        pcn = [0]
        for nm, n in PCOLS + [("rho", 16), ("lnrho", 16), ("theta", 16),
                              ("cth", 16), ("sth", 16), ("ainit", 16),
                              ("binit", 16), ("dpk", 16), ("pb", 16),
                              ("t1", 16), ("t2", 16), ("t3", 16), ("t4", 16),
                              ("u1", 1)]:
            PC[nm] = pcn[0]
            pcn[0] += n
        assert pcn[0] <= 1024

        def P(nm, i=0, n=1):
            c = PC[nm] + i
            return params[:, c:c + n]

        def A16(nm):
            return P(nm, 0, 16)

        for nm, n in PCOLS:
            DMA("sync", P(nm, 0, n), pin[nm])
        v_pld = self.semn["ds"]

        O("vector", "memset", c_pi2[:, :], PI_2)
        O("vector", "memset", c_eps[:, :], float(EPS))
        O("vector", "memset", c_zero[:, :], 0.0)
        O("vector", "memset", ones_bf[:, :], 1.0 / DI)
        O("vector", "memset", onesr_f[:, :], 1.0)
        O("vector", "memset", nonesr_f[:, :], -1.0, inc=True)
        O("gpsimd", "iota", tglob[:, :], [[1, T]], base=0, channel_multiplier=0,
          allow_small_or_imprecise_dtypes=True, inc=True)
        W("vector", "vm", self.semn["vm"])
        W("vector", "ds", v_pld)
        O("vector", "tensor_scalar", tglob[:, :], tglob[:, :], P("toff"), None,
          ALU.add, ALU.bypass, inc=True)

        W("scalar", "ds", v_pld)
        W("scalar", "vm", self.semn["vm"])
        O("scalar", "activation", A16("t1"), A16("pre"), AF.Square, bias=c_zero[:, 0:1])
        O("scalar", "activation", A16("t2"), A16("pim"), AF.Square, drain=True,
          inc=True, bias=c_zero[:, 0:1])
        W("vector", "am", self.semn["am"])
        O("vector", "tensor_tensor", A16("t1"), A16("t1"), A16("t2"), ALU.add,
          drain=True, inc=True)
        W("scalar", "vm", self.semn["vm"])
        O("scalar", "activation", A16("t2"), A16("t1"), AF.Sqrt, drain=True,
          inc=True, bias=c_zero[:, 0:1])
        W("vector", "am", self.semn["am"])
        O("vector", "reciprocal", A16("t3"), A16("t2"), drain=True)
        O("vector", "tensor_tensor", A16("t3"), A16("t3"), A16("t1"), ALU.mult,
          drain=True)
        O("vector", "tensor_tensor", A16("t3"), A16("t3"), A16("t2"), ALU.add,
          drain=True)
        O("vector", "tensor_scalar", A16("t3"), A16("t3"), 0.5, None, ALU.mult,
          drain=True, inc=True)
        W("scalar", "vm", self.semn["vm"])
        O("scalar", "activation", A16("rho"), A16("t3"), AF.Tanh, drain=True,
          inc=True, bias=c_zero[:, 0:1])
        W("vector", "am", self.semn["am"])
        O("vector", "tensor_scalar", A16("t4"), A16("rho"), 1e-36, None,
          ALU.max, drain=True, inc=True)
        W("scalar", "vm", self.semn["vm"])
        O("scalar", "activation", A16("lnrho"), A16("t4"), AF.Ln, drain=True, bias=c_zero[:, 0:1])
        O("scalar", "activation", A16("dpk"), A16("lnrho"), AF.Exp,
          scale=float(T), drain=True, bias=c_zero[:, 0:1])
        O("scalar", "activation", A16("t1"), A16("pre"), AF.Abs, drain=True, bias=c_zero[:, 0:1])
        O("scalar", "activation", A16("t2"), A16("pim"), AF.Abs, drain=True,
          inc=True, bias=c_zero[:, 0:1])
        W("vector", "am", self.semn["am"])
        O("vector", "tensor_tensor", A16("t3"), A16("t2"), A16("t1"), ALU.is_gt,
          drain=True)
        O("vector", "tensor_tensor", A16("t4"), A16("t1"), A16("t2"), ALU.min,
          drain=True)
        O("vector", "tensor_tensor", A16("t2"), A16("t1"), A16("t2"), ALU.max,
          drain=True)
        O("vector", "tensor_scalar", A16("t2"), A16("t2"), 1e-30, None, ALU.max,
          drain=True)
        O("vector", "reciprocal", A16("t1"), A16("t2"), drain=True)
        O("vector", "tensor_tensor", A16("t1"), A16("t1"), A16("t4"), ALU.mult,
          drain=True, inc=True)
        W("scalar", "vm", self.semn["vm"])
        O("scalar", "activation", A16("t4"), A16("t1"), AF.Arctan, drain=True,
          inc=True, bias=c_zero[:, 0:1])
        W("vector", "am", self.semn["am"])
        O("vector", "tensor_scalar", A16("t1"), A16("t4"), -2.0, PI_2, ALU.mult,
          ALU.add, drain=True)
        O("vector", "tensor_tensor", A16("t1"), A16("t1"), A16("t3"), ALU.mult,
          drain=True)
        O("vector", "tensor_tensor", A16("t4"), A16("t4"), A16("t1"), ALU.add,
          drain=True)
        O("vector", "tensor_scalar", A16("t3"), A16("pre"), 0.0, None, ALU.is_lt,
          drain=True)
        O("vector", "tensor_scalar", A16("t1"), A16("t4"), -2.0, PI, ALU.mult,
          ALU.add, drain=True)
        O("vector", "tensor_tensor", A16("t1"), A16("t1"), A16("t3"), ALU.mult,
          drain=True)
        O("vector", "tensor_tensor", A16("t4"), A16("t4"), A16("t1"), ALU.add,
          drain=True)
        O("vector", "tensor_scalar", A16("t1"), A16("pim"), 0.0, None, ALU.is_gt,
          drain=True)
        O("vector", "tensor_scalar", A16("t2"), A16("pim"), 0.0, None, ALU.is_lt,
          drain=True)
        O("vector", "tensor_tensor", A16("t1"), A16("t1"), A16("t2"),
          ALU.subtract, drain=True)
        O("vector", "tensor_tensor", A16("theta"), A16("t4"), A16("t1"),
          ALU.mult, drain=True, inc=True)
        W("scalar", "vm", self.semn["vm"])
        O("scalar", "activation", A16("sth"), A16("theta"), AF.Sin, drain=True, bias=c_zero[:, 0:1])
        O("scalar", "activation", A16("t1"), A16("theta"), AF.Abs, drain=True, bias=c_zero[:, 0:1])
        O("scalar", "activation", A16("cth"), A16("t1"), AF.Sin, scale=-1.0,
          bias=c_pi2[:, 0:1], drain=True, inc=True)
        W("vector", "am", self.semn["am"])
        O("vector", "tensor_tensor", A16("t1"), A16("cth"), A16("lcr"), ALU.mult,
          drain=True)
        O("vector", "tensor_tensor", A16("t2"), A16("sth"), A16("lci"), ALU.mult,
          drain=True)
        O("vector", "tensor_tensor", A16("ainit"), A16("t1"), A16("t2"),
          ALU.subtract, drain=True)
        O("vector", "tensor_tensor", A16("t1"), A16("sth"), A16("lcr"), ALU.mult,
          drain=True)
        O("vector", "tensor_tensor", A16("t2"), A16("cth"), A16("lci"), ALU.mult,
          drain=True)
        O("vector", "tensor_tensor", A16("binit"), A16("t1"), A16("t2"), ALU.add,
          drain=True)
        O("vector", "tensor_scalar", A16("binit"), A16("binit"), -1.0, None,
          ALU.mult, drain=True)
        O("vector", "tensor_scalar", A16("ainit"), A16("ainit"), P("mask", 7),
          None, ALU.mult, drain=True)
        O("vector", "tensor_scalar", A16("binit"), A16("binit"), P("mask", 7),
          None, ALU.mult, drain=True)
        O("vector", "tensor_scalar", P("u1"), P("toff"), -1.0, 1.0, ALU.mult,
          ALU.add, drain=True)
        O("vector", "tensor_scalar", A16("pb"), A16("lnrho"), P("u1"), None,
          ALU.mult, drain=True)
        O("vector", "engine_nop", inc=True)
        v_p0 = self.semn["vm"]
        if "theta" in dbg_aps:
            W("sync", "vm", v_p0)
            DMA("sync", dbg_aps["theta"], A16("theta"))
            DMA("sync", dbg_aps["rho"], A16("rho"))
            DMA("sync", dbg_aps["ainit"], A16("ainit"))
            DMA("sync", dbg_aps["lnrho"], A16("lnrho"))
            DMA("sync", dbg_aps["tg"], tglob[:, :])

        g_win = DMA("gpsimd", winsb[:, :], winT)
        g_w1 = {0: DMA("gpsimd", w1sb[:, :], w1h[0])}
        g_w2 = {0: DMA("gpsimd", w2sb[:, :], w2h[0])}

        def drain_pc(n):
            pass

        musum, sqsum = statsf[:, 0:16], statsf[:, 16:32]
        nmr, rst = statsf[:, 32:48], statsf[:, 48:64]

        def tm_stats(src_ap, junk_ap, col, ld, vwaits=()):
            c = slice(col, col + 1)
            W("vector", "ds", ld)
            for (ss, vv) in vwaits:
                W("vector", ss, vv)
            O("vector", "tensor_scalar", junk_ap, src_ap, 1.0, 0.0, ALU.mult,
              ALU.add, accum_out=musum[:, c], inc=True)
            W("scalar", "ds", ld)
            for (ss, vv) in vwaits:
                W("scalar", ss, vv)
            O("scalar", "activation", junk_ap, src_ap, AF.Square,
              accum_out=sqsum[:, c], inc=True, bias=c_zero[:, 0:1])
            a_sq = self.semn["am"]
            W("vector", "am", a_sq)
            O("vector", "tensor_scalar", musum[:, c], musum[:, c], 1.0 / D, None,
              ALU.mult, drain=True)
            O("vector", "tensor_scalar", sqsum[:, c], sqsum[:, c], 1.0 / D, None,
              ALU.mult, drain=True)
            O("vector", "tensor_tensor", rst[:, c], musum[:, c], musum[:, c],
              ALU.mult, drain=True)
            O("vector", "tensor_tensor", rst[:, c], sqsum[:, c], rst[:, c],
              ALU.subtract, drain=True, inc=True)
            W("scalar", "vm", self.semn["vm"])
            O("scalar", "activation", rst[:, c], rst[:, c], AF.Ln,
              bias=c_eps[:, 0:1], drain=True)
            O("scalar", "activation", rst[:, c], rst[:, c], AF.Exp, scale=-0.5,
              drain=True, inc=True, bias=c_zero[:, 0:1])
            a_r = self.semn["am"]
            W("vector", "am", a_r)
            O("vector", "tensor_tensor", nmr[:, c], musum[:, c], rst[:, c],
              ALU.mult, drain=True)
            O("vector", "tensor_scalar", nmr[:, c], nmr[:, c], -1.0, None,
              ALU.mult, drain=True, inc=True)
            return self.semn["vm"], a_r

        v_used, st_done = {}, {}
        a_used = {}
        for tt in range(NTT):
            sl = tt % 2
            if tt >= 2:
                W("sync", "vm", v_used[tt - 2])
                W("sync", "am", a_used[tt - 2])
            ld = DMA("sync", xtm[sl][:, :], x_in[tt * 128:(tt + 1) * 128, :])
            if tt == 0:
                W("vector", "ds", ld)
                W("vector", "vm", v_p0)
            col = tt % 16
            v_s, a_s = tm_stats(xtm[sl][:, :], jnkb[:, :], col, ld)
            a_used[tt] = a_s
            if tt >= 1:
                W("vector", "ds", st_done[tt - 1])
            cc = slice(col, col + 1)
            O("vector", "tensor_scalar", aotm[:, :], xtm[sl][:, :], rst[:, cc],
              nmr[:, cc], ALU.mult, ALU.add, inc=True)
            v_x = self.semn["vm"]
            v_used[tt] = v_x
            W("sync", "vm", v_x)
            st_done[tt] = DMA("sync", xlna_tm[tt * 128:(tt + 1) * 128, :],
                              aotm[:, :])
            if tt % 4 == 3:
                drain_pc(4)

        xb_mm = {}
        a_p2 = 0
        for k in range(NCK):
            W("sync", "dg", g_win)
            W("sync", "ds", st_done[4 * k + 3])
            if k >= 1:
                W("sync", "tm", xb_mm[k - 1])
            ldv = 0
            for kv in range(NKD):
                ldv = DMA("sync", xlnw[:, kv * 512:(kv + 1) * 512],
                          xlna_tm[k * 512:(k + 1) * 512,
                                  kv * 128:(kv + 1) * 128], transpose=True)
            for dtile in range(4):
                bank = dtile
                pairs = [(winsb[:, kv * 512 + dtile * 128:
                                kv * 512 + dtile * 128 + 128],
                          xlnw[:, kv * 512:(kv + 1) * 512]) for kv in range(NKD)]
                self.mm_group(bank, pairs, waits=[("ds", ldv)])
                a_p2 = self.evac(bank, "scalar", lambda b=bank, d=dtile, k=k:
                                 O("scalar", "activation",
                                   h_t[d][:, k * 512:(k + 1) * 512],
                                   self.psum[b][:, :], AF.Identity,
                                   bias=P("bin", d), inc=True))
            xb_mm[k] = self.semn["tm"]
            drain_pc(6)
        if "h_win" in dbg_aps:
            W("sync", "am", a_p2)
            for c in range(4):
                DMA("sync", dbg_aps["h_win"][c * 128:(c + 1) * 128, :],
                    h_t[c][:, :])

        def inner_ln_stats(extra_waits):
            t_c = {}
            for c in range(4):
                sl = c % 2
                if c == 0:
                    for (ss, vv) in extra_waits:
                        W("vector", ss, vv)
                if c >= 2:
                    W("vector", "tm", t_c[c - 2])
                O("vector", "tensor_copy", hb[sl][:, :], h_t[c][:, :], inc=True)
                v_cast = self.semn["vm"]
                W("scalar", "vm", v_cast)
                if c == 0:
                    for (ss, vv) in extra_waits:
                        W("scalar", ss, vv)
                else:
                    W("scalar", "tm", t_c[c - 1])
                O("scalar", "activation", ffn1[:, 0:T], hb[sl][:, :], AF.Square,
                  inc=True, bias=c_zero[:, 0:1])
                a_sqc = self.semn["am"]
                W("tensor", "vm", v_cast)
                W("tensor", "am", a_sqc)
                if c == 0:
                    for b in range(8):
                        s, v = self.bank_r[b]
                        W("tensor", s, v)
                for k in range(NCK):
                    O("tensor", "matmul", self.psum[k][0:1, :], ones_bf[:, :],
                      hb[sl][:, k * 512:(k + 1) * 512],
                      start=(c == 0), stop=(c == 3))
                for k in range(NCK):
                    O("tensor", "matmul", self.psum[4 + k][0:1, :],
                      ones_bf[:, :], ffn1[:, k * 512:(k + 1) * 512],
                      start=(c == 0), stop=(c == 3),
                      inc=(k == NCK - 1))
                t_c[c] = self.semn["tm"]
            t_stats = self.semn["tm"]
            for b in range(8):
                self.bank_w[b] = ("tm", t_stats)
            W("vector", "tm", t_stats)
            for k in range(NCK):
                O("vector", "tensor_copy", mu_row[0:1, k * 512:(k + 1) * 512],
                  self.psum[k][0:1, :])
                O("vector", "tensor_copy", sq_row[0:1, k * 512:(k + 1) * 512],
                  self.psum[4 + k][0:1, :])
            O("vector", "engine_nop", inc=True)
            v_ev = self.semn["vm"]
            for b in range(8):
                self.bank_r[b] = ("vm", v_ev)
            O("vector", "tensor_tensor", var_scr[0:1, 0:T], mu_row[0:1, 0:T],
              mu_row[0:1, 0:T], ALU.mult)
            O("vector", "tensor_tensor", sq_row[0:1, 0:T], sq_row[0:1, 0:T],
              var_scr[0:1, 0:T], ALU.subtract, inc=True)
            W("scalar", "vm", self.semn["vm"])
            O("scalar", "activation", sq_row[0:1, 0:T], sq_row[0:1, 0:T], AF.Ln,
              bias=c_eps[0:1, 0:1])
            O("scalar", "activation", sq_row[0:1, 0:T], sq_row[0:1, 0:T], AF.Exp,
              scale=-0.5, inc=True, bias=c_zero[0:1, 0:1])
            W("vector", "am", self.semn["am"])
            O("vector", "tensor_tensor", mu_row[0:1, 0:T], mu_row[0:1, 0:T],
              sq_row[0:1, 0:T], ALU.mult, inc=True)
            v_rr = self.semn["vm"]
            if "murow" in dbg_aps and not self.dumped_stats:
                self.dumped_stats = True
                W("sync", "vm", v_rr)
                dmp = DMA("sync", dbg_aps["murow"], mu_row[0:1, 0:T])
                dmp = DMA("sync", dbg_aps["sqrow"], sq_row[0:1, 0:T])
                W("vector", "ds", dmp)
            for k in range(NCK):
                self.mm_group(k, [(nonesr_f[:, :],
                                   mu_row[0:1, k * 512:(k + 1) * 512])],
                              waits=[("vm", v_rr)])
                self.mm_group(4 + k, [(onesr_f[:, :],
                                       sq_row[0:1, k * 512:(k + 1) * 512])])

        def normalize_to_sc1(c):
            for k in range(NCK):
                for b in (k, 4 + k):
                    s, v = self.bank_w[b]
                    W("vector", s, v)
                sl5 = slice(k * 512, (k + 1) * 512)
                O("vector", "tensor_tensor", sc1[:, sl5], h_t[c][:, sl5],
                  self.psum[4 + k][:, :], ALU.mult)
                O("vector", "tensor_tensor", sc1[:, sl5], sc1[:, sl5],
                  self.psum[k][:, :], ALU.add, inc=(k == NCK - 1))
            v_n = self.semn["vm"]
            for k in range(NCK):
                self.bank_r[k] = ("vm", v_n)
                self.bank_r[4 + k] = ("vm", v_n)
            return v_n

        def corr_combine(g_rd):
            W("vector", "dg", g_rd)
            O("vector", "memset", czt[:, :], 0.0)
            for j in range(7):
                O("vector", "tensor_tensor", czt[:, :], czt[:, :],
                  ccd[:, j * 8:j * 8 + 8], ALU.mult, drain=True)
                O("vector", "scalar_tensor_tensor", czt[:, :],
                  ccg[:, j * 8:j * 8 + 8], P("mask", j), czt[:, :],
                  ALU.mult, ALU.add, drain=True)
            O("vector", "engine_nop", drain=True, inc=True)
            return self.semn["vm"]

        def apply_corr(l, c, slot, recompute):
            dcol = l * 4 + c
            cosb, sinb = SC_COS[slot][:, :], SC_SIN[slot][:, :]
            if recompute:
                O("vector", "tensor_scalar", cosb, tglob[:, :],
                  P("theta", dcol), None, ALU.mult)
                O("vector", "tensor_scalar", sinb, cosb, INV2PI, MAGIC,
                  ALU.mult, ALU.add)
                O("vector", "tensor_scalar", sinb, sinb, -MAGIC, None, ALU.add)
                O("vector", "scalar_tensor_tensor", cosb, sinb, -CW_C1, cosb,
                  ALU.mult, ALU.add)
                O("vector", "scalar_tensor_tensor", cosb, sinb, -CW_C2, cosb,
                  ALU.mult, ALU.add)
                O("vector", "scalar_tensor_tensor", cosb, sinb, -CW_C3, cosb,
                  ALU.mult, ALU.add)
                O("vector", "tensor_scalar", cosb, cosb, PI, -PI, ALU.min,
                  ALU.max, inc=True)
                W("scalar", "vm", self.semn["vm"])
                O("scalar", "activation", sinb, cosb, AF.Sin,
                  bias=c_zero[:, 0:1])
                O("scalar", "activation", sc2[:, :], cosb, AF.Abs,
                  bias=c_zero[:, 0:1])
                O("scalar", "activation", cosb, sc2[:, :], AF.Sin, scale=-1.0,
                  bias=c_pi2[:, 0:1], inc=True)
                W("vector", "am", self.semn["am"])
            W("scalar", "vm", self.semn["vm"])
            O("scalar", "activation", sc2[:, :], tglob[:, :], AF.Exp,
              scale=P("lnrho", dcol), bias=P("pb", dcol), inc=True)
            a_rp = self.semn["am"]
            W("vector", "am", a_rp)
            O("vector", "tensor_scalar", sc1[:, :], cosb,
              czt[:, 2 * c:2 * c + 1], None, ALU.mult)
            O("vector", "scalar_tensor_tensor", sc1[:, :], sinb,
              czt[:, 2 * c + 1:2 * c + 2], sc1[:, :], ALU.mult, ALU.add)
            O("vector", "tensor_tensor", sc1[:, :], sc1[:, :], sc2[:, :],
              ALU.mult)
            O("vector", "scalar_tensor_tensor", h_t[c][:, :], sc1[:, :],
              P("scl", dcol), h_t[c][:, :], ALU.mult, ALU.add, inc=True)
            return self.semn["vm"]

        t_w2_prev = [xb_mm[NCK - 1]]
        a_prev = [a_p2]
        for l in range(NLAYER):
            inner_ln_stats([("am", a_prev[0]), ("tm", t_w2_prev[0])])
            W("vector", "vm", v_p0)
            for c4 in range(4):
                O("vector", "tensor_copy", dpk2[:, 2 * c4:2 * c4 + 1],
                  P("dpk", l * 4 + c4), drain=True)
                O("vector", "tensor_copy", dpk2[:, 2 * c4 + 1:2 * c4 + 2],
                  P("dpk", l * 4 + c4), drain=True)
            for j in range(7):
                O("vector", "tensor_scalar", ccd[:, j * 8:j * 8 + 8],
                  dpk2[:, :], P("mask", j), None, ALU.mult, drain=True)
                O("vector", "tensor_scalar", P("u1"), P("mask", j), -1.0, 1.0,
                  ALU.mult, ALU.add, drain=True)
                O("vector", "tensor_scalar", ccd[:, j * 8:j * 8 + 8],
                  ccd[:, j * 8:j * 8 + 8], P("u1"), None, ALU.add, drain=True)
            O("vector", "engine_nop", inc=True)

            for c in range(4):
                slot = c % 2
                dcol = l * 4 + c
                v_n = normalize_to_sc1(c)
                W("scalar", "vm", v_n)
                O("scalar", "activation", sc1[:, :], sc1[:, :], AF.Silu,
                  scale=P("ln1w", dcol), bias=P("ln1b", dcol), inc=True)
                a_yt = self.semn["am"]
                cosb = SC_COS[slot][:, :]
                sinb = SC_SIN[slot][:, :]
                W("vector", "am", a_yt)
                O("vector", "tensor_scalar", cosb, tglob[:, :],
                  P("theta", dcol), None, ALU.mult, inc=True)
                if l == 0 and c == 0 and "v00" in dbg_aps:
                    W("sync", "vm", self.semn["vm"])
                    dmp = DMA("sync", dbg_aps["v00"], cosb)
                    W("vector", "ds", dmp)
                O("vector", "tensor_scalar", sinb, cosb, INV2PI, MAGIC,
                  ALU.mult, ALU.add)
                O("vector", "tensor_scalar", sinb, sinb, -MAGIC, None, ALU.add)
                O("vector", "scalar_tensor_tensor", cosb, sinb, -CW_C1, cosb,
                  ALU.mult, ALU.add)
                O("vector", "scalar_tensor_tensor", cosb, sinb, -CW_C2, cosb,
                  ALU.mult, ALU.add)
                O("vector", "scalar_tensor_tensor", cosb, sinb, -CW_C3, cosb,
                  ALU.mult, ALU.add)
                O("vector", "tensor_scalar", cosb, cosb, PI, -PI, ALU.min,
                  ALU.max, inc=True)
                if l == 0 and c == 0 and "r00" in dbg_aps:
                    W("sync", "vm", self.semn["vm"])
                    dmp = DMA("sync", dbg_aps["r00"], cosb)
                    W("scalar", "ds", dmp)
                W("scalar", "vm", self.semn["vm"])
                O("scalar", "activation", sinb, cosb, AF.Sin, bias=c_zero[:, 0:1])
                O("scalar", "activation", sc2[:, :], cosb, AF.Abs, bias=c_zero[:, 0:1])
                O("scalar", "activation", cosb, sc2[:, :], AF.Sin, scale=-1.0,
                  bias=c_pi2[:, 0:1], inc=True)
                a_cs = self.semn["am"]
                if l == 0 and c == 0 and "yt00" in dbg_aps:
                    W("sync", "am", a_yt)
                    dmp = DMA("sync", dbg_aps["yt00"], sc1[:, :])
                    W("vector", "ds", dmp)
                if l == 0 and c == 0 and "cos00" in dbg_aps:
                    W("sync", "am", a_cs)
                    dmp = DMA("sync", dbg_aps["cos00"], cosb)
                    dmp = DMA("sync", dbg_aps["sin00"], sinb)
                    W("vector", "ds", dmp)
                    W("scalar", "ds", dmp)
                W("vector", "am", a_cs)
                rho_b = P("rho", dcol).broadcast_to((128, T))
                O("vector", "tensor_tensor", sc2[:, :], cosb, sc1[:, :],
                  ALU.mult)
                O("vector", "tensor_tensor_scan", sc2[:, :], rho_b, sc2[:, :],
                  P("ainit", dcol), ALU.mult, ALU.add)
                O("vector", "tensor_tensor", sc1[:, :], sinb, sc1[:, :],
                  ALU.mult)
                O("vector", "tensor_tensor_scan", sc1[:, :], rho_b, sc1[:, :],
                  P("binit", dcol), ALU.mult, ALU.add, inc=True)
                O("vector", "drain")
                if c == 0:
                    W("vector", "dg", getattr(self, "last_g_st", 0))
                O("vector", "tensor_copy", state_pk[:, 2 * c:2 * c + 1],
                  sc2[:, T - 1:T])
                O("vector", "tensor_copy", state_pk[:, 2 * c + 1:2 * c + 2],
                  sc1[:, T - 1:T])
                O("vector", "engine_nop", drain=True, inc=True)
                v_st = self.semn["vm"]
                if l == 0 and c == 0 and "A00" in dbg_aps:
                    W("sync", "vm", v_st)
                    dmp = DMA("sync", dbg_aps["A00"], sc2[:, :])
                    dmp = DMA("sync", dbg_aps["B00"], sc1[:, :])
                    W("vector", "ds", dmp)
                O("vector", "tensor_tensor", sc2[:, :], cosb, sc2[:, :],
                  ALU.mult)
                O("vector", "tensor_tensor", sc1[:, :], sinb, sc1[:, :],
                  ALU.mult)
                O("vector", "tensor_tensor", sc1[:, :], sc1[:, :], sc2[:, :],
                  ALU.add)
                O("vector", "scalar_tensor_tensor", h_t[c][:, :], sc1[:, :],
                  P("scl", dcol), h_t[c][:, :], ALU.mult, ALU.add, inc=True)
                drain_pc(8)
            W("gpsimd", "vm", v_st)
            g_st = DMA("gpsimd", cc_i[l], state_pk[:, :])
            self.last_g_st = g_st
            W("gpsimd", "dg", g_st)
            self.bump("cc", 1)
            E("gpsimd", lambda e, i=l, h=self.semh["cc"]:
              e.collective_compute("AllGather", ALU.bypass,
                                   [list(range(NCORE))],
                                   ins=[cc_i[i].opt()],
                                   outs=[cc_o[i].opt()]).then_inc(h, 1))
            W("gpsimd", "cc", self.semn["cc"])
            g_rd = DMA("gpsimd",
                       ccg[:, :].rearrange("p (r c) -> p r c", r=NCORE),
                       cc_o[l].rearrange("(r p) c -> p r c", p=128))
            corr_combine(g_rd)
            apply_corr(l, 2, 0, False)
            apply_corr(l, 3, 1, False)
            apply_corr(l, 0, 0, True)
            apply_corr(l, 1, 1, True)
            v_mix = self.semn["vm"]

            inner_ln_stats([("vm", v_mix)])
            W("vector", "tm", t_w2_prev[0])
            for c in range(4):
                dcol = l * 4 + c
                normalize_to_sc1(c)
                O("vector", "tensor_scalar", yb[c][:, :], sc1[:, :],
                  P("ln2w", dcol), P("ln2b", dcol), ALU.mult, ALU.add,
                  inc=True)
            v_yb = self.semn["vm"]

            W("tensor", "dg", g_w1[l])
            W("tensor", "dg", g_w2[l])
            W("tensor", "vm", v_yb)
            for k in range(NCK):
                for ft in range(16):
                    bank = ft % 4
                    pairs = [(w1sb[:, c * DFI + ft * 128:c * DFI + ft * 128 + 128],
                              yb[c][:, k * 512:(k + 1) * 512]) for c in range(4)]
                    self.mm_group(bank, pairs)
                    self.evac(bank, "scalar", lambda b=bank, ft=ft, l=l:
                              O("scalar", "activation",
                                ffn1[:, ft * 512:(ft + 1) * 512],
                                self.psum[b][:, :], AF.Silu, scale=1.0,
                                bias=P("b1", l * 16 + ft), inc=True))
                a_f1 = self.semn["am"]
                for dtile in range(4):
                    bank = 4 + (dtile % 2)
                    pairs = [(w2sb[:, ft * DI + dtile * 128:
                                   ft * DI + dtile * 128 + 128],
                              ffn1[:, ft * 512:(ft + 1) * 512])
                             for ft in range(16)]
                    self.mm_group(bank, pairs, waits=[("am", a_f1)])
                    self.evac(bank, "vector", lambda b=bank, d=dtile, k=k, l=l:
                              O("vector", "scalar_tensor_tensor",
                                h_t[d][:, k * 512:(k + 1) * 512],
                                self.psum[b][:, :], P("b2", l * 4 + d),
                                h_t[d][:, k * 512:(k + 1) * 512],
                                ALU.add, ALU.add, inc=True))
                drain_pc(3)
            t_w2 = self.semn["tm"]
            t_w2_prev[0] = t_w2
            a_prev[0] = self.semn["am"]
            v_hdone = self.semn["vm"]
            if (l + 1) < NLAYER:
                W("gpsimd", "tm", t_w2)
                g_w1[l + 1] = DMA("gpsimd", w1sb[:, :], w1h[l + 1])
                g_w2[l + 1] = DMA("gpsimd", w2sb[:, :], w2h[l + 1])
            if f"h_l{l}" in dbg_aps:
                W("sync", "vm", v_hdone)
                W("sync", "am", self.semn["am"])
                for c in range(4):
                    DMA("sync", dbg_aps[f"h_l{l}"][c * 128:(c + 1) * 128, :],
                        h_t[c][:, :])
        v_layers = self.semn["vm"]

        W("gpsimd", "vm", v_layers)
        g_wout = DMA("gpsimd", woutsb[:, :], woutT)

        inner_ln_stats([("vm", v_layers), ("tm", t_w2_prev[0]),
                        ("am", a_prev[0])])
        W("vector", "tm", t_w2_prev[0])
        for c in range(4):
            normalize_to_sc1(c)
            O("vector", "tensor_scalar", yb[c][:, :], sc1[:, :],
              P("lnlw", c), P("lnlb", c), ALU.mult, ALU.add, inc=True)
        v_hlb = self.semn["vm"]

        st_ao = {}
        i_ao = 0
        for k in range(NCK):
            for vt in range(NKD):
                bank = vt % 4
                slv = i_ao % 2
                pairs = [(woutsb[:, c * D + vt * 128:c * D + vt * 128 + 128],
                          yb[c][:, k * 512:(k + 1) * 512]) for c in range(4)]
                self.mm_group(bank, pairs,
                              waits=([("dg", g_wout), ("vm", v_hlb)]
                                     if (k == 0 and vt == 0) else ()))
                ew = [("ds", st_ao[i_ao - 2])] if i_ao >= 2 else []
                self.evac(bank, "scalar", lambda b=bank, vt=vt, s=slv:
                          O("scalar", "activation", aoev[s][:, :],
                            self.psum[b][:, :], AF.Identity,
                            bias=P("bout", vt), inc=True), waits=ew)
                W("sync", "am", self.semn["am"])
                st_ao[i_ao] = DMA(
                    "sync", ao_fm[vt * 128:(vt + 1) * 128, k * 512:(k + 1) * 512],
                    aoev[slv][:, :])
                i_ao += 1
            drain_pc(6)
        v_ao_all = self.semn["ds"]

        v_jnk = {}
        st4 = {}
        v_p4tail = 0
        for tt in range(NTT):
            sl = tt % 2
            W("sync", "ds", st_ao[(tt // 4) * NKD + NKD - 1])
            if tt >= 2:
                W("sync", "vm", v_used[tt - 2])
                W("sync", "ds", st4.get(tt - 2, 0))
                W("sync", "vm", v_jnk[tt - 2])
            ld_ao = DMA("sync", jnkb4[sl][:, :],
                        ao_fm[0:D, tt * 128:(tt + 1) * 128], transpose=True)
            ld_x = DMA("sync", xtm4[sl][:, :], x_in[tt * 128:(tt + 1) * 128, :])
            col = tt % 16
            cc = slice(col, col + 1)
            v_s, a_s = tm_stats(jnkb4[sl][:, :], aotm4[:, :], col, ld_ao)
            W("scalar", "ds", ld_x)
            W("scalar", "vm", v_s)
            O("scalar", "activation", xtm4[sl][:, :], xtm4[sl][:, :],
              AF.Identity, bias=nmr[:, cc], scale=1.0, inc=True)
            a_xn = self.semn["am"]
            W("vector", "am", a_xn)
            O("vector", "scalar_tensor_tensor", xtm4[sl][:, :], jnkb4[sl][:, :],
              rst[:, cc], xtm4[sl][:, :], ALU.mult, ALU.add, inc=True)
            v_x1 = self.semn["vm"]
            W("sync", "vm", v_x1)
            st4[tt] = DMA("sync", x1_tm[tt * 128:(tt + 1) * 128, :],
                          xtm4[sl][:, :])
            v_s2, a_s2 = tm_stats(xtm4[sl][:, :], aotm4[:, :], col, 0,
                                  vwaits=[("vm", v_x1)])
            W("scalar", "vm", v_s2)
            O("scalar", "activation", jnkb4[sl][:, :], xtm4[sl][:, :],
              AF.Identity, bias=nmr[:, cc], scale=rst[:, cc], inc=True)
            a_xb = self.semn["am"]
            v_used[tt] = v_s2
            v_jnk[tt] = v_s2
            W("sync", "am", a_xb)
            st4[tt] = DMA("sync", xlnb_tm[tt * 128:(tt + 1) * 128, :],
                          jnkb4[sl][:, :])
            if tt % 4 == 3:
                drain_pc(4)
        v_p4 = self.semn["ds"]
        v_p4vm = self.semn["vm"]
        if "x1d" in dbg_aps:
            W("sync", "ds", v_p4)
            DMA("sync", dbg_aps["x1d"], x1_tm)
            W("gpsimd", "ds", v_p4)
            DMA("gpsimd", dbg_aps["aod"], ao_fm)
            DMA("gpsimd", dbg_aps["xlnbd"], xlnb_tm)

        drain_pc(10 ** 9)
        fchunks = [(2 * FCH, NFT), (0, FCH), (FCH, 2 * FCH)]
        ft_order = [ft for (f0, f1) in fchunks for ft in range(f0, f1)]
        ft_next = {ft_order[i]: ft_order[i + 1]
                   for i in range(len(ft_order) - 1)}
        st_ffo = {}
        self.gsil_v = {}
        gu_t = {}
        ds_t = {}
        t_guplast = {}
        ldx_next = [0]
        for blk in range(NCK):
            tsl = slice(blk * 512, (blk + 1) * 512)
            if blk == 0:
                W("sync", "ds", st4[3])
                ldx = 0
                for kv in range(NKD):
                    ldx = DMA("sync", xlnffn[:, kv * 512:(kv + 1) * 512],
                              xlnb_tm[tsl, kv * 128:(kv + 1) * 128],
                              transpose=True)
            else:
                ldx = ldx_next[0]
            pend = {}

            def issue_gu_loads(ft, blk=blk):
                gu2 = ft % 2
                if gu2 in gu_t:
                    W("gpsimd", "tm", gu_t[gu2])
                lg = DMA("gpsimd", gusb[gu2][:, :], gateT[ft])
                lu = DMA("gpsimd", gusb[2 + gu2][:, :], upT[ft])
                pend[ft] = (lg, lu)

            first_issued = [False]
            for fci, (f0, f1) in enumerate(fchunks):
                first_chunk = (fci == 0)
                v_h = 0
                if not first_issued[0]:
                    issue_gu_loads(f0)
                    first_issued[0] = True
                for ft in range(f0, f1):
                    gu = ft % 2
                    nft_ = ft_next.get(ft)
                    if nft_ is not None and nft_ not in pend:
                        issue_gu_loads(nft_)
                    ldg, ldu = pend.pop(ft)
                    bank_g, bank_u = ft % 4, 4 + ft % 4
                    pairs = [(gusb[gu][:, kv * 128:(kv + 1) * 128],
                              xlnffn[:, kv * 512:(kv + 1) * 512])
                             for kv in range(NKD)]
                    self.mm_group(bank_g, pairs, waits=[("dg", ldg), ("ds", ldx)])
                    gw = ([("vm", self.gsil_v[ft % 2])]
                          if (ft % 2) in self.gsil_v else [])
                    a_g = self.evac(bank_g, "scalar",
                                    lambda b=bank_g, ft=ft, s=ft % 2:
                                    O("scalar", "activation", gsil[s][:, :],
                                      self.psum[b][:, :], AF.Silu,
                                      scale=P("gs", ft), inc=True, bias=c_zero[:, 0:1]), waits=gw)
                    pairs = [(gusb[2 + gu][:, kv * 128:(kv + 1) * 128],
                              xlnffn[:, kv * 512:(kv + 1) * 512])
                             for kv in range(NKD)]
                    gu_t[gu] = self.mm_group(bank_u, pairs,
                                             waits=[("dg", ldu)])
                    t_guplast[blk] = gu_t[gu]
                    v_h = self.evac(bank_u, "vector",
                                    lambda b=bank_u, ft=ft, f0=f0, s=ft % 2:
                                    O("vector", "scalar_tensor_tensor",
                                      hch[:, (ft - f0) * 512:(ft - f0 + 1) * 512],
                                      self.psum[b][:, :], P("us", ft),
                                      gsil[s][:, :], ALU.mult, ALU.mult,
                                      inc=True),
                                    waits=[("am", a_g)])
                    self.gsil_v[ft % 2] = v_h
                if blk == 0 and first_chunk and "hch0" in dbg_aps:
                    W("gpsimd", "vm", v_h)
                    DMA("gpsimd", dbg_aps["hch0"], hch[:, :])
                for dg_i in range(4):
                    pend_d = {}

                    def issue_d_loads(ft, dg_i=dg_i):
                        sl4 = ft % 4
                        if sl4 in ds_t:
                            W("gpsimd", "tm", ds_t[sl4])
                        pend_d[ft] = DMA(
                            "gpsimd", dsb[sl4][:, :],
                            downT[ft][:, dg_i * 1024:dg_i * 1024 + 1024])

                    issue_d_loads(f0)
                    issue_d_loads(f0 + 1)
                    issue_d_loads(f0 + 2)
                    for ft in range(f0, f1):
                        sl4 = ft % 4
                        if ft + 3 < f1:
                            issue_d_loads(ft + 3)
                        W("tensor", "dg", pend_d.pop(ft))
                        if ft == f0:
                            W("tensor", "vm", v_h)
                            for b in range(8):
                                s, v = self.bank_r[b]
                                W("tensor", s, v)
                        for di in range(8):
                            O("tensor", "matmul", self.psum[di][:, :],
                              dsb[sl4][:, di * 128:di * 128 + 128],
                              hch[:, (ft - f0) * 512:(ft - f0 + 1) * 512],
                              start=(ft == f0), stop=(ft == f1 - 1),
                              inc=(di == 7))
                        ds_t[sl4] = self.semn["tm"]
                    t_dg = self.semn["tm"]
                    for b in range(8):
                        self.bank_w[b] = ("tm", t_dg)
                    W("vector", "tm", t_dg)
                    if blk == 0 and first_chunk and dg_i == 0:
                        W("vector", "vm", v_p4vm)
                        W("vector", "ds", v_p4)
                    for di in range(8):
                        dtile = dg_i * 8 + di
                        if first_chunk:
                            O("vector", "tensor_copy", accum[dtile][:, :],
                              self.psum[di][:, :], inc=True)
                        else:
                            O("vector", "tensor_tensor", accum[dtile][:, :],
                              accum[dtile][:, :], self.psum[di][:, :], ALU.add,
                              inc=True)
                        self.bank_r[di] = ("vm", self.semn["vm"])
            if blk + 1 < NCK:
                nts = slice((blk + 1) * 512, (blk + 2) * 512)
                W("sync", "ds", st4[4 * (blk + 1) + 3])
                W("sync", "tm", t_guplast[blk])
                lx = 0
                for kv in range(NKD):
                    lx = DMA("sync", xlnffn[:, kv * 512:(kv + 1) * 512],
                             xlnb_tm[nts, kv * 128:(kv + 1) * 128],
                             transpose=True)
                ldx_next[0] = lx
            for dtile in range(NKD):
                s = dtile % 2
                if dtile >= 2:
                    W("vector", "ds", st_ffo[(blk, dtile - 2)])
                O("vector", "tensor_scalar", aoev[s][:, :], accum[dtile][:, :],
                  P("ds", dtile), None, ALU.mult, inc=True)
                W("sync", "vm", self.semn["vm"])
                st_ffo[(blk, dtile)] = DMA(
                    "sync", ffo_fm[dtile * 128:(dtile + 1) * 128, tsl],
                    aoev[s][:, :])
        v_ffo_all = self.semn["ds"]
        if "ffod" in dbg_aps:
            W("gpsimd", "ds", v_ffo_all)
            DMA("gpsimd", dbg_aps["ffod"], ffo_fm)

        v_o = {}
        st6 = {}
        for tt in range(NTT):
            sl = tt % 2
            if tt == 0:
                W("sync", "ds", v_ffo_all)
            if tt >= 2:
                W("sync", "vm", v_o[tt - 2])
                W("sync", "ds", st6.get(tt - 2, 0))
            ld_fo = DMA("sync", jnkb4[sl][:, :],
                        ffo_fm[0:D, tt * 128:(tt + 1) * 128], transpose=True)
            ld_x1 = DMA("sync", xtm4[sl][:, :], x1_tm[tt * 128:(tt + 1) * 128, :])
            col = tt % 16
            cc = slice(col, col + 1)
            v_s6, _ = tm_stats(jnkb4[sl][:, :], aotm4[:, :], col, ld_fo)
            W("scalar", "ds", ld_x1)
            W("scalar", "vm", v_s6)
            O("scalar", "activation", xtm4[sl][:, :], xtm4[sl][:, :],
              AF.Identity, bias=nmr[:, cc], scale=1.0, inc=True)
            a_xn6 = self.semn["am"]
            W("vector", "am", a_xn6)
            O("vector", "scalar_tensor_tensor", xtm4[sl][:, :], jnkb4[sl][:, :],
              rst[:, cc], xtm4[sl][:, :], ALU.mult, ALU.add, inc=True)
            v_o[tt] = self.semn["vm"]
            W("sync", "vm", v_o[tt])
            st6[tt] = DMA("sync", out_d[tt * 128:(tt + 1) * 128, :],
                          xtm4[sl][:, :])
        W("sync", "ds", self.semn["ds"])

        with nc.Block() as blk_:
            for meth, eng in ((blk_.sync, "sync"), (blk_.gpsimd, "gpsimd"),
                              (blk_.tensor, "tensor"), (blk_.vector, "vector"),
                              (blk_.scalar, "scalar")):
                ops = self.q[eng]

                def body(e, ops=ops):
                    for f in ops:
                        f(e)
                meth(body)
        return nc


def _host_layout(inputs):
    rs = np.ascontiguousarray

    def pack_chan(a):
        return rs(np.asarray(a).reshape(4, 4, 128).transpose(2, 0, 1)
                  .reshape(128, 16))

    W1 = np.asarray(inputs["L0_attn_W1"])
    W2 = np.asarray(inputs["L0_attn_W2"])
    g = {}
    g["winT"] = rs(np.transpose(np.asarray(inputs["L0_attn_Win"]))
                   .reshape(32, 128, 4, 128).transpose(1, 0, 2, 3)
                   .reshape(128, 32 * 512))
    g["w1h"] = rs(np.stack([np.transpose(W1[l]).reshape(4, 128, 2048)
                            .transpose(1, 0, 2).reshape(128, 4 * 2048)
                            for l in range(4)]))
    g["w2h"] = rs(np.stack([np.transpose(W2[l]).reshape(16, 128, 512)
                            .transpose(1, 0, 2).reshape(128, 16 * 512)
                            for l in range(4)]))
    g["woutT"] = rs(np.transpose(np.asarray(inputs["L0_attn_Wout"]))
                    .reshape(4, 128, 4096).transpose(1, 0, 2)
                    .reshape(128, 4 * 4096))
    def low_bytes(a):
        return rs(rs(a).view(np.int8)[..., 0::4])

    for nm, key in [("gateT", "L0_ffn_gate_w"), ("upT", "L0_ffn_up_w")]:
        w = np.asarray(inputs[key])
        g[nm] = low_bytes(w.reshape(NFT, 128, NKD, 128).transpose(0, 3, 2, 1)
                          .reshape(NFT, 128, NKD * 128))
    g["downT"] = low_bytes(np.transpose(np.asarray(inputs["L0_ffn_down_w"]))
                           .reshape(NFT, 128, NKD * 128))
    pp = {"ln1w": pack_chan(inputs["L0_attn_ln1_w"]),
          "ln1b": pack_chan(inputs["L0_attn_ln1_b"]),
          "ln2w": pack_chan(inputs["L0_attn_ln2_w"]),
          "ln2b": pack_chan(inputs["L0_attn_ln2_b"]),
          "scl": pack_chan(inputs["L0_attn_sc_lin"]),
          "b2": pack_chan(inputs["L0_attn_b2"]),
          "pre": pack_chan(inputs["L0_attn_phz_re"]),
          "pim": pack_chan(inputs["L0_attn_phz_im"]),
          "lcr": pack_chan(inputs["L0_attn_lc_re"]),
          "lci": pack_chan(inputs["L0_attn_lc_im"]),
          "b1": rs(np.asarray(inputs["L0_attn_b1"]).reshape(4, 16, 128)
                   .transpose(2, 0, 1).reshape(128, 64)),
          "bin": rs(np.asarray(inputs["L0_attn_bin"]).reshape(4, 128).T),
          "lnlw": rs(np.asarray(inputs["L0_attn_lnl_w"]).reshape(4, 128).T),
          "lnlb": rs(np.asarray(inputs["L0_attn_lnl_b"]).reshape(4, 128).T),
          "bout": rs(np.asarray(inputs["L0_attn_bout"]).reshape(32, 128).T),
          "gs": rs(np.asarray(inputs["L0_ffn_gate_s"]).reshape(NFT, 128).T),
          "us": rs(np.asarray(inputs["L0_ffn_up_s"]).reshape(NFT, 128).T),
          "ds": rs(np.asarray(inputs["L0_ffn_down_s"]).reshape(32, 128).T)}
    for k, v in pp.items():
        g["pp_" + k] = rs(np.asarray(v, np.float32))
    return g


def kernel(**inputs):
    T = 16384 // NCORE
    shared = _host_layout(inputs)
    x = np.asarray(inputs["L0_attn_x"])[0]
    in_maps = []
    for ci in range(NCORE):
        m = dict(shared)
        m["x_in"] = np.ascontiguousarray(x[ci * T:(ci + 1) * T, :], np.float32)
        m["pp_toff"] = np.full((128, 1), float(ci * T), np.float32)
        msk = np.zeros((128, 8), np.float32)
        msk[:, :7] = (np.arange(7)[None, :] < ci)
        msk[:, 7] = 1.0 if ci == 0 else 0.0
        m["pp_mask"] = msk
        in_maps.append(m)
    kb = KB(T)
    nc = kb.build()
    res = run_bass_kernel_spmd(nc, in_maps, core_ids=list(range(NCORE)))
    out = np.concatenate([np.asarray(r["out"]) for r in res.results], axis=0)
    return out[None, :, :].astype(np.float32)


# version 09
# speedup vs baseline: 1.2115x; 1.0107x over previous
import math

import numpy as np
import ml_dtypes

import concourse.bass as bass
import concourse.mybir as mybir
from concourse.bass_utils import run_bass_kernel_spmd

F32 = mybir.dt.float32
BF16 = mybir.dt.bfloat16
I32 = mybir.dt.int32
AF = mybir.ActivationFunctionType
ALU = mybir.AluOpType
BF = ml_dtypes.bfloat16

NCORE = 8
D = 4096
DI = 512
DFI = 2048
FF = 11008
NFT = FF // 128
NKD = D // 128
HH = D // 2
NLAYER = 4
EPS = 1e-5
FCH = 40

_TP = 2 * math.pi
CW_C1 = float(np.float32(np.ldexp(np.round(np.ldexp(_TP, 8)), -8)))
CW_C2 = float(np.float32(np.ldexp(np.round(np.ldexp(_TP - CW_C1, 19)), -19)))
CW_C3 = float(np.float32(_TP - CW_C1 - CW_C2))
MAGIC = float(np.float32(1.5 * 2.0 ** 23))
INV2PI = float(np.float32(1.0 / _TP))
PI = float(np.float32(math.pi))
PI_2 = float(np.float32(math.pi / 2))

_DEBUG = []


class KB:
    def __init__(self, T):
        self.T = T
        self.NCK = T // 512
        self.NTT = T // 128
        nc = bass.Bass("TRN2", target_bir_lowering=False)
        self.nc = nc
        self.q = {e: [] for e in ("sync", "gpsimd", "tensor", "vector", "scalar")}
        self.semh, self.semn = {}, {}
        for s in ("dg", "ds", "tm", "vm", "am", "cc"):
            self.semh[s] = nc.alloc_semaphore("sem_" + s)
            self.semn[s] = 0
        self.bank_w = [("tm", 0)] * 8
        self.bank_r = [("tm", 0)] * 8
        self.pc_items = []
        self.pc_mark = {}
        self.dumped_stats = False

    def emit(self, eng, fn):
        self.q[eng].append(fn)

    def W(self, eng, sem, val):
        if val:
            self.emit(eng, lambda e, h=self.semh[sem], v=val: e.wait_ge(h, v))

    def bump(self, sem, n):
        self.semn[sem] += n
        return self.semn[sem]

    def DMA(self, eng, dst, src, transpose=False):
        sem = {"sync": "ds", "gpsimd": "dg"}[eng]
        self.bump(sem, 16)
        self.emit(eng, lambda e, d=dst, s=src, h=self.semh[sem], t=transpose:
                  e.dma_start(d, s, transpose=t).then_inc(h, 16))
        return self.semn[sem]

    def O(self, eng, meth, *args, inc=False, drain=False, **kw):
        sem = {"tensor": "tm", "vector": "vm", "scalar": "am", "gpsimd": "vm"}[eng]
        if inc:
            self.bump(sem, 1)
        def f(e, meth=meth, args=args, kw=kw, inc=inc, drain=drain,
              h=self.semh[sem]):
            if drain:
                e.drain()
            r = getattr(e, meth)(*args, **kw)
            if inc:
                r.then_inc(h, 1)
        self.emit(eng, f)
        return self.semn[sem]

    def mm_group(self, bank, pairs, waits=()):
        s, v = self.bank_r[bank]
        self.W("tensor", s, v)
        for (ss, vv) in waits:
            self.W("tensor", ss, vv)
        n = len(pairs)
        for i, (l, r) in enumerate(pairs):
            self.O("tensor", "matmul", self.psum[bank][:, :], l, r,
                   start=(i == 0), stop=(i == n - 1), inc=(i == n - 1))
        self.bank_w[bank] = ("tm", self.semn["tm"])
        return self.semn["tm"]

    def evac(self, bank, eng, emit_cb, waits=()):
        s, v = self.bank_w[bank]
        self.W(eng, s, v)
        for (ss, vv) in waits:
            self.W(eng, ss, vv)
        emit_cb()
        sem = {"vector": "vm", "scalar": "am"}[eng]
        self.bank_r[bank] = (sem, self.semn[sem])
        return self.semn[sem]

    def build(self):
        nc, T, NCK, NTT = self.nc, self.T, self.NCK, self.NTT
        W, O, DMA, E = self.W, self.O, self.DMA, self.emit
        dt = nc.dram_tensor

        x_in = dt("x_in", [T, D], F32, kind="ExternalInput").ap()
        winT = dt("winT", [128, NKD * DI], F32, kind="ExternalInput").ap()
        w1h = dt("w1h", [NLAYER, 128, 4 * DFI], F32, kind="ExternalInput").ap()
        w2h = dt("w2h", [NLAYER, 128, 16 * DI], F32, kind="ExternalInput").ap()
        woutT = dt("woutT", [128, 4 * D], F32, kind="ExternalInput").ap()
        gateT = dt("gateT", [NFT, 128, D], mybir.dt.int8, kind="ExternalInput").ap()
        upT = dt("upT", [NFT, 128, D], mybir.dt.int8, kind="ExternalInput").ap()
        downT = dt("downT", [NFT, 128, D], mybir.dt.int8, kind="ExternalInput").ap()
        PCOLS = [("ln1w", 16), ("ln1b", 16), ("ln2w", 16), ("ln2b", 16),
                 ("scl", 16), ("b2", 16), ("pre", 16), ("pim", 16),
                 ("lcr", 16), ("lci", 16), ("b1", 64), ("bin", 4),
                 ("lnlw", 4), ("lnlb", 4), ("bout", 32), ("gs", NFT),
                 ("us", NFT), ("ds", 32), ("toff", 1), ("mask", 8)]
        pin = {nm: dt("pp_" + nm, [128, n], F32, kind="ExternalInput").ap()
               for nm, n in PCOLS}
        out_d = dt("out", [T, D], F32, kind="ExternalOutput").ap()
        dbg_aps = {nm: dt("dbg_" + nm, list(sh), F32, kind="ExternalOutput").ap()
                   for nm, sh in _DEBUG}

        xlna_tm = dt("xlna_tm", [T, D], BF16, kind="Internal").ap()
        ao_fm = dt("ao_fm", [D, T], BF16, kind="Internal").ap()
        x1_tm = dt("x1_tm", [T, D], F32, kind="Internal").ap()
        xlnb_tm = dt("xlnb_tm", [T, D], BF16, kind="Internal").ap()
        ffo_fm = dt("ffo_fm", [D, T], BF16, kind="Internal").ap()
        ncc = NLAYER
        cc_i = [dt(f"cc_i{k}", [128, 8], F32, kind="Internal").ap() for k in range(ncc)]
        cc_o = [dt(f"cc_o{k}", [NCORE * 128, 8], F32, kind="Internal").ap()
                for k in range(ncc)]

        pos = [16384]

        def place(name, shape, dtype):
            nb = int(np.prod(shape[1:])) * (4 if dtype != BF16 else 2)
            at = pos[0]
            pos[0] = at + ((nb + 63) // 64) * 64
            return nc.alloc_sbuf_tensor_at(name, list(shape), dtype, offset=at), at

        def over(name, shape, dtype, at):
            return nc.alloc_sbuf_tensor_at(name, list(shape), dtype, offset=at)

        TB = T * 4
        h_t = []
        H0 = pos[0]
        for c in range(4):
            h_t.append(place(f"h{c}", [128, T], F32)[0])
        tglob, TG0 = place("tglob", [128, T], F32)
        params, _ = place("params", [128, 1024], F32)
        statsf, _ = place("statsf", [128, 64], F32)
        var_scr, VS0 = place("var_scr", [1, T], F32)
        ccg, _ = place("ccg", [128, 64], F32)
        czt, _ = place("czt", [128, 8], F32)
        dpk2, _ = place("dpk2", [128, 8], F32)
        ccd, _ = place("ccd", [128, 64], F32)
        ones_bf, _ = place("ones_bf", [128, 1], BF16)
        onesr_f, _ = place("onesr_f", [1, 128], F32)
        nonesr_f, _ = place("nonesr_f", [1, 128], F32)
        c_pi2, _ = place("c_pi2", [128, 1], F32)
        c_eps, _ = place("c_eps", [128, 1], F32)
        c_zero, _ = place("c_zero", [128, 1], F32)
        state_pk, _ = place("state_pk", [128, 8], F32)
        STG0 = pos[0]
        stg = [place(f"stg{s}", [128, D], BF16)[0] for s in range(2)]
        aoev = [place(f"aoev{s}", [128, 512], BF16)[0] for s in range(2)]

        SCAN0 = pos[0]
        scn = [place(f"scn{i}", [128, T], F32)[0] for i in range(4)]
        SC_COS = [scn[0], scn[2]]
        SC_SIN = [scn[1], scn[3]]
        if pos[0] - SCAN0 < 32768:
            pos[0] = SCAN0 + 32768
        WA0 = pos[0]
        w1sb, _ = place("w1sb", [128, 4 * DFI], BF16)
        w2sb, _ = place("w2sb", [128, 16 * DI], BF16)
        MM0 = pos[0]
        yb = [place(f"yb{c}", [128, T], BF16)[0] for c in range(4)]
        hb = [place(f"hb{s}", [128, T], BF16)[0] for s in range(2)]
        ffn1, _ = place("ffn1", [128, 16 * 512], BF16)
        sc1, SC1_0 = place("sc1", [128, T], F32)
        sc2, SC2_0 = place("sc2", [128, T], F32)
        if pos[0] - MM0 < 49152:
            pos[0] = MM0 + 49152
        END0 = pos[0]
        assert END0 <= 229376, f"sbuf static overflow: {END0}"
        SCAN_B = WA0 - SCAN0
        MM_B = END0 - MM0

        xtm = [over(f"xtm{i}", [128, D], F32, SCAN0 + i * 16384) for i in range(2)]
        xtm4 = [over(f"xtm4_{i}", [128, D], F32, H0 + i * 16384) for i in range(2)]
        aotm4 = over("aotm4", [128, D], BF16, TG0)
        jnkb4 = [over(f"jnkb4_{i}", [128, D], BF16, STG0 + i * 8192)
                 for i in range(2)]
        assert SCAN_B >= 2 * 16384
        aotm = over("aotm", [128, D], BF16, MM0)
        jnkb = over("jnkb", [128, D], BF16, MM0 + 8192)
        winsb = over("winsb", [128, NKD * DI], BF16, MM0 + 16384)
        xlnw = over("xlnw", [128, NKD * 512], BF16, SCAN0)
        woutsb = over("woutsb", [128, 4 * D], BF16, SCAN0)
        xlnffn = over("xlnffn", [128, NKD * 512], BF16, SCAN0)
        gusb = [over(f"gusb{s}", [128, D], BF16, WA0 + s * 8192) for s in range(4)]
        assert WA0 + 4 * 8192 <= END0 + 4096
        if T * 4 >= 8192:
            dsb = [over(f"dsb{s}", [128, 1024], BF16, VS0 + s * 2048)
                   for s in range(4)]
        else:
            dsb = [place(f"dsb{s}", [128, 1024], BF16)[0] for s in range(4)]
        hch = over("hch", [128, FCH * 512], BF16, MM0)
        if T == 2048:
            acc_off = []
            for i in range(5 * TB // 2048):
                acc_off.append(H0 + i * 2048)
            need = NKD - len(acc_off)
            for i in range(need):
                acc_off.append(MM0 + FCH * 1024 + i * 2048)
            assert MM0 + FCH * 1024 + need * 2048 <= 229376
            accum = [over(f"acc{i}", [128, 512], F32, acc_off[i])
                     for i in range(NKD)]
            pos[0] = max(pos[0], MM0 + FCH * 1024 + need * 2048)
        else:
            acc_off = [H0 + i * 2048 for i in range(5 * TB // 2048)]
            acc_off += [MM0 + FCH * 1024 + i * 2048
                        for i in range((49152 - FCH * 1024) // 2048)]
            accum = [over(f"acc{i}", [128, 512], F32, acc_off[i])
                     if i < len(acc_off)
                     else place(f"acc{i}", [128, 512], F32)[0]
                     for i in range(NKD)]
            assert pos[0] <= 229376, pos[0]
        gsil = [place(f"gsil{s}", [128, 512], BF16)[0] for s in range(2)]
        assert pos[0] <= 229376, pos[0]

        mu_row = over("mu_row", [1, T], F32, SC1_0)
        sq_row = over("sq_row", [1, T], F32, SC2_0)
        self.psum = [nc.alloc_psum_tensor(f"ps{i}", [128, 512], F32)
                     for i in range(8)]

        PC = {}
        pcn = [0]
        for nm, n in PCOLS + [("rho", 16), ("lnrho", 16), ("theta", 16),
                              ("cth", 16), ("sth", 16), ("ainit", 16),
                              ("binit", 16), ("dpk", 16), ("pb", 16),
                              ("t1", 16), ("t2", 16), ("t3", 16), ("t4", 16),
                              ("u1", 1)]:
            PC[nm] = pcn[0]
            pcn[0] += n
        assert pcn[0] <= 1024

        def P(nm, i=0, n=1):
            c = PC[nm] + i
            return params[:, c:c + n]

        def A16(nm):
            return P(nm, 0, 16)

        for nm, n in PCOLS:
            DMA("sync", P(nm, 0, n), pin[nm])
        v_pld = self.semn["ds"]

        O("vector", "memset", c_pi2[:, :], PI_2)
        O("vector", "memset", c_eps[:, :], float(EPS))
        O("vector", "memset", c_zero[:, :], 0.0)
        O("vector", "memset", ones_bf[:, :], 1.0 / DI)
        O("vector", "memset", onesr_f[:, :], 1.0)
        O("vector", "memset", nonesr_f[:, :], -1.0, inc=True)
        O("gpsimd", "iota", tglob[:, :], [[1, T]], base=0, channel_multiplier=0,
          allow_small_or_imprecise_dtypes=True, inc=True)
        W("vector", "vm", self.semn["vm"])
        W("vector", "ds", v_pld)
        O("vector", "tensor_scalar", tglob[:, :], tglob[:, :], P("toff"), None,
          ALU.add, ALU.bypass, inc=True)

        W("scalar", "ds", v_pld)
        W("scalar", "vm", self.semn["vm"])
        O("scalar", "activation", A16("t1"), A16("pre"), AF.Square, bias=c_zero[:, 0:1])
        O("scalar", "activation", A16("t2"), A16("pim"), AF.Square, drain=True,
          inc=True, bias=c_zero[:, 0:1])
        W("vector", "am", self.semn["am"])
        O("vector", "tensor_tensor", A16("t1"), A16("t1"), A16("t2"), ALU.add,
          drain=True, inc=True)
        W("scalar", "vm", self.semn["vm"])
        O("scalar", "activation", A16("t2"), A16("t1"), AF.Sqrt, drain=True,
          inc=True, bias=c_zero[:, 0:1])
        W("vector", "am", self.semn["am"])
        O("vector", "reciprocal", A16("t3"), A16("t2"), drain=True)
        O("vector", "tensor_tensor", A16("t3"), A16("t3"), A16("t1"), ALU.mult,
          drain=True)
        O("vector", "tensor_tensor", A16("t3"), A16("t3"), A16("t2"), ALU.add,
          drain=True)
        O("vector", "tensor_scalar", A16("t3"), A16("t3"), 0.5, None, ALU.mult,
          drain=True, inc=True)
        W("scalar", "vm", self.semn["vm"])
        O("scalar", "activation", A16("rho"), A16("t3"), AF.Tanh, drain=True,
          inc=True, bias=c_zero[:, 0:1])
        W("vector", "am", self.semn["am"])
        O("vector", "tensor_scalar", A16("t4"), A16("rho"), 1e-36, None,
          ALU.max, drain=True, inc=True)
        W("scalar", "vm", self.semn["vm"])
        O("scalar", "activation", A16("lnrho"), A16("t4"), AF.Ln, drain=True, bias=c_zero[:, 0:1])
        O("scalar", "activation", A16("dpk"), A16("lnrho"), AF.Exp,
          scale=float(T), drain=True, bias=c_zero[:, 0:1])
        O("scalar", "activation", A16("t1"), A16("pre"), AF.Abs, drain=True, bias=c_zero[:, 0:1])
        O("scalar", "activation", A16("t2"), A16("pim"), AF.Abs, drain=True,
          inc=True, bias=c_zero[:, 0:1])
        W("vector", "am", self.semn["am"])
        O("vector", "tensor_tensor", A16("t3"), A16("t2"), A16("t1"), ALU.is_gt,
          drain=True)
        O("vector", "tensor_tensor", A16("t4"), A16("t1"), A16("t2"), ALU.min,
          drain=True)
        O("vector", "tensor_tensor", A16("t2"), A16("t1"), A16("t2"), ALU.max,
          drain=True)
        O("vector", "tensor_scalar", A16("t2"), A16("t2"), 1e-30, None, ALU.max,
          drain=True)
        O("vector", "reciprocal", A16("t1"), A16("t2"), drain=True)
        O("vector", "tensor_tensor", A16("t1"), A16("t1"), A16("t4"), ALU.mult,
          drain=True, inc=True)
        W("scalar", "vm", self.semn["vm"])
        O("scalar", "activation", A16("t4"), A16("t1"), AF.Arctan, drain=True,
          inc=True, bias=c_zero[:, 0:1])
        W("vector", "am", self.semn["am"])
        O("vector", "tensor_scalar", A16("t1"), A16("t4"), -2.0, PI_2, ALU.mult,
          ALU.add, drain=True)
        O("vector", "tensor_tensor", A16("t1"), A16("t1"), A16("t3"), ALU.mult,
          drain=True)
        O("vector", "tensor_tensor", A16("t4"), A16("t4"), A16("t1"), ALU.add,
          drain=True)
        O("vector", "tensor_scalar", A16("t3"), A16("pre"), 0.0, None, ALU.is_lt,
          drain=True)
        O("vector", "tensor_scalar", A16("t1"), A16("t4"), -2.0, PI, ALU.mult,
          ALU.add, drain=True)
        O("vector", "tensor_tensor", A16("t1"), A16("t1"), A16("t3"), ALU.mult,
          drain=True)
        O("vector", "tensor_tensor", A16("t4"), A16("t4"), A16("t1"), ALU.add,
          drain=True)
        O("vector", "tensor_scalar", A16("t1"), A16("pim"), 0.0, None, ALU.is_gt,
          drain=True)
        O("vector", "tensor_scalar", A16("t2"), A16("pim"), 0.0, None, ALU.is_lt,
          drain=True)
        O("vector", "tensor_tensor", A16("t1"), A16("t1"), A16("t2"),
          ALU.subtract, drain=True)
        O("vector", "tensor_tensor", A16("theta"), A16("t4"), A16("t1"),
          ALU.mult, drain=True, inc=True)
        W("scalar", "vm", self.semn["vm"])
        O("scalar", "activation", A16("sth"), A16("theta"), AF.Sin, drain=True, bias=c_zero[:, 0:1])
        O("scalar", "activation", A16("t1"), A16("theta"), AF.Abs, drain=True, bias=c_zero[:, 0:1])
        O("scalar", "activation", A16("cth"), A16("t1"), AF.Sin, scale=-1.0,
          bias=c_pi2[:, 0:1], drain=True, inc=True)
        W("vector", "am", self.semn["am"])
        O("vector", "tensor_tensor", A16("t1"), A16("cth"), A16("lcr"), ALU.mult,
          drain=True)
        O("vector", "tensor_tensor", A16("t2"), A16("sth"), A16("lci"), ALU.mult,
          drain=True)
        O("vector", "tensor_tensor", A16("ainit"), A16("t1"), A16("t2"),
          ALU.subtract, drain=True)
        O("vector", "tensor_tensor", A16("t1"), A16("sth"), A16("lcr"), ALU.mult,
          drain=True)
        O("vector", "tensor_tensor", A16("t2"), A16("cth"), A16("lci"), ALU.mult,
          drain=True)
        O("vector", "tensor_tensor", A16("binit"), A16("t1"), A16("t2"), ALU.add,
          drain=True)
        O("vector", "tensor_scalar", A16("binit"), A16("binit"), -1.0, None,
          ALU.mult, drain=True)
        O("vector", "tensor_scalar", A16("ainit"), A16("ainit"), P("mask", 7),
          None, ALU.mult, drain=True)
        O("vector", "tensor_scalar", A16("binit"), A16("binit"), P("mask", 7),
          None, ALU.mult, drain=True)
        O("vector", "tensor_scalar", P("u1"), P("toff"), -1.0, 1.0, ALU.mult,
          ALU.add, drain=True)
        O("vector", "tensor_scalar", A16("pb"), A16("lnrho"), P("u1"), None,
          ALU.mult, drain=True)
        O("vector", "engine_nop", inc=True)
        v_p0 = self.semn["vm"]
        if "theta" in dbg_aps:
            W("sync", "vm", v_p0)
            DMA("sync", dbg_aps["theta"], A16("theta"))
            DMA("sync", dbg_aps["rho"], A16("rho"))
            DMA("sync", dbg_aps["ainit"], A16("ainit"))
            DMA("sync", dbg_aps["lnrho"], A16("lnrho"))
            DMA("sync", dbg_aps["tg"], tglob[:, :])

        g_win = DMA("gpsimd", winsb[:, :], winT)
        g_w1 = {0: DMA("gpsimd", w1sb[:, :], w1h[0])}
        g_w2 = {0: DMA("gpsimd", w2sb[:, :], w2h[0])}

        def drain_pc(n):
            pass

        musum, sqsum = statsf[:, 0:16], statsf[:, 16:32]
        nmr, rst = statsf[:, 32:48], statsf[:, 48:64]

        def tm_stats(src_ap, junk_ap, col, ld, vwaits=()):
            c = slice(col, col + 1)
            W("vector", "ds", ld)
            for (ss, vv) in vwaits:
                W("vector", ss, vv)
            O("vector", "tensor_scalar", junk_ap, src_ap, 1.0, 0.0, ALU.mult,
              ALU.add, accum_out=musum[:, c], inc=True)
            W("scalar", "ds", ld)
            for (ss, vv) in vwaits:
                W("scalar", ss, vv)
            O("scalar", "activation", junk_ap, src_ap, AF.Square,
              accum_out=sqsum[:, c], inc=True, bias=c_zero[:, 0:1])
            a_sq = self.semn["am"]
            W("vector", "am", a_sq)
            O("vector", "tensor_scalar", musum[:, c], musum[:, c], 1.0 / D, None,
              ALU.mult, drain=True)
            O("vector", "tensor_scalar", sqsum[:, c], sqsum[:, c], 1.0 / D, None,
              ALU.mult, drain=True)
            O("vector", "tensor_tensor", rst[:, c], musum[:, c], musum[:, c],
              ALU.mult, drain=True)
            O("vector", "tensor_tensor", rst[:, c], sqsum[:, c], rst[:, c],
              ALU.subtract, drain=True, inc=True)
            W("scalar", "vm", self.semn["vm"])
            O("scalar", "activation", rst[:, c], rst[:, c], AF.Ln,
              bias=c_eps[:, 0:1], drain=True)
            O("scalar", "activation", rst[:, c], rst[:, c], AF.Exp, scale=-0.5,
              drain=True, inc=True, bias=c_zero[:, 0:1])
            a_r = self.semn["am"]
            W("vector", "am", a_r)
            O("vector", "tensor_tensor", nmr[:, c], musum[:, c], rst[:, c],
              ALU.mult, drain=True)
            O("vector", "tensor_scalar", nmr[:, c], nmr[:, c], -1.0, None,
              ALU.mult, drain=True, inc=True)
            return self.semn["vm"], a_r

        v_used, st_done = {}, {}
        a_used = {}
        for tt in range(NTT):
            sl = tt % 2
            if tt >= 2:
                W("sync", "vm", v_used[tt - 2])
                W("sync", "am", a_used[tt - 2])
            ld = DMA("sync", xtm[sl][:, :], x_in[tt * 128:(tt + 1) * 128, :])
            if tt == 0:
                W("vector", "ds", ld)
                W("vector", "vm", v_p0)
            col = tt % 16
            v_s, a_s = tm_stats(xtm[sl][:, :], jnkb[:, :], col, ld)
            a_used[tt] = a_s
            if tt >= 1:
                W("vector", "ds", st_done[tt - 1])
            cc = slice(col, col + 1)
            O("vector", "tensor_scalar", aotm[:, :], xtm[sl][:, :], rst[:, cc],
              nmr[:, cc], ALU.mult, ALU.add, inc=True)
            v_x = self.semn["vm"]
            v_used[tt] = v_x
            W("sync", "vm", v_x)
            st_done[tt] = DMA("sync", xlna_tm[tt * 128:(tt + 1) * 128, :],
                              aotm[:, :])
            if tt % 4 == 3:
                drain_pc(4)

        xb_mm = {}
        a_p2 = 0
        for k in range(NCK):
            W("sync", "dg", g_win)
            W("sync", "ds", st_done[4 * k + 3])
            if k >= 1:
                W("sync", "tm", xb_mm[k - 1])
            ldv = 0
            for kv in range(NKD):
                ldv = DMA("sync", xlnw[:, kv * 512:(kv + 1) * 512],
                          xlna_tm[k * 512:(k + 1) * 512,
                                  kv * 128:(kv + 1) * 128], transpose=True)
            for dtile in range(4):
                bank = dtile
                pairs = [(winsb[:, kv * 512 + dtile * 128:
                                kv * 512 + dtile * 128 + 128],
                          xlnw[:, kv * 512:(kv + 1) * 512]) for kv in range(NKD)]
                self.mm_group(bank, pairs, waits=[("ds", ldv)])
                a_p2 = self.evac(bank, "scalar", lambda b=bank, d=dtile, k=k:
                                 O("scalar", "activation",
                                   h_t[d][:, k * 512:(k + 1) * 512],
                                   self.psum[b][:, :], AF.Identity,
                                   bias=P("bin", d), inc=True))
            xb_mm[k] = self.semn["tm"]
            drain_pc(6)
        if "h_win" in dbg_aps:
            W("sync", "am", a_p2)
            for c in range(4):
                DMA("sync", dbg_aps["h_win"][c * 128:(c + 1) * 128, :],
                    h_t[c][:, :])

        def inner_ln_stats(extra_waits):
            t_c = {}
            for c in range(4):
                sl = c % 2
                if c == 0:
                    for (ss, vv) in extra_waits:
                        W("vector", ss, vv)
                if c >= 2:
                    W("vector", "tm", t_c[c - 2])
                O("vector", "tensor_copy", hb[sl][:, :], h_t[c][:, :], inc=True)
                v_cast = self.semn["vm"]
                W("scalar", "vm", v_cast)
                if c == 0:
                    for (ss, vv) in extra_waits:
                        W("scalar", ss, vv)
                else:
                    W("scalar", "tm", t_c[c - 1])
                O("scalar", "activation", ffn1[:, 0:T], hb[sl][:, :], AF.Square,
                  inc=True, bias=c_zero[:, 0:1])
                a_sqc = self.semn["am"]
                W("tensor", "vm", v_cast)
                W("tensor", "am", a_sqc)
                if c == 0:
                    for b in range(8):
                        s, v = self.bank_r[b]
                        W("tensor", s, v)
                for k in range(NCK):
                    O("tensor", "matmul", self.psum[k][0:1, :], ones_bf[:, :],
                      hb[sl][:, k * 512:(k + 1) * 512],
                      start=(c == 0), stop=(c == 3))
                for k in range(NCK):
                    O("tensor", "matmul", self.psum[4 + k][0:1, :],
                      ones_bf[:, :], ffn1[:, k * 512:(k + 1) * 512],
                      start=(c == 0), stop=(c == 3),
                      inc=(k == NCK - 1))
                t_c[c] = self.semn["tm"]
            t_stats = self.semn["tm"]
            for b in range(8):
                self.bank_w[b] = ("tm", t_stats)
            W("vector", "tm", t_stats)
            for k in range(NCK):
                O("vector", "tensor_copy", mu_row[0:1, k * 512:(k + 1) * 512],
                  self.psum[k][0:1, :])
                O("vector", "tensor_copy", sq_row[0:1, k * 512:(k + 1) * 512],
                  self.psum[4 + k][0:1, :])
            O("vector", "engine_nop", inc=True)
            v_ev = self.semn["vm"]
            for b in range(8):
                self.bank_r[b] = ("vm", v_ev)
            O("vector", "tensor_tensor", var_scr[0:1, 0:T], mu_row[0:1, 0:T],
              mu_row[0:1, 0:T], ALU.mult)
            O("vector", "tensor_tensor", sq_row[0:1, 0:T], sq_row[0:1, 0:T],
              var_scr[0:1, 0:T], ALU.subtract, inc=True)
            W("scalar", "vm", self.semn["vm"])
            O("scalar", "activation", sq_row[0:1, 0:T], sq_row[0:1, 0:T], AF.Ln,
              bias=c_eps[0:1, 0:1])
            O("scalar", "activation", sq_row[0:1, 0:T], sq_row[0:1, 0:T], AF.Exp,
              scale=-0.5, inc=True, bias=c_zero[0:1, 0:1])
            W("vector", "am", self.semn["am"])
            O("vector", "tensor_tensor", mu_row[0:1, 0:T], mu_row[0:1, 0:T],
              sq_row[0:1, 0:T], ALU.mult, inc=True)
            v_rr = self.semn["vm"]
            if "murow" in dbg_aps and not self.dumped_stats:
                self.dumped_stats = True
                W("sync", "vm", v_rr)
                dmp = DMA("sync", dbg_aps["murow"], mu_row[0:1, 0:T])
                dmp = DMA("sync", dbg_aps["sqrow"], sq_row[0:1, 0:T])
                W("vector", "ds", dmp)
            for k in range(NCK):
                self.mm_group(k, [(nonesr_f[:, :],
                                   mu_row[0:1, k * 512:(k + 1) * 512])],
                              waits=[("vm", v_rr)])
                self.mm_group(4 + k, [(onesr_f[:, :],
                                       sq_row[0:1, k * 512:(k + 1) * 512])])

        def normalize_to_sc1(c):
            for k in range(NCK):
                for b in (k, 4 + k):
                    s, v = self.bank_w[b]
                    W("vector", s, v)
                sl5 = slice(k * 512, (k + 1) * 512)
                O("vector", "tensor_tensor", sc1[:, sl5], h_t[c][:, sl5],
                  self.psum[4 + k][:, :], ALU.mult)
                O("vector", "tensor_tensor", sc1[:, sl5], sc1[:, sl5],
                  self.psum[k][:, :], ALU.add, inc=(k == NCK - 1))
            v_n = self.semn["vm"]
            for k in range(NCK):
                self.bank_r[k] = ("vm", v_n)
                self.bank_r[4 + k] = ("vm", v_n)
            return v_n

        def corr_combine(g_rd):
            W("vector", "dg", g_rd)
            O("vector", "memset", czt[:, :], 0.0)
            for j in range(7):
                O("vector", "tensor_tensor", czt[:, :], czt[:, :],
                  ccd[:, j * 8:j * 8 + 8], ALU.mult, drain=True)
                O("vector", "scalar_tensor_tensor", czt[:, :],
                  ccg[:, j * 8:j * 8 + 8], P("mask", j), czt[:, :],
                  ALU.mult, ALU.add, drain=True)
            O("vector", "engine_nop", drain=True, inc=True)
            return self.semn["vm"]

        def apply_corr(l, c, slot, recompute):
            dcol = l * 4 + c
            cosb, sinb = SC_COS[slot][:, :], SC_SIN[slot][:, :]
            if recompute:
                O("vector", "tensor_scalar", cosb, tglob[:, :],
                  P("theta", dcol), None, ALU.mult)
                O("vector", "tensor_scalar", sinb, cosb, INV2PI, MAGIC,
                  ALU.mult, ALU.add)
                O("vector", "tensor_scalar", sinb, sinb, -MAGIC, None, ALU.add)
                O("vector", "scalar_tensor_tensor", cosb, sinb, -CW_C1, cosb,
                  ALU.mult, ALU.add)
                O("vector", "scalar_tensor_tensor", cosb, sinb, -CW_C2, cosb,
                  ALU.mult, ALU.add)
                O("vector", "scalar_tensor_tensor", cosb, sinb, -CW_C3, cosb,
                  ALU.mult, ALU.add)
                O("vector", "tensor_scalar", cosb, cosb, PI, -PI, ALU.min,
                  ALU.max, inc=True)
                W("scalar", "vm", self.semn["vm"])
                O("scalar", "activation", sinb, cosb, AF.Sin,
                  bias=c_zero[:, 0:1])
                O("scalar", "activation", sc2[:, :], cosb, AF.Abs,
                  bias=c_zero[:, 0:1])
                O("scalar", "activation", cosb, sc2[:, :], AF.Sin, scale=-1.0,
                  bias=c_pi2[:, 0:1], inc=True)
                W("vector", "am", self.semn["am"])
            W("scalar", "vm", self.semn["vm"])
            O("scalar", "activation", sc2[:, :], tglob[:, :], AF.Exp,
              scale=P("lnrho", dcol), bias=P("pb", dcol), inc=True)
            a_rp = self.semn["am"]
            W("vector", "am", a_rp)
            O("vector", "tensor_scalar", sc1[:, :], cosb,
              czt[:, 2 * c:2 * c + 1], None, ALU.mult)
            O("vector", "scalar_tensor_tensor", sc1[:, :], sinb,
              czt[:, 2 * c + 1:2 * c + 2], sc1[:, :], ALU.mult, ALU.add)
            O("vector", "tensor_tensor", sc1[:, :], sc1[:, :], sc2[:, :],
              ALU.mult)
            O("vector", "scalar_tensor_tensor", h_t[c][:, :], sc1[:, :],
              P("scl", dcol), h_t[c][:, :], ALU.mult, ALU.add, inc=True)
            return self.semn["vm"]

        t_w2_prev = [xb_mm[NCK - 1]]
        a_prev = [a_p2]
        for l in range(NLAYER):
            inner_ln_stats([("am", a_prev[0]), ("tm", t_w2_prev[0])])
            W("vector", "vm", v_p0)
            for c4 in range(4):
                O("vector", "tensor_copy", dpk2[:, 2 * c4:2 * c4 + 1],
                  P("dpk", l * 4 + c4), drain=True)
                O("vector", "tensor_copy", dpk2[:, 2 * c4 + 1:2 * c4 + 2],
                  P("dpk", l * 4 + c4), drain=True)
            for j in range(7):
                O("vector", "tensor_scalar", ccd[:, j * 8:j * 8 + 8],
                  dpk2[:, :], P("mask", j), None, ALU.mult, drain=True)
                O("vector", "tensor_scalar", P("u1"), P("mask", j), -1.0, 1.0,
                  ALU.mult, ALU.add, drain=True)
                O("vector", "tensor_scalar", ccd[:, j * 8:j * 8 + 8],
                  ccd[:, j * 8:j * 8 + 8], P("u1"), None, ALU.add, drain=True)
            O("vector", "engine_nop", inc=True)

            for c in range(4):
                slot = c % 2
                dcol = l * 4 + c
                v_n = normalize_to_sc1(c)
                W("scalar", "vm", v_n)
                O("scalar", "activation", sc1[:, :], sc1[:, :], AF.Silu,
                  scale=P("ln1w", dcol), bias=P("ln1b", dcol), inc=True)
                a_yt = self.semn["am"]
                cosb = SC_COS[slot][:, :]
                sinb = SC_SIN[slot][:, :]
                W("vector", "am", a_yt)
                O("vector", "tensor_scalar", cosb, tglob[:, :],
                  P("theta", dcol), None, ALU.mult, inc=True)
                if l == 0 and c == 0 and "v00" in dbg_aps:
                    W("sync", "vm", self.semn["vm"])
                    dmp = DMA("sync", dbg_aps["v00"], cosb)
                    W("vector", "ds", dmp)
                O("vector", "tensor_scalar", sinb, cosb, INV2PI, MAGIC,
                  ALU.mult, ALU.add)
                O("vector", "tensor_scalar", sinb, sinb, -MAGIC, None, ALU.add)
                O("vector", "scalar_tensor_tensor", cosb, sinb, -CW_C1, cosb,
                  ALU.mult, ALU.add)
                O("vector", "scalar_tensor_tensor", cosb, sinb, -CW_C2, cosb,
                  ALU.mult, ALU.add)
                O("vector", "scalar_tensor_tensor", cosb, sinb, -CW_C3, cosb,
                  ALU.mult, ALU.add)
                O("vector", "tensor_scalar", cosb, cosb, PI, -PI, ALU.min,
                  ALU.max, inc=True)
                if l == 0 and c == 0 and "r00" in dbg_aps:
                    W("sync", "vm", self.semn["vm"])
                    dmp = DMA("sync", dbg_aps["r00"], cosb)
                    W("scalar", "ds", dmp)
                W("scalar", "vm", self.semn["vm"])
                O("scalar", "activation", sinb, cosb, AF.Sin, bias=c_zero[:, 0:1])
                O("scalar", "activation", sc2[:, :], cosb, AF.Abs, bias=c_zero[:, 0:1])
                O("scalar", "activation", cosb, sc2[:, :], AF.Sin, scale=-1.0,
                  bias=c_pi2[:, 0:1], inc=True)
                a_cs = self.semn["am"]
                if l == 0 and c == 0 and "yt00" in dbg_aps:
                    W("sync", "am", a_yt)
                    dmp = DMA("sync", dbg_aps["yt00"], sc1[:, :])
                    W("vector", "ds", dmp)
                if l == 0 and c == 0 and "cos00" in dbg_aps:
                    W("sync", "am", a_cs)
                    dmp = DMA("sync", dbg_aps["cos00"], cosb)
                    dmp = DMA("sync", dbg_aps["sin00"], sinb)
                    W("vector", "ds", dmp)
                    W("scalar", "ds", dmp)
                W("vector", "am", a_cs)
                rho_b = P("rho", dcol).broadcast_to((128, T))
                O("vector", "tensor_tensor", sc2[:, :], cosb, sc1[:, :],
                  ALU.mult)
                O("vector", "tensor_tensor_scan", sc2[:, :], rho_b, sc2[:, :],
                  P("ainit", dcol), ALU.mult, ALU.add)
                O("vector", "tensor_tensor", sc1[:, :], sinb, sc1[:, :],
                  ALU.mult)
                O("vector", "tensor_tensor_scan", sc1[:, :], rho_b, sc1[:, :],
                  P("binit", dcol), ALU.mult, ALU.add, inc=True)
                O("vector", "drain")
                if c == 0:
                    W("vector", "dg", getattr(self, "last_g_st", 0))
                O("vector", "tensor_copy", state_pk[:, 2 * c:2 * c + 1],
                  sc2[:, T - 1:T])
                O("vector", "tensor_copy", state_pk[:, 2 * c + 1:2 * c + 2],
                  sc1[:, T - 1:T])
                O("vector", "engine_nop", drain=True, inc=True)
                v_st = self.semn["vm"]
                if l == 0 and c == 0 and "A00" in dbg_aps:
                    W("sync", "vm", v_st)
                    dmp = DMA("sync", dbg_aps["A00"], sc2[:, :])
                    dmp = DMA("sync", dbg_aps["B00"], sc1[:, :])
                    W("vector", "ds", dmp)
                O("vector", "tensor_tensor", sc2[:, :], cosb, sc2[:, :],
                  ALU.mult)
                O("vector", "tensor_tensor", sc1[:, :], sinb, sc1[:, :],
                  ALU.mult)
                O("vector", "tensor_tensor", sc1[:, :], sc1[:, :], sc2[:, :],
                  ALU.add)
                O("vector", "scalar_tensor_tensor", h_t[c][:, :], sc1[:, :],
                  P("scl", dcol), h_t[c][:, :], ALU.mult, ALU.add, inc=True)
                drain_pc(8)
            W("gpsimd", "vm", v_st)
            g_st = DMA("gpsimd", cc_i[l], state_pk[:, :])
            self.last_g_st = g_st
            W("gpsimd", "dg", g_st)
            self.bump("cc", 1)
            E("gpsimd", lambda e, i=l, h=self.semh["cc"]:
              e.collective_compute("AllGather", ALU.bypass,
                                   [list(range(NCORE))],
                                   ins=[cc_i[i].opt()],
                                   outs=[cc_o[i].opt()]).then_inc(h, 1))
            W("gpsimd", "cc", self.semn["cc"])
            g_rd = DMA("gpsimd",
                       ccg[:, :].rearrange("p (r c) -> p r c", r=NCORE),
                       cc_o[l].rearrange("(r p) c -> p r c", p=128))
            corr_combine(g_rd)
            apply_corr(l, 2, 0, False)
            apply_corr(l, 3, 1, False)
            apply_corr(l, 0, 0, True)
            apply_corr(l, 1, 1, True)
            v_mix = self.semn["vm"]

            inner_ln_stats([("vm", v_mix)])
            W("vector", "tm", t_w2_prev[0])
            for c in range(4):
                dcol = l * 4 + c
                normalize_to_sc1(c)
                O("vector", "tensor_scalar", yb[c][:, :], sc1[:, :],
                  P("ln2w", dcol), P("ln2b", dcol), ALU.mult, ALU.add,
                  inc=True)
            v_yb = self.semn["vm"]

            W("tensor", "dg", g_w1[l])
            W("tensor", "dg", g_w2[l])
            W("tensor", "vm", v_yb)
            for k in range(NCK):
                for ft in range(16):
                    bank = ft % 4
                    pairs = [(w1sb[:, c * DFI + ft * 128:c * DFI + ft * 128 + 128],
                              yb[c][:, k * 512:(k + 1) * 512]) for c in range(4)]
                    self.mm_group(bank, pairs)
                    self.evac(bank, "scalar", lambda b=bank, ft=ft, l=l:
                              O("scalar", "activation",
                                ffn1[:, ft * 512:(ft + 1) * 512],
                                self.psum[b][:, :], AF.Silu, scale=1.0,
                                bias=P("b1", l * 16 + ft), inc=True))
                a_f1 = self.semn["am"]
                for dtile in range(4):
                    bank = 4 + (dtile % 2)
                    pairs = [(w2sb[:, ft * DI + dtile * 128:
                                   ft * DI + dtile * 128 + 128],
                              ffn1[:, ft * 512:(ft + 1) * 512])
                             for ft in range(16)]
                    self.mm_group(bank, pairs, waits=[("am", a_f1)])
                    self.evac(bank, "vector", lambda b=bank, d=dtile, k=k, l=l:
                              O("vector", "scalar_tensor_tensor",
                                h_t[d][:, k * 512:(k + 1) * 512],
                                self.psum[b][:, :], P("b2", l * 4 + d),
                                h_t[d][:, k * 512:(k + 1) * 512],
                                ALU.add, ALU.add, inc=True))
                drain_pc(3)
            t_w2 = self.semn["tm"]
            t_w2_prev[0] = t_w2
            a_prev[0] = self.semn["am"]
            v_hdone = self.semn["vm"]
            if (l + 1) < NLAYER:
                W("gpsimd", "tm", t_w2)
                g_w1[l + 1] = DMA("gpsimd", w1sb[:, :], w1h[l + 1])
                g_w2[l + 1] = DMA("gpsimd", w2sb[:, :], w2h[l + 1])
            if f"h_l{l}" in dbg_aps:
                W("sync", "vm", v_hdone)
                W("sync", "am", self.semn["am"])
                for c in range(4):
                    DMA("sync", dbg_aps[f"h_l{l}"][c * 128:(c + 1) * 128, :],
                        h_t[c][:, :])
        v_layers = self.semn["vm"]

        W("gpsimd", "vm", v_layers)
        g_wout = DMA("gpsimd", woutsb[:, :], woutT)

        inner_ln_stats([("vm", v_layers), ("tm", t_w2_prev[0]),
                        ("am", a_prev[0])])
        W("vector", "tm", t_w2_prev[0])
        for c in range(4):
            normalize_to_sc1(c)
            O("vector", "tensor_scalar", yb[c][:, :], sc1[:, :],
              P("lnlw", c), P("lnlb", c), ALU.mult, ALU.add, inc=True)
        v_hlb = self.semn["vm"]

        st_ao = {}
        i_ao = 0
        for k in range(NCK):
            for vt in range(NKD):
                bank = vt % 4
                slv = i_ao % 2
                pairs = [(woutsb[:, c * D + vt * 128:c * D + vt * 128 + 128],
                          yb[c][:, k * 512:(k + 1) * 512]) for c in range(4)]
                self.mm_group(bank, pairs,
                              waits=([("dg", g_wout), ("vm", v_hlb)]
                                     if (k == 0 and vt == 0) else ()))
                ew = [("ds", st_ao[i_ao - 2])] if i_ao >= 2 else []
                self.evac(bank, "scalar", lambda b=bank, vt=vt, s=slv:
                          O("scalar", "activation", aoev[s][:, :],
                            self.psum[b][:, :], AF.Identity,
                            bias=P("bout", vt), inc=True), waits=ew)
                W("sync", "am", self.semn["am"])
                st_ao[i_ao] = DMA(
                    "sync", ao_fm[vt * 128:(vt + 1) * 128, k * 512:(k + 1) * 512],
                    aoev[slv][:, :])
                i_ao += 1
            drain_pc(6)
        v_ao_all = self.semn["ds"]

        v_jnk = {}
        st4 = {}
        v_p4tail = 0
        for tt in range(NTT):
            sl = tt % 2
            W("sync", "ds", st_ao[(tt // 4) * NKD + NKD - 1])
            if tt >= 2:
                W("sync", "vm", v_used[tt - 2])
                W("sync", "ds", st4.get(tt - 2, 0))
                W("sync", "vm", v_jnk[tt - 2])
            ld_ao = DMA("sync", jnkb4[sl][:, :],
                        ao_fm[0:D, tt * 128:(tt + 1) * 128], transpose=True)
            ld_x = DMA("sync", xtm4[sl][:, :], x_in[tt * 128:(tt + 1) * 128, :])
            col = tt % 16
            cc = slice(col, col + 1)
            v_s, a_s = tm_stats(jnkb4[sl][:, :], aotm4[:, :], col, ld_ao)
            W("scalar", "ds", ld_x)
            W("scalar", "vm", v_s)
            O("scalar", "activation", xtm4[sl][:, HH:D], xtm4[sl][:, HH:D],
              AF.Identity, bias=nmr[:, cc], scale=1.0, inc=True)
            a_xn = self.semn["am"]
            W("vector", "ds", ld_x)
            O("vector", "tensor_scalar", xtm4[sl][:, 0:HH], xtm4[sl][:, 0:HH],
              nmr[:, cc], None, ALU.add)
            W("vector", "am", a_xn)
            O("vector", "scalar_tensor_tensor", xtm4[sl][:, :], jnkb4[sl][:, :],
              rst[:, cc], xtm4[sl][:, :], ALU.mult, ALU.add, inc=True)
            v_x1 = self.semn["vm"]
            W("sync", "vm", v_x1)
            st4[tt] = DMA("sync", x1_tm[tt * 128:(tt + 1) * 128, :],
                          xtm4[sl][:, :])
            v_s2, a_s2 = tm_stats(xtm4[sl][:, :], aotm4[:, :], col, 0,
                                  vwaits=[("vm", v_x1)])
            W("scalar", "vm", v_s2)
            O("scalar", "activation", jnkb4[sl][:, HH:D], xtm4[sl][:, HH:D],
              AF.Identity, bias=nmr[:, cc], scale=rst[:, cc], inc=True)
            a_xb = self.semn["am"]
            O("vector", "tensor_scalar", jnkb4[sl][:, 0:HH], xtm4[sl][:, 0:HH],
              rst[:, cc], nmr[:, cc], ALU.mult, ALU.add, inc=True)
            v_xb = self.semn["vm"]
            v_used[tt] = v_xb
            v_jnk[tt] = v_xb
            W("sync", "am", a_xb)
            W("sync", "vm", v_xb)
            st4[tt] = DMA("sync", xlnb_tm[tt * 128:(tt + 1) * 128, :],
                          jnkb4[sl][:, :])
            if tt % 4 == 3:
                drain_pc(4)
        v_p4 = self.semn["ds"]
        v_p4vm = self.semn["vm"]
        if "x1d" in dbg_aps:
            W("sync", "ds", v_p4)
            DMA("sync", dbg_aps["x1d"], x1_tm)
            W("gpsimd", "ds", v_p4)
            DMA("gpsimd", dbg_aps["aod"], ao_fm)
            DMA("gpsimd", dbg_aps["xlnbd"], xlnb_tm)

        drain_pc(10 ** 9)
        fchunks = [(2 * FCH, NFT), (0, FCH), (FCH, 2 * FCH)]
        ft_order = [ft for (f0, f1) in fchunks for ft in range(f0, f1)]
        ft_next = {ft_order[i]: ft_order[i + 1]
                   for i in range(len(ft_order) - 1)}
        st_ffo = {}
        self.gsil_v = {}
        gu_t = {}
        ds_t = {}
        t_guplast = {}
        ldx_next = [0]
        for blk in range(NCK):
            tsl = slice(blk * 512, (blk + 1) * 512)
            if blk == 0:
                W("sync", "ds", st4[3])
                ldx = 0
                for kv in range(NKD):
                    ldx = DMA("sync", xlnffn[:, kv * 512:(kv + 1) * 512],
                              xlnb_tm[tsl, kv * 128:(kv + 1) * 128],
                              transpose=True)
            else:
                ldx = ldx_next[0]
            pend = {}

            def issue_gu_loads(ft, blk=blk):
                gu2 = ft % 2
                if gu2 in gu_t:
                    W("gpsimd", "tm", gu_t[gu2])
                lg = DMA("gpsimd", gusb[gu2][:, :], gateT[ft])
                lu = DMA("gpsimd", gusb[2 + gu2][:, :], upT[ft])
                pend[ft] = (lg, lu)

            first_issued = [False]
            for fci, (f0, f1) in enumerate(fchunks):
                first_chunk = (fci == 0)
                v_h = 0
                if not first_issued[0]:
                    issue_gu_loads(f0)
                    first_issued[0] = True
                for ft in range(f0, f1):
                    gu = ft % 2
                    nft_ = ft_next.get(ft)
                    if nft_ is not None and nft_ not in pend:
                        issue_gu_loads(nft_)
                    ldg, ldu = pend.pop(ft)
                    bank_g, bank_u = ft % 4, 4 + ft % 4
                    pairs = [(gusb[gu][:, kv * 128:(kv + 1) * 128],
                              xlnffn[:, kv * 512:(kv + 1) * 512])
                             for kv in range(NKD)]
                    self.mm_group(bank_g, pairs, waits=[("dg", ldg), ("ds", ldx)])
                    gw = ([("vm", self.gsil_v[ft % 2])]
                          if (ft % 2) in self.gsil_v else [])
                    a_g = self.evac(bank_g, "scalar",
                                    lambda b=bank_g, ft=ft, s=ft % 2:
                                    O("scalar", "activation", gsil[s][:, :],
                                      self.psum[b][:, :], AF.Silu,
                                      scale=P("gs", ft), inc=True, bias=c_zero[:, 0:1]), waits=gw)
                    pairs = [(gusb[2 + gu][:, kv * 128:(kv + 1) * 128],
                              xlnffn[:, kv * 512:(kv + 1) * 512])
                             for kv in range(NKD)]
                    gu_t[gu] = self.mm_group(bank_u, pairs,
                                             waits=[("dg", ldu)])
                    t_guplast[blk] = gu_t[gu]
                    v_h = self.evac(bank_u, "vector",
                                    lambda b=bank_u, ft=ft, f0=f0, s=ft % 2:
                                    O("vector", "scalar_tensor_tensor",
                                      hch[:, (ft - f0) * 512:(ft - f0 + 1) * 512],
                                      self.psum[b][:, :], P("us", ft),
                                      gsil[s][:, :], ALU.mult, ALU.mult,
                                      inc=True),
                                    waits=[("am", a_g)])
                    self.gsil_v[ft % 2] = v_h
                if blk == 0 and first_chunk and "hch0" in dbg_aps:
                    W("gpsimd", "vm", v_h)
                    DMA("gpsimd", dbg_aps["hch0"], hch[:, :])
                for dg_i in range(4):
                    pend_d = {}

                    def issue_d_loads(ft, dg_i=dg_i):
                        sl4 = ft % 4
                        if sl4 in ds_t:
                            W("gpsimd", "tm", ds_t[sl4])
                        pend_d[ft] = DMA(
                            "gpsimd", dsb[sl4][:, :],
                            downT[ft][:, dg_i * 1024:dg_i * 1024 + 1024])

                    issue_d_loads(f0)
                    issue_d_loads(f0 + 1)
                    issue_d_loads(f0 + 2)
                    for ft in range(f0, f1):
                        sl4 = ft % 4
                        if ft + 3 < f1:
                            issue_d_loads(ft + 3)
                        W("tensor", "dg", pend_d.pop(ft))
                        if ft == f0:
                            W("tensor", "vm", v_h)
                            for b in range(8):
                                s, v = self.bank_r[b]
                                W("tensor", s, v)
                        for di in range(8):
                            O("tensor", "matmul", self.psum[di][:, :],
                              dsb[sl4][:, di * 128:di * 128 + 128],
                              hch[:, (ft - f0) * 512:(ft - f0 + 1) * 512],
                              start=(ft == f0), stop=(ft == f1 - 1),
                              inc=(di == 7))
                        ds_t[sl4] = self.semn["tm"]
                    t_dg = self.semn["tm"]
                    for b in range(8):
                        self.bank_w[b] = ("tm", t_dg)
                    W("vector", "tm", t_dg)
                    if blk == 0 and first_chunk and dg_i == 0:
                        W("vector", "vm", v_p4vm)
                        W("vector", "ds", v_p4)
                    for di in range(8):
                        dtile = dg_i * 8 + di
                        if first_chunk:
                            O("vector", "tensor_copy", accum[dtile][:, :],
                              self.psum[di][:, :], inc=True)
                        else:
                            O("vector", "tensor_tensor", accum[dtile][:, :],
                              accum[dtile][:, :], self.psum[di][:, :], ALU.add,
                              inc=True)
                        self.bank_r[di] = ("vm", self.semn["vm"])
            if blk + 1 < NCK:
                nts = slice((blk + 1) * 512, (blk + 2) * 512)
                W("sync", "ds", st4[4 * (blk + 1) + 3])
                W("sync", "tm", t_guplast[blk])
                lx = 0
                for kv in range(NKD):
                    lx = DMA("sync", xlnffn[:, kv * 512:(kv + 1) * 512],
                             xlnb_tm[nts, kv * 128:(kv + 1) * 128],
                             transpose=True)
                ldx_next[0] = lx
            for dtile in range(NKD):
                s = dtile % 2
                if dtile >= 2:
                    W("vector", "ds", st_ffo[(blk, dtile - 2)])
                O("vector", "tensor_scalar", aoev[s][:, :], accum[dtile][:, :],
                  P("ds", dtile), None, ALU.mult, inc=True)
                W("sync", "vm", self.semn["vm"])
                st_ffo[(blk, dtile)] = DMA(
                    "sync", ffo_fm[dtile * 128:(dtile + 1) * 128, tsl],
                    aoev[s][:, :])
        v_ffo_all = self.semn["ds"]
        if "ffod" in dbg_aps:
            W("gpsimd", "ds", v_ffo_all)
            DMA("gpsimd", dbg_aps["ffod"], ffo_fm)

        v_o = {}
        st6 = {}
        for tt in range(NTT):
            sl = tt % 2
            if tt == 0:
                W("sync", "ds", v_ffo_all)
            if tt >= 2:
                W("sync", "vm", v_o[tt - 2])
                W("sync", "ds", st6.get(tt - 2, 0))
            ld_fo = DMA("sync", jnkb4[sl][:, :],
                        ffo_fm[0:D, tt * 128:(tt + 1) * 128], transpose=True)
            ld_x1 = DMA("sync", xtm4[sl][:, :], x1_tm[tt * 128:(tt + 1) * 128, :])
            col = tt % 16
            cc = slice(col, col + 1)
            v_s6, _ = tm_stats(jnkb4[sl][:, :], aotm4[:, :], col, ld_fo)
            W("scalar", "ds", ld_x1)
            W("scalar", "vm", v_s6)
            O("scalar", "activation", xtm4[sl][:, HH:D], xtm4[sl][:, HH:D],
              AF.Identity, bias=nmr[:, cc], scale=1.0, inc=True)
            a_xn6 = self.semn["am"]
            W("vector", "ds", ld_x1)
            O("vector", "tensor_scalar", xtm4[sl][:, 0:HH], xtm4[sl][:, 0:HH],
              nmr[:, cc], None, ALU.add)
            W("vector", "am", a_xn6)
            O("vector", "scalar_tensor_tensor", xtm4[sl][:, :], jnkb4[sl][:, :],
              rst[:, cc], xtm4[sl][:, :], ALU.mult, ALU.add, inc=True)
            v_o[tt] = self.semn["vm"]
            W("sync", "vm", v_o[tt])
            st6[tt] = DMA("sync", out_d[tt * 128:(tt + 1) * 128, :],
                          xtm4[sl][:, :])
        W("sync", "ds", self.semn["ds"])

        with nc.Block() as blk_:
            for meth, eng in ((blk_.sync, "sync"), (blk_.gpsimd, "gpsimd"),
                              (blk_.tensor, "tensor"), (blk_.vector, "vector"),
                              (blk_.scalar, "scalar")):
                ops = self.q[eng]

                def body(e, ops=ops):
                    for f in ops:
                        f(e)
                meth(body)
        return nc


def _host_layout(inputs):
    rs = np.ascontiguousarray

    def pack_chan(a):
        return rs(np.asarray(a).reshape(4, 4, 128).transpose(2, 0, 1)
                  .reshape(128, 16))

    W1 = np.asarray(inputs["L0_attn_W1"])
    W2 = np.asarray(inputs["L0_attn_W2"])
    g = {}
    g["winT"] = rs(np.transpose(np.asarray(inputs["L0_attn_Win"]))
                   .reshape(32, 128, 4, 128).transpose(1, 0, 2, 3)
                   .reshape(128, 32 * 512))
    g["w1h"] = rs(np.stack([np.transpose(W1[l]).reshape(4, 128, 2048)
                            .transpose(1, 0, 2).reshape(128, 4 * 2048)
                            for l in range(4)]))
    g["w2h"] = rs(np.stack([np.transpose(W2[l]).reshape(16, 128, 512)
                            .transpose(1, 0, 2).reshape(128, 16 * 512)
                            for l in range(4)]))
    g["woutT"] = rs(np.transpose(np.asarray(inputs["L0_attn_Wout"]))
                    .reshape(4, 128, 4096).transpose(1, 0, 2)
                    .reshape(128, 4 * 4096))
    def low_bytes(a):
        return rs(rs(a).view(np.int8)[..., 0::4])

    for nm, key in [("gateT", "L0_ffn_gate_w"), ("upT", "L0_ffn_up_w")]:
        w = np.asarray(inputs[key])
        g[nm] = low_bytes(w.reshape(NFT, 128, NKD, 128).transpose(0, 3, 2, 1)
                          .reshape(NFT, 128, NKD * 128))
    g["downT"] = low_bytes(np.transpose(np.asarray(inputs["L0_ffn_down_w"]))
                           .reshape(NFT, 128, NKD * 128))
    pp = {"ln1w": pack_chan(inputs["L0_attn_ln1_w"]),
          "ln1b": pack_chan(inputs["L0_attn_ln1_b"]),
          "ln2w": pack_chan(inputs["L0_attn_ln2_w"]),
          "ln2b": pack_chan(inputs["L0_attn_ln2_b"]),
          "scl": pack_chan(inputs["L0_attn_sc_lin"]),
          "b2": pack_chan(inputs["L0_attn_b2"]),
          "pre": pack_chan(inputs["L0_attn_phz_re"]),
          "pim": pack_chan(inputs["L0_attn_phz_im"]),
          "lcr": pack_chan(inputs["L0_attn_lc_re"]),
          "lci": pack_chan(inputs["L0_attn_lc_im"]),
          "b1": rs(np.asarray(inputs["L0_attn_b1"]).reshape(4, 16, 128)
                   .transpose(2, 0, 1).reshape(128, 64)),
          "bin": rs(np.asarray(inputs["L0_attn_bin"]).reshape(4, 128).T),
          "lnlw": rs(np.asarray(inputs["L0_attn_lnl_w"]).reshape(4, 128).T),
          "lnlb": rs(np.asarray(inputs["L0_attn_lnl_b"]).reshape(4, 128).T),
          "bout": rs(np.asarray(inputs["L0_attn_bout"]).reshape(32, 128).T),
          "gs": rs(np.asarray(inputs["L0_ffn_gate_s"]).reshape(NFT, 128).T),
          "us": rs(np.asarray(inputs["L0_ffn_up_s"]).reshape(NFT, 128).T),
          "ds": rs(np.asarray(inputs["L0_ffn_down_s"]).reshape(32, 128).T)}
    for k, v in pp.items():
        g["pp_" + k] = rs(np.asarray(v, np.float32))
    return g


def kernel(**inputs):
    T = 16384 // NCORE
    shared = _host_layout(inputs)
    x = np.asarray(inputs["L0_attn_x"])[0]
    in_maps = []
    for ci in range(NCORE):
        m = dict(shared)
        m["x_in"] = np.ascontiguousarray(x[ci * T:(ci + 1) * T, :], np.float32)
        m["pp_toff"] = np.full((128, 1), float(ci * T), np.float32)
        msk = np.zeros((128, 8), np.float32)
        msk[:, :7] = (np.arange(7)[None, :] < ci)
        msk[:, 7] = 1.0 if ci == 0 else 0.0
        m["pp_mask"] = msk
        in_maps.append(m)
    kb = KB(T)
    nc = kb.build()
    res = run_bass_kernel_spmd(nc, in_maps, core_ids=list(range(NCORE)))
    out = np.concatenate([np.asarray(r["out"]) for r in res.results], axis=0)
    return out[None, :, :].astype(np.float32)


# version 10 (final)
# speedup vs baseline: 1.2213x; 1.0080x over previous
import math

import numpy as np
import ml_dtypes

import concourse.bass as bass
import concourse.mybir as mybir
from concourse.bass_utils import run_bass_kernel_spmd

F32 = mybir.dt.float32
BF16 = mybir.dt.bfloat16
I32 = mybir.dt.int32
AF = mybir.ActivationFunctionType
ALU = mybir.AluOpType
BF = ml_dtypes.bfloat16

NCORE = 8
D = 4096
DI = 512
DFI = 2048
FF = 11008
NFT = FF // 128
NKD = D // 128
HH = D // 2
NLAYER = 4
EPS = 1e-5
FCH = 40

_TP = 2 * math.pi
CW_C1 = float(np.float32(np.ldexp(np.round(np.ldexp(_TP, 8)), -8)))
CW_C2 = float(np.float32(np.ldexp(np.round(np.ldexp(_TP - CW_C1, 19)), -19)))
CW_C3 = float(np.float32(_TP - CW_C1 - CW_C2))
MAGIC = float(np.float32(1.5 * 2.0 ** 23))
INV2PI = float(np.float32(1.0 / _TP))
PI = float(np.float32(math.pi))
PI_2 = float(np.float32(math.pi / 2))

_DEBUG = []


class KB:
    def __init__(self, T):
        self.T = T
        self.NCK = T // 512
        self.NTT = T // 128
        nc = bass.Bass("TRN2", target_bir_lowering=False)
        self.nc = nc
        self.q = {e: [] for e in ("sync", "gpsimd", "tensor", "vector", "scalar")}
        self.semh, self.semn = {}, {}
        for s in ("dg", "ds", "tm", "vm", "am", "cc"):
            self.semh[s] = nc.alloc_semaphore("sem_" + s)
            self.semn[s] = 0
        self.bank_w = [("tm", 0)] * 8
        self.bank_r = [("tm", 0)] * 8
        self.pc_items = []
        self.pc_mark = {}
        self.dumped_stats = False

    def emit(self, eng, fn):
        self.q[eng].append(fn)

    def W(self, eng, sem, val):
        if val:
            self.emit(eng, lambda e, h=self.semh[sem], v=val: e.wait_ge(h, v))

    def bump(self, sem, n):
        self.semn[sem] += n
        return self.semn[sem]

    def DMA(self, eng, dst, src, transpose=False):
        sem = {"sync": "ds", "gpsimd": "dg"}[eng]
        self.bump(sem, 16)
        self.emit(eng, lambda e, d=dst, s=src, h=self.semh[sem], t=transpose:
                  e.dma_start(d, s, transpose=t).then_inc(h, 16))
        return self.semn[sem]

    def O(self, eng, meth, *args, inc=False, drain=False, **kw):
        sem = {"tensor": "tm", "vector": "vm", "scalar": "am", "gpsimd": "vm"}[eng]
        if inc:
            self.bump(sem, 1)
        def f(e, meth=meth, args=args, kw=kw, inc=inc, drain=drain,
              h=self.semh[sem]):
            if drain:
                e.drain()
            r = getattr(e, meth)(*args, **kw)
            if inc:
                r.then_inc(h, 1)
        self.emit(eng, f)
        return self.semn[sem]

    def mm_group(self, bank, pairs, waits=()):
        s, v = self.bank_r[bank]
        self.W("tensor", s, v)
        for (ss, vv) in waits:
            self.W("tensor", ss, vv)
        n = len(pairs)
        for i, (l, r) in enumerate(pairs):
            self.O("tensor", "matmul", self.psum[bank][:, :], l, r,
                   start=(i == 0), stop=(i == n - 1), inc=(i == n - 1))
        self.bank_w[bank] = ("tm", self.semn["tm"])
        return self.semn["tm"]

    def evac(self, bank, eng, emit_cb, waits=()):
        s, v = self.bank_w[bank]
        self.W(eng, s, v)
        for (ss, vv) in waits:
            self.W(eng, ss, vv)
        emit_cb()
        sem = {"vector": "vm", "scalar": "am"}[eng]
        self.bank_r[bank] = (sem, self.semn[sem])
        return self.semn[sem]

    def build(self):
        nc, T, NCK, NTT = self.nc, self.T, self.NCK, self.NTT
        W, O, DMA, E = self.W, self.O, self.DMA, self.emit
        dt = nc.dram_tensor

        x_in = dt("x_in", [T, D], F32, kind="ExternalInput").ap()
        winT = dt("winT", [128, NKD * DI], F32, kind="ExternalInput").ap()
        w1h = dt("w1h", [NLAYER, 128, 4 * DFI], F32, kind="ExternalInput").ap()
        w2h = dt("w2h", [NLAYER, 128, 16 * DI], F32, kind="ExternalInput").ap()
        woutT = dt("woutT", [128, 4 * D], F32, kind="ExternalInput").ap()
        gateT = dt("gateT", [NFT, 128, D], mybir.dt.int8, kind="ExternalInput").ap()
        upT = dt("upT", [NFT, 128, D], mybir.dt.int8, kind="ExternalInput").ap()
        downT = dt("downT", [NFT, 128, D], mybir.dt.int8, kind="ExternalInput").ap()
        PCOLS = [("ln1w", 16), ("ln1b", 16), ("ln2w", 16), ("ln2b", 16),
                 ("scl", 16), ("b2", 16), ("pre", 16), ("pim", 16),
                 ("lcr", 16), ("lci", 16), ("b1", 64), ("bin", 4),
                 ("lnlw", 4), ("lnlb", 4), ("bout", 32), ("gs", NFT),
                 ("us", NFT), ("ds", 32), ("toff", 1), ("mask", 8)]
        pin = {nm: dt("pp_" + nm, [128, n], F32, kind="ExternalInput").ap()
               for nm, n in PCOLS}
        out_d = dt("out", [T, D], F32, kind="ExternalOutput").ap()
        dbg_aps = {nm: dt("dbg_" + nm, list(sh), F32, kind="ExternalOutput").ap()
                   for nm, sh in _DEBUG}

        xlna_tm = dt("xlna_tm", [T, D], BF16, kind="Internal").ap()
        ao_fm = dt("ao_fm", [D, T], BF16, kind="Internal").ap()
        x1_tm = dt("x1_tm", [T, D], F32, kind="Internal").ap()
        xlnb_tm = dt("xlnb_tm", [T, D], BF16, kind="Internal").ap()
        ffo_fm = dt("ffo_fm", [D, T], BF16, kind="Internal").ap()
        ncc = NLAYER
        cc_i = [dt(f"cc_i{k}", [128, 8], F32, kind="Internal").ap() for k in range(ncc)]
        cc_o = [dt(f"cc_o{k}", [NCORE * 128, 8], F32, kind="Internal").ap()
                for k in range(ncc)]

        pos = [16384]

        def place(name, shape, dtype):
            nb = int(np.prod(shape[1:])) * (4 if dtype != BF16 else 2)
            at = pos[0]
            pos[0] = at + ((nb + 63) // 64) * 64
            return nc.alloc_sbuf_tensor_at(name, list(shape), dtype, offset=at), at

        def over(name, shape, dtype, at):
            return nc.alloc_sbuf_tensor_at(name, list(shape), dtype, offset=at)

        TB = T * 4
        h_t = []
        H0 = pos[0]
        for c in range(4):
            h_t.append(place(f"h{c}", [128, T], F32)[0])
        tglob, TG0 = place("tglob", [128, T], F32)
        params, _ = place("params", [128, 1024], F32)
        statsf, _ = place("statsf", [128, 64], F32)
        var_scr, VS0 = place("var_scr", [1, T], F32)
        ccg, _ = place("ccg", [128, 64], F32)
        czt, _ = place("czt", [128, 8], F32)
        dpk2, _ = place("dpk2", [128, 8], F32)
        ccd, _ = place("ccd", [128, 64], F32)
        ones_bf, _ = place("ones_bf", [128, 1], BF16)
        onesr_f, _ = place("onesr_f", [1, 128], F32)
        nonesr_f, _ = place("nonesr_f", [1, 128], F32)
        c_pi2, _ = place("c_pi2", [128, 1], F32)
        c_eps, _ = place("c_eps", [128, 1], F32)
        c_zero, _ = place("c_zero", [128, 1], F32)
        state_pk, _ = place("state_pk", [128, 8], F32)
        STG0 = pos[0]
        stg = [place(f"stg{s}", [128, D], BF16)[0] for s in range(2)]
        aoev = [place(f"aoev{s}", [128, 512], BF16)[0] for s in range(2)]

        SCAN0 = pos[0]
        scn = [place(f"scn{i}", [128, T], F32)[0] for i in range(4)]
        SC_COS = [scn[0], scn[2]]
        SC_SIN = [scn[1], scn[3]]
        if pos[0] - SCAN0 < 32768:
            pos[0] = SCAN0 + 32768
        WA0 = pos[0]
        w1sb, _ = place("w1sb", [128, 4 * DFI], BF16)
        w2sb, _ = place("w2sb", [128, 16 * DI], BF16)
        MM0 = pos[0]
        yb = [place(f"yb{c}", [128, T], BF16)[0] for c in range(4)]
        hb = [place(f"hb{s}", [128, T], BF16)[0] for s in range(2)]
        ffn1, _ = place("ffn1", [128, 16 * 512], BF16)
        sc1, SC1_0 = place("sc1", [128, T], F32)
        sc2, SC2_0 = place("sc2", [128, T], F32)
        if pos[0] - MM0 < 49152:
            pos[0] = MM0 + 49152
        END0 = pos[0]
        assert END0 <= 229376, f"sbuf static overflow: {END0}"
        SCAN_B = WA0 - SCAN0
        MM_B = END0 - MM0

        xtm = [over(f"xtm{i}", [128, D], F32, SCAN0 + i * 16384) for i in range(2)]
        xtm4 = [over(f"xtm4_{i}", [128, D], F32, H0 + i * 16384) for i in range(2)]
        aotm4 = over("aotm4", [128, D], BF16, TG0)
        jnkb4 = [over(f"jnkb4_{i}", [128, D], BF16, STG0 + i * 8192)
                 for i in range(2)]
        assert SCAN_B >= 2 * 16384
        aotm = over("aotm", [128, D], BF16, MM0)
        jnkb = over("jnkb", [128, D], BF16, MM0 + 8192)
        winsb = over("winsb", [128, NKD * DI], BF16, MM0 + 16384)
        xlnw = over("xlnw", [128, NKD * 512], BF16, SCAN0)
        woutsb = over("woutsb", [128, 4 * D], BF16, SCAN0)
        xlnffn = over("xlnffn", [128, NKD * 512], BF16, SCAN0)
        gusb = [over(f"gusb{s}", [128, D], BF16, WA0 + s * 8192) for s in range(4)]
        assert WA0 + 4 * 8192 <= END0 + 4096
        if T * 4 >= 8192:
            dsb = [over(f"dsb{s}", [128, 1024], BF16, VS0 + s * 2048)
                   for s in range(4)]
        else:
            dsb = [place(f"dsb{s}", [128, 1024], BF16)[0] for s in range(4)]
        hch = over("hch", [128, FCH * 512], BF16, MM0)
        if T == 2048:
            acc_off = []
            for i in range(5 * TB // 2048):
                acc_off.append(H0 + i * 2048)
            need = NKD - len(acc_off)
            for i in range(need):
                acc_off.append(MM0 + FCH * 1024 + i * 2048)
            assert MM0 + FCH * 1024 + need * 2048 <= 229376
            accum = [over(f"acc{i}", [128, 512], F32, acc_off[i])
                     for i in range(NKD)]
            pos[0] = max(pos[0], MM0 + FCH * 1024 + need * 2048)
        else:
            acc_off = [H0 + i * 2048 for i in range(5 * TB // 2048)]
            acc_off += [MM0 + FCH * 1024 + i * 2048
                        for i in range((49152 - FCH * 1024) // 2048)]
            accum = [over(f"acc{i}", [128, 512], F32, acc_off[i])
                     if i < len(acc_off)
                     else place(f"acc{i}", [128, 512], F32)[0]
                     for i in range(NKD)]
            assert pos[0] <= 229376, pos[0]
        gsil = [place(f"gsil{s}", [128, 512], BF16)[0] for s in range(2)]
        assert pos[0] <= 229376, pos[0]

        mu_row = over("mu_row", [1, T], F32, SC1_0)
        sq_row = over("sq_row", [1, T], F32, SC2_0)
        self.psum = [nc.alloc_psum_tensor(f"ps{i}", [128, 512], F32)
                     for i in range(8)]

        PC = {}
        pcn = [0]
        for nm, n in PCOLS + [("rho", 16), ("lnrho", 16), ("theta", 16),
                              ("cth", 16), ("sth", 16), ("ainit", 16),
                              ("binit", 16), ("dpk", 16), ("pb", 16),
                              ("t1", 16), ("t2", 16), ("t3", 16), ("t4", 16),
                              ("u1", 1)]:
            PC[nm] = pcn[0]
            pcn[0] += n
        assert pcn[0] <= 1024

        def P(nm, i=0, n=1):
            c = PC[nm] + i
            return params[:, c:c + n]

        def A16(nm):
            return P(nm, 0, 16)

        for nm, n in PCOLS:
            DMA("sync", P(nm, 0, n), pin[nm])
        v_pld = self.semn["ds"]

        O("vector", "memset", c_pi2[:, :], PI_2)
        O("vector", "memset", c_eps[:, :], float(EPS))
        O("vector", "memset", c_zero[:, :], 0.0)
        O("vector", "memset", ones_bf[:, :], 1.0 / DI)
        O("vector", "memset", onesr_f[:, :], 1.0)
        O("vector", "memset", nonesr_f[:, :], -1.0, inc=True)
        O("gpsimd", "iota", tglob[:, :], [[1, T]], base=0, channel_multiplier=0,
          allow_small_or_imprecise_dtypes=True, inc=True)
        W("vector", "vm", self.semn["vm"])
        W("vector", "ds", v_pld)
        O("vector", "tensor_scalar", tglob[:, :], tglob[:, :], P("toff"), None,
          ALU.add, ALU.bypass, inc=True)

        W("scalar", "ds", v_pld)
        W("scalar", "vm", self.semn["vm"])
        O("scalar", "activation", A16("t1"), A16("pre"), AF.Square, bias=c_zero[:, 0:1])
        O("scalar", "activation", A16("t2"), A16("pim"), AF.Square, drain=True,
          inc=True, bias=c_zero[:, 0:1])
        W("vector", "am", self.semn["am"])
        O("vector", "tensor_tensor", A16("t1"), A16("t1"), A16("t2"), ALU.add,
          drain=True, inc=True)
        W("scalar", "vm", self.semn["vm"])
        O("scalar", "activation", A16("t2"), A16("t1"), AF.Sqrt, drain=True,
          inc=True, bias=c_zero[:, 0:1])
        W("vector", "am", self.semn["am"])
        O("vector", "reciprocal", A16("t3"), A16("t2"), drain=True)
        O("vector", "tensor_tensor", A16("t3"), A16("t3"), A16("t1"), ALU.mult,
          drain=True)
        O("vector", "tensor_tensor", A16("t3"), A16("t3"), A16("t2"), ALU.add,
          drain=True)
        O("vector", "tensor_scalar", A16("t3"), A16("t3"), 0.5, None, ALU.mult,
          drain=True, inc=True)
        W("scalar", "vm", self.semn["vm"])
        O("scalar", "activation", A16("rho"), A16("t3"), AF.Tanh, drain=True,
          inc=True, bias=c_zero[:, 0:1])
        W("vector", "am", self.semn["am"])
        O("vector", "tensor_scalar", A16("t4"), A16("rho"), 1e-36, None,
          ALU.max, drain=True, inc=True)
        W("scalar", "vm", self.semn["vm"])
        O("scalar", "activation", A16("lnrho"), A16("t4"), AF.Ln, drain=True, bias=c_zero[:, 0:1])
        O("scalar", "activation", A16("dpk"), A16("lnrho"), AF.Exp,
          scale=float(T), drain=True, bias=c_zero[:, 0:1])
        O("scalar", "activation", A16("t1"), A16("pre"), AF.Abs, drain=True, bias=c_zero[:, 0:1])
        O("scalar", "activation", A16("t2"), A16("pim"), AF.Abs, drain=True,
          inc=True, bias=c_zero[:, 0:1])
        W("vector", "am", self.semn["am"])
        O("vector", "tensor_tensor", A16("t3"), A16("t2"), A16("t1"), ALU.is_gt,
          drain=True)
        O("vector", "tensor_tensor", A16("t4"), A16("t1"), A16("t2"), ALU.min,
          drain=True)
        O("vector", "tensor_tensor", A16("t2"), A16("t1"), A16("t2"), ALU.max,
          drain=True)
        O("vector", "tensor_scalar", A16("t2"), A16("t2"), 1e-30, None, ALU.max,
          drain=True)
        O("vector", "reciprocal", A16("t1"), A16("t2"), drain=True)
        O("vector", "tensor_tensor", A16("t1"), A16("t1"), A16("t4"), ALU.mult,
          drain=True, inc=True)
        W("scalar", "vm", self.semn["vm"])
        O("scalar", "activation", A16("t4"), A16("t1"), AF.Arctan, drain=True,
          inc=True, bias=c_zero[:, 0:1])
        W("vector", "am", self.semn["am"])
        O("vector", "tensor_scalar", A16("t1"), A16("t4"), -2.0, PI_2, ALU.mult,
          ALU.add, drain=True)
        O("vector", "tensor_tensor", A16("t1"), A16("t1"), A16("t3"), ALU.mult,
          drain=True)
        O("vector", "tensor_tensor", A16("t4"), A16("t4"), A16("t1"), ALU.add,
          drain=True)
        O("vector", "tensor_scalar", A16("t3"), A16("pre"), 0.0, None, ALU.is_lt,
          drain=True)
        O("vector", "tensor_scalar", A16("t1"), A16("t4"), -2.0, PI, ALU.mult,
          ALU.add, drain=True)
        O("vector", "tensor_tensor", A16("t1"), A16("t1"), A16("t3"), ALU.mult,
          drain=True)
        O("vector", "tensor_tensor", A16("t4"), A16("t4"), A16("t1"), ALU.add,
          drain=True)
        O("vector", "tensor_scalar", A16("t1"), A16("pim"), 0.0, None, ALU.is_gt,
          drain=True)
        O("vector", "tensor_scalar", A16("t2"), A16("pim"), 0.0, None, ALU.is_lt,
          drain=True)
        O("vector", "tensor_tensor", A16("t1"), A16("t1"), A16("t2"),
          ALU.subtract, drain=True)
        O("vector", "tensor_tensor", A16("theta"), A16("t4"), A16("t1"),
          ALU.mult, drain=True, inc=True)
        W("scalar", "vm", self.semn["vm"])
        O("scalar", "activation", A16("sth"), A16("theta"), AF.Sin, drain=True, bias=c_zero[:, 0:1])
        O("scalar", "activation", A16("t1"), A16("theta"), AF.Abs, drain=True, bias=c_zero[:, 0:1])
        O("scalar", "activation", A16("cth"), A16("t1"), AF.Sin, scale=-1.0,
          bias=c_pi2[:, 0:1], drain=True, inc=True)
        W("vector", "am", self.semn["am"])
        O("vector", "tensor_tensor", A16("t1"), A16("cth"), A16("lcr"), ALU.mult,
          drain=True)
        O("vector", "tensor_tensor", A16("t2"), A16("sth"), A16("lci"), ALU.mult,
          drain=True)
        O("vector", "tensor_tensor", A16("ainit"), A16("t1"), A16("t2"),
          ALU.subtract, drain=True)
        O("vector", "tensor_tensor", A16("t1"), A16("sth"), A16("lcr"), ALU.mult,
          drain=True)
        O("vector", "tensor_tensor", A16("t2"), A16("cth"), A16("lci"), ALU.mult,
          drain=True)
        O("vector", "tensor_tensor", A16("binit"), A16("t1"), A16("t2"), ALU.add,
          drain=True)
        O("vector", "tensor_scalar", A16("binit"), A16("binit"), -1.0, None,
          ALU.mult, drain=True)
        O("vector", "tensor_scalar", A16("ainit"), A16("ainit"), P("mask", 7),
          None, ALU.mult, drain=True)
        O("vector", "tensor_scalar", A16("binit"), A16("binit"), P("mask", 7),
          None, ALU.mult, drain=True)
        O("vector", "tensor_scalar", P("u1"), P("toff"), -1.0, 1.0, ALU.mult,
          ALU.add, drain=True)
        O("vector", "tensor_scalar", A16("pb"), A16("lnrho"), P("u1"), None,
          ALU.mult, drain=True)
        O("vector", "engine_nop", inc=True)
        v_p0 = self.semn["vm"]
        if "theta" in dbg_aps:
            W("sync", "vm", v_p0)
            DMA("sync", dbg_aps["theta"], A16("theta"))
            DMA("sync", dbg_aps["rho"], A16("rho"))
            DMA("sync", dbg_aps["ainit"], A16("ainit"))
            DMA("sync", dbg_aps["lnrho"], A16("lnrho"))
            DMA("sync", dbg_aps["tg"], tglob[:, :])

        g_win = DMA("gpsimd", winsb[:, :], winT)
        g_w1 = {0: DMA("gpsimd", w1sb[:, :], w1h[0])}
        g_w2 = {0: DMA("gpsimd", w2sb[:, :], w2h[0])}

        def drain_pc(n):
            pass

        musum, sqsum = statsf[:, 0:16], statsf[:, 16:32]
        nmr, rst = statsf[:, 32:48], statsf[:, 48:64]

        def tm_stats(src_ap, junk_ap, col, ld, vwaits=()):
            c = slice(col, col + 1)
            W("vector", "ds", ld)
            for (ss, vv) in vwaits:
                W("vector", ss, vv)
            O("vector", "tensor_scalar", junk_ap, src_ap, 1.0, 0.0, ALU.mult,
              ALU.add, accum_out=musum[:, c], inc=True)
            W("scalar", "ds", ld)
            for (ss, vv) in vwaits:
                W("scalar", ss, vv)
            O("scalar", "activation", junk_ap, src_ap, AF.Square,
              accum_out=sqsum[:, c], inc=True, bias=c_zero[:, 0:1])
            a_sq = self.semn["am"]
            W("vector", "am", a_sq)
            O("vector", "tensor_scalar", musum[:, c], musum[:, c], 1.0 / D, None,
              ALU.mult, drain=True)
            O("vector", "tensor_scalar", sqsum[:, c], sqsum[:, c], 1.0 / D, None,
              ALU.mult, drain=True)
            O("vector", "tensor_tensor", rst[:, c], musum[:, c], musum[:, c],
              ALU.mult, drain=True)
            O("vector", "tensor_tensor", rst[:, c], sqsum[:, c], rst[:, c],
              ALU.subtract, drain=True, inc=True)
            W("scalar", "vm", self.semn["vm"])
            O("scalar", "activation", rst[:, c], rst[:, c], AF.Ln,
              bias=c_eps[:, 0:1], drain=True)
            O("scalar", "activation", rst[:, c], rst[:, c], AF.Exp, scale=-0.5,
              drain=True, inc=True, bias=c_zero[:, 0:1])
            a_r = self.semn["am"]
            W("vector", "am", a_r)
            O("vector", "tensor_tensor", nmr[:, c], musum[:, c], rst[:, c],
              ALU.mult, drain=True)
            O("vector", "tensor_scalar", nmr[:, c], nmr[:, c], -1.0, None,
              ALU.mult, drain=True, inc=True)
            return self.semn["vm"], a_r

        v_used, st_done = {}, {}
        a_used = {}
        for tt in range(NTT):
            sl = tt % 2
            if tt >= 2:
                W("sync", "vm", v_used[tt - 2])
                W("sync", "am", a_used[tt - 2])
            ld = DMA("sync", xtm[sl][:, :], x_in[tt * 128:(tt + 1) * 128, :])
            if tt == 0:
                W("vector", "ds", ld)
                W("vector", "vm", v_p0)
            col = tt % 16
            v_s, a_s = tm_stats(xtm[sl][:, :], jnkb[:, :], col, ld)
            a_used[tt] = a_s
            if tt >= 1:
                W("vector", "ds", st_done[tt - 1])
            cc = slice(col, col + 1)
            O("vector", "tensor_scalar", aotm[:, :], xtm[sl][:, :], rst[:, cc],
              nmr[:, cc], ALU.mult, ALU.add, inc=True)
            v_x = self.semn["vm"]
            v_used[tt] = v_x
            W("sync", "vm", v_x)
            st_done[tt] = DMA("sync", xlna_tm[tt * 128:(tt + 1) * 128, :],
                              aotm[:, :])
            if tt % 4 == 3:
                drain_pc(4)

        xb_mm = {}
        a_p2 = 0
        for k in range(NCK):
            W("sync", "dg", g_win)
            W("sync", "ds", st_done[4 * k + 3])
            if k >= 1:
                W("sync", "tm", xb_mm[k - 1])
            for kv in range(NKD):
                ldv = DMA("sync", xlnw[:, kv * 512:(kv + 1) * 512],
                          xlna_tm[k * 512:(k + 1) * 512,
                                  kv * 128:(kv + 1) * 128], transpose=True)
                W("tensor", "ds", ldv)
                if kv == 0:
                    for b in range(4):
                        s, v = self.bank_r[b]
                        W("tensor", s, v)
                for dtile in range(4):
                    O("tensor", "matmul", self.psum[dtile][:, :],
                      winsb[:, kv * 512 + dtile * 128:
                            kv * 512 + dtile * 128 + 128],
                      xlnw[:, kv * 512:(kv + 1) * 512],
                      start=(kv == 0), stop=(kv == NKD - 1),
                      inc=(kv == NKD - 1 and dtile == 3))
            t_mm = self.semn["tm"]
            for b in range(4):
                self.bank_w[b] = ("tm", t_mm)
            for dtile in range(4):
                a_p2 = self.evac(dtile, "scalar", lambda b=dtile, k=k:
                                 O("scalar", "activation",
                                   h_t[b][:, k * 512:(k + 1) * 512],
                                   self.psum[b][:, :], AF.Identity,
                                   bias=P("bin", b), inc=True))
            xb_mm[k] = t_mm
            drain_pc(6)
        if "h_win" in dbg_aps:
            W("sync", "am", a_p2)
            for c in range(4):
                DMA("sync", dbg_aps["h_win"][c * 128:(c + 1) * 128, :],
                    h_t[c][:, :])

        def inner_ln_stats(extra_waits):
            t_c = {}
            for c in range(4):
                sl = c % 2
                if c == 0:
                    for (ss, vv) in extra_waits:
                        W("vector", ss, vv)
                if c >= 2:
                    W("vector", "tm", t_c[c - 2])
                O("vector", "tensor_copy", hb[sl][:, :], h_t[c][:, :], inc=True)
                v_cast = self.semn["vm"]
                W("scalar", "vm", v_cast)
                if c == 0:
                    for (ss, vv) in extra_waits:
                        W("scalar", ss, vv)
                else:
                    W("scalar", "tm", t_c[c - 1])
                O("scalar", "activation", ffn1[:, 0:T], hb[sl][:, :], AF.Square,
                  inc=True, bias=c_zero[:, 0:1])
                a_sqc = self.semn["am"]
                W("tensor", "vm", v_cast)
                W("tensor", "am", a_sqc)
                if c == 0:
                    for b in range(8):
                        s, v = self.bank_r[b]
                        W("tensor", s, v)
                for k in range(NCK):
                    O("tensor", "matmul", self.psum[k][0:1, :], ones_bf[:, :],
                      hb[sl][:, k * 512:(k + 1) * 512],
                      start=(c == 0), stop=(c == 3))
                for k in range(NCK):
                    O("tensor", "matmul", self.psum[4 + k][0:1, :],
                      ones_bf[:, :], ffn1[:, k * 512:(k + 1) * 512],
                      start=(c == 0), stop=(c == 3),
                      inc=(k == NCK - 1))
                t_c[c] = self.semn["tm"]
            t_stats = self.semn["tm"]
            for b in range(8):
                self.bank_w[b] = ("tm", t_stats)
            W("vector", "tm", t_stats)
            for k in range(NCK):
                O("vector", "tensor_copy", mu_row[0:1, k * 512:(k + 1) * 512],
                  self.psum[k][0:1, :])
                O("vector", "tensor_copy", sq_row[0:1, k * 512:(k + 1) * 512],
                  self.psum[4 + k][0:1, :])
            O("vector", "engine_nop", inc=True)
            v_ev = self.semn["vm"]
            for b in range(8):
                self.bank_r[b] = ("vm", v_ev)
            O("vector", "tensor_tensor", var_scr[0:1, 0:T], mu_row[0:1, 0:T],
              mu_row[0:1, 0:T], ALU.mult)
            O("vector", "tensor_tensor", sq_row[0:1, 0:T], sq_row[0:1, 0:T],
              var_scr[0:1, 0:T], ALU.subtract, inc=True)
            W("scalar", "vm", self.semn["vm"])
            O("scalar", "activation", sq_row[0:1, 0:T], sq_row[0:1, 0:T], AF.Ln,
              bias=c_eps[0:1, 0:1])
            O("scalar", "activation", sq_row[0:1, 0:T], sq_row[0:1, 0:T], AF.Exp,
              scale=-0.5, inc=True, bias=c_zero[0:1, 0:1])
            W("vector", "am", self.semn["am"])
            O("vector", "tensor_tensor", mu_row[0:1, 0:T], mu_row[0:1, 0:T],
              sq_row[0:1, 0:T], ALU.mult, inc=True)
            v_rr = self.semn["vm"]
            if "murow" in dbg_aps and not self.dumped_stats:
                self.dumped_stats = True
                W("sync", "vm", v_rr)
                dmp = DMA("sync", dbg_aps["murow"], mu_row[0:1, 0:T])
                dmp = DMA("sync", dbg_aps["sqrow"], sq_row[0:1, 0:T])
                W("vector", "ds", dmp)
            for k in range(NCK):
                self.mm_group(k, [(nonesr_f[:, :],
                                   mu_row[0:1, k * 512:(k + 1) * 512])],
                              waits=[("vm", v_rr)])
                self.mm_group(4 + k, [(onesr_f[:, :],
                                       sq_row[0:1, k * 512:(k + 1) * 512])])

        def normalize_to_sc1(c):
            for k in range(NCK):
                for b in (k, 4 + k):
                    s, v = self.bank_w[b]
                    W("vector", s, v)
                sl5 = slice(k * 512, (k + 1) * 512)
                O("vector", "tensor_tensor", sc1[:, sl5], h_t[c][:, sl5],
                  self.psum[4 + k][:, :], ALU.mult)
                O("vector", "tensor_tensor", sc1[:, sl5], sc1[:, sl5],
                  self.psum[k][:, :], ALU.add, inc=(k == NCK - 1))
            v_n = self.semn["vm"]
            for k in range(NCK):
                self.bank_r[k] = ("vm", v_n)
                self.bank_r[4 + k] = ("vm", v_n)
            return v_n

        def corr_combine(g_rd):
            W("vector", "dg", g_rd)
            O("vector", "memset", czt[:, :], 0.0)
            for j in range(7):
                O("vector", "tensor_tensor", czt[:, :], czt[:, :],
                  ccd[:, j * 8:j * 8 + 8], ALU.mult, drain=True)
                O("vector", "scalar_tensor_tensor", czt[:, :],
                  ccg[:, j * 8:j * 8 + 8], P("mask", j), czt[:, :],
                  ALU.mult, ALU.add, drain=True)
            O("vector", "engine_nop", drain=True, inc=True)
            return self.semn["vm"]

        def apply_corr(l, c, slot, recompute):
            dcol = l * 4 + c
            cosb, sinb = SC_COS[slot][:, :], SC_SIN[slot][:, :]
            if recompute:
                O("vector", "tensor_scalar", cosb, tglob[:, :],
                  P("theta", dcol), None, ALU.mult)
                O("vector", "tensor_scalar", sinb, cosb, INV2PI, MAGIC,
                  ALU.mult, ALU.add)
                O("vector", "tensor_scalar", sinb, sinb, -MAGIC, None, ALU.add)
                O("vector", "scalar_tensor_tensor", cosb, sinb, -CW_C1, cosb,
                  ALU.mult, ALU.add)
                O("vector", "scalar_tensor_tensor", cosb, sinb, -CW_C2, cosb,
                  ALU.mult, ALU.add)
                O("vector", "scalar_tensor_tensor", cosb, sinb, -CW_C3, cosb,
                  ALU.mult, ALU.add)
                O("vector", "tensor_scalar", cosb, cosb, PI, -PI, ALU.min,
                  ALU.max, inc=True)
                W("scalar", "vm", self.semn["vm"])
                O("scalar", "activation", sinb, cosb, AF.Sin,
                  bias=c_zero[:, 0:1])
                O("scalar", "activation", sc2[:, :], cosb, AF.Abs,
                  bias=c_zero[:, 0:1])
                O("scalar", "activation", cosb, sc2[:, :], AF.Sin, scale=-1.0,
                  bias=c_pi2[:, 0:1], inc=True)
                W("vector", "am", self.semn["am"])
            W("scalar", "vm", self.semn["vm"])
            O("scalar", "activation", sc2[:, :], tglob[:, :], AF.Exp,
              scale=P("lnrho", dcol), bias=P("pb", dcol), inc=True)
            a_rp = self.semn["am"]
            W("vector", "am", a_rp)
            O("vector", "tensor_scalar", sc1[:, :], cosb,
              czt[:, 2 * c:2 * c + 1], None, ALU.mult)
            O("vector", "scalar_tensor_tensor", sc1[:, :], sinb,
              czt[:, 2 * c + 1:2 * c + 2], sc1[:, :], ALU.mult, ALU.add)
            O("vector", "tensor_tensor", sc1[:, :], sc1[:, :], sc2[:, :],
              ALU.mult)
            O("vector", "scalar_tensor_tensor", h_t[c][:, :], sc1[:, :],
              P("scl", dcol), h_t[c][:, :], ALU.mult, ALU.add, inc=True)
            return self.semn["vm"]

        t_w2_prev = [xb_mm[NCK - 1]]
        a_prev = [a_p2]
        for l in range(NLAYER):
            inner_ln_stats([("am", a_prev[0]), ("tm", t_w2_prev[0])])
            W("vector", "vm", v_p0)
            for c4 in range(4):
                O("vector", "tensor_copy", dpk2[:, 2 * c4:2 * c4 + 1],
                  P("dpk", l * 4 + c4), drain=True)
                O("vector", "tensor_copy", dpk2[:, 2 * c4 + 1:2 * c4 + 2],
                  P("dpk", l * 4 + c4), drain=True)
            for j in range(7):
                O("vector", "tensor_scalar", ccd[:, j * 8:j * 8 + 8],
                  dpk2[:, :], P("mask", j), None, ALU.mult, drain=True)
                O("vector", "tensor_scalar", P("u1"), P("mask", j), -1.0, 1.0,
                  ALU.mult, ALU.add, drain=True)
                O("vector", "tensor_scalar", ccd[:, j * 8:j * 8 + 8],
                  ccd[:, j * 8:j * 8 + 8], P("u1"), None, ALU.add, drain=True)
            O("vector", "engine_nop", inc=True)

            for c in range(4):
                slot = c % 2
                dcol = l * 4 + c
                v_n = normalize_to_sc1(c)
                W("scalar", "vm", v_n)
                O("scalar", "activation", sc1[:, :], sc1[:, :], AF.Silu,
                  scale=P("ln1w", dcol), bias=P("ln1b", dcol), inc=True)
                a_yt = self.semn["am"]
                cosb = SC_COS[slot][:, :]
                sinb = SC_SIN[slot][:, :]
                W("vector", "am", a_yt)
                O("vector", "tensor_scalar", cosb, tglob[:, :],
                  P("theta", dcol), None, ALU.mult, inc=True)
                if l == 0 and c == 0 and "v00" in dbg_aps:
                    W("sync", "vm", self.semn["vm"])
                    dmp = DMA("sync", dbg_aps["v00"], cosb)
                    W("vector", "ds", dmp)
                O("vector", "tensor_scalar", sinb, cosb, INV2PI, MAGIC,
                  ALU.mult, ALU.add)
                O("vector", "tensor_scalar", sinb, sinb, -MAGIC, None, ALU.add)
                O("vector", "scalar_tensor_tensor", cosb, sinb, -CW_C1, cosb,
                  ALU.mult, ALU.add)
                O("vector", "scalar_tensor_tensor", cosb, sinb, -CW_C2, cosb,
                  ALU.mult, ALU.add)
                O("vector", "scalar_tensor_tensor", cosb, sinb, -CW_C3, cosb,
                  ALU.mult, ALU.add)
                O("vector", "tensor_scalar", cosb, cosb, PI, -PI, ALU.min,
                  ALU.max, inc=True)
                if l == 0 and c == 0 and "r00" in dbg_aps:
                    W("sync", "vm", self.semn["vm"])
                    dmp = DMA("sync", dbg_aps["r00"], cosb)
                    W("scalar", "ds", dmp)
                W("scalar", "vm", self.semn["vm"])
                O("scalar", "activation", sinb, cosb, AF.Sin, bias=c_zero[:, 0:1])
                O("scalar", "activation", sc2[:, :], cosb, AF.Abs, bias=c_zero[:, 0:1])
                O("scalar", "activation", cosb, sc2[:, :], AF.Sin, scale=-1.0,
                  bias=c_pi2[:, 0:1], inc=True)
                a_cs = self.semn["am"]
                if l == 0 and c == 0 and "yt00" in dbg_aps:
                    W("sync", "am", a_yt)
                    dmp = DMA("sync", dbg_aps["yt00"], sc1[:, :])
                    W("vector", "ds", dmp)
                if l == 0 and c == 0 and "cos00" in dbg_aps:
                    W("sync", "am", a_cs)
                    dmp = DMA("sync", dbg_aps["cos00"], cosb)
                    dmp = DMA("sync", dbg_aps["sin00"], sinb)
                    W("vector", "ds", dmp)
                    W("scalar", "ds", dmp)
                W("vector", "am", a_cs)
                rho_b = P("rho", dcol).broadcast_to((128, T))
                O("vector", "tensor_tensor", sc2[:, :], cosb, sc1[:, :],
                  ALU.mult)
                O("vector", "tensor_tensor_scan", sc2[:, :], rho_b, sc2[:, :],
                  P("ainit", dcol), ALU.mult, ALU.add)
                O("vector", "tensor_tensor", sc1[:, :], sinb, sc1[:, :],
                  ALU.mult)
                O("vector", "tensor_tensor_scan", sc1[:, :], rho_b, sc1[:, :],
                  P("binit", dcol), ALU.mult, ALU.add, inc=True)
                O("vector", "drain")
                if c == 0:
                    W("vector", "dg", getattr(self, "last_g_st", 0))
                O("vector", "tensor_copy", state_pk[:, 2 * c:2 * c + 1],
                  sc2[:, T - 1:T])
                O("vector", "tensor_copy", state_pk[:, 2 * c + 1:2 * c + 2],
                  sc1[:, T - 1:T])
                O("vector", "engine_nop", drain=True, inc=True)
                v_st = self.semn["vm"]
                if l == 0 and c == 0 and "A00" in dbg_aps:
                    W("sync", "vm", v_st)
                    dmp = DMA("sync", dbg_aps["A00"], sc2[:, :])
                    dmp = DMA("sync", dbg_aps["B00"], sc1[:, :])
                    W("vector", "ds", dmp)
                O("vector", "tensor_tensor", sc2[:, :], cosb, sc2[:, :],
                  ALU.mult)
                O("vector", "tensor_tensor", sc1[:, :], sinb, sc1[:, :],
                  ALU.mult)
                O("vector", "tensor_tensor", sc1[:, :], sc1[:, :], sc2[:, :],
                  ALU.add)
                O("vector", "scalar_tensor_tensor", h_t[c][:, :], sc1[:, :],
                  P("scl", dcol), h_t[c][:, :], ALU.mult, ALU.add, inc=True)
                drain_pc(8)
            W("gpsimd", "vm", v_st)
            g_st = DMA("gpsimd", cc_i[l], state_pk[:, :])
            self.last_g_st = g_st
            W("gpsimd", "dg", g_st)
            self.bump("cc", 1)
            E("gpsimd", lambda e, i=l, h=self.semh["cc"]:
              e.collective_compute("AllGather", ALU.bypass,
                                   [list(range(NCORE))],
                                   ins=[cc_i[i].opt()],
                                   outs=[cc_o[i].opt()]).then_inc(h, 1))
            W("gpsimd", "cc", self.semn["cc"])
            g_rd = DMA("gpsimd",
                       ccg[:, :].rearrange("p (r c) -> p r c", r=NCORE),
                       cc_o[l].rearrange("(r p) c -> p r c", p=128))
            corr_combine(g_rd)
            apply_corr(l, 2, 0, False)
            apply_corr(l, 3, 1, False)
            apply_corr(l, 0, 0, True)
            apply_corr(l, 1, 1, True)
            v_mix = self.semn["vm"]

            inner_ln_stats([("vm", v_mix)])
            W("vector", "tm", t_w2_prev[0])
            for c in range(4):
                dcol = l * 4 + c
                normalize_to_sc1(c)
                O("vector", "tensor_scalar", yb[c][:, :], sc1[:, :],
                  P("ln2w", dcol), P("ln2b", dcol), ALU.mult, ALU.add,
                  inc=True)
            v_yb = self.semn["vm"]

            W("tensor", "dg", g_w1[l])
            W("tensor", "dg", g_w2[l])
            W("tensor", "vm", v_yb)
            for k in range(NCK):
                for ft in range(16):
                    bank = ft % 4
                    pairs = [(w1sb[:, c * DFI + ft * 128:c * DFI + ft * 128 + 128],
                              yb[c][:, k * 512:(k + 1) * 512]) for c in range(4)]
                    self.mm_group(bank, pairs)
                    self.evac(bank, "scalar", lambda b=bank, ft=ft, l=l:
                              O("scalar", "activation",
                                ffn1[:, ft * 512:(ft + 1) * 512],
                                self.psum[b][:, :], AF.Silu, scale=1.0,
                                bias=P("b1", l * 16 + ft), inc=True))
                a_f1 = self.semn["am"]
                for dtile in range(4):
                    bank = 4 + (dtile % 2)
                    pairs = [(w2sb[:, ft * DI + dtile * 128:
                                   ft * DI + dtile * 128 + 128],
                              ffn1[:, ft * 512:(ft + 1) * 512])
                             for ft in range(16)]
                    self.mm_group(bank, pairs, waits=[("am", a_f1)])
                    self.evac(bank, "vector", lambda b=bank, d=dtile, k=k, l=l:
                              O("vector", "scalar_tensor_tensor",
                                h_t[d][:, k * 512:(k + 1) * 512],
                                self.psum[b][:, :], P("b2", l * 4 + d),
                                h_t[d][:, k * 512:(k + 1) * 512],
                                ALU.add, ALU.add, inc=True))
                drain_pc(3)
            t_w2 = self.semn["tm"]
            t_w2_prev[0] = t_w2
            a_prev[0] = self.semn["am"]
            v_hdone = self.semn["vm"]
            if (l + 1) < NLAYER:
                W("gpsimd", "tm", t_w2)
                g_w1[l + 1] = DMA("gpsimd", w1sb[:, :], w1h[l + 1])
                g_w2[l + 1] = DMA("gpsimd", w2sb[:, :], w2h[l + 1])
            if f"h_l{l}" in dbg_aps:
                W("sync", "vm", v_hdone)
                W("sync", "am", self.semn["am"])
                for c in range(4):
                    DMA("sync", dbg_aps[f"h_l{l}"][c * 128:(c + 1) * 128, :],
                        h_t[c][:, :])
        v_layers = self.semn["vm"]

        W("gpsimd", "vm", v_layers)
        g_wout = DMA("gpsimd", woutsb[:, :], woutT)

        inner_ln_stats([("vm", v_layers), ("tm", t_w2_prev[0]),
                        ("am", a_prev[0])])
        W("vector", "tm", t_w2_prev[0])
        for c in range(4):
            normalize_to_sc1(c)
            O("vector", "tensor_scalar", yb[c][:, :], sc1[:, :],
              P("lnlw", c), P("lnlb", c), ALU.mult, ALU.add, inc=True)
        v_hlb = self.semn["vm"]

        st_ao = {}
        i_ao = 0
        for k in range(NCK):
            for vt in range(NKD):
                bank = vt % 4
                slv = i_ao % 2
                pairs = [(woutsb[:, c * D + vt * 128:c * D + vt * 128 + 128],
                          yb[c][:, k * 512:(k + 1) * 512]) for c in range(4)]
                self.mm_group(bank, pairs,
                              waits=([("dg", g_wout), ("vm", v_hlb)]
                                     if (k == 0 and vt == 0) else ()))
                ew = [("ds", st_ao[i_ao - 2])] if i_ao >= 2 else []
                self.evac(bank, "scalar", lambda b=bank, vt=vt, s=slv:
                          O("scalar", "activation", aoev[s][:, :],
                            self.psum[b][:, :], AF.Identity,
                            bias=P("bout", vt), inc=True), waits=ew)
                W("sync", "am", self.semn["am"])
                st_ao[i_ao] = DMA(
                    "sync", ao_fm[vt * 128:(vt + 1) * 128, k * 512:(k + 1) * 512],
                    aoev[slv][:, :])
                i_ao += 1
            drain_pc(6)
        v_ao_all = self.semn["ds"]

        v_jnk = {}
        st4 = {}
        v_p4tail = 0
        for tt in range(NTT):
            sl = tt % 2
            W("sync", "ds", st_ao[(tt // 4) * NKD + NKD - 1])
            if tt >= 2:
                W("sync", "vm", v_used[tt - 2])
                W("sync", "ds", st4.get(tt - 2, 0))
                W("sync", "vm", v_jnk[tt - 2])
            ld_ao = DMA("sync", jnkb4[sl][:, :],
                        ao_fm[0:D, tt * 128:(tt + 1) * 128], transpose=True)
            ld_x = DMA("sync", xtm4[sl][:, :], x_in[tt * 128:(tt + 1) * 128, :])
            col = tt % 16
            cc = slice(col, col + 1)
            v_s, a_s = tm_stats(jnkb4[sl][:, :], aotm4[:, :], col, ld_ao)
            W("scalar", "ds", ld_x)
            W("scalar", "vm", v_s)
            O("scalar", "activation", xtm4[sl][:, HH:D], xtm4[sl][:, HH:D],
              AF.Identity, bias=nmr[:, cc], scale=1.0, inc=True)
            a_xn = self.semn["am"]
            W("vector", "ds", ld_x)
            O("vector", "tensor_scalar", xtm4[sl][:, 0:HH], xtm4[sl][:, 0:HH],
              nmr[:, cc], None, ALU.add)
            W("vector", "am", a_xn)
            O("vector", "scalar_tensor_tensor", xtm4[sl][:, :], jnkb4[sl][:, :],
              rst[:, cc], xtm4[sl][:, :], ALU.mult, ALU.add, inc=True)
            v_x1 = self.semn["vm"]
            W("sync", "vm", v_x1)
            st4[tt] = DMA("sync", x1_tm[tt * 128:(tt + 1) * 128, :],
                          xtm4[sl][:, :])
            v_s2, a_s2 = tm_stats(xtm4[sl][:, :], aotm4[:, :], col, 0,
                                  vwaits=[("vm", v_x1)])
            W("scalar", "vm", v_s2)
            O("scalar", "activation", jnkb4[sl][:, HH:D], xtm4[sl][:, HH:D],
              AF.Identity, bias=nmr[:, cc], scale=rst[:, cc], inc=True)
            a_xb = self.semn["am"]
            O("vector", "tensor_scalar", jnkb4[sl][:, 0:HH], xtm4[sl][:, 0:HH],
              rst[:, cc], nmr[:, cc], ALU.mult, ALU.add, inc=True)
            v_xb = self.semn["vm"]
            v_used[tt] = v_xb
            v_jnk[tt] = v_xb
            W("sync", "am", a_xb)
            W("sync", "vm", v_xb)
            st4[tt] = DMA("sync", xlnb_tm[tt * 128:(tt + 1) * 128, :],
                          jnkb4[sl][:, :])
            if tt % 4 == 3:
                drain_pc(4)
        v_p4 = self.semn["ds"]
        v_p4vm = self.semn["vm"]
        if "x1d" in dbg_aps:
            W("sync", "ds", v_p4)
            DMA("sync", dbg_aps["x1d"], x1_tm)
            W("gpsimd", "ds", v_p4)
            DMA("gpsimd", dbg_aps["aod"], ao_fm)
            DMA("gpsimd", dbg_aps["xlnbd"], xlnb_tm)

        drain_pc(10 ** 9)
        fchunks = [(2 * FCH, NFT), (0, FCH), (FCH, 2 * FCH)]
        ft_order = [ft for (f0, f1) in fchunks for ft in range(f0, f1)]
        ft_next = {ft_order[i]: ft_order[i + 1]
                   for i in range(len(ft_order) - 1)}
        st_ffo = {}
        self.gsil_v = {}
        gu_t = {}
        ds_t = {}
        t_guplast = {}
        ldx_next = [0]
        for blk in range(NCK):
            tsl = slice(blk * 512, (blk + 1) * 512)
            if blk == 0:
                W("sync", "ds", st4[3])
                ldx = 0
                for kv in range(NKD):
                    ldx = DMA("sync", xlnffn[:, kv * 512:(kv + 1) * 512],
                              xlnb_tm[tsl, kv * 128:(kv + 1) * 128],
                              transpose=True)
            else:
                ldx = ldx_next[0]
            pend = {}

            def issue_gu_loads(ft, blk=blk):
                gu2 = ft % 2
                if gu2 in gu_t:
                    W("gpsimd", "tm", gu_t[gu2])
                lg = DMA("gpsimd", gusb[gu2][:, :], gateT[ft])
                lu = DMA("gpsimd", gusb[2 + gu2][:, :], upT[ft])
                pend[ft] = (lg, lu)

            first_issued = [False]
            for fci, (f0, f1) in enumerate(fchunks):
                first_chunk = (fci == 0)
                v_h = 0
                if not first_issued[0]:
                    issue_gu_loads(f0)
                    first_issued[0] = True
                for ft in range(f0, f1):
                    gu = ft % 2
                    nft_ = ft_next.get(ft)
                    if nft_ is not None and nft_ not in pend:
                        issue_gu_loads(nft_)
                    ldg, ldu = pend.pop(ft)
                    bank_g, bank_u = ft % 4, 4 + ft % 4
                    pairs = [(gusb[gu][:, kv * 128:(kv + 1) * 128],
                              xlnffn[:, kv * 512:(kv + 1) * 512])
                             for kv in range(NKD)]
                    self.mm_group(bank_g, pairs, waits=[("dg", ldg), ("ds", ldx)])
                    gw = ([("vm", self.gsil_v[ft % 2])]
                          if (ft % 2) in self.gsil_v else [])
                    a_g = self.evac(bank_g, "scalar",
                                    lambda b=bank_g, ft=ft, s=ft % 2:
                                    O("scalar", "activation", gsil[s][:, :],
                                      self.psum[b][:, :], AF.Silu,
                                      scale=P("gs", ft), inc=True, bias=c_zero[:, 0:1]), waits=gw)
                    pairs = [(gusb[2 + gu][:, kv * 128:(kv + 1) * 128],
                              xlnffn[:, kv * 512:(kv + 1) * 512])
                             for kv in range(NKD)]
                    gu_t[gu] = self.mm_group(bank_u, pairs,
                                             waits=[("dg", ldu)])
                    t_guplast[blk] = gu_t[gu]
                    v_h = self.evac(bank_u, "vector",
                                    lambda b=bank_u, ft=ft, f0=f0, s=ft % 2:
                                    O("vector", "scalar_tensor_tensor",
                                      hch[:, (ft - f0) * 512:(ft - f0 + 1) * 512],
                                      self.psum[b][:, :], P("us", ft),
                                      gsil[s][:, :], ALU.mult, ALU.mult,
                                      inc=True),
                                    waits=[("am", a_g)])
                    self.gsil_v[ft % 2] = v_h
                if blk == 0 and first_chunk and "hch0" in dbg_aps:
                    W("gpsimd", "vm", v_h)
                    DMA("gpsimd", dbg_aps["hch0"], hch[:, :])
                for dg_i in range(4):
                    pend_d = {}

                    def issue_d_loads(ft, dg_i=dg_i):
                        sl4 = ft % 4
                        if sl4 in ds_t:
                            W("gpsimd", "tm", ds_t[sl4])
                        pend_d[ft] = DMA(
                            "gpsimd", dsb[sl4][:, :],
                            downT[ft][:, dg_i * 1024:dg_i * 1024 + 1024])

                    issue_d_loads(f0)
                    issue_d_loads(f0 + 1)
                    issue_d_loads(f0 + 2)
                    for ft in range(f0, f1):
                        sl4 = ft % 4
                        if ft + 3 < f1:
                            issue_d_loads(ft + 3)
                        W("tensor", "dg", pend_d.pop(ft))
                        if ft == f0:
                            W("tensor", "vm", v_h)
                            for b in range(8):
                                s, v = self.bank_r[b]
                                W("tensor", s, v)
                        for di in range(8):
                            O("tensor", "matmul", self.psum[di][:, :],
                              dsb[sl4][:, di * 128:di * 128 + 128],
                              hch[:, (ft - f0) * 512:(ft - f0 + 1) * 512],
                              start=(ft == f0), stop=(ft == f1 - 1),
                              inc=(di == 7))
                        ds_t[sl4] = self.semn["tm"]
                    t_dg = self.semn["tm"]
                    for b in range(8):
                        self.bank_w[b] = ("tm", t_dg)
                    W("vector", "tm", t_dg)
                    if blk == 0 and first_chunk and dg_i == 0:
                        W("vector", "vm", v_p4vm)
                        W("vector", "ds", v_p4)
                    for di in range(8):
                        dtile = dg_i * 8 + di
                        if first_chunk:
                            O("vector", "tensor_copy", accum[dtile][:, :],
                              self.psum[di][:, :], inc=True)
                        else:
                            O("vector", "tensor_tensor", accum[dtile][:, :],
                              accum[dtile][:, :], self.psum[di][:, :], ALU.add,
                              inc=True)
                        self.bank_r[di] = ("vm", self.semn["vm"])
            if blk + 1 < NCK:
                nts = slice((blk + 1) * 512, (blk + 2) * 512)
                W("sync", "ds", st4[4 * (blk + 1) + 3])
                W("sync", "tm", t_guplast[blk])
                lx = 0
                for kv in range(NKD):
                    lx = DMA("sync", xlnffn[:, kv * 512:(kv + 1) * 512],
                             xlnb_tm[nts, kv * 128:(kv + 1) * 128],
                             transpose=True)
                ldx_next[0] = lx
            for dtile in range(NKD):
                s = dtile % 2
                if dtile >= 2:
                    W("vector", "ds", st_ffo[(blk, dtile - 2)])
                O("vector", "tensor_scalar", aoev[s][:, :], accum[dtile][:, :],
                  P("ds", dtile), None, ALU.mult, inc=True)
                W("sync", "vm", self.semn["vm"])
                st_ffo[(blk, dtile)] = DMA(
                    "sync", ffo_fm[dtile * 128:(dtile + 1) * 128, tsl],
                    aoev[s][:, :])
        v_ffo_all = self.semn["ds"]
        if "ffod" in dbg_aps:
            W("gpsimd", "ds", v_ffo_all)
            DMA("gpsimd", dbg_aps["ffod"], ffo_fm)

        v_o = {}
        st6 = {}
        for tt in range(NTT):
            sl = tt % 2
            if tt == 0:
                W("sync", "ds", v_ffo_all)
            if tt >= 2:
                W("sync", "vm", v_o[tt - 2])
                W("sync", "ds", st6.get(tt - 2, 0))
            ld_fo = DMA("sync", jnkb4[sl][:, :],
                        ffo_fm[0:D, tt * 128:(tt + 1) * 128], transpose=True)
            ld_x1 = DMA("sync", xtm4[sl][:, :], x1_tm[tt * 128:(tt + 1) * 128, :])
            col = tt % 16
            cc = slice(col, col + 1)
            v_s6, _ = tm_stats(jnkb4[sl][:, :], aotm4[:, :], col, ld_fo)
            W("scalar", "ds", ld_x1)
            W("scalar", "vm", v_s6)
            O("scalar", "activation", xtm4[sl][:, HH:D], xtm4[sl][:, HH:D],
              AF.Identity, bias=nmr[:, cc], scale=1.0, inc=True)
            a_xn6 = self.semn["am"]
            W("vector", "ds", ld_x1)
            O("vector", "tensor_scalar", xtm4[sl][:, 0:HH], xtm4[sl][:, 0:HH],
              nmr[:, cc], None, ALU.add)
            W("vector", "am", a_xn6)
            O("vector", "scalar_tensor_tensor", xtm4[sl][:, :], jnkb4[sl][:, :],
              rst[:, cc], xtm4[sl][:, :], ALU.mult, ALU.add, inc=True)
            v_o[tt] = self.semn["vm"]
            W("sync", "vm", v_o[tt])
            st6[tt] = DMA("sync", out_d[tt * 128:(tt + 1) * 128, :],
                          xtm4[sl][:, :])
        W("sync", "ds", self.semn["ds"])

        with nc.Block() as blk_:
            for meth, eng in ((blk_.sync, "sync"), (blk_.gpsimd, "gpsimd"),
                              (blk_.tensor, "tensor"), (blk_.vector, "vector"),
                              (blk_.scalar, "scalar")):
                ops = self.q[eng]

                def body(e, ops=ops):
                    for f in ops:
                        f(e)
                meth(body)
        return nc


def _host_layout(inputs):
    rs = np.ascontiguousarray

    def pack_chan(a):
        return rs(np.asarray(a).reshape(4, 4, 128).transpose(2, 0, 1)
                  .reshape(128, 16))

    W1 = np.asarray(inputs["L0_attn_W1"])
    W2 = np.asarray(inputs["L0_attn_W2"])
    g = {}
    g["winT"] = rs(np.transpose(np.asarray(inputs["L0_attn_Win"]))
                   .reshape(32, 128, 4, 128).transpose(1, 0, 2, 3)
                   .reshape(128, 32 * 512))
    g["w1h"] = rs(np.stack([np.transpose(W1[l]).reshape(4, 128, 2048)
                            .transpose(1, 0, 2).reshape(128, 4 * 2048)
                            for l in range(4)]))
    g["w2h"] = rs(np.stack([np.transpose(W2[l]).reshape(16, 128, 512)
                            .transpose(1, 0, 2).reshape(128, 16 * 512)
                            for l in range(4)]))
    g["woutT"] = rs(np.transpose(np.asarray(inputs["L0_attn_Wout"]))
                    .reshape(4, 128, 4096).transpose(1, 0, 2)
                    .reshape(128, 4 * 4096))
    def low_bytes(a):
        return rs(rs(a).view(np.int8)[..., 0::4])

    for nm, key in [("gateT", "L0_ffn_gate_w"), ("upT", "L0_ffn_up_w")]:
        w = np.asarray(inputs[key])
        g[nm] = low_bytes(w.reshape(NFT, 128, NKD, 128).transpose(0, 3, 2, 1)
                          .reshape(NFT, 128, NKD * 128))
    g["downT"] = low_bytes(np.transpose(np.asarray(inputs["L0_ffn_down_w"]))
                           .reshape(NFT, 128, NKD * 128))
    pp = {"ln1w": pack_chan(inputs["L0_attn_ln1_w"]),
          "ln1b": pack_chan(inputs["L0_attn_ln1_b"]),
          "ln2w": pack_chan(inputs["L0_attn_ln2_w"]),
          "ln2b": pack_chan(inputs["L0_attn_ln2_b"]),
          "scl": pack_chan(inputs["L0_attn_sc_lin"]),
          "b2": pack_chan(inputs["L0_attn_b2"]),
          "pre": pack_chan(inputs["L0_attn_phz_re"]),
          "pim": pack_chan(inputs["L0_attn_phz_im"]),
          "lcr": pack_chan(inputs["L0_attn_lc_re"]),
          "lci": pack_chan(inputs["L0_attn_lc_im"]),
          "b1": rs(np.asarray(inputs["L0_attn_b1"]).reshape(4, 16, 128)
                   .transpose(2, 0, 1).reshape(128, 64)),
          "bin": rs(np.asarray(inputs["L0_attn_bin"]).reshape(4, 128).T),
          "lnlw": rs(np.asarray(inputs["L0_attn_lnl_w"]).reshape(4, 128).T),
          "lnlb": rs(np.asarray(inputs["L0_attn_lnl_b"]).reshape(4, 128).T),
          "bout": rs(np.asarray(inputs["L0_attn_bout"]).reshape(32, 128).T),
          "gs": rs(np.asarray(inputs["L0_ffn_gate_s"]).reshape(NFT, 128).T),
          "us": rs(np.asarray(inputs["L0_ffn_up_s"]).reshape(NFT, 128).T),
          "ds": rs(np.asarray(inputs["L0_ffn_down_s"]).reshape(32, 128).T)}
    for k, v in pp.items():
        g["pp_" + k] = rs(np.asarray(v, np.float32))
    return g


def kernel(**inputs):
    T = 16384 // NCORE
    shared = _host_layout(inputs)
    x = np.asarray(inputs["L0_attn_x"])[0]
    in_maps = []
    for ci in range(NCORE):
        m = dict(shared)
        m["x_in"] = np.ascontiguousarray(x[ci * T:(ci + 1) * T, :], np.float32)
        m["pp_toff"] = np.full((128, 1), float(ci * T), np.float32)
        msk = np.zeros((128, 8), np.float32)
        msk[:, :7] = (np.arange(7)[None, :] < ci)
        msk[:, 7] = 1.0 if ci == 0 else 0.0
        m["pp_mask"] = msk
        in_maps.append(m)
    kb = KB(T)
    nc = kb.build()
    res = run_bass_kernel_spmd(nc, in_maps, core_ids=list(range(NCORE)))
    out = np.concatenate([np.asarray(r["out"]) for r in res.results], axis=0)
    return out[None, :, :].astype(np.float32)


# version 11
# speedup vs baseline: 1.2246x; 1.0027x over previous
import math

import numpy as np
import ml_dtypes

import concourse.bass as bass
import concourse.mybir as mybir
from concourse.bass_utils import run_bass_kernel_spmd

F32 = mybir.dt.float32
BF16 = mybir.dt.bfloat16
I32 = mybir.dt.int32
AF = mybir.ActivationFunctionType
ALU = mybir.AluOpType
BF = ml_dtypes.bfloat16

NCORE = 8
D = 4096
DI = 512
DFI = 2048
FF = 11008
NFT = FF // 128
NKD = D // 128
HH = D // 2
NLAYER = 4
EPS = 1e-5
FCH = 40

_TP = 2 * math.pi
CW_C1 = float(np.float32(np.ldexp(np.round(np.ldexp(_TP, 8)), -8)))
CW_C2 = float(np.float32(np.ldexp(np.round(np.ldexp(_TP - CW_C1, 19)), -19)))
CW_C3 = float(np.float32(_TP - CW_C1 - CW_C2))
MAGIC = float(np.float32(1.5 * 2.0 ** 23))
INV2PI = float(np.float32(1.0 / _TP))
PI = float(np.float32(math.pi))
PI_2 = float(np.float32(math.pi / 2))

_DEBUG = []


class KB:
    def __init__(self, T):
        self.T = T
        self.NCK = T // 512
        self.NTT = T // 128
        nc = bass.Bass("TRN2", target_bir_lowering=False)
        self.nc = nc
        self.q = {e: [] for e in ("sync", "gpsimd", "tensor", "vector", "scalar")}
        self.semh, self.semn = {}, {}
        for s in ("dg", "ds", "tm", "vm", "am", "cc"):
            self.semh[s] = nc.alloc_semaphore("sem_" + s)
            self.semn[s] = 0
        self.bank_w = [("tm", 0)] * 8
        self.bank_r = [("tm", 0)] * 8
        self.pc_items = []
        self.pc_mark = {}
        self.dumped_stats = False

    def emit(self, eng, fn):
        self.q[eng].append(fn)

    def W(self, eng, sem, val):
        if val:
            self.emit(eng, lambda e, h=self.semh[sem], v=val: e.wait_ge(h, v))

    def bump(self, sem, n):
        self.semn[sem] += n
        return self.semn[sem]

    def DMA(self, eng, dst, src, transpose=False):
        sem = {"sync": "ds", "gpsimd": "dg"}[eng]
        self.bump(sem, 16)
        self.emit(eng, lambda e, d=dst, s=src, h=self.semh[sem], t=transpose:
                  e.dma_start(d, s, transpose=t).then_inc(h, 16))
        return self.semn[sem]

    def O(self, eng, meth, *args, inc=False, drain=False, **kw):
        sem = {"tensor": "tm", "vector": "vm", "scalar": "am", "gpsimd": "vm"}[eng]
        if inc:
            self.bump(sem, 1)
        def f(e, meth=meth, args=args, kw=kw, inc=inc, drain=drain,
              h=self.semh[sem]):
            if drain:
                e.drain()
            r = getattr(e, meth)(*args, **kw)
            if inc:
                r.then_inc(h, 1)
        self.emit(eng, f)
        return self.semn[sem]

    def mm_group(self, bank, pairs, waits=()):
        s, v = self.bank_r[bank]
        self.W("tensor", s, v)
        for (ss, vv) in waits:
            self.W("tensor", ss, vv)
        n = len(pairs)
        for i, (l, r) in enumerate(pairs):
            self.O("tensor", "matmul", self.psum[bank][:, :], l, r,
                   start=(i == 0), stop=(i == n - 1), inc=(i == n - 1))
        self.bank_w[bank] = ("tm", self.semn["tm"])
        return self.semn["tm"]

    def evac(self, bank, eng, emit_cb, waits=()):
        s, v = self.bank_w[bank]
        self.W(eng, s, v)
        for (ss, vv) in waits:
            self.W(eng, ss, vv)
        emit_cb()
        sem = {"vector": "vm", "scalar": "am"}[eng]
        self.bank_r[bank] = (sem, self.semn[sem])
        return self.semn[sem]

    def build(self):
        nc, T, NCK, NTT = self.nc, self.T, self.NCK, self.NTT
        W, O, DMA, E = self.W, self.O, self.DMA, self.emit
        dt = nc.dram_tensor

        x_in = dt("x_in", [T, D], F32, kind="ExternalInput").ap()
        winT = dt("winT", [128, NKD * DI], F32, kind="ExternalInput").ap()
        w1h = dt("w1h", [NLAYER, 128, 4 * DFI], F32, kind="ExternalInput").ap()
        w2h = dt("w2h", [NLAYER, 128, 16 * DI], F32, kind="ExternalInput").ap()
        woutT = dt("woutT", [128, 4 * D], F32, kind="ExternalInput").ap()
        gateT = dt("gateT", [NFT, 128, D], mybir.dt.int8, kind="ExternalInput").ap()
        upT = dt("upT", [NFT, 128, D], mybir.dt.int8, kind="ExternalInput").ap()
        downT = dt("downT", [NFT, 128, D], mybir.dt.int8, kind="ExternalInput").ap()
        PCOLS = [("ln1w", 16), ("ln1b", 16), ("ln2w", 16), ("ln2b", 16),
                 ("scl", 16), ("b2", 16), ("pre", 16), ("pim", 16),
                 ("lcr", 16), ("lci", 16), ("b1", 64), ("bin", 4),
                 ("lnlw", 4), ("lnlb", 4), ("bout", 32), ("gs", NFT),
                 ("us", NFT), ("ds", 32), ("toff", 1), ("mask", 8)]
        pin = {nm: dt("pp_" + nm, [128, n], F32, kind="ExternalInput").ap()
               for nm, n in PCOLS}
        out_d = dt("out", [T, D], F32, kind="ExternalOutput").ap()
        dbg_aps = {nm: dt("dbg_" + nm, list(sh), F32, kind="ExternalOutput").ap()
                   for nm, sh in _DEBUG}

        xlna_tm = dt("xlna_tm", [T, D], BF16, kind="Internal").ap()
        ao_fm = dt("ao_fm", [D, T], BF16, kind="Internal").ap()
        x1_tm = dt("x1_tm", [T, D], F32, kind="Internal").ap()
        xlnb_tm = dt("xlnb_tm", [T, D], BF16, kind="Internal").ap()
        ffo_fm = dt("ffo_fm", [D, T], BF16, kind="Internal").ap()
        ncc = NLAYER
        cc_i = [dt(f"cc_i{k}", [128, 8], F32, kind="Internal").ap() for k in range(ncc)]
        cc_o = [dt(f"cc_o{k}", [NCORE * 128, 8], F32, kind="Internal").ap()
                for k in range(ncc)]

        pos = [16384]

        def place(name, shape, dtype):
            nb = int(np.prod(shape[1:])) * (4 if dtype != BF16 else 2)
            at = pos[0]
            pos[0] = at + ((nb + 63) // 64) * 64
            return nc.alloc_sbuf_tensor_at(name, list(shape), dtype, offset=at), at

        def over(name, shape, dtype, at):
            return nc.alloc_sbuf_tensor_at(name, list(shape), dtype, offset=at)

        TB = T * 4
        h_t = []
        H0 = pos[0]
        for c in range(4):
            h_t.append(place(f"h{c}", [128, T], F32)[0])
        tglob, TG0 = place("tglob", [128, T], F32)
        params, _ = place("params", [128, 1024], F32)
        statsf, _ = place("statsf", [128, 64], F32)
        var_scr, VS0 = place("var_scr", [1, T], F32)
        ccg, _ = place("ccg", [128, 64], F32)
        czt, _ = place("czt", [128, 8], F32)
        dpk2, _ = place("dpk2", [128, 8], F32)
        ccd, _ = place("ccd", [128, 64], F32)
        ones_bf, _ = place("ones_bf", [128, 1], BF16)
        onesr_f, _ = place("onesr_f", [1, 128], F32)
        nonesr_f, _ = place("nonesr_f", [1, 128], F32)
        c_pi2, _ = place("c_pi2", [128, 1], F32)
        c_eps, _ = place("c_eps", [128, 1], F32)
        c_zero, _ = place("c_zero", [128, 1], F32)
        state_pk, _ = place("state_pk", [128, 8], F32)
        STG0 = pos[0]
        stg = [place(f"stg{s}", [128, D], BF16)[0] for s in range(2)]
        aoev = [place(f"aoev{s}", [128, 512], BF16)[0] for s in range(2)]

        SCAN0 = pos[0]
        scn = [place(f"scn{i}", [128, T], F32)[0] for i in range(4)]
        SC_COS = [scn[0], scn[2]]
        SC_SIN = [scn[1], scn[3]]
        if pos[0] - SCAN0 < 32768:
            pos[0] = SCAN0 + 32768
        WA0 = pos[0]
        w1sb, _ = place("w1sb", [128, 4 * DFI], BF16)
        w2sb, _ = place("w2sb", [128, 16 * DI], BF16)
        MM0 = pos[0]
        yb = [place(f"yb{c}", [128, T], BF16)[0] for c in range(4)]
        hb = [place(f"hb{s}", [128, T], BF16)[0] for s in range(2)]
        ffn1, _ = place("ffn1", [128, 16 * 512], BF16)
        sc1, SC1_0 = place("sc1", [128, T], F32)
        sc2, SC2_0 = place("sc2", [128, T], F32)
        if pos[0] - MM0 < 49152:
            pos[0] = MM0 + 49152
        END0 = pos[0]
        assert END0 <= 229376, f"sbuf static overflow: {END0}"
        SCAN_B = WA0 - SCAN0
        MM_B = END0 - MM0

        xtm = [over(f"xtm{i}", [128, D], F32, SCAN0 + i * 16384) for i in range(2)]
        xtm4 = [over(f"xtm4_{i}", [128, D], F32, H0 + i * 16384) for i in range(2)]
        aotm4 = over("aotm4", [128, D], BF16, TG0)
        jnkb4 = [over(f"jnkb4_{i}", [128, D], BF16, STG0 + i * 8192)
                 for i in range(2)]
        assert SCAN_B >= 2 * 16384
        aotm = over("aotm", [128, D], BF16, MM0)
        jnkb = over("jnkb", [128, D], BF16, MM0 + 8192)
        winsb = over("winsb", [128, NKD * DI], BF16, MM0 + 16384)
        xlnw = over("xlnw", [128, NKD * 512], BF16, SCAN0)
        woutsb = over("woutsb", [128, 4 * D], BF16, SCAN0)
        xlnffn = over("xlnffn", [128, NKD * 512], BF16, SCAN0)
        gusb = [over(f"gusb{s}", [128, D], BF16, WA0 + s * 8192) for s in range(4)]
        assert WA0 + 4 * 8192 <= END0 + 4096
        if T * 4 >= 8192:
            dsb = [over(f"dsb{s}", [128, 1024], BF16, VS0 + s * 2048)
                   for s in range(4)]
        else:
            dsb = [place(f"dsb{s}", [128, 1024], BF16)[0] for s in range(4)]
        hch = over("hch", [128, FCH * 512], BF16, MM0)
        if T == 2048:
            acc_off = []
            for i in range(5 * TB // 2048):
                acc_off.append(H0 + i * 2048)
            need = NKD - len(acc_off)
            for i in range(need):
                acc_off.append(MM0 + FCH * 1024 + i * 2048)
            assert MM0 + FCH * 1024 + need * 2048 <= 229376
            accum = [over(f"acc{i}", [128, 512], F32, acc_off[i])
                     for i in range(NKD)]
            pos[0] = max(pos[0], MM0 + FCH * 1024 + need * 2048)
        else:
            acc_off = [H0 + i * 2048 for i in range(5 * TB // 2048)]
            acc_off += [MM0 + FCH * 1024 + i * 2048
                        for i in range((49152 - FCH * 1024) // 2048)]
            accum = [over(f"acc{i}", [128, 512], F32, acc_off[i])
                     if i < len(acc_off)
                     else place(f"acc{i}", [128, 512], F32)[0]
                     for i in range(NKD)]
            assert pos[0] <= 229376, pos[0]
        gsil = [place(f"gsil{s}", [128, 512], BF16)[0] for s in range(2)]
        assert pos[0] <= 229376, pos[0]

        mu_row = over("mu_row", [1, T], F32, SC1_0)
        sq_row = over("sq_row", [1, T], F32, SC2_0)
        self.psum = [nc.alloc_psum_tensor(f"ps{i}", [128, 512], F32)
                     for i in range(8)]

        PC = {}
        pcn = [0]
        for nm, n in PCOLS + [("rho", 16), ("lnrho", 16), ("theta", 16),
                              ("cth", 16), ("sth", 16), ("ainit", 16),
                              ("binit", 16), ("dpk", 16), ("pb", 16),
                              ("t1", 16), ("t2", 16), ("t3", 16), ("t4", 16),
                              ("u1", 1)]:
            PC[nm] = pcn[0]
            pcn[0] += n
        assert pcn[0] <= 1024

        def P(nm, i=0, n=1):
            c = PC[nm] + i
            return params[:, c:c + n]

        def A16(nm):
            return P(nm, 0, 16)

        for nm, n in PCOLS:
            DMA("sync", P(nm, 0, n), pin[nm])
        v_pld = self.semn["ds"]

        O("vector", "memset", c_pi2[:, :], PI_2)
        O("vector", "memset", c_eps[:, :], float(EPS))
        O("vector", "memset", c_zero[:, :], 0.0)
        O("vector", "memset", ones_bf[:, :], 1.0 / DI)
        O("vector", "memset", onesr_f[:, :], 1.0)
        O("vector", "memset", nonesr_f[:, :], -1.0, inc=True)
        O("gpsimd", "iota", tglob[:, :], [[1, T]], base=0, channel_multiplier=0,
          allow_small_or_imprecise_dtypes=True, inc=True)
        W("vector", "vm", self.semn["vm"])
        W("vector", "ds", v_pld)
        O("vector", "tensor_scalar", tglob[:, :], tglob[:, :], P("toff"), None,
          ALU.add, ALU.bypass, inc=True)

        W("scalar", "ds", v_pld)
        W("scalar", "vm", self.semn["vm"])
        O("scalar", "activation", A16("t1"), A16("pre"), AF.Square, bias=c_zero[:, 0:1])
        O("scalar", "activation", A16("t2"), A16("pim"), AF.Square, drain=True,
          inc=True, bias=c_zero[:, 0:1])
        W("vector", "am", self.semn["am"])
        O("vector", "tensor_tensor", A16("t1"), A16("t1"), A16("t2"), ALU.add,
          drain=True, inc=True)
        W("scalar", "vm", self.semn["vm"])
        O("scalar", "activation", A16("t2"), A16("t1"), AF.Sqrt, drain=True,
          inc=True, bias=c_zero[:, 0:1])
        W("vector", "am", self.semn["am"])
        O("vector", "reciprocal", A16("t3"), A16("t2"), drain=True)
        O("vector", "tensor_tensor", A16("t3"), A16("t3"), A16("t1"), ALU.mult,
          drain=True)
        O("vector", "tensor_tensor", A16("t3"), A16("t3"), A16("t2"), ALU.add,
          drain=True)
        O("vector", "tensor_scalar", A16("t3"), A16("t3"), 0.5, None, ALU.mult,
          drain=True, inc=True)
        W("scalar", "vm", self.semn["vm"])
        O("scalar", "activation", A16("rho"), A16("t3"), AF.Tanh, drain=True,
          inc=True, bias=c_zero[:, 0:1])
        W("vector", "am", self.semn["am"])
        O("vector", "tensor_scalar", A16("t4"), A16("rho"), 1e-36, None,
          ALU.max, drain=True, inc=True)
        W("scalar", "vm", self.semn["vm"])
        O("scalar", "activation", A16("lnrho"), A16("t4"), AF.Ln, drain=True, bias=c_zero[:, 0:1])
        O("scalar", "activation", A16("dpk"), A16("lnrho"), AF.Exp,
          scale=float(T), drain=True, bias=c_zero[:, 0:1])
        O("scalar", "activation", A16("t1"), A16("pre"), AF.Abs, drain=True, bias=c_zero[:, 0:1])
        O("scalar", "activation", A16("t2"), A16("pim"), AF.Abs, drain=True,
          inc=True, bias=c_zero[:, 0:1])
        W("vector", "am", self.semn["am"])
        O("vector", "tensor_tensor", A16("t3"), A16("t2"), A16("t1"), ALU.is_gt,
          drain=True)
        O("vector", "tensor_tensor", A16("t4"), A16("t1"), A16("t2"), ALU.min,
          drain=True)
        O("vector", "tensor_tensor", A16("t2"), A16("t1"), A16("t2"), ALU.max,
          drain=True)
        O("vector", "tensor_scalar", A16("t2"), A16("t2"), 1e-30, None, ALU.max,
          drain=True)
        O("vector", "reciprocal", A16("t1"), A16("t2"), drain=True)
        O("vector", "tensor_tensor", A16("t1"), A16("t1"), A16("t4"), ALU.mult,
          drain=True, inc=True)
        W("scalar", "vm", self.semn["vm"])
        O("scalar", "activation", A16("t4"), A16("t1"), AF.Arctan, drain=True,
          inc=True, bias=c_zero[:, 0:1])
        W("vector", "am", self.semn["am"])
        O("vector", "tensor_scalar", A16("t1"), A16("t4"), -2.0, PI_2, ALU.mult,
          ALU.add, drain=True)
        O("vector", "tensor_tensor", A16("t1"), A16("t1"), A16("t3"), ALU.mult,
          drain=True)
        O("vector", "tensor_tensor", A16("t4"), A16("t4"), A16("t1"), ALU.add,
          drain=True)
        O("vector", "tensor_scalar", A16("t3"), A16("pre"), 0.0, None, ALU.is_lt,
          drain=True)
        O("vector", "tensor_scalar", A16("t1"), A16("t4"), -2.0, PI, ALU.mult,
          ALU.add, drain=True)
        O("vector", "tensor_tensor", A16("t1"), A16("t1"), A16("t3"), ALU.mult,
          drain=True)
        O("vector", "tensor_tensor", A16("t4"), A16("t4"), A16("t1"), ALU.add,
          drain=True)
        O("vector", "tensor_scalar", A16("t1"), A16("pim"), 0.0, None, ALU.is_gt,
          drain=True)
        O("vector", "tensor_scalar", A16("t2"), A16("pim"), 0.0, None, ALU.is_lt,
          drain=True)
        O("vector", "tensor_tensor", A16("t1"), A16("t1"), A16("t2"),
          ALU.subtract, drain=True)
        O("vector", "tensor_tensor", A16("theta"), A16("t4"), A16("t1"),
          ALU.mult, drain=True, inc=True)
        W("scalar", "vm", self.semn["vm"])
        O("scalar", "activation", A16("sth"), A16("theta"), AF.Sin, drain=True, bias=c_zero[:, 0:1])
        O("scalar", "activation", A16("t1"), A16("theta"), AF.Abs, drain=True, bias=c_zero[:, 0:1])
        O("scalar", "activation", A16("cth"), A16("t1"), AF.Sin, scale=-1.0,
          bias=c_pi2[:, 0:1], drain=True, inc=True)
        W("vector", "am", self.semn["am"])
        O("vector", "tensor_tensor", A16("t1"), A16("cth"), A16("lcr"), ALU.mult,
          drain=True)
        O("vector", "tensor_tensor", A16("t2"), A16("sth"), A16("lci"), ALU.mult,
          drain=True)
        O("vector", "tensor_tensor", A16("ainit"), A16("t1"), A16("t2"),
          ALU.subtract, drain=True)
        O("vector", "tensor_tensor", A16("t1"), A16("sth"), A16("lcr"), ALU.mult,
          drain=True)
        O("vector", "tensor_tensor", A16("t2"), A16("cth"), A16("lci"), ALU.mult,
          drain=True)
        O("vector", "tensor_tensor", A16("binit"), A16("t1"), A16("t2"), ALU.add,
          drain=True)
        O("vector", "tensor_scalar", A16("binit"), A16("binit"), -1.0, None,
          ALU.mult, drain=True)
        O("vector", "tensor_scalar", A16("ainit"), A16("ainit"), P("mask", 7),
          None, ALU.mult, drain=True)
        O("vector", "tensor_scalar", A16("binit"), A16("binit"), P("mask", 7),
          None, ALU.mult, drain=True)
        O("vector", "tensor_scalar", P("u1"), P("toff"), -1.0, 1.0, ALU.mult,
          ALU.add, drain=True)
        O("vector", "tensor_scalar", A16("pb"), A16("lnrho"), P("u1"), None,
          ALU.mult, drain=True)
        O("vector", "engine_nop", inc=True)
        v_p0 = self.semn["vm"]
        if "theta" in dbg_aps:
            W("sync", "vm", v_p0)
            DMA("sync", dbg_aps["theta"], A16("theta"))
            DMA("sync", dbg_aps["rho"], A16("rho"))
            DMA("sync", dbg_aps["ainit"], A16("ainit"))
            DMA("sync", dbg_aps["lnrho"], A16("lnrho"))
            DMA("sync", dbg_aps["tg"], tglob[:, :])

        g_win = DMA("gpsimd", winsb[:, :], winT)
        g_w1 = {0: DMA("gpsimd", w1sb[:, :], w1h[0])}
        g_w2 = {0: DMA("gpsimd", w2sb[:, :], w2h[0])}

        def drain_pc(n):
            pass

        musum, sqsum = statsf[:, 0:16], statsf[:, 16:32]
        nmr, rst = statsf[:, 32:48], statsf[:, 48:64]

        def tm_stats(src_ap, junk_ap, col, ld, vwaits=()):
            c = slice(col, col + 1)
            W("vector", "ds", ld)
            for (ss, vv) in vwaits:
                W("vector", ss, vv)
            O("vector", "tensor_scalar", junk_ap, src_ap, 1.0, 0.0, ALU.mult,
              ALU.add, accum_out=musum[:, c], inc=True)
            W("scalar", "ds", ld)
            for (ss, vv) in vwaits:
                W("scalar", ss, vv)
            O("scalar", "activation", junk_ap, src_ap, AF.Square,
              accum_out=sqsum[:, c], inc=True, bias=c_zero[:, 0:1])
            a_sq = self.semn["am"]
            W("vector", "am", a_sq)
            O("vector", "tensor_scalar", musum[:, c], musum[:, c], 1.0 / D, None,
              ALU.mult, drain=True)
            O("vector", "tensor_scalar", sqsum[:, c], sqsum[:, c], 1.0 / D, None,
              ALU.mult, drain=True)
            O("vector", "tensor_tensor", rst[:, c], musum[:, c], musum[:, c],
              ALU.mult, drain=True)
            O("vector", "tensor_tensor", rst[:, c], sqsum[:, c], rst[:, c],
              ALU.subtract, drain=True, inc=True)
            W("scalar", "vm", self.semn["vm"])
            O("scalar", "activation", rst[:, c], rst[:, c], AF.Ln,
              bias=c_eps[:, 0:1], drain=True)
            O("scalar", "activation", rst[:, c], rst[:, c], AF.Exp, scale=-0.5,
              drain=True, inc=True, bias=c_zero[:, 0:1])
            a_r = self.semn["am"]
            W("vector", "am", a_r)
            O("vector", "tensor_tensor", nmr[:, c], musum[:, c], rst[:, c],
              ALU.mult, drain=True)
            O("vector", "tensor_scalar", nmr[:, c], nmr[:, c], -1.0, None,
              ALU.mult, drain=True, inc=True)
            return self.semn["vm"], a_r

        v_used, st_done = {}, {}
        a_used = {}
        for tt in range(NTT):
            sl = tt % 2
            if tt >= 2:
                W("sync", "vm", v_used[tt - 2])
                W("sync", "am", a_used[tt - 2])
            ld = DMA("sync", xtm[sl][:, :], x_in[tt * 128:(tt + 1) * 128, :])
            if tt == 0:
                W("vector", "ds", ld)
                W("vector", "vm", v_p0)
            col = tt % 16
            v_s, a_s = tm_stats(xtm[sl][:, :], jnkb[:, :], col, ld)
            a_used[tt] = a_s
            if tt >= 1:
                W("vector", "ds", st_done[tt - 1])
            cc = slice(col, col + 1)
            O("vector", "tensor_scalar", aotm[:, :], xtm[sl][:, :], rst[:, cc],
              nmr[:, cc], ALU.mult, ALU.add, inc=True)
            v_x = self.semn["vm"]
            v_used[tt] = v_x
            W("sync", "vm", v_x)
            st_done[tt] = DMA("sync", xlna_tm[tt * 128:(tt + 1) * 128, :],
                              aotm[:, :])
            if tt % 4 == 3:
                drain_pc(4)

        xb_mm = {}
        a_p2 = 0
        for k in range(NCK):
            W("sync", "dg", g_win)
            W("sync", "ds", st_done[4 * k + 3])
            if k >= 1:
                W("sync", "tm", xb_mm[k - 1])
            for kv in range(NKD):
                ldv = DMA("sync", xlnw[:, kv * 512:(kv + 1) * 512],
                          xlna_tm[k * 512:(k + 1) * 512,
                                  kv * 128:(kv + 1) * 128], transpose=True)
                W("tensor", "ds", ldv)
                if kv == 0:
                    for b in range(4):
                        s, v = self.bank_r[b]
                        W("tensor", s, v)
                for dtile in range(4):
                    O("tensor", "matmul", self.psum[dtile][:, :],
                      winsb[:, kv * 512 + dtile * 128:
                            kv * 512 + dtile * 128 + 128],
                      xlnw[:, kv * 512:(kv + 1) * 512],
                      start=(kv == 0), stop=(kv == NKD - 1),
                      inc=(kv == NKD - 1 and dtile == 3))
            t_mm = self.semn["tm"]
            for b in range(4):
                self.bank_w[b] = ("tm", t_mm)
            for dtile in range(4):
                a_p2 = self.evac(dtile, "scalar", lambda b=dtile, k=k:
                                 O("scalar", "activation",
                                   h_t[b][:, k * 512:(k + 1) * 512],
                                   self.psum[b][:, :], AF.Identity,
                                   bias=P("bin", b), inc=True))
            xb_mm[k] = t_mm
            drain_pc(6)
        if "h_win" in dbg_aps:
            W("sync", "am", a_p2)
            for c in range(4):
                DMA("sync", dbg_aps["h_win"][c * 128:(c + 1) * 128, :],
                    h_t[c][:, :])

        def inner_ln_stats(extra_waits):
            t_c = {}
            for c in range(4):
                sl = c % 2
                if c == 0:
                    for (ss, vv) in extra_waits:
                        W("vector", ss, vv)
                if c >= 2:
                    W("vector", "tm", t_c[c - 2])
                O("vector", "tensor_copy", hb[sl][:, :], h_t[c][:, :], inc=True)
                v_cast = self.semn["vm"]
                W("scalar", "vm", v_cast)
                if c == 0:
                    for (ss, vv) in extra_waits:
                        W("scalar", ss, vv)
                else:
                    W("scalar", "tm", t_c[c - 1])
                O("scalar", "activation", ffn1[:, 0:T], hb[sl][:, :], AF.Square,
                  inc=True, bias=c_zero[:, 0:1])
                a_sqc = self.semn["am"]
                W("tensor", "vm", v_cast)
                W("tensor", "am", a_sqc)
                if c == 0:
                    for b in range(8):
                        s, v = self.bank_r[b]
                        W("tensor", s, v)
                for k in range(NCK):
                    O("tensor", "matmul", self.psum[k][0:1, :], ones_bf[:, :],
                      hb[sl][:, k * 512:(k + 1) * 512],
                      start=(c == 0), stop=(c == 3))
                for k in range(NCK):
                    O("tensor", "matmul", self.psum[4 + k][0:1, :],
                      ones_bf[:, :], ffn1[:, k * 512:(k + 1) * 512],
                      start=(c == 0), stop=(c == 3),
                      inc=(k == NCK - 1))
                t_c[c] = self.semn["tm"]
            t_stats = self.semn["tm"]
            for b in range(8):
                self.bank_w[b] = ("tm", t_stats)
            W("vector", "tm", t_stats)
            for k in range(NCK):
                O("vector", "tensor_copy", mu_row[0:1, k * 512:(k + 1) * 512],
                  self.psum[k][0:1, :])
                O("vector", "tensor_copy", sq_row[0:1, k * 512:(k + 1) * 512],
                  self.psum[4 + k][0:1, :])
            O("vector", "engine_nop", inc=True)
            v_ev = self.semn["vm"]
            for b in range(8):
                self.bank_r[b] = ("vm", v_ev)
            O("vector", "tensor_tensor", var_scr[0:1, 0:T], mu_row[0:1, 0:T],
              mu_row[0:1, 0:T], ALU.mult)
            O("vector", "tensor_tensor", sq_row[0:1, 0:T], sq_row[0:1, 0:T],
              var_scr[0:1, 0:T], ALU.subtract, inc=True)
            W("scalar", "vm", self.semn["vm"])
            O("scalar", "activation", sq_row[0:1, 0:T], sq_row[0:1, 0:T], AF.Ln,
              bias=c_eps[0:1, 0:1])
            O("scalar", "activation", sq_row[0:1, 0:T], sq_row[0:1, 0:T], AF.Exp,
              scale=-0.5, inc=True, bias=c_zero[0:1, 0:1])
            W("vector", "am", self.semn["am"])
            O("vector", "tensor_tensor", mu_row[0:1, 0:T], mu_row[0:1, 0:T],
              sq_row[0:1, 0:T], ALU.mult, inc=True)
            v_rr = self.semn["vm"]
            if "murow" in dbg_aps and not self.dumped_stats:
                self.dumped_stats = True
                W("sync", "vm", v_rr)
                dmp = DMA("sync", dbg_aps["murow"], mu_row[0:1, 0:T])
                dmp = DMA("sync", dbg_aps["sqrow"], sq_row[0:1, 0:T])
                W("vector", "ds", dmp)
            for k in range(NCK):
                self.mm_group(k, [(nonesr_f[:, :],
                                   mu_row[0:1, k * 512:(k + 1) * 512])],
                              waits=[("vm", v_rr)])
                self.mm_group(4 + k, [(onesr_f[:, :],
                                       sq_row[0:1, k * 512:(k + 1) * 512])])

        def normalize_to_sc1(c):
            for k in range(NCK):
                for b in (k, 4 + k):
                    s, v = self.bank_w[b]
                    W("vector", s, v)
                sl5 = slice(k * 512, (k + 1) * 512)
                O("vector", "tensor_tensor", sc1[:, sl5], h_t[c][:, sl5],
                  self.psum[4 + k][:, :], ALU.mult)
                O("vector", "tensor_tensor", sc1[:, sl5], sc1[:, sl5],
                  self.psum[k][:, :], ALU.add, inc=(k == NCK - 1))
            v_n = self.semn["vm"]
            for k in range(NCK):
                self.bank_r[k] = ("vm", v_n)
                self.bank_r[4 + k] = ("vm", v_n)
            return v_n

        def corr_combine(g_rd):
            W("vector", "dg", g_rd)
            O("vector", "memset", czt[:, :], 0.0)
            for j in range(7):
                O("vector", "tensor_tensor", czt[:, :], czt[:, :],
                  ccd[:, j * 8:j * 8 + 8], ALU.mult, drain=True)
                O("vector", "scalar_tensor_tensor", czt[:, :],
                  ccg[:, j * 8:j * 8 + 8], P("mask", j), czt[:, :],
                  ALU.mult, ALU.add, drain=True)
            O("vector", "engine_nop", drain=True, inc=True)
            return self.semn["vm"]

        def apply_corr(l, c, slot, recompute):
            dcol = l * 4 + c
            cosb, sinb = SC_COS[slot][:, :], SC_SIN[slot][:, :]
            if recompute:
                O("vector", "tensor_scalar", cosb, tglob[:, :],
                  P("theta", dcol), None, ALU.mult)
                O("vector", "tensor_scalar", sinb, cosb, INV2PI, MAGIC,
                  ALU.mult, ALU.add)
                O("vector", "tensor_scalar", sinb, sinb, -MAGIC, None, ALU.add)
                O("vector", "scalar_tensor_tensor", cosb, sinb, -CW_C1, cosb,
                  ALU.mult, ALU.add)
                O("vector", "scalar_tensor_tensor", cosb, sinb, -CW_C2, cosb,
                  ALU.mult, ALU.add)
                O("vector", "scalar_tensor_tensor", cosb, sinb, -CW_C3, cosb,
                  ALU.mult, ALU.add)
                O("vector", "tensor_scalar", cosb, cosb, PI, -PI, ALU.min,
                  ALU.max, inc=True)
                W("scalar", "vm", self.semn["vm"])
                O("scalar", "activation", sinb, cosb, AF.Sin,
                  bias=c_zero[:, 0:1])
                O("scalar", "activation", sc2[:, :], cosb, AF.Abs,
                  bias=c_zero[:, 0:1])
                O("scalar", "activation", cosb, sc2[:, :], AF.Sin, scale=-1.0,
                  bias=c_pi2[:, 0:1], inc=True)
                W("vector", "am", self.semn["am"])
            W("scalar", "vm", self.semn["vm"])
            O("scalar", "activation", sc2[:, :], tglob[:, :], AF.Exp,
              scale=P("lnrho", dcol), bias=P("pb", dcol), inc=True)
            a_rp = self.semn["am"]
            W("vector", "am", a_rp)
            O("vector", "tensor_scalar", sc1[:, :], cosb,
              czt[:, 2 * c:2 * c + 1], None, ALU.mult)
            O("vector", "scalar_tensor_tensor", sc1[:, :], sinb,
              czt[:, 2 * c + 1:2 * c + 2], sc1[:, :], ALU.mult, ALU.add)
            O("vector", "tensor_tensor", sc1[:, :], sc1[:, :], sc2[:, :],
              ALU.mult)
            O("vector", "scalar_tensor_tensor", h_t[c][:, :], sc1[:, :],
              P("scl", dcol), h_t[c][:, :], ALU.mult, ALU.add, inc=True)
            return self.semn["vm"]

        t_w2_prev = [xb_mm[NCK - 1]]
        a_prev = [a_p2]
        for l in range(NLAYER):
            inner_ln_stats([("am", a_prev[0]), ("tm", t_w2_prev[0])])
            W("vector", "vm", v_p0)
            for c4 in range(4):
                O("vector", "tensor_copy", dpk2[:, 2 * c4:2 * c4 + 1],
                  P("dpk", l * 4 + c4), drain=True)
                O("vector", "tensor_copy", dpk2[:, 2 * c4 + 1:2 * c4 + 2],
                  P("dpk", l * 4 + c4), drain=True)
            for j in range(7):
                O("vector", "tensor_scalar", ccd[:, j * 8:j * 8 + 8],
                  dpk2[:, :], P("mask", j), None, ALU.mult, drain=True)
                O("vector", "tensor_scalar", P("u1"), P("mask", j), -1.0, 1.0,
                  ALU.mult, ALU.add, drain=True)
                O("vector", "tensor_scalar", ccd[:, j * 8:j * 8 + 8],
                  ccd[:, j * 8:j * 8 + 8], P("u1"), None, ALU.add, drain=True)
            O("vector", "engine_nop", inc=True)

            for c in range(4):
                slot = c % 2
                dcol = l * 4 + c
                v_n = normalize_to_sc1(c)
                W("scalar", "vm", v_n)
                O("scalar", "activation", sc1[:, :], sc1[:, :], AF.Silu,
                  scale=P("ln1w", dcol), bias=P("ln1b", dcol), inc=True)
                a_yt = self.semn["am"]
                cosb = SC_COS[slot][:, :]
                sinb = SC_SIN[slot][:, :]
                W("vector", "am", a_yt)
                O("vector", "tensor_scalar", cosb, tglob[:, :],
                  P("theta", dcol), None, ALU.mult, inc=True)
                if l == 0 and c == 0 and "v00" in dbg_aps:
                    W("sync", "vm", self.semn["vm"])
                    dmp = DMA("sync", dbg_aps["v00"], cosb)
                    W("vector", "ds", dmp)
                O("vector", "tensor_scalar", sinb, cosb, INV2PI, MAGIC,
                  ALU.mult, ALU.add)
                O("vector", "tensor_scalar", sinb, sinb, -MAGIC, None, ALU.add)
                O("vector", "scalar_tensor_tensor", cosb, sinb, -CW_C1, cosb,
                  ALU.mult, ALU.add)
                O("vector", "scalar_tensor_tensor", cosb, sinb, -CW_C2, cosb,
                  ALU.mult, ALU.add)
                O("vector", "scalar_tensor_tensor", cosb, sinb, -CW_C3, cosb,
                  ALU.mult, ALU.add)
                O("vector", "tensor_scalar", cosb, cosb, PI, -PI, ALU.min,
                  ALU.max, inc=True)
                if l == 0 and c == 0 and "r00" in dbg_aps:
                    W("sync", "vm", self.semn["vm"])
                    dmp = DMA("sync", dbg_aps["r00"], cosb)
                    W("scalar", "ds", dmp)
                W("scalar", "vm", self.semn["vm"])
                O("scalar", "activation", sinb, cosb, AF.Sin, bias=c_zero[:, 0:1])
                O("scalar", "activation", sc2[:, :], cosb, AF.Abs, bias=c_zero[:, 0:1])
                O("scalar", "activation", cosb, sc2[:, :], AF.Sin, scale=-1.0,
                  bias=c_pi2[:, 0:1], inc=True)
                a_cs = self.semn["am"]
                if l == 0 and c == 0 and "yt00" in dbg_aps:
                    W("sync", "am", a_yt)
                    dmp = DMA("sync", dbg_aps["yt00"], sc1[:, :])
                    W("vector", "ds", dmp)
                if l == 0 and c == 0 and "cos00" in dbg_aps:
                    W("sync", "am", a_cs)
                    dmp = DMA("sync", dbg_aps["cos00"], cosb)
                    dmp = DMA("sync", dbg_aps["sin00"], sinb)
                    W("vector", "ds", dmp)
                    W("scalar", "ds", dmp)
                W("vector", "am", a_cs)
                rho_b = P("rho", dcol).broadcast_to((128, T))
                O("vector", "tensor_tensor", sc2[:, :], cosb, sc1[:, :],
                  ALU.mult)
                O("vector", "tensor_tensor_scan", sc2[:, :], rho_b, sc2[:, :],
                  P("ainit", dcol), ALU.mult, ALU.add)
                O("vector", "tensor_tensor", sc1[:, :], sinb, sc1[:, :],
                  ALU.mult)
                O("vector", "tensor_tensor_scan", sc1[:, :], rho_b, sc1[:, :],
                  P("binit", dcol), ALU.mult, ALU.add, inc=True)
                O("vector", "drain")
                if c == 0:
                    W("vector", "dg", getattr(self, "last_g_st", 0))
                O("vector", "tensor_copy", state_pk[:, 2 * c:2 * c + 1],
                  sc2[:, T - 1:T])
                O("vector", "tensor_copy", state_pk[:, 2 * c + 1:2 * c + 2],
                  sc1[:, T - 1:T])
                O("vector", "engine_nop", drain=True, inc=True)
                v_st = self.semn["vm"]
                if l == 0 and c == 0 and "A00" in dbg_aps:
                    W("sync", "vm", v_st)
                    dmp = DMA("sync", dbg_aps["A00"], sc2[:, :])
                    dmp = DMA("sync", dbg_aps["B00"], sc1[:, :])
                    W("vector", "ds", dmp)
                O("vector", "tensor_tensor", sc2[:, :], cosb, sc2[:, :],
                  ALU.mult)
                O("vector", "tensor_tensor", sc1[:, :], sinb, sc1[:, :],
                  ALU.mult)
                O("vector", "tensor_tensor", sc1[:, :], sc1[:, :], sc2[:, :],
                  ALU.add)
                O("vector", "scalar_tensor_tensor", h_t[c][:, :], sc1[:, :],
                  P("scl", dcol), h_t[c][:, :], ALU.mult, ALU.add, inc=True)
                drain_pc(8)
            W("gpsimd", "vm", v_st)
            g_st = DMA("gpsimd", cc_i[l], state_pk[:, :])
            self.last_g_st = g_st
            W("gpsimd", "dg", g_st)
            self.bump("cc", 1)
            E("gpsimd", lambda e, i=l, h=self.semh["cc"]:
              e.collective_compute("AllGather", ALU.bypass,
                                   [list(range(NCORE))],
                                   ins=[cc_i[i].opt()],
                                   outs=[cc_o[i].opt()]).then_inc(h, 1))
            W("gpsimd", "cc", self.semn["cc"])
            g_rd = DMA("gpsimd",
                       ccg[:, :].rearrange("p (r c) -> p r c", r=NCORE),
                       cc_o[l].rearrange("(r p) c -> p r c", p=128))
            corr_combine(g_rd)
            apply_corr(l, 2, 0, False)
            apply_corr(l, 3, 1, False)
            apply_corr(l, 0, 0, True)
            apply_corr(l, 1, 1, True)
            v_mix = self.semn["vm"]

            inner_ln_stats([("vm", v_mix)])
            W("vector", "tm", t_w2_prev[0])
            for c in range(4):
                dcol = l * 4 + c
                normalize_to_sc1(c)
                O("vector", "tensor_scalar", yb[c][:, :], sc1[:, :],
                  P("ln2w", dcol), P("ln2b", dcol), ALU.mult, ALU.add,
                  inc=True)
            v_yb = self.semn["vm"]

            W("tensor", "dg", g_w1[l])
            W("tensor", "dg", g_w2[l])
            W("tensor", "vm", v_yb)
            for k in range(NCK):
                for ft in range(16):
                    bank = ft % 4
                    pairs = [(w1sb[:, c * DFI + ft * 128:c * DFI + ft * 128 + 128],
                              yb[c][:, k * 512:(k + 1) * 512]) for c in range(4)]
                    self.mm_group(bank, pairs)
                    self.evac(bank, "scalar", lambda b=bank, ft=ft, l=l:
                              O("scalar", "activation",
                                ffn1[:, ft * 512:(ft + 1) * 512],
                                self.psum[b][:, :], AF.Silu, scale=1.0,
                                bias=P("b1", l * 16 + ft), inc=True))
                a_f1 = self.semn["am"]
                for dtile in range(4):
                    bank = 4 + (dtile % 2)
                    pairs = [(w2sb[:, ft * DI + dtile * 128:
                                   ft * DI + dtile * 128 + 128],
                              ffn1[:, ft * 512:(ft + 1) * 512])
                             for ft in range(16)]
                    self.mm_group(bank, pairs, waits=[("am", a_f1)])
                    self.evac(bank, "vector", lambda b=bank, d=dtile, k=k, l=l:
                              O("vector", "scalar_tensor_tensor",
                                h_t[d][:, k * 512:(k + 1) * 512],
                                self.psum[b][:, :], P("b2", l * 4 + d),
                                h_t[d][:, k * 512:(k + 1) * 512],
                                ALU.add, ALU.add, inc=True))
                drain_pc(3)
            t_w2 = self.semn["tm"]
            t_w2_prev[0] = t_w2
            a_prev[0] = self.semn["am"]
            v_hdone = self.semn["vm"]
            if (l + 1) < NLAYER:
                W("gpsimd", "tm", t_w2)
                g_w1[l + 1] = DMA("gpsimd", w1sb[:, :], w1h[l + 1])
                g_w2[l + 1] = DMA("gpsimd", w2sb[:, :], w2h[l + 1])
            if f"h_l{l}" in dbg_aps:
                W("sync", "vm", v_hdone)
                W("sync", "am", self.semn["am"])
                for c in range(4):
                    DMA("sync", dbg_aps[f"h_l{l}"][c * 128:(c + 1) * 128, :],
                        h_t[c][:, :])
        v_layers = self.semn["vm"]

        W("gpsimd", "vm", v_layers)
        g_wout = DMA("gpsimd", woutsb[:, :], woutT)

        inner_ln_stats([("vm", v_layers), ("tm", t_w2_prev[0]),
                        ("am", a_prev[0])])
        W("vector", "tm", t_w2_prev[0])
        for c in range(4):
            normalize_to_sc1(c)
            O("vector", "tensor_scalar", yb[c][:, :], sc1[:, :],
              P("lnlw", c), P("lnlb", c), ALU.mult, ALU.add, inc=True)
        v_hlb = self.semn["vm"]

        st_ao = {}
        i_ao = 0
        for k in range(NCK):
            for vt in range(NKD):
                bank = vt % 4
                slv = i_ao % 2
                pairs = [(woutsb[:, c * D + vt * 128:c * D + vt * 128 + 128],
                          yb[c][:, k * 512:(k + 1) * 512]) for c in range(4)]
                self.mm_group(bank, pairs,
                              waits=([("dg", g_wout), ("vm", v_hlb)]
                                     if (k == 0 and vt == 0) else ()))
                ew = [("ds", st_ao[i_ao - 2])] if i_ao >= 2 else []
                self.evac(bank, "scalar", lambda b=bank, vt=vt, s=slv:
                          O("scalar", "activation", aoev[s][:, :],
                            self.psum[b][:, :], AF.Identity,
                            bias=P("bout", vt), inc=True), waits=ew)
                W("sync", "am", self.semn["am"])
                st_ao[i_ao] = DMA(
                    "sync", ao_fm[vt * 128:(vt + 1) * 128, k * 512:(k + 1) * 512],
                    aoev[slv][:, :])
                i_ao += 1
            drain_pc(6)
        v_ao_all = self.semn["ds"]

        v_jnk = {}
        st4 = {}
        v_p4tail = 0
        for tt in range(NTT):
            sl = tt % 2
            W("sync", "ds", st_ao[(tt // 4) * NKD + NKD - 1])
            if tt >= 2:
                W("sync", "vm", v_used[tt - 2])
                W("sync", "ds", st4.get(tt - 2, 0))
                W("sync", "vm", v_jnk[tt - 2])
            ld_ao = DMA("sync", jnkb4[sl][:, :],
                        ao_fm[0:D, tt * 128:(tt + 1) * 128], transpose=True)
            ld_x = DMA("sync", xtm4[sl][:, :], x_in[tt * 128:(tt + 1) * 128, :])
            col = tt % 16
            cc = slice(col, col + 1)
            v_s, a_s = tm_stats(jnkb4[sl][:, :], aotm4[:, :], col, ld_ao)
            W("scalar", "ds", ld_x)
            W("scalar", "vm", v_s)
            O("scalar", "activation", xtm4[sl][:, HH:D], xtm4[sl][:, HH:D],
              AF.Identity, bias=nmr[:, cc], scale=1.0, inc=True)
            a_xn = self.semn["am"]
            W("vector", "ds", ld_x)
            O("vector", "tensor_scalar", xtm4[sl][:, 0:HH], xtm4[sl][:, 0:HH],
              nmr[:, cc], None, ALU.add)
            W("vector", "am", a_xn)
            O("vector", "scalar_tensor_tensor", xtm4[sl][:, :], jnkb4[sl][:, :],
              rst[:, cc], xtm4[sl][:, :], ALU.mult, ALU.add, inc=True)
            v_x1 = self.semn["vm"]
            W("sync", "vm", v_x1)
            st4[tt] = DMA("sync", x1_tm[tt * 128:(tt + 1) * 128, :],
                          xtm4[sl][:, :])
            v_s2, a_s2 = tm_stats(xtm4[sl][:, :], aotm4[:, :], col, 0,
                                  vwaits=[("vm", v_x1)])
            W("scalar", "vm", v_s2)
            O("scalar", "activation", jnkb4[sl][:, HH:D], xtm4[sl][:, HH:D],
              AF.Identity, bias=nmr[:, cc], scale=rst[:, cc], inc=True)
            a_xb = self.semn["am"]
            O("vector", "tensor_scalar", jnkb4[sl][:, 0:HH], xtm4[sl][:, 0:HH],
              rst[:, cc], nmr[:, cc], ALU.mult, ALU.add, inc=True)
            v_xb = self.semn["vm"]
            v_used[tt] = v_xb
            v_jnk[tt] = v_xb
            W("sync", "am", a_xb)
            W("sync", "vm", v_xb)
            st4[tt] = DMA("sync", xlnb_tm[tt * 128:(tt + 1) * 128, :],
                          jnkb4[sl][:, :])
            if tt % 4 == 3:
                drain_pc(4)
        v_p4 = self.semn["ds"]
        v_p4vm = self.semn["vm"]
        if "x1d" in dbg_aps:
            W("sync", "ds", v_p4)
            DMA("sync", dbg_aps["x1d"], x1_tm)
            W("gpsimd", "ds", v_p4)
            DMA("gpsimd", dbg_aps["aod"], ao_fm)
            DMA("gpsimd", dbg_aps["xlnbd"], xlnb_tm)

        drain_pc(10 ** 9)
        fchunks = [(2 * FCH, NFT), (0, FCH), (FCH, 2 * FCH)]
        ft_order = [ft for (f0, f1) in fchunks for ft in range(f0, f1)]
        ft_next = {ft_order[i]: ft_order[i + 1]
                   for i in range(len(ft_order) - 1)}
        st_ffo = {}
        self.gsil_v = {}
        gu_t = {}
        ds_t = {}
        t_guplast = {}
        ldx_next = [0]
        for blk in range(NCK):
            tsl = slice(blk * 512, (blk + 1) * 512)
            if blk == 0:
                W("sync", "ds", st4[3])
                ldx = 0
                for kv in range(NKD):
                    ldx = DMA("sync", xlnffn[:, kv * 512:(kv + 1) * 512],
                              xlnb_tm[tsl, kv * 128:(kv + 1) * 128],
                              transpose=True)
            else:
                ldx = ldx_next[0]
            pend = {}

            def issue_gu_loads(ft, blk=blk):
                gu2 = ft % 2
                if gu2 in gu_t:
                    W("gpsimd", "tm", gu_t[gu2])
                lg = DMA("gpsimd", gusb[gu2][:, :], gateT[ft])
                lu = DMA("gpsimd", gusb[2 + gu2][:, :], upT[ft])
                pend[ft] = (lg, lu)

            first_issued = [False]
            for fci, (f0, f1) in enumerate(fchunks):
                first_chunk = (fci == 0)
                v_h = 0
                if not first_issued[0]:
                    issue_gu_loads(f0)
                    first_issued[0] = True
                for ft in range(f0, f1):
                    gu = ft % 2
                    nft_ = ft_next.get(ft)
                    if nft_ is not None and nft_ not in pend:
                        issue_gu_loads(nft_)
                    ldg, ldu = pend.pop(ft)
                    bank_g, bank_u = ft % 4, 4 + ft % 4
                    pairs = [(gusb[gu][:, kv * 128:(kv + 1) * 128],
                              xlnffn[:, kv * 512:(kv + 1) * 512])
                             for kv in range(NKD)]
                    self.mm_group(bank_g, pairs, waits=[("dg", ldg), ("ds", ldx)])
                    gw = ([("vm", self.gsil_v[ft % 2])]
                          if (ft % 2) in self.gsil_v else [])
                    a_g = self.evac(bank_g, "scalar",
                                    lambda b=bank_g, ft=ft, s=ft % 2:
                                    O("scalar", "activation", gsil[s][:, :],
                                      self.psum[b][:, :], AF.Silu,
                                      scale=P("gs", ft), inc=True, bias=c_zero[:, 0:1]), waits=gw)
                    pairs = [(gusb[2 + gu][:, kv * 128:(kv + 1) * 128],
                              xlnffn[:, kv * 512:(kv + 1) * 512])
                             for kv in range(NKD)]
                    gu_t[gu] = self.mm_group(bank_u, pairs,
                                             waits=[("dg", ldu)])
                    t_guplast[blk] = gu_t[gu]
                    v_h = self.evac(bank_u, "vector",
                                    lambda b=bank_u, ft=ft, f0=f0, s=ft % 2:
                                    O("vector", "scalar_tensor_tensor",
                                      hch[:, (ft - f0) * 512:(ft - f0 + 1) * 512],
                                      self.psum[b][:, :], P("us", ft),
                                      gsil[s][:, :], ALU.mult, ALU.mult,
                                      inc=True),
                                    waits=[("am", a_g)])
                    self.gsil_v[ft % 2] = v_h
                if blk == 0 and first_chunk and "hch0" in dbg_aps:
                    W("gpsimd", "vm", v_h)
                    DMA("gpsimd", dbg_aps["hch0"], hch[:, :])
                for dg_i in range(4):
                    pend_d = {}

                    def issue_d_loads(ft, dg_i=dg_i):
                        sl4 = ft % 4
                        if sl4 in ds_t:
                            W("gpsimd", "tm", ds_t[sl4])
                        pend_d[ft] = DMA(
                            "gpsimd", dsb[sl4][:, :],
                            downT[ft][:, dg_i * 1024:dg_i * 1024 + 1024])

                    for fi in range(f0, min(f0 + 4, f1)):
                        issue_d_loads(fi)
                    for ft in range(f0, f1):
                        sl4 = ft % 4
                        W("tensor", "dg", pend_d.pop(ft))
                        if ft == f0:
                            W("tensor", "vm", v_h)
                            for b in range(8):
                                s, v = self.bank_r[b]
                                W("tensor", s, v)
                        for di in range(8):
                            O("tensor", "matmul", self.psum[di][:, :],
                              dsb[sl4][:, di * 128:di * 128 + 128],
                              hch[:, (ft - f0) * 512:(ft - f0 + 1) * 512],
                              start=(ft == f0), stop=(ft == f1 - 1),
                              inc=(di == 7))
                        ds_t[sl4] = self.semn["tm"]
                        if ft + 4 < f1:
                            issue_d_loads(ft + 4)
                    t_dg = self.semn["tm"]
                    for b in range(8):
                        self.bank_w[b] = ("tm", t_dg)
                    W("vector", "tm", t_dg)
                    if blk == 0 and first_chunk and dg_i == 0:
                        W("vector", "vm", v_p4vm)
                        W("vector", "ds", v_p4)
                    for di in range(8):
                        dtile = dg_i * 8 + di
                        if first_chunk:
                            O("vector", "tensor_copy", accum[dtile][:, :],
                              self.psum[di][:, :], inc=True)
                        else:
                            O("vector", "tensor_tensor", accum[dtile][:, :],
                              accum[dtile][:, :], self.psum[di][:, :], ALU.add,
                              inc=True)
                        self.bank_r[di] = ("vm", self.semn["vm"])
            if blk + 1 < NCK:
                nts = slice((blk + 1) * 512, (blk + 2) * 512)
                W("sync", "ds", st4[4 * (blk + 1) + 3])
                W("sync", "tm", t_guplast[blk])
                lx = 0
                for kv in range(NKD):
                    lx = DMA("sync", xlnffn[:, kv * 512:(kv + 1) * 512],
                             xlnb_tm[nts, kv * 128:(kv + 1) * 128],
                             transpose=True)
                ldx_next[0] = lx
            for dtile in range(NKD):
                s = dtile % 2
                if dtile >= 2:
                    W("vector", "ds", st_ffo[(blk, dtile - 2)])
                O("vector", "tensor_scalar", aoev[s][:, :], accum[dtile][:, :],
                  P("ds", dtile), None, ALU.mult, inc=True)
                W("sync", "vm", self.semn["vm"])
                st_ffo[(blk, dtile)] = DMA(
                    "sync", ffo_fm[dtile * 128:(dtile + 1) * 128, tsl],
                    aoev[s][:, :])
        v_ffo_all = self.semn["ds"]
        if "ffod" in dbg_aps:
            W("gpsimd", "ds", v_ffo_all)
            DMA("gpsimd", dbg_aps["ffod"], ffo_fm)

        v_o = {}
        st6 = {}
        for tt in range(NTT):
            sl = tt % 2
            if tt == 0:
                W("sync", "ds", v_ffo_all)
            if tt >= 2:
                W("sync", "vm", v_o[tt - 2])
                W("sync", "ds", st6.get(tt - 2, 0))
            ld_fo = DMA("sync", jnkb4[sl][:, :],
                        ffo_fm[0:D, tt * 128:(tt + 1) * 128], transpose=True)
            ld_x1 = DMA("sync", xtm4[sl][:, :], x1_tm[tt * 128:(tt + 1) * 128, :])
            col = tt % 16
            cc = slice(col, col + 1)
            v_s6, _ = tm_stats(jnkb4[sl][:, :], aotm4[:, :], col, ld_fo)
            W("scalar", "ds", ld_x1)
            W("scalar", "vm", v_s6)
            O("scalar", "activation", xtm4[sl][:, HH:D], xtm4[sl][:, HH:D],
              AF.Identity, bias=nmr[:, cc], scale=1.0, inc=True)
            a_xn6 = self.semn["am"]
            W("vector", "ds", ld_x1)
            O("vector", "tensor_scalar", xtm4[sl][:, 0:HH], xtm4[sl][:, 0:HH],
              nmr[:, cc], None, ALU.add)
            W("vector", "am", a_xn6)
            O("vector", "scalar_tensor_tensor", xtm4[sl][:, :], jnkb4[sl][:, :],
              rst[:, cc], xtm4[sl][:, :], ALU.mult, ALU.add, inc=True)
            v_o[tt] = self.semn["vm"]
            W("sync", "vm", v_o[tt])
            st6[tt] = DMA("sync", out_d[tt * 128:(tt + 1) * 128, :],
                          xtm4[sl][:, :])
        W("sync", "ds", self.semn["ds"])

        with nc.Block() as blk_:
            for meth, eng in ((blk_.sync, "sync"), (blk_.gpsimd, "gpsimd"),
                              (blk_.tensor, "tensor"), (blk_.vector, "vector"),
                              (blk_.scalar, "scalar")):
                ops = self.q[eng]

                def body(e, ops=ops):
                    for f in ops:
                        f(e)
                meth(body)
        return nc


def _host_layout(inputs):
    rs = np.ascontiguousarray

    def pack_chan(a):
        return rs(np.asarray(a).reshape(4, 4, 128).transpose(2, 0, 1)
                  .reshape(128, 16))

    W1 = np.asarray(inputs["L0_attn_W1"])
    W2 = np.asarray(inputs["L0_attn_W2"])
    g = {}
    g["winT"] = rs(np.transpose(np.asarray(inputs["L0_attn_Win"]))
                   .reshape(32, 128, 4, 128).transpose(1, 0, 2, 3)
                   .reshape(128, 32 * 512))
    g["w1h"] = rs(np.stack([np.transpose(W1[l]).reshape(4, 128, 2048)
                            .transpose(1, 0, 2).reshape(128, 4 * 2048)
                            for l in range(4)]))
    g["w2h"] = rs(np.stack([np.transpose(W2[l]).reshape(16, 128, 512)
                            .transpose(1, 0, 2).reshape(128, 16 * 512)
                            for l in range(4)]))
    g["woutT"] = rs(np.transpose(np.asarray(inputs["L0_attn_Wout"]))
                    .reshape(4, 128, 4096).transpose(1, 0, 2)
                    .reshape(128, 4 * 4096))
    def low_bytes(a):
        return rs(rs(a).view(np.int8)[..., 0::4])

    for nm, key in [("gateT", "L0_ffn_gate_w"), ("upT", "L0_ffn_up_w")]:
        w = np.asarray(inputs[key])
        g[nm] = low_bytes(w.reshape(NFT, 128, NKD, 128).transpose(0, 3, 2, 1)
                          .reshape(NFT, 128, NKD * 128))
    g["downT"] = low_bytes(np.transpose(np.asarray(inputs["L0_ffn_down_w"]))
                           .reshape(NFT, 128, NKD * 128))
    pp = {"ln1w": pack_chan(inputs["L0_attn_ln1_w"]),
          "ln1b": pack_chan(inputs["L0_attn_ln1_b"]),
          "ln2w": pack_chan(inputs["L0_attn_ln2_w"]),
          "ln2b": pack_chan(inputs["L0_attn_ln2_b"]),
          "scl": pack_chan(inputs["L0_attn_sc_lin"]),
          "b2": pack_chan(inputs["L0_attn_b2"]),
          "pre": pack_chan(inputs["L0_attn_phz_re"]),
          "pim": pack_chan(inputs["L0_attn_phz_im"]),
          "lcr": pack_chan(inputs["L0_attn_lc_re"]),
          "lci": pack_chan(inputs["L0_attn_lc_im"]),
          "b1": rs(np.asarray(inputs["L0_attn_b1"]).reshape(4, 16, 128)
                   .transpose(2, 0, 1).reshape(128, 64)),
          "bin": rs(np.asarray(inputs["L0_attn_bin"]).reshape(4, 128).T),
          "lnlw": rs(np.asarray(inputs["L0_attn_lnl_w"]).reshape(4, 128).T),
          "lnlb": rs(np.asarray(inputs["L0_attn_lnl_b"]).reshape(4, 128).T),
          "bout": rs(np.asarray(inputs["L0_attn_bout"]).reshape(32, 128).T),
          "gs": rs(np.asarray(inputs["L0_ffn_gate_s"]).reshape(NFT, 128).T),
          "us": rs(np.asarray(inputs["L0_ffn_up_s"]).reshape(NFT, 128).T),
          "ds": rs(np.asarray(inputs["L0_ffn_down_s"]).reshape(32, 128).T)}
    for k, v in pp.items():
        g["pp_" + k] = rs(np.asarray(v, np.float32))
    return g


def kernel(**inputs):
    T = 16384 // NCORE
    shared = _host_layout(inputs)
    x = np.asarray(inputs["L0_attn_x"])[0]
    in_maps = []
    for ci in range(NCORE):
        m = dict(shared)
        m["x_in"] = np.ascontiguousarray(x[ci * T:(ci + 1) * T, :], np.float32)
        m["pp_toff"] = np.full((128, 1), float(ci * T), np.float32)
        msk = np.zeros((128, 8), np.float32)
        msk[:, :7] = (np.arange(7)[None, :] < ci)
        msk[:, 7] = 1.0 if ci == 0 else 0.0
        m["pp_mask"] = msk
        in_maps.append(m)
    kb = KB(T)
    nc = kb.build()
    res = run_bass_kernel_spmd(nc, in_maps, core_ids=list(range(NCORE)))
    out = np.concatenate([np.asarray(r["out"]) for r in res.results], axis=0)
    return out[None, :, :].astype(np.float32)
